# version 01
# baseline (speedup 1.0000x reference)
import numpy as np
import ml_dtypes
from contextlib import ExitStack
import concourse.bass as bass
import concourse.mybir as mybir
from concourse.bass_utils import run_bass_kernel_spmd

F32 = mybir.dt.float32
BF16 = mybir.dt.bfloat16
AF = mybir.ActivationFunctionType
ALU = mybir.AluOpType
AX = mybir.AxisListType
bfloat16 = ml_dtypes.bfloat16

NCORE = 8
SEQ = 8192
DM = 2048
NDC = 16
TPC = 1024
NTB = 2
NGB = SEQ // 512

CORE_BLOCKS = [[c, NGB - 1 - c] for c in range(NCORE)]
BLOCK_OWNER = {}
for _c in range(NCORE):
    for _s, _g in enumerate(CORE_BLOCKS[_c]):
        BLOCK_OWNER[_g] = (_c, _s)

H = 16
NOPE_D, ROPE_D, V_D = 64, 32, 64
Q_RANK, KV_RANK = 768, 256
QR_C, KV_C = Q_RANK // 128, KV_RANK // 128
CAT_D = 97
VA = V_D + 1
SCALE0 = 1.0 / float(np.sqrt(NOPE_D + ROPE_D))
DK = 128
SCALE1 = 1.0 / float(np.sqrt(DK))
NEXP, MOE_I = 8, 1408
MOE_HC = MOE_I // 128
SI = 2816
SI_C = SI // 128
FH = 8192
FH_C = FH // 128

NEG = -1.0e30
DEBUG_PLAN = False
NSL_E0 = 8
NSL_L0 = 16
EPS_LN = 1e-5
EPS_RMS = 1e-6



def host_constants(core):
    cst = {}
    pos = np.concatenate([np.arange(g * 512, g * 512 + 512) for g in CORE_BLOCKS[core]])
    f0 = 1.0 / (10000.0 ** (np.arange(0, ROPE_D, 2, dtype=np.float64) / ROPE_D))
    ang0 = np.outer(pos.astype(np.float64), f0)
    c0 = np.cos(ang0)
    t0 = np.zeros((128, TPC), np.float64)
    for r in range(128):
        t0[r] = c0[:, (r % 32) // 2]
    cst["cos0q"] = (t0 * SCALE0).astype(np.float32)
    cst["cos0k"] = (t0[:32] / H).astype(np.float32)
    f1 = 1.0 / (10000.0 ** (np.arange(0, DK, 2, dtype=np.float64) / DK))
    ang1 = np.outer(pos.astype(np.float64), f1)
    c1, s1 = np.cos(ang1), np.sin(ang1)
    t1c = np.zeros((128, TPC), np.float64)
    t1s = np.zeros((128, TPC), np.float64)
    for r in range(128):
        t1c[r] = c1[:, r % 64]
        t1s[r] = s1[:, r % 64]
    cst["cos1"] = t1c.astype(np.float32)
    cst["sin1"] = t1s.astype(np.float32)
    cst["cos1q"] = (t1c * SCALE1).astype(np.float32)
    cst["sin1q"] = (t1s * SCALE1).astype(np.float32)
    i_ = np.arange(128)[:, None]
    j_ = np.arange(512)[None, :]
    mA = np.stack([np.where(j_ - i_ > d, NEG, 0.0) for d in (0, 128, 256, 384)])
    mB = np.stack([np.where(i_ - j_ > d, NEG, 0.0) for d in (0, -128, -256, -384)])
    cst["maskA"] = mA.astype(np.float32)
    cst["maskB"] = mB.astype(np.float32)
    R32 = np.zeros((32, 32), np.float32)
    for p_ in range(16):
        R32[2 * p_, 2 * p_] = 1.0
        R32[2 * p_, 2 * p_ + 1] = -1.0
        R32[2 * p_ + 1, 2 * p_] = 1.0
        R32[2 * p_ + 1, 2 * p_ + 1] = 1.0
    R4 = np.zeros((128, 128), np.float32)
    for hh in range(4):
        R4[hh * 32:(hh + 1) * 32, hh * 32:(hh + 1) * 32] = R32
    cst["R4T"] = np.ascontiguousarray(R4.T).astype(bfloat16)
    cst["R32T"] = np.ascontiguousarray(R32.T).astype(bfloat16)
    Rn = np.zeros((128, 128), np.float32)
    for i in range(64):
        Rn[i, 64 + i] = -1.0
        Rn[64 + i, i] = 1.0
    cst["RneoxT"] = np.ascontiguousarray(Rn.T).astype(bfloat16)
    cst["id_f32"] = np.eye(128, dtype=np.float32)
    sel = np.zeros((8, 8 * 128), np.float32)
    for e in range(8):
        sel[e, e * 128:(e + 1) * 128] = 1.0
    cst["moe_sel"] = sel
    return cst


MASKA_D = {0: 0, 128: 1, 256: 2, 384: 3}
MASKB_D = {0: 0, -128: 1, -256: 2, -384: 3}



class Sem:
    def __init__(self, h):
        self.h = h
        self.n = 0

    def bump(self, k=1):
        self.n += k
        return self.n


class Prog:
    ENGINES = ("tensor", "vector", "scalar", "gpsimd", "sync")

    def __init__(self, nc, es):
        self.nc = nc
        self.es = es
        self.q = {e: [] for e in self.ENGINES}
        self._sems = {}
        self.taps = {}

    def sem(self, name):
        if name not in self._sems:
            h = self.es.enter_context(self.nc.semaphore(name))
            self._sems[name] = Sem(h)
        return self._sems[name]

    _NOCHAIN = ("nop", "dma_start", "collective_compute", "wait_ge", "sem_inc")

    def op(self, eng, meth, *args, inc=None, incv=1, **kw):
        chained = eng in ("vector", "scalar", "gpsimd") and meth not in self._NOCHAIN
        if chained:
            if not hasattr(self, "last"):
                self.last = {}
            prev = self.last.get(eng)
            if prev is not None:
                self.q[eng].append(lambda e, s=prev[0].h, v=prev[1]: e.wait_ge(s, v))
            if inc is None:
                inc = self.sem("ch_" + eng)
                incv = 1
        post = inc.bump(incv) if inc is not None else None
        if chained:
            self.last[eng] = (inc, post)

        import traceback
        stk = traceback.extract_stack()[-3:-1] if DEBUG_PLAN else None

        def f(e, meth=meth, args=args, kw=kw, inc=inc, incv=incv, stk=stk):
            try:
                ins = getattr(e, meth)(*args, **kw)
            except Exception:
                if stk:
                    print("PLAN SITE:", "".join(traceback.format_list(stk)))
                raise
            if inc is not None:
                ins.then_inc(inc.h, incv)
        self.q[eng].append(f)
        return post

    def wait(self, eng, sem, val):
        self.q[eng].append(lambda e, s=sem.h, v=val: e.wait_ge(s, v))

    def barrier(self, engines=None):
        engines = engines or self.ENGINES
        b = self.sem("bar")
        for eng in ("tensor", "sync"):
            if eng in engines:
                self.op(eng, "nop", inc=b)
        nv = b.n
        lasts = []
        if hasattr(self, "last"):
            for e in ("vector", "scalar", "gpsimd"):
                lp = self.last.get(e)
                if lp is not None:
                    lasts.append(lp)
        for eng in engines:
            self.wait(eng, b, nv)
            for (s, v) in lasts:
                self.wait(eng, s, v)

    def finish(self):
        nc = self.nc
        with nc.Block() as block:
            @block.tensor
            def _(t):
                for f in self.q["tensor"]:
                    f(t)

            @block.vector
            def _(v):
                for f in self.q["vector"]:
                    f(v)

            @block.scalar
            def _(s):
                for f in self.q["scalar"]:
                    f(s)

            @block.gpsimd
            def _(g):
                for f in self.q["gpsimd"]:
                    f(g)

            @block.sync
            def _(s):
                for f in self.q["sync"]:
                    f(s)


class WStream:

    def __init__(self, P, name, nbuf, shape, es=None):
        self.P = P
        self.nbuf = nbuf
        es = es if es is not None else P.es
        self.tiles = [es.enter_context(P.nc.sbuf_tensor(f"{name}{i}", list(shape), BF16))
                      for i in range(nbuf)]
        self.ld = [P.sem(f"{name}_ld{i}") for i in range(nbuf)]
        self.done_pts = {}
        self.nload = 0

    def load(self, in_ap, out_ap_fn=None):
        t = self.nload
        slot = t % self.nbuf
        if t >= self.nbuf:
            s, v = self.done_pts[t - self.nbuf]
            self.P.wait("gpsimd", s, v)
        out = self.tiles[slot]
        out_ap = out_ap_fn(out) if out_ap_fn else out[:]
        self.P.op("gpsimd", "dma_start", inc=self.ld[slot], incv=16, out=out_ap, in_=in_ap)
        self.nload += 1
        return t, out

    def begin(self, eng, t):
        self.P.wait(eng, self.ld[t % self.nbuf], 16 * (t // self.nbuf + 1))

    def done(self, t, sem, val):
        self.done_pts[t] = (sem, val)


class XStream:

    def __init__(self, P, name, queue, nslots):
        self.P = P
        self.queue = queue
        self.nslots = nslots
        self.sems = [P.sem(f"{name}_s{i}") for i in range(nslots)]
        self.cnt = [0] * nslots

    def dma(self, slot, out, in_, **kw):
        self.cnt[slot] += 1
        self.P.op(self.queue, "dma_start", inc=self.sems[slot], incv=16,
                  out=out, in_=in_, **kw)
        return slot, 16 * self.cnt[slot]

    def wait(self, eng, token):
        slot, v = token
        self.P.wait(eng, self.sems[slot], v)


def bcast_row(P, row_ap, width, ps, dst_ap, ones_row_f32, pe, dv, src_ready=None):
    if src_ready is not None:
        P.wait("tensor", src_ready[0], src_ready[1])
    v = P.op("tensor", "matmul", ps[:, 0:width], ones_row_f32[0:1, 0:128], row_ap,
             start=True, stop=True, inc=pe)
    P.wait("vector", pe, v)
    return P.op("vector", "tensor_copy", inc=dv, out=dst_ap, in_=ps[:, 0:width])



def emit_l0(P, nc, D, shared, out_dram, stop_after=None):
    es = P.es
    pe = P.sem("pe")
    dv = P.sem("dv")
    ac = P.sem("ac")
    xld = P.sem("xld")
    kvld = P.sem("kvld")
    agst = P.sem("agst")
    cc = P.sem("cc")
    ost = P.sem("ost")
    sb = P.nc.sbuf_tensor
    ones_col = shared["ones_col"]
    ones_row = shared["ones_row"]
    id_f32 = shared["id_f32"]
    PS = shared["PS"]
    psT = shared["psT"]
    gates = {}

    def gate_wait(eng, res):
        if res in gates:
            s, v = gates[res]
            P.wait(eng, s, v)

    def gate_set(res, sem, val):
        gates[res] = (sem, val)

    l0es = ExitStack()
    xn = l0es.enter_context(sb("xn0", [128, NDC, TPC], BF16))
    g_sb = l0es.enter_context(sb("g0", [128, NDC, TPC], BF16))
    rows = l0es.enter_context(sb("rows0", [1, 4 * TPC], F32))
    nm_row = rows[:, 0:TPC]
    rstd_row = rows[:, TPC:2 * TPC]
    row_sc = rows[:, 2 * TPC:3 * TPC]
    row_sc2 = rows[:, 3 * TPC:4 * TPC]

    with ExitStack() as ph:
        xbuf = [ph.enter_context(sb(f"xb{i}", [128, TPC], F32)) for i in range(2)]
        sqt = [ph.enter_context(sb(f"sq{i}", [128, 512], F32)) for i in range(2)]
        nm_b = ph.enter_context(sb("nm_b", [128, TPC], F32))
        rstd_b = ph.enter_context(sb("rstd_b", [128, TPC], F32))
        xsc = ph.enter_context(sb("xsc", [128, TPC], F32))
        xr = D["x"].rearrange("(c p) t -> c p t", p=128)
        xs = XStream(P, "x0", "sync", 2)
        for c in range(NDC):
            slot = c % 2
            gate_wait("sync", f"xb{slot}")
            tok = xs.dma(slot, xbuf[slot][:], xr[c])
            xs.wait("scalar", tok)
            for tb in range(NTB):
                gate_wait("scalar", f"sq{slot}")
                a = P.op("scalar", "activation", sqt[slot][:, :],
                         xbuf[slot][:, tb * 512:(tb + 1) * 512], AF.Square, inc=ac)
                P.wait("tensor", ac, a)
                P.op("tensor", "matmul", PS[3 + tb][0:1, :], ones_col[:, :],
                     xbuf[slot][:, tb * 512:(tb + 1) * 512],
                     start=(c == 0), stop=(c == NDC - 1))
                u = P.op("tensor", "matmul", PS[5 + tb][0:1, :], ones_col[:, :], sqt[slot][:, :],
                         start=(c == 0), stop=(c == NDC - 1), inc=pe)
                gate_set(f"sq{slot}", pe, u)
            gate_set(f"xb{slot}", pe, u)
        P.wait("vector", pe, pe.n)
        for tb in range(NTB):
            cs = slice(tb * 512, (tb + 1) * 512)
            P.op("vector", "tensor_scalar", nm_row[:, cs], PS[3 + tb][0:1, :], -1.0 / DM, None, ALU.mult)
            P.op("vector", "tensor_scalar", row_sc[:, cs], PS[5 + tb][0:1, :], 1.0 / DM, EPS_LN,
                 ALU.mult, ALU.add)
            P.op("vector", "tensor_mul", row_sc2[:, cs], nm_row[:, cs], nm_row[:, cs])
            v = P.op("vector", "tensor_sub", row_sc[:, cs], row_sc[:, cs], row_sc2[:, cs], inc=dv)
        P.wait("scalar", dv, dv.n)
        P.op("scalar", "activation", row_sc2[:, 0:TPC], row_sc[:, 0:TPC], AF.Ln)
        a = P.op("scalar", "activation", rstd_row[:, 0:TPC], row_sc2[:, 0:TPC], AF.Exp,
                 scale=-0.5, inc=ac)
        P.wait("tensor", ac, a)
        for tb in range(NTB):
            cs = slice(tb * 512, (tb + 1) * 512)
            gate_wait("tensor", "ps0")
            gate_wait("tensor", "ps1")
            v = P.op("tensor", "matmul", PS[0][:, :], ones_row[0:1, :], nm_row[:, cs],
                     start=True, stop=True)
            v = P.op("tensor", "matmul", PS[1][:, :], ones_row[0:1, :], rstd_row[:, cs],
                     start=True, stop=True, inc=pe)
            P.wait("vector", pe, v)
            u0 = P.op("vector", "tensor_copy", out=nm_b[:, cs], in_=PS[0][:, :], inc=dv)
            u = P.op("vector", "tensor_copy", out=rstd_b[:, cs], in_=PS[1][:, :], inc=dv)
            gate_set("ps0", dv, u0)
            gate_set("ps1", dv, u)
        for c in range(NDC):
            slot = c % 2
            gate_wait("sync", f"xb{slot}")
            tok = xs.dma(slot, xbuf[slot][:], xr[c])
            xs.wait("vector", tok)
            P.op("vector", "tensor_add", xsc[:, :], xbuf[slot][:, :], nm_b[:, :])
            u = P.op("vector", "tensor_mul", xn[:, c, :], xsc[:, :], rstd_b[:, :], inc=dv)
            gate_set(f"xb{slot}", dv, u)
    P.barrier()
    if stop_after == "ln":
        P.wait("gpsimd", dv, dv.n)
        for c in range(NDC):
            P.op("gpsimd", "dma_start", inc=ost, incv=16,
                 out=out_dram.rearrange("(c p) t -> c p t", p=128)[c], in_=xn[:, c, :])
        P.wait("gpsimd", ost, ost.n)
        l0es.close()
        return

    with ExitStack() as ph:
        cq = ph.enter_context(sb("cq", [128, QR_C, TPC], BF16))
        ckv = ph.enter_context(sb("ckv", [128, KV_C, TPC], BF16))
        qcat = ph.enter_context(sb("qcat", [CAT_D, H, TPC], BF16))
        kcat = ph.enter_context(sb("kcat", [CAT_D, H, TPC], BF16))
        vaug = ph.enter_context(sb("vaug", [128, 8, H * VA], BF16))
        sqt = [ph.enter_context(sb(f"qsq{i}", [128, 512], F32)) for i in range(2)]
        rb = ph.enter_context(sb("rbq", [128, TPC], F32))
        rsc = [ph.enter_context(sb(f"rsc{i}", [128, 512], BF16)) for i in range(2)]
        ws = WStream(P, "w0", 2, (128, NDC, 128), es=ph)
        cst_l0 = {}
        for cname, shp, cdt in [("R4T", [128, 128], BF16), ("R32T", [32, 32], BF16),
                                ("cos0q", [128, TPC], F32), ("cos0k", [32, TPC], F32)]:
            tt = ph.enter_context(sb(cname + "_s", shp, cdt))
            P.op("sync", "dma_start", inc=P.sem("cld"), incv=16, out=tt[:, :], in_=D[cname])
            cst_l0[cname] = tt
        P.wait("tensor", P.sem("cld"), P.sem("cld").n)
        P.wait("vector", P.sem("cld"), P.sem("cld").n)
        P.op("vector", "memset", qcat[:, :, :], 0.0)
        P.op("vector", "memset", kcat[:, :, :], 1.0)
        P.op("vector", "memset", vaug[:, :, :], 1.0, inc=P.sem("dv"))

        def proj_block(wt_ap, nk, rhs_fn, t, psum, first_last=(True, True)):
            ws.begin("tensor", t)
            for k in range(nk):
                v = P.op("tensor", "matmul", psum, ws.tiles[t % ws.nbuf][:, k, :], rhs_fn(k),
                         start=(k == 0 and first_last[0]), stop=(k == nk - 1 and first_last[1]),
                         inc=(pe if k == nk - 1 else None))
            return v

        wcq = D["w_cq"].rearrange("(c p) m -> p c m", p=128)
        for rc in range(QR_C):
            t, _ = ws.load(wcq[:, :, rc * 128:(rc + 1) * 128])
            for tb in range(NTB):
                ps = PS[rc % 2]
                gate_wait("tensor", f"ps{rc % 2}")
                v = proj_block(None, NDC, lambda k, tb=tb: xn[:, k, tb * 512:(tb + 1) * 512],
                               t, ps[:, :])
                if tb == NTB - 1:
                    ws.done(t, pe, v)
                P.wait("scalar", pe, v)
                P.op("scalar", "activation", cq[:, rc, tb * 512:(tb + 1) * 512], ps[:, :], AF.Copy)
                gate_wait("scalar", f"qsq{rc % 2}")
                a = P.op("scalar", "activation", sqt[rc % 2][:, :], ps[:, :], AF.Square, inc=ac)
                gate_set(f"ps{rc % 2}", ac, a)
                P.wait("tensor", ac, a)
                u = P.op("tensor", "matmul", PS[5 + tb][0:1, :], ones_col[:, :], sqt[rc % 2][:, :],
                         start=(rc == 0), stop=(rc == QR_C - 1), inc=pe)
                gate_set(f"qsq{rc % 2}", pe, u)
        P.wait("vector", pe, pe.n)
        for tb in range(NTB):
            cs = slice(tb * 512, (tb + 1) * 512)
            v = P.op("vector", "tensor_scalar", row_sc[:, cs], PS[5 + tb][0:1, :],
                     1.0 / Q_RANK, EPS_RMS, ALU.mult, ALU.add, inc=dv)
        P.wait("scalar", dv, dv.n)
        P.op("scalar", "activation", row_sc2[:, 0:TPC], row_sc[:, 0:TPC], AF.Ln)
        a = P.op("scalar", "activation", row_sc[:, 0:TPC], row_sc2[:, 0:TPC], AF.Exp,
                 scale=-0.5, inc=ac)
        P.wait("tensor", ac, a)
        for tb in range(NTB):
            cs = slice(tb * 512, (tb + 1) * 512)
            gate_wait("tensor", "ps0")
            v = P.op("tensor", "matmul", PS[0][:, :], ones_row[0:1, :], row_sc[:, cs],
                     start=True, stop=True, inc=pe)
            P.wait("vector", pe, v)
            u = P.op("vector", "tensor_copy", out=rb[:, cs], in_=PS[0][:, :], inc=dv)
            gate_set("ps0", dv, u)
        for rc in range(QR_C):
            u = P.op("vector", "scalar_tensor_tensor", cq[:, rc, :], cq[:, rc, :],
                     shared["vecs"][:, shared["vi"]["qg"] + rc:shared["vi"]["qg"] + rc + 1],
                     rb[:, :], ALU.mult, ALU.mult, inc=dv)
        normq_done = dv.n

        wqn = D["w_q_nope"].rearrange("(c p) m -> p c m", p=128)
        P.wait("tensor", dv, normq_done)
        for oc in range(H * NOPE_D // 128):
            t, wt = ws.load(wqn[:, :, oc * 128:(oc + 1) * 128],
                            out_ap_fn=lambda o: o[:, 0:QR_C, :])
            for tb in range(NTB):
                ps = PS[(oc + tb) % 2]
                gate_wait("tensor", f"ps{(oc + tb) % 2}")
                ws.begin("tensor", t)
                for k in range(QR_C):
                    v = P.op("tensor", "matmul", ps[:, :], wt[:, k, :],
                             cq[:, k, tb * 512:(tb + 1) * 512],
                             start=(k == 0), stop=(k == QR_C - 1),
                             inc=(pe if k == QR_C - 1 else None))
                if tb == NTB - 1:
                    ws.done(t, pe, v)
                P.wait("vector", pe, v)
                cs = slice(tb * 512, (tb + 1) * 512)
                P.op("vector", "tensor_scalar", qcat[0:64, 2 * oc, cs], ps[0:64, :],
                     SCALE0, None, ALU.mult)
                u = P.op("vector", "tensor_scalar", qcat[0:64, 2 * oc + 1, cs], ps[64:128, :],
                         SCALE0, None, ALU.mult, inc=dv)
                gate_set(f"ps{(oc + tb) % 2}", dv, u)

        wqr = D["w_q_rope"].rearrange("(c p) m -> p c m", p=128)
        for oc4 in range(4):
            t, wt = ws.load(wqr[:, :, oc4 * 128:(oc4 + 1) * 128],
                            out_ap_fn=lambda o: o[:, 0:QR_C, :])
            for tb in range(NTB):
                ps = PS[(oc4 + tb) % 2]
                gate_wait("tensor", f"ps{(oc4 + tb) % 2}")
                ws.begin("tensor", t)
                for k in range(QR_C):
                    v = P.op("tensor", "matmul", ps[:, :], wt[:, k, :],
                             cq[:, k, tb * 512:(tb + 1) * 512],
                             start=(k == 0), stop=(k == QR_C - 1),
                             inc=(pe if k == QR_C - 1 else None))
                if tb == NTB - 1:
                    ws.done(t, pe, v)
                P.wait("scalar", pe, v)
                slot = tb % 2
                gate_wait("scalar", f"rsc{slot}")
                a = P.op("scalar", "activation", rsc[slot][:, :], ps[:, :], AF.Copy, inc=ac)
                gate_set(f"ps{(oc4 + tb) % 2}", ac, a)
                P.wait("tensor", ac, a)
                gate_wait("tensor", "ps4")
                v = P.op("tensor", "matmul", PS[4][:, :], cst_l0["R4T"][:, :], rsc[slot][:, :],
                         start=True, stop=True, inc=pe)
                P.wait("vector", pe, v)
                cs = slice(tb * 512, (tb + 1) * 512)
                u = P.op("vector", "tensor_tensor", rsc[slot][:, :], PS[4][:, :],
                         cst_l0["cos0q"][:, cs], ALU.mult, inc=dv)
                gate_set("ps4", dv, u)
                for hh in range(4):
                    u = P.op("vector", "tensor_copy",
                             out=qcat[64:96, oc4 * 4 + hh, cs],
                             in_=rsc[slot][hh * 32:(hh + 1) * 32, :], inc=dv)
                gate_set(f"rsc{slot}", dv, u)

        wckv = D["w_ckv"].rearrange("(c p) m -> p c m", p=128)
        for rc in range(KV_C):
            t, wt = ws.load(wckv[:, :, rc * 128:(rc + 1) * 128])
            for tb in range(NTB):
                ps = PS[(rc + tb) % 2]
                gate_wait("tensor", f"ps{(rc + tb) % 2}")
                ws.begin("tensor", t)
                for k in range(NDC):
                    v = P.op("tensor", "matmul", ps[:, :], wt[:, k, :],
                             xn[:, k, tb * 512:(tb + 1) * 512],
                             start=(k == 0), stop=(k == NDC - 1),
                             inc=(pe if k == NDC - 1 else None))
                if tb == NTB - 1:
                    ws.done(t, pe, v)
                P.wait("scalar", pe, v)
                P.op("scalar", "activation", ckv[:, rc, tb * 512:(tb + 1) * 512], ps[:, :], AF.Copy)
                gate_wait("scalar", f"qsq{rc % 2}")
                a = P.op("scalar", "activation", sqt[rc % 2][:, :], ps[:, :], AF.Square, inc=ac)
                gate_set(f"ps{(rc + tb) % 2}", ac, a)
                P.wait("tensor", ac, a)
                u = P.op("tensor", "matmul", PS[5 + tb][0:1, :], ones_col[:, :], sqt[rc % 2][:, :],
                         start=(rc == 0), stop=(rc == KV_C - 1), inc=pe)
                gate_set(f"qsq{rc % 2}", pe, u)
        P.wait("vector", pe, pe.n)
        for tb in range(NTB):
            cs = slice(tb * 512, (tb + 1) * 512)
            v = P.op("vector", "tensor_scalar", row_sc[:, cs], PS[5 + tb][0:1, :],
                     1.0 / KV_RANK, EPS_RMS, ALU.mult, ALU.add, inc=dv)
        P.wait("scalar", dv, dv.n)
        P.op("scalar", "activation", row_sc2[:, 0:TPC], row_sc[:, 0:TPC], AF.Ln)
        a = P.op("scalar", "activation", row_sc[:, 0:TPC], row_sc2[:, 0:TPC], AF.Exp,
                 scale=-0.5, inc=ac)
        P.wait("tensor", ac, a)
        for tb in range(NTB):
            cs = slice(tb * 512, (tb + 1) * 512)
            gate_wait("tensor", "ps0")
            v = P.op("tensor", "matmul", PS[0][:, :], ones_row[0:1, :], row_sc[:, cs],
                     start=True, stop=True, inc=pe)
            P.wait("vector", pe, v)
            u = P.op("vector", "tensor_copy", out=rb[:, cs], in_=PS[0][:, :], inc=dv)
            gate_set("ps0", dv, u)
        for rc in range(KV_C):
            u = P.op("vector", "scalar_tensor_tensor", ckv[:, rc, :], ckv[:, rc, :],
                     shared["vecs"][:, shared["vi"]["kvg"] + rc:shared["vi"]["kvg"] + rc + 1],
                     rb[:, :], ALU.mult, ALU.mult, inc=dv)
        normkv_done = dv.n

        wkn = D["w_k_nope"].rearrange("(c p) m -> p c m", p=128)
        P.wait("tensor", dv, normkv_done)
        for oc in range(H * NOPE_D // 128):
            t, wt = ws.load(wkn[:, :, oc * 128:(oc + 1) * 128],
                            out_ap_fn=lambda o: o[:, 0:KV_C, :])
            for tb in range(NTB):
                ps = PS[(oc + tb) % 2]
                gate_wait("tensor", f"ps{(oc + tb) % 2}")
                ws.begin("tensor", t)
                for k in range(KV_C):
                    v = P.op("tensor", "matmul", ps[:, :], wt[:, k, :],
                             ckv[:, k, tb * 512:(tb + 1) * 512],
                             start=(k == 0), stop=(k == KV_C - 1),
                             inc=(pe if k == KV_C - 1 else None))
                if tb == NTB - 1:
                    ws.done(t, pe, v)
                P.wait("vector", pe, v)
                cs = slice(tb * 512, (tb + 1) * 512)
                P.op("vector", "tensor_copy", out=kcat[0:64, 2 * oc, cs], in_=ps[0:64, :])
                u = P.op("vector", "tensor_copy", out=kcat[0:64, 2 * oc + 1, cs],
                         in_=ps[64:128, :], inc=dv)
                gate_set(f"ps{(oc + tb) % 2}", dv, u)

        wkr = D["w_k_rope"].rearrange("(c p) m -> p c m", p=128)
        t, wt = ws.load(wkr[:, :, :], out_ap_fn=lambda o: o[:, :, 0:32])
        for tb in range(NTB):
            ps = PS[2]
            gate_wait("tensor", "ps2")
            ws.begin("tensor", t)
            for k in range(NDC):
                v = P.op("tensor", "matmul", ps[0:32, :], wt[:, k, 0:32],
                         xn[:, k, tb * 512:(tb + 1) * 512],
                         start=(k == 0), stop=(k == NDC - 1),
                         inc=(pe if k == NDC - 1 else None))
            if tb == NTB - 1:
                ws.done(t, pe, v)
            P.wait("scalar", pe, v)
            slot = tb % 2
            gate_wait("scalar", f"rsc{slot}")
            a = P.op("scalar", "activation", rsc[slot][0:32, :], ps[0:32, :], AF.Copy, inc=ac)
            gate_set("ps2", ac, a)
            P.wait("tensor", ac, a)
            gate_wait("tensor", "ps4")
            v = P.op("tensor", "matmul", PS[4][0:32, :], cst_l0["R32T"][:, :], rsc[slot][0:32, :],
                     start=True, stop=True, inc=pe)
            P.wait("vector", pe, v)
            cs = slice(tb * 512, (tb + 1) * 512)
            u = P.op("vector", "tensor_tensor", rsc[slot][0:32, :], PS[4][0:32, :],
                     cst_l0["cos0k"][:, cs], ALU.mult, inc=dv)
            gate_set("ps4", dv, u)
            for h in range(H):
                u = P.op("vector", "tensor_copy", out=kcat[64:96, h, cs],
                         in_=rsc[slot][0:32, :], inc=dv)
            gate_set(f"rsc{slot}", dv, u)

        wv = D["w_v"].rearrange("(c p) m -> p c m", p=128)
        for oc in range(H * V_D // 128):
            t, wt = ws.load(wv[:, :, oc * 128:(oc + 1) * 128],
                            out_ap_fn=lambda o: o[:, 0:KV_C, :])
            for tb in range(NTB):
                ps = PS[(oc + tb) % 2]
                gate_wait("tensor", f"ps{(oc + tb) % 2}")
                ws.begin("tensor", t)
                for k in range(KV_C):
                    v = P.op("tensor", "matmul", ps[:, :], wt[:, k, :],
                             ckv[:, k, tb * 512:(tb + 1) * 512],
                             start=(k == 0), stop=(k == KV_C - 1),
                             inc=(pe if k == KV_C - 1 else None))
                if tb == NTB - 1:
                    ws.done(t, pe, v)
                P.wait("vector", pe, v)
                slot = tb % 2
                gate_wait("vector", f"rsc{slot}")
                u = P.op("vector", "tensor_copy", out=rsc[slot][:, :], in_=ps[:, :], inc=dv)
                gate_set(f"ps{(oc + tb) % 2}", dv, u)
                P.wait("tensor", dv, u)
                gate_wait("tensor", "psT")
                for j in range(4):
                    v = P.op("tensor", "transpose", psT[:, j * 128:(j + 1) * 128],
                             rsc[slot][:, j * 128:(j + 1) * 128], shared["id_bf"][:, :],
                             inc=(pe if j == 3 else None))
                P.wait("vector", pe, v)
                for j in range(4):
                    for half in range(2):
                        h = 2 * oc + half
                        u = P.op("vector", "tensor_copy",
                                 out=vaug[:, tb * 4 + j, h * VA:h * VA + V_D],
                                 in_=psT[:, j * 128 + half * 64: j * 128 + half * 64 + 64],
                                 inc=dv)
                gate_set("psT", dv, u)
                gate_set(f"rsc{slot}", dv, u)

        evacs_done = dv.n
        P.wait("scalar", dv, evacs_done)
        P.op("scalar", "dma_start", inc=agst, incv=16, out=D["ag0k_in"], in_=kcat[:, :, :])
        for j in range(8):
            P.op("scalar", "dma_start", inc=agst, incv=16,
                 out=D["ag0v_in_r"][j], in_=vaug[:, j, :])
        P.op("scalar", "dma_start", inc=agst, incv=16, out=D["qcat_dram"], in_=qcat[:, :, :])
        P.wait("gpsimd", agst, agst.n)
        P.op("gpsimd", "collective_compute", "AllGather", ALU.bypass,
             replica_groups=[list(range(NCORE))],
             ins=[D["ag0k_in"].opt()], outs=[D["ag0k_out"].opt()], inc=cc)
        P.op("gpsimd", "collective_compute", "AllGather", ALU.bypass,
             replica_groups=[list(range(NCORE))],
             ins=[D["ag0v_in"].opt()], outs=[D["ag0v_out"].opt()], inc=cc)
        ag_done = cc.n
        if stop_after == "qkv":
            P.wait("sync", cc, ag_done)
            for c in range(NDC):
                P.op("gpsimd", "dma_start", inc=ost, incv=16,
                     out=out_dram.rearrange("(c p) t -> c p t", p=128)[c], in_=xn[:, c, :])
            P.wait("gpsimd", ost, ost.n)
    P.barrier()
    if stop_after == "qkv":
        l0es.close()
        return

    GSZ = 4
    with ExitStack() as ph:
        yacc = ph.enter_context(sb("yacc", [128, NDC, TPC], F32))
        gated = ph.enter_context(sb("gated", [128, GSZ, TPC], BF16))
        silu_sc = [ph.enter_context(sb(f"sil{i}", [128, 512], F32)) for i in range(2)]
        wsg = WStream(P, "wg", 2, (128, NDC, 128), es=ph)
        wsu = WStream(P, "wu", 2, (128, NDC, 128), es=ph)
        wsd = WStream(P, "wd", GSZ + 1, (128, NDC, 128), es=ph)
        wg_r = D["wg_t"].rearrange("(c p) m -> p c m", p=128)
        wu_r = D["wu_t"].rearrange("(c p) m -> p c m", p=128)
        wd_r = D["wd_t"].rearrange("(hc p) (c m) -> hc p c m", p=128, m=128)
        for grp in range(FH_C // GSZ):
            dts = []
            for hcl in range(GSZ):
                hc = grp * GSZ + hcl
                tg, wtg = wsg.load(wg_r[:, :, hc * 128:(hc + 1) * 128])
                tu, wtu = wsu.load(wu_r[:, :, hc * 128:(hc + 1) * 128])
                td, wtd = wsd.load(wd_r[hc])
                dts.append((td, wtd))
                for tb in range(NTB):
                    cs = slice(tb * 512, (tb + 1) * 512)
                    psg, psu = PS[tb], PS[4 + tb]
                    gate_wait("tensor", f"ps{tb}")
                    gate_wait("tensor", f"ps{4 + tb}")
                    wsg.begin("tensor", tg)
                    wsu.begin("tensor", tu)
                    for k in range(NDC):
                        P.op("tensor", "matmul", psg[:, :], wtg[:, k, :], xn[:, k, cs],
                             start=(k == 0), stop=(k == NDC - 1),
                             inc=(pe if k == NDC - 1 else None))
                    for k in range(NDC):
                        P.op("tensor", "matmul", psu[:, :], wtu[:, k, :], xn[:, k, cs],
                             start=(k == 0), stop=(k == NDC - 1),
                             inc=(pe if k == NDC - 1 else None))
                    v = pe.n
                    if tb == NTB - 1:
                        wsg.done(tg, pe, v - 1)
                        wsu.done(tu, pe, v)
                    P.wait("scalar", pe, v - 1)
                    slot = tb
                    gate_wait("scalar", f"sil{slot}")
                    a = P.op("scalar", "activation", silu_sc[slot][:, :], psg[:, :], AF.Silu,
                             inc=ac)
                    gate_set(f"ps{tb}", ac, a)
                    P.wait("vector", ac, a)
                    P.wait("vector", pe, v)
                    u = P.op("vector", "tensor_tensor", gated[:, hcl, cs], psu[:, :],
                             silu_sc[slot][:, :], ALU.mult, inc=dv)
                    gate_set(f"ps{4 + tb}", dv, u)
                    gate_set(f"sil{slot}", dv, u)
            gated_done = dv.n
            P.wait("tensor", dv, gated_done)
            for dc in range(NDC):
                for tb in range(NTB):
                    cs = slice(tb * 512, (tb + 1) * 512)
                    gate_wait("tensor", "ps6")
                    for hcl in range(GSZ):
                        td, wtd = dts[hcl]
                        wsd.begin("tensor", td)
                        v = P.op("tensor", "matmul", PS[6][:, :], wtd[:, dc, :],
                                 gated[:, hcl, cs],
                                 start=(hcl == 0), stop=(hcl == GSZ - 1),
                                 inc=(pe if hcl == GSZ - 1 else None))
                    P.wait("vector", pe, v)
                    if grp == 0:
                        u = P.op("vector", "tensor_copy", out=yacc[:, dc, cs],
                                 in_=PS[6][:, :], inc=dv)
                    else:
                        u = P.op("vector", "tensor_add", yacc[:, dc, cs], yacc[:, dc, cs],
                                 PS[6][:, :], inc=dv)
                    gate_set("ps6", dv, u)
            for hcl in range(GSZ):
                wsd.done(dts[hcl][0], dv, u)
        P.wait("scalar", dv, dv.n)
        for dc in range(NDC):
            a = P.op("scalar", "activation", g_sb[:, dc, :], yacc[:, dc, :], AF.Silu, inc=ac)
    P.barrier()
    if stop_after == "ffn":
        P.wait("sync", cc, ag_done)
        P.wait("gpsimd", ac, ac.n)
        for c in range(NDC):
            P.op("gpsimd", "dma_start", inc=ost, incv=16,
                 out=out_dram.rearrange("(c p) t -> c p t", p=128)[c], in_=g_sb[:, c, :])
        P.wait("gpsimd", ost, ost.n)
        l0es.close()
        return

    CAT2 = CAT_D + 1
    NSL = 24
    with ExitStack() as ph:
        qh = ph.enter_context(sb("qh", [CAT2, TPC], BF16))
        kslot = ph.enter_context(sb("kslot", [CAT2, NSL, 512], BF16))
        kown = ph.enter_context(sb("kown", [CAT2, NTB, 512], BF16))
        vslot = ph.enter_context(sb("vslot", [128, NSL * 4, VA], BF16))
        vown = ph.enter_context(sb("vown", [128, NTB * 4, VA], BF16))
        cm = ph.enter_context(sb("cm", [128, 8, NSL + 2], F32))
        mall = ph.enter_context(sb("mall", [128, 8], F32))
        negm = ph.enter_context(sb("negm", [1, 512], F32))
        negm_bf = ph.enter_context(sb("negm_bf", [1, 512], BF16))
        pT = [ph.enter_context(sb(f"pT{i}", [128, 512], BF16)) for i in range(3)]
        rbv = ph.enter_context(sb("rbv", [64, 512], F32))
        attn0 = ph.enter_context(sb("attn0", [128, 8, TPC], BF16))
        maskA_sb = ph.enter_context(sb("maskA_sb", [128, 4, 512], F32))
        maskB_sb = ph.enter_context(sb("maskB_sb", [128, 4, 512], F32))
        P.op("sync", "dma_start", inc=kvld, incv=16, out=maskA_sb[:, :, :], in_=D["maskA"])
        P.op("sync", "dma_start", inc=kvld, incv=16, out=maskB_sb[:, :, :], in_=D["maskB"])
        P.op("sync", "dma_start", inc=kvld, incv=16, out=kslot[97:98, :, :], in_=D["padk0"])
        P.op("sync", "dma_start", inc=kvld, incv=16, out=qh[97:98, :], in_=D["ones_row_tpc"])
        P.op("sync", "dma_start", inc=kvld, incv=16, out=kown[97:98, :, :], in_=D["zero_row_2x512"])
        P.wait("sync", cc, ag_done)
        P.wait("tensor", cc, ag_done)
        agk = D["ag0k_out"].rearrange("(r p) (h t) -> r p h t", p=CAT_D, t=TPC)
        agv = D["ag0v_out"].rearrange("(b p) f -> p b f", p=128)
        agk_own = D["ag0k_in"].rearrange("p (h t) -> p h t", t=TPC)
        agv_own = D["ag0v_in"].rearrange("(b p) f -> p b f", p=128)

        for h in range(H):
            gate_wait("sync", "heads")
            P.op("sync", "dma_start", inc=kvld, incv=16,
                 out=qh[0:CAT_D, :], in_=D["qcat_dram_r"][:, h, :])
            for j in range(NSL):
                g = j if j < NSL_E0 else j - NSL_E0
                r, s = BLOCK_OWNER[g]
                P.op("sync", "dma_start", inc=kvld, incv=16,
                     out=kslot[0:CAT_D, j, :], in_=agk[r, :, h, s * 512:(s + 1) * 512])
                P.op("sync", "dma_start", inc=kvld, incv=16,
                     out=vslot[:, 4 * j:4 * j + 4, :],
                     in_=agv[:, 8 * r + 4 * s:8 * r + 4 * s + 4, h * VA:(h + 1) * VA])
            for s in range(NTB):
                P.op("sync", "dma_start", inc=kvld, incv=16,
                     out=kown[0:CAT_D, s, :], in_=agk_own[:, h, s * 512:(s + 1) * 512])
                v = P.op("sync", "dma_start", inc=kvld, incv=16,
                         out=vown[:, 4 * s:4 * s + 4, :],
                         in_=agv_own[:, 4 * s:4 * s + 4, h * VA:(h + 1) * VA])
            ld_done = v
            P.wait("tensor", kvld, ld_done)

            for qtl in range(4):
                for role in range(2):
                    qs = role * 512 + qtl * 128
                    nsl = NSL_E0 if role == 0 else NSL_L0
                    base = 0 if role == 0 else NSL_E0
                    for jj in range(nsl + 1):
                        ps = PS[jj % 2]
                        gate_wait("tensor", f"ps{jj % 2}")
                        if jj < nsl:
                            rhs = kslot[:, base + jj, :]
                        else:
                            rhs = kown[:, role, :]
                        v = P.op("tensor", "matmul", ps[:, :], qh[:, qs:qs + 128], rhs,
                                 start=True, stop=True, inc=pe)
                        P.wait("vector", pe, v)
                        if jj == nsl:
                            P.op("vector", "tensor_add", ps[:, :], ps[:, :],
                                 maskA_sb[:, qtl, :])
                        u = P.op("vector", "reduce_max",
                                 out=cm[:, role * 4 + qtl, (base + jj if jj < nsl else NSL + role):(base + jj if jj < nsl else NSL + role) + 1],
                                 in_=ps[:, :], axis=AX.X, inc=dv)
                        gate_set(f"ps{jj % 2}", dv, u)
            P.wait("vector", dv, dv.n)
            for qtl in range(4):
                for role in range(2):
                    qi = role * 4 + qtl
                    if role == 0:
                        P.op("vector", "reduce_max", out=mall[:, qi:qi + 1],
                             in_=cm[:, qi, 0:NSL_E0], axis=AX.X)
                        u = P.op("vector", "tensor_max", mall[:, qi:qi + 1], mall[:, qi:qi + 1],
                                 cm[:, qi, NSL:NSL + 1], inc=dv)
                    else:
                        P.op("vector", "reduce_max", out=mall[:, qi:qi + 1],
                             in_=cm[:, qi, NSL_E0:NSL_E0 + NSL_L0], axis=AX.X)
                        u = P.op("vector", "tensor_max", mall[:, qi:qi + 1], mall[:, qi:qi + 1],
                                 cm[:, qi, NSL + 1:NSL + 2], inc=dv)
            for role in range(2):
                P.wait("tensor", dv, dv.n)
                gate_wait("tensor", "ps4")
                for qtl in range(4):
                    v = P.op("tensor", "transpose", PS[4][0:1, qtl * 128:(qtl + 1) * 128],
                             mall[:, role * 4 + qtl:role * 4 + qtl + 1], id_f32[:, :],
                             inc=(pe if qtl == 3 else None))
                P.wait("vector", pe, v)
                gate_wait("vector", "negm")
                P.op("vector", "tensor_scalar", negm[0:1, :], PS[4][0:1, 0:512], -1.0, None, ALU.mult)
                u = P.op("vector", "tensor_copy", out=negm_bf[0:1, :], in_=negm[0:1, :], inc=dv)
                gate_set("ps4", dv, u)
                P.wait("sync", dv, u)
                v = P.op("sync", "dma_start", inc=kvld, incv=16,
                         out=qh[96:97, role * 512:(role + 1) * 512], in_=negm_bf[0:1, :])
                gate_set("negm", kvld, v)
                P.wait("tensor", kvld, v)
                nsl = NSL_E0 if role == 0 else NSL_L0
                base = 0 if role == 0 else NSL_E0
                qs = slice(role * 512, (role + 1) * 512)
                first = True
                gate_wait("tensor", "ps6")
                nkc = (nsl + 1) * 4
                for t_i in range(nkc):
                    jj, kc = divmod(t_i, 4)
                    own = jj == nsl
                    psb = PS[2 + (t_i % 2)]
                    gate_wait("tensor", f"ps{2 + (t_i % 2)}")
                    if own:
                        lhsT = kown[:, role, kc * 128:(kc + 1) * 128]
                    else:
                        lhsT = kslot[:, base + jj, kc * 128:(kc + 1) * 128]
                    v = P.op("tensor", "matmul", psb[:, :], lhsT, qh[:, qs],
                             start=True, stop=True, inc=pe)
                    if own:
                        P.wait("vector", pe, v)
                        v2 = P.op("vector", "tensor_add", psb[:, :], psb[:, :],
                                  maskB_sb[:, kc, :], inc=dv)
                        P.wait("scalar", dv, v2)
                    else:
                        P.wait("scalar", pe, v)
                    pt = pT[t_i % 3]
                    gate_wait("scalar", f"pT{t_i % 3}")
                    a = P.op("scalar", "activation", pt[:, :], psb[:, :], AF.Exp, inc=ac)
                    gate_set(f"ps{2 + (t_i % 2)}", ac, a)
                    P.wait("tensor", ac, a)
                    vv = vown[:, 4 * role + kc, :] if own else vslot[:, 4 * (base + jj) + kc, :]
                    u = P.op("tensor", "matmul", PS[6][0:VA, :], vv, pt[:, :],
                             start=first, stop=(t_i == nkc - 1), inc=pe)
                    gate_set(f"pT{t_i % 3}", pe, u)
                    first = False
                P.wait("vector", pe, pe.n)
                gate_wait("vector", "negm")
                u = P.op("vector", "reciprocal", inc=dv, out=negm[0:1, :], in_=PS[6][64:65, :])
                P.wait("tensor", dv, u)
                gate_wait("tensor", "ps4")
                v = P.op("tensor", "matmul", PS[4][0:64, :], ones_row[0:1, 0:64], negm[0:1, :],
                         start=True, stop=True, inc=pe)
                P.wait("vector", pe, v)
                P.op("vector", "tensor_copy", out=rbv[:, :], in_=PS[4][0:64, :])
                u = P.op("vector", "tensor_tensor",
                         attn0[(h % 2) * 64:(h % 2) * 64 + 64, h // 2, role * 512:(role + 1) * 512],
                         PS[6][0:64, :], rbv[:, :], ALU.mult, inc=dv)
                gate_set("ps6", dv, u)
                gate_set("ps4", dv, u)
            gate_set("heads", dv, dv.n)
        if stop_after == "attn":
            P.wait("gpsimd", dv, dv.n)
            for c in range(8):
                P.op("gpsimd", "dma_start", inc=ost, incv=16,
                     out=out_dram.rearrange("(c p) t -> c p t", p=128)[c], in_=attn0[:, c, :])
            P.wait("gpsimd", ost, ost.n)
    P.barrier()
    if stop_after == "attn":
        l0es.close()
        return

    with ExitStack() as ph:
        xbuf = [ph.enter_context(sb(f"xc{i}", [128, TPC], F32)) for i in range(2)]
        x1b = [ph.enter_context(sb(f"x1b{i}", [128, TPC], F32)) for i in range(2)]
        tsc = ph.enter_context(sb("tsc", [128, 512], F32))
        st_toks = []
        ws = WStream(P, "wp", 2, (128, 8, 128), es=ph)
        wp = D["w_proj"].rearrange("(c p) m -> p c m", p=128)
        xr = D["x"].rearrange("(c p) t -> c p t", p=128)
        xs2 = XStream(P, "xc", "sync", 2)
        st2 = XStream(P, "x1s", "sync", 2)
        for dc in range(NDC):
            t, wt = ws.load(wp[:, :, dc * 128:(dc + 1) * 128])
            slot = dc % 2
            gate_wait("sync", f"xc{slot}")
            xtok = xs2.dma(slot, xbuf[slot][:], xr[dc])
            if dc >= 2:
                st2.wait("vector", st_toks[dc - 2])
            for tb in range(NTB):
                cs = slice(tb * 512, (tb + 1) * 512)
                ps = PS[tb]
                gate_wait("tensor", f"ps{tb}")
                ws.begin("tensor", t)
                for k in range(8):
                    v = P.op("tensor", "matmul", ps[:, :], wt[:, k, :], attn0[:, k, cs],
                             start=(k == 0), stop=(k == 7), inc=(pe if k == 7 else None))
                if tb == NTB - 1:
                    ws.done(t, pe, v)
                P.wait("vector", pe, v)
                xs2.wait("vector", xtok)
                P.op("vector", "tensor_tensor", tsc[:, :], ps[:, :], g_sb[:, dc, cs], ALU.mult)
                u = P.op("vector", "tensor_add", x1b[slot][:, cs], tsc[:, :], xbuf[slot][:, cs],
                         inc=dv)
                gate_set(f"ps{tb}", dv, u)
            gate_set(f"xc{slot}", dv, u)
            P.wait("sync", dv, u)
            st_toks.append(st2.dma(slot, out_dram.rearrange("(c p) t -> c p t", p=128)[dc],
                                   x1b[slot][:]))
        for tok in st_toks[-2:]:
            st2.wait("sync", tok)
    l0es.close()
    P.barrier()



VI_L0 = {"qg": 0, "kvg": 6}
NV_L0 = 8
VI_L1 = {"bq": 0, "bk": 16, "bv": 32, "bo": 48, "lng": 64, "lnb": 80,
         "sgb": 96, "sub": 118, "sdb": 140}
NV_L1 = 156


def emit_shared(P, nc, D, es, layers):
    sb = nc.sbuf_tensor
    shared = {}
    shared["PS"] = [es.enter_context(nc.psum_tensor(f"ps{i}", [128, 512], F32))
                    for i in range(7)]
    shared["psT"] = es.enter_context(nc.psum_tensor("psT", [128, 1024], BF16))
    ld = P.sem("cld")
    dv = P.sem("dv")
    pe = P.sem("pe")
    ones_col = es.enter_context(sb("ones_col", [128, 1], F32))
    ones_row = es.enter_context(sb("ones_row", [1, 128], F32))
    id_f32 = es.enter_context(sb("id_f32s", [128, 128], F32))
    id_bf = es.enter_context(sb("id_bfs", [128, 128], BF16))
    ones_col_bf = es.enter_context(sb("ones_col_bf", [128, 1], BF16))
    P.op("vector", "memset", ones_col[:, :], 1.0)
    P.op("vector", "memset", ones_col_bf[:, :], 1.0)
    P.op("vector", "memset", ones_row[:, :], 1.0, inc=dv)
    shared["ones_col_bf"] = ones_col_bf
    gld = P.sem("gld")
    P.op("sync", "dma_start", inc=ld, incv=16, out=id_f32[:, :], in_=D["id_f32"])
    P.op("gpsimd", "dma_start", inc=gld, incv=16, out=id_bf[:, :], in_=D["id_f32"])
    shared.update(ones_col=ones_col, ones_row=ones_row, id_f32=id_f32, id_bf=id_bf)
    nv = (NV_L0 if 0 in layers else 0) + (NV_L1 if 1 in layers else 0)
    vecs = es.enter_context(sb("vecs", [128, nv], F32))
    row_es = ExitStack()
    rowbuf = row_es.enter_context(sb("rowbuf", [1, 4096], F32))
    vi = {}
    off = 0
    specs = []
    if 0 in layers:
        specs += [("q_g", "qg", QR_C), ("kv_g", "kvg", KV_C)]
    if 1 in layers:
        specs += [("bq1", "bq", 16), ("bk1", "bk", 16), ("bv1", "bv", 16), ("bo1", "bo", 16),
                  ("lng1", "lng", 16), ("lnb1", "lnb", 16), ("sgb1", "sgb", SI_C),
                  ("sub1", "sub", SI_C), ("sdb1", "sdb", 16)]
    P.wait("vector", ld, ld.n)
    P.wait("vector", gld, gld.n)
    P.wait("tensor", dv, dv.n)
    rld = P.sem("rld")
    for dname, key, ncol in specs:
        vi[key] = off
        P.wait("sync", dv, dv.n)
        v = P.op("sync", "dma_start", inc=rld, incv=16,
                 out=rowbuf[0:1, 0:ncol * 128], in_=D[dname])
        P.wait("tensor", rld, v)
        for c in range(ncol):
            pv = P.op("tensor", "matmul", shared["PS"][0][:, c % 4:c % 4 + 1],
                      rowbuf[0:1, c * 128:(c + 1) * 128], ones_row[0:1, 0:1],
                      start=True, stop=True, inc=pe)
            P.wait("vector", pe, pv)
            u = P.op("vector", "tensor_copy", out=vecs[:, off + c:off + c + 1],
                     in_=shared["PS"][0][:, c % 4:c % 4 + 1], inc=dv)
            P.wait("tensor", dv, u)
        off += ncol
    shared["vecs"] = vecs
    shared["vi"] = vi
    P.barrier()
    row_es.close()
    return shared


def declare_dram_l0(nc, with_out=True):
    D = {}

    def inp(name, shape, dt=F32):
        D[name] = nc.dram_tensor(name, list(shape), dt, kind="ExternalInput").ap()

    inp("x", (DM, TPC))
    inp("w_cq", (DM, Q_RANK))
    inp("w_q_nope", (Q_RANK, 1024))
    inp("w_q_rope", (Q_RANK, 512))
    inp("q_g", (Q_RANK,))
    inp("w_ckv", (DM, KV_RANK))
    inp("w_k_nope", (KV_RANK, 1024))
    inp("w_v", (KV_RANK, 1024))
    inp("kv_g", (KV_RANK,))
    inp("w_k_rope", (DM, 32))
    inp("w_proj", (1024, DM))
    inp("wg_t", (DM, FH))
    inp("wu_t", (DM, FH))
    inp("wd_t", (FH, DM))
    inp("id_f32", (128, 128))
    inp("R4T", (128, 128), BF16)
    inp("R32T", (32, 32), BF16)
    inp("cos0q", (128, TPC))
    inp("cos0k", (32, TPC))
    inp("maskA_h", (4, 128, 512))
    inp("maskB_h", (4, 128, 512))
    inp("padk0", (1, NSL_E0 + NSL_L0, 512), BF16)
    inp("ones_row_tpc", (1, TPC), BF16)
    inp("ones_row_2x512", (1, 2, 512), BF16)
    inp("zero_row_2x512", (1, 2, 512), BF16)
    D["maskA"] = D["maskA_h"].rearrange("d p f -> p d f")
    D["maskB"] = D["maskB_h"].rearrange("d p f -> p d f")
    D["qcat_dram_t"] = nc.dram_tensor("qcat_dram", [CAT_D, H * TPC], BF16)
    D["qcat_dram"] = D["qcat_dram_t"].ap().rearrange("p (h t) -> p h t", t=TPC)
    D["qcat_dram_r"] = D["qcat_dram"]
    a = nc.dram_tensor("ag0k_in", [CAT_D, H * TPC], BF16).ap()
    D["ag0k_in"] = a
    D["ag0k_in_r"] = a.rearrange("p (h t) -> p h t", t=TPC)
    D["ag0k_out"] = nc.dram_tensor("ag0k_out", [NCORE * CAT_D, H * TPC], BF16).ap()
    av = nc.dram_tensor("ag0v_in", [TPC, H * VA], BF16).ap()
    D["ag0v_in"] = av
    D["ag0v_in_r"] = [av.rearrange("(b p) f -> b p f", p=128)[j] for j in range(8)]
    D["ag0v_out"] = nc.dram_tensor("ag0v_out", [SEQ, H * VA], BF16).ap()
    if with_out:
        D["x1"] = nc.dram_tensor("x1", [DM, TPC], F32, kind="ExternalOutput").ap()
    return D


def host_stage_l0(core, inputs, cst):
    x = np.asarray(inputs["L0_attn_x"])[0]
    blocks = CORE_BLOCKS[core]
    xs = np.concatenate([x[g * 512:(g + 1) * 512] for g in blocks], 0)
    m = {
        "x": np.ascontiguousarray(xs.T),
        "w_cq": inputs["L0_attn_w_cq"],
        "w_q_nope": inputs["L0_attn_w_q_nope"],
        "w_q_rope": inputs["L0_attn_w_q_rope"],
        "q_g": inputs["L0_attn_q_g"],
        "w_ckv": inputs["L0_attn_w_ckv"],
        "w_k_nope": inputs["L0_attn_w_k_nope"],
        "w_v": inputs["L0_attn_w_v"],
        "kv_g": inputs["L0_attn_kv_g"],
        "w_k_rope": inputs["L0_attn_w_k_rope"],
        "w_proj": inputs["L0_attn_w_proj"],
        "wg_t": np.ascontiguousarray(np.asarray(inputs["L0_ffn_w_gate"]).T),
        "wu_t": np.ascontiguousarray(np.asarray(inputs["L0_ffn_w_up"]).T),
        "wd_t": np.ascontiguousarray(np.asarray(inputs["L0_ffn_w_down"]).T),
        "id_f32": cst["id_f32"],
        "R4T": cst["R4T"],
        "R32T": cst["R32T"],
        "cos0q": cst["cos0q"],
        "cos0k": cst["cos0k"],
        "maskA_h": cst["maskA"],
        "maskB_h": cst["maskB"],
        "ones_row_tpc": np.ones((1, TPC), bfloat16),
        "ones_row_2x512": np.ones((1, 2, 512), bfloat16),
        "zero_row_2x512": np.zeros((1, 2, 512), bfloat16),
    }
    c = core
    pad = np.zeros((NSL_E0 + NSL_L0, 512), np.float32)
    for j in range(NSL_E0):
        if j >= blocks[0]:
            pad[j] = NEG
    for j in range(NSL_L0):
        if j >= blocks[1]:
            pad[NSL_E0 + j] = NEG
    m["padk0"] = pad[None].astype(bfloat16)
    m = {k: np.ascontiguousarray(np.asarray(v, dtype=(bfloat16 if k in
         ("R4T", "R32T", "padk0", "ones_row_tpc", "ones_row_2x512", "zero_row_2x512")
         else np.float32)))
         for k, v in m.items()}
    return m


def build_l0_program(stop_after=None):
    nc = bass.Bass("TRN2", target_bir_lowering=False, debug=False)
    es = ExitStack()
    D = declare_dram_l0(nc)
    P = Prog(nc, es)
    shared = emit_shared(P, nc, D, es, layers=[0])
    if stop_after == "shared":
        ost = P.sem("ost")
        for c in range(NDC):
            P.op("gpsimd", "dma_start", inc=ost, incv=16,
                 out=D["x1"].rearrange("(c p) t -> c p t", p=128)[c][:, 0:1],
                 in_=shared["vecs"][:, 0:1], allow_slow_non_contiguous=True)
        P.wait("gpsimd", ost, ost.n)
    else:
        emit_l0(P, nc, D, shared, D["x1"], stop_after=stop_after)
    P.finish()
    es.close()
    return nc



def emit_l1(P, nc, D, shared, out_dram, stop_after=None):
    es = P.es
    pe = P.sem("pe")
    dv = P.sem("dv")
    ac = P.sem("ac")
    kvld = P.sem("kvld1")
    agst = P.sem("agst1")
    cc = P.sem("cc")
    ost = P.sem("ost1")
    sb = P.nc.sbuf_tensor
    ones_col = shared["ones_col"]
    ones_row = shared["ones_row"]
    id_f32 = shared["id_f32"]
    id_bf = shared["id_bf"]
    PS = shared["PS"]
    psT = shared["psT"]
    V = shared["vecs"]
    vi = shared["vi"]
    gates = {}

    def gate_wait(eng, res):
        if res in gates:
            s, v = gates[res]
            P.wait(eng, s, v)

    def gate_set(res, sem, val):
        gates[res] = (sem, val)

    def vcol(key, i):
        return V[:, vi[key] + i:vi[key] + i + 1]

    l1es = ExitStack()
    xn = l1es.enter_context(sb("xn1", [128, NDC, TPC], BF16))
    rows = l1es.enter_context(sb("rows1", [1, 5 * TPC], F32))
    nm_row = rows[:, 0:TPC]
    rstd_row = rows[:, TPC:2 * TPC]
    row_sc = rows[:, 2 * TPC:3 * TPC]
    row_sc2 = rows[:, 3 * TPC:4 * TPC]
    row_s3 = rows[:, 4 * TPC:5 * TPC]
    xr = D["x"].rearrange("(c p) t -> c p t", p=128)

    with ExitStack() as ph:
        xbuf = [ph.enter_context(sb(f"xb1{i}", [128, TPC], F32)) for i in range(2)]
        sqt = [ph.enter_context(sb(f"sq1{i}", [128, 512], F32)) for i in range(2)]
        nm_b = ph.enter_context(sb("nm1_b", [128, TPC], F32))
        rstd_b = ph.enter_context(sb("rstd1_b", [128, TPC], F32))
        xsc = ph.enter_context(sb("xsc1", [128, TPC], F32))
        xs = XStream(P, "x1ld", "sync", 2)
        for c in range(NDC):
            slot = c % 2
            gate_wait("sync", f"xb{slot}")
            tok = xs.dma(slot, xbuf[slot][:], xr[c])
            xs.wait("scalar", tok)
            for tb in range(NTB):
                gate_wait("scalar", f"sq{slot}")
                a = P.op("scalar", "activation", sqt[slot][:, :],
                         xbuf[slot][:, tb * 512:(tb + 1) * 512], AF.Square, inc=ac)
                P.wait("tensor", ac, a)
                P.op("tensor", "matmul", PS[3 + tb][0:1, :], ones_col[:, :],
                     xbuf[slot][:, tb * 512:(tb + 1) * 512],
                     start=(c == 0), stop=(c == NDC - 1))
                u = P.op("tensor", "matmul", PS[5 + tb][0:1, :], ones_col[:, :], sqt[slot][:, :],
                         start=(c == 0), stop=(c == NDC - 1), inc=pe)
                gate_set(f"sq{slot}", pe, u)
            gate_set(f"xb{slot}", pe, u)
        P.wait("vector", pe, pe.n)
        for tb in range(NTB):
            cs = slice(tb * 512, (tb + 1) * 512)
            P.op("vector", "tensor_scalar", nm_row[:, cs], PS[3 + tb][0:1, :], -1.0 / DM, None, ALU.mult)
            P.op("vector", "tensor_scalar", row_sc[:, cs], PS[5 + tb][0:1, :], 1.0 / DM, EPS_LN,
                 ALU.mult, ALU.add)
            P.op("vector", "tensor_mul", row_sc2[:, cs], nm_row[:, cs], nm_row[:, cs])
            v = P.op("vector", "tensor_sub", row_sc[:, cs], row_sc[:, cs], row_sc2[:, cs], inc=dv)
        P.wait("scalar", dv, dv.n)
        P.op("scalar", "activation", row_sc2[:, 0:TPC], row_sc[:, 0:TPC], AF.Ln)
        a = P.op("scalar", "activation", rstd_row[:, 0:TPC], row_sc2[:, 0:TPC], AF.Exp,
                 scale=-0.5, inc=ac)
        P.wait("tensor", ac, a)
        for tb in range(NTB):
            cs = slice(tb * 512, (tb + 1) * 512)
            gate_wait("tensor", "ps0")
            gate_wait("tensor", "ps1")
            P.op("tensor", "matmul", PS[0][:, :], ones_row[0:1, :], nm_row[:, cs],
                 start=True, stop=True)
            v = P.op("tensor", "matmul", PS[1][:, :], ones_row[0:1, :], rstd_row[:, cs],
                     start=True, stop=True, inc=pe)
            P.wait("vector", pe, v)
            u0 = P.op("vector", "tensor_copy", out=nm_b[:, cs], in_=PS[0][:, :], inc=dv)
            u = P.op("vector", "tensor_copy", out=rstd_b[:, cs], in_=PS[1][:, :], inc=dv)
            gate_set("ps0", dv, u0)
            gate_set("ps1", dv, u)
        for c in range(NDC):
            slot = c % 2
            gate_wait("sync", f"xb{slot}")
            tok = xs.dma(slot, xbuf[slot][:], xr[c])
            xs.wait("vector", tok)
            P.op("vector", "tensor_add", xsc[:, :], xbuf[slot][:, :], nm_b[:, :])
            u = P.op("vector", "tensor_mul", xn[:, c, :], xsc[:, :], rstd_b[:, :], inc=dv)
            gate_set(f"xb{slot}", dv, u)
    P.barrier()
    if stop_after == "ln1":
        P.wait("gpsimd", dv, dv.n)
        for c in range(NDC):
            P.op("gpsimd", "dma_start", inc=ost, incv=16,
                 out=out_dram.rearrange("(c p) t -> c p t", p=128)[c], in_=xn[:, c, :])
        P.wait("gpsimd", ost, ost.n)
        l1es.close()
        return

    with ExitStack() as ph:
        xa = ph.enter_context(sb("xa1", [128, NDC, TPC], BF16))
        q1 = ph.enter_context(sb("q1", [128, NDC, TPC], BF16))
        k1 = ph.enter_context(sb("k1b", [128, NDC, 512], BF16))
        v1 = ph.enter_context(sb("v1b", [128, 4, DM], BF16))
        sqt = [ph.enter_context(sb(f"sqa{i}", [128, 512], F32)) for i in range(2)]
        nm2_b = ph.enter_context(sb("nm2_b", [128, TPC], F32))
        rstd2_b = ph.enter_context(sb("rstd2_b", [128, TPC], F32))
        xsc2 = ph.enter_context(sb("xsc2", [128, TPC], F32))
        rsc = [ph.enter_context(sb(f"rsc1{i}", [128, 512], BF16)) for i in range(2)]
        cst = {}
        cld = P.sem("cld")
        for cname in ("RneoxT", "cos1", "sin1", "cos1q", "sin1q"):
            shp = [128, 128] if cname == "RneoxT" else [128, TPC]
            cdt = BF16 if cname == "RneoxT" else F32
            tt = ph.enter_context(sb(cname + "_s", shp, cdt))
            P.op("sync", "dma_start", inc=cld, incv=16, out=tt[:, :], in_=D[cname])
            cst[cname] = tt
        P.wait("tensor", cld, cld.n)
        P.wait("vector", cld, cld.n)
        for c in range(NDC):
            for tb in range(NTB):
                gate_wait("scalar", f"sqa{c % 2}")
                a = P.op("scalar", "activation", sqt[c % 2][:, :],
                         xn[:, c, tb * 512:(tb + 1) * 512], AF.Square, inc=ac)
                P.wait("tensor", ac, a)
                P.op("tensor", "matmul", PS[3 + tb][0:1, :], shared["ones_col_bf"][:, :],
                     xn[:, c, tb * 512:(tb + 1) * 512],
                     start=(c == 0), stop=(c == NDC - 1))
                u = P.op("tensor", "matmul", PS[5 + tb][0:1, :], ones_col[:, :],
                         sqt[c % 2][:, :],
                         start=(c == 0), stop=(c == NDC - 1), inc=pe)
                gate_set(f"sqa{c % 2}", pe, u)
        P.wait("vector", pe, pe.n)
        for tb in range(NTB):
            cs = slice(tb * 512, (tb + 1) * 512)
            P.op("vector", "tensor_scalar", row_s3[:, cs], PS[3 + tb][0:1, :], -1.0 / DM, None, ALU.mult)
            P.op("vector", "tensor_scalar", row_sc[:, cs], PS[5 + tb][0:1, :], 1.0 / DM, EPS_LN,
                 ALU.mult, ALU.add)
            P.op("vector", "tensor_mul", row_sc2[:, cs], row_s3[:, cs], row_s3[:, cs])
            v = P.op("vector", "tensor_sub", row_sc[:, cs], row_sc[:, cs], row_sc2[:, cs], inc=dv)
        P.wait("scalar", dv, dv.n)
        P.op("scalar", "activation", row_sc2[:, 0:TPC], row_sc[:, 0:TPC], AF.Ln)
        a = P.op("scalar", "activation", row_sc[:, 0:TPC], row_sc2[:, 0:TPC], AF.Exp,
                 scale=-0.5, inc=ac)
        P.wait("tensor", ac, a)
        for tb in range(NTB):
            cs = slice(tb * 512, (tb + 1) * 512)
            gate_wait("tensor", "ps0")
            gate_wait("tensor", "ps1")
            P.op("tensor", "matmul", PS[0][:, :], ones_row[0:1, :], row_s3[:, cs],
                 start=True, stop=True)
            v = P.op("tensor", "matmul", PS[1][:, :], ones_row[0:1, :], row_sc[:, cs],
                     start=True, stop=True, inc=pe)
            P.wait("vector", pe, v)
            u0 = P.op("vector", "tensor_copy", out=nm2_b[:, cs], in_=PS[0][:, :], inc=dv)
            u = P.op("vector", "tensor_copy", out=rstd2_b[:, cs], in_=PS[1][:, :], inc=dv)
            gate_set("ps0", dv, u0)
            gate_set("ps1", dv, u)
        for c in range(NDC):
            P.op("vector", "tensor_add", xsc2[:, :], xn[:, c, :], nm2_b[:, :])
            P.op("vector", "tensor_mul", xsc2[:, :], xsc2[:, :], rstd2_b[:, :])
            u = P.op("vector", "tensor_scalar", xa[:, c, :], xsc2[:, :],
                     vcol("lng", c), vcol("lnb", c), ALU.mult, ALU.add, inc=dv)
        xa_done = dv.n
        ws = WStream(P, "w1", 2, (128, NDC, 128), es=ph)
        agk1 = D["ag1k_in"].rearrange("(c p) t -> c p t", p=128)
        agv1 = D["ag1v_in"].rearrange("(b p) f -> b p f", p=128)
        wq = D["wq_t"].rearrange("(c p) m -> p c m", p=128)
        wk = D["wk_t"].rearrange("(c p) m -> p c m", p=128)
        wv = D["wv_t"].rearrange("(c p) m -> p c m", p=128)
        P.wait("tensor", dv, xa_done)
        for tb in range(NTB):
            cs = slice(tb * 512, (tb + 1) * 512)
            for hc in range(NDC):
                t, wt = ws.load(wq[:, :, hc * 128:(hc + 1) * 128])
                ps = PS[hc % 2]
                gate_wait("tensor", f"ps{hc % 2}")
                ws.begin("tensor", t)
                for kk in range(NDC):
                    v = P.op("tensor", "matmul", ps[:, :], wt[:, kk, :], xa[:, kk, cs],
                             start=(kk == 0), stop=(kk == NDC - 1),
                             inc=(pe if kk == NDC - 1 else None))
                ws.done(t, pe, v)
                P.wait("vector", pe, v)
                slot = hc % 2
                gate_wait("vector", f"rsc1{slot}")
                a = P.op("vector", "tensor_scalar", rsc[slot][:, :], ps[:, :],
                         vcol("bq", hc), None, ALU.add, inc=ac)
                P.wait("tensor", ac, a)
                gate_wait("tensor", "ps4")
                v = P.op("tensor", "matmul", PS[4][:, :], cst["RneoxT"][:, :], rsc[slot][:, :],
                         start=True, stop=True, inc=pe)
                P.wait("vector", pe, v)
                P.wait("vector", ac, a)
                P.op("vector", "tensor_tensor", xsc2[:, 0:512], PS[4][:, :],
                     cst["sin1q"][:, cs], ALU.mult)
                gate_set("ps4", P.last["vector"][0], P.last["vector"][1])
                P.op("vector", "tensor_tensor", xsc2[:, 512:1024], rsc[slot][:, :],
                     cst["cos1q"][:, cs], ALU.mult)
                u = P.op("vector", "tensor_add", q1[:, hc, cs], xsc2[:, 0:512],
                         xsc2[:, 512:1024], inc=dv)
                gate_set(f"ps{hc % 2}", ac, a)
                gate_set(f"rsc1{slot}", dv, u)
                t, wt = ws.load(wk[:, :, hc * 128:(hc + 1) * 128])
                ps = PS[2 + hc % 2]
                gate_wait("tensor", f"ps{2 + hc % 2}")
                ws.begin("tensor", t)
                for kk in range(NDC):
                    v = P.op("tensor", "matmul", ps[:, :], wt[:, kk, :], xa[:, kk, cs],
                             start=(kk == 0), stop=(kk == NDC - 1),
                             inc=(pe if kk == NDC - 1 else None))
                ws.done(t, pe, v)
                P.wait("vector", pe, v)
                slot = hc % 2
                gate_wait("vector", f"rsc1{slot}")
                a = P.op("vector", "tensor_scalar", rsc[slot][:, :], ps[:, :],
                         vcol("bk", hc), None, ALU.add, inc=ac)
                P.wait("tensor", ac, a)
                gate_wait("tensor", "ps4")
                v = P.op("tensor", "matmul", PS[4][:, :], cst["RneoxT"][:, :], rsc[slot][:, :],
                         start=True, stop=True, inc=pe)
                P.wait("vector", pe, v)
                P.wait("vector", ac, a)
                P.op("vector", "tensor_tensor", xsc2[:, 0:512], PS[4][:, :],
                     cst["sin1"][:, cs], ALU.mult)
                gate_set("ps4", P.last["vector"][0], P.last["vector"][1])
                P.op("vector", "tensor_tensor", xsc2[:, 512:1024], rsc[slot][:, :],
                     cst["cos1"][:, cs], ALU.mult)
                gate_wait("vector", "k1buf")
                u = P.op("vector", "tensor_add", k1[:, hc, :], xsc2[:, 0:512],
                         xsc2[:, 512:1024], inc=dv)
                gate_set(f"ps{2 + hc % 2}", ac, a)
                gate_set(f"rsc1{slot}", dv, u)
                t, wt = ws.load(wv[:, :, hc * 128:(hc + 1) * 128])
                ps = PS[hc % 2]
                gate_wait("tensor", f"ps{hc % 2}")
                ws.begin("tensor", t)
                for kk in range(NDC):
                    v = P.op("tensor", "matmul", ps[:, :], wt[:, kk, :], xa[:, kk, cs],
                             start=(kk == 0), stop=(kk == NDC - 1),
                             inc=(pe if kk == NDC - 1 else None))
                ws.done(t, pe, v)
                P.wait("vector", pe, v)
                slot = hc % 2
                gate_wait("vector", f"rsc1{slot}")
                u = P.op("vector", "tensor_scalar", rsc[slot][:, :], ps[:, :],
                         vcol("bv", hc), None, ALU.add, inc=dv)
                gate_set(f"ps{hc % 2}", dv, u)
                P.wait("tensor", dv, u)
                gate_wait("tensor", "psT")
                for j in range(4):
                    v = P.op("tensor", "transpose", psT[:, j * 128:(j + 1) * 128],
                             rsc[slot][:, j * 128:(j + 1) * 128], id_bf[:, :],
                             inc=(pe if j == 3 else None))
                P.wait("vector", pe, v)
                gate_wait("vector", "k1buf")
                for j in range(4):
                    u = P.op("vector", "tensor_copy",
                             out=v1[:, j, hc * 128:(hc + 1) * 128],
                             in_=psT[:, j * 128:(j + 1) * 128], inc=dv)
                gate_set("psT", dv, u)
                gate_set(f"rsc1{slot}", dv, u)
            P.wait("scalar", dv, dv.n)
            for c in range(NDC):
                P.op("scalar", "dma_start", inc=agst, incv=16,
                     out=agk1[c][:, tb * 512:(tb + 1) * 512], in_=k1[:, c, :])
            for j in range(4):
                P.op("scalar", "dma_start", inc=agst, incv=16,
                     out=agv1[tb * 4 + j], in_=v1[:, j, :])
            gate_set("k1buf", agst, agst.n)
        P.wait("scalar", dv, dv.n)
        P.op("scalar", "dma_start", inc=agst, incv=16,
             out=D["q1_dram"].rearrange("(c p) t -> p c t", p=128), in_=q1[:, :, :])
        P.wait("gpsimd", agst, agst.n)
        P.op("gpsimd", "collective_compute", "AllGather", ALU.bypass,
             replica_groups=[list(range(NCORE))],
             ins=[D["ag1k_in"].opt()], outs=[D["ag1k_out"].opt()], inc=cc)
        P.op("gpsimd", "collective_compute", "AllGather", ALU.bypass,
             replica_groups=[list(range(NCORE))],
             ins=[D["ag1v_in"].opt()], outs=[D["ag1v_out"].opt()], inc=cc)
        ag_done = cc.n
    P.barrier()

    GSZM = 4
    g_sb = l1es.enter_context(sb("g1", [128, NDC, TPC], BF16))
    with ExitStack() as ph:
        yacc = ph.enter_context(sb("yacc1", [128, NDC, TPC], F32))
        gated = ph.enter_context(sb("gated1", [128, GSZM, TPC], BF16))
        silu_sc = [ph.enter_context(sb(f"sil1{i}", [128, 512], F32)) for i in range(2)]
        cT = ph.enter_context(sb("cT", [8, TPC], F32))
        msel = ph.enter_context(sb("msel", [8, 8 * 128], F32))
        r_es = ExitStack()
        lgT = r_es.enter_context(sb("lgT", [8, TPC], F32))
        lgt = r_es.enter_context(sb("lgt", [128, 8 * 8], F32))
        comb = r_es.enter_context(sb("comb", [128, 8 * 8], F32))
        stat = r_es.enter_context(sb("stat1", [128, 8], F32))
        wsr = WStream(P, "wr", 1, (128, NDC, 8), es=r_es)
        cld = P.sem("cld")
        P.op("sync", "dma_start", inc=cld, incv=16, out=msel[:, :], in_=D["moe_sel"])
        rt = D["router_t"].rearrange("(c p) m -> p c m", p=128)
        t, wt = wsr.load(rt[:, :, :], out_ap_fn=lambda o: o[:, :, 0:8])
        for tb in range(NTB):
            ps = PS[2]
            gate_wait("tensor", "ps2")
            wsr.begin("tensor", t)
            for kk in range(NDC):
                v = P.op("tensor", "matmul", ps[0:8, :], wt[:, kk, 0:8],
                         xn[:, kk, tb * 512:(tb + 1) * 512],
                         start=(kk == 0), stop=(kk == NDC - 1),
                         inc=(pe if kk == NDC - 1 else None))
            P.wait("vector", pe, v)
            u = P.op("vector", "tensor_copy", out=lgT[:, tb * 512:(tb + 1) * 512],
                     in_=ps[0:8, :], inc=dv)
            gate_set("ps2", dv, u)
        P.wait("tensor", dv, dv.n)
        P.wait("tensor", cld, cld.n)
        for j in range(8):
            gate_wait("tensor", "ps4")
            v = P.op("tensor", "transpose", PS[4][0:128, 0:8],
                     lgT[:, j * 128:(j + 1) * 128], id_f32[0:8, 0:8], inc=pe)
            P.wait("vector", pe, v)
            u = P.op("vector", "tensor_copy", out=lgt[:, j * 8:(j + 1) * 8],
                     in_=PS[4][0:128, 0:8], inc=dv)
            gate_set("ps4", dv, u)
        for j in range(8):
            tl = lgt[:, j * 8:(j + 1) * 8]
            P.op("vector", "reduce_max", out=stat[:, 0:1], in_=tl, axis=AX.X)
            P.op("vector", "tensor_scalar", comb[:, j * 8:(j + 1) * 8], tl, stat[:, 0:1],
                 None, ALU.is_equal)
            P.op("vector", "scalar_tensor_tensor", comb[:, j * 8:(j + 1) * 8],
                 comb[:, j * 8:(j + 1) * 8], NEG, tl, ALU.mult, ALU.add)
            P.op("vector", "reduce_max", out=stat[:, 1:2], in_=comb[:, j * 8:(j + 1) * 8],
                 axis=AX.X)
            P.op("vector", "tensor_add", stat[:, 2:3], stat[:, 0:1], stat[:, 1:2])
            P.op("vector", "tensor_scalar", stat[:, 2:3], stat[:, 2:3], 0.5, None, ALU.mult)
            v = P.op("vector", "tensor_scalar", comb[:, j * 8:(j + 1) * 8], tl, stat[:, 2:3],
                     None, ALU.subtract, inc=dv)
            P.wait("scalar", dv, v)
            a = P.op("scalar", "activation", comb[:, j * 8:(j + 1) * 8],
                     comb[:, j * 8:(j + 1) * 8], AF.Sigmoid, scale=2.0, inc=ac)
            P.wait("vector", ac, a)
            P.op("vector", "tensor_scalar", lgt[:, j * 8:(j + 1) * 8], tl, stat[:, 1:2],
                 None, ALU.is_ge)
            u = P.op("vector", "tensor_mul", comb[:, j * 8:(j + 1) * 8],
                     comb[:, j * 8:(j + 1) * 8], lgt[:, j * 8:(j + 1) * 8], inc=dv)
        P.wait("tensor", dv, dv.n)
        for j in range(8):
            gate_wait("tensor", "ps4")
            v = P.op("tensor", "transpose", PS[4][0:8, (j % 4) * 128:(j % 4 + 1) * 128],
                     comb[:, j * 8:(j + 1) * 8], id_f32[:, :], inc=pe)
            P.wait("vector", pe, v)
            u = P.op("vector", "tensor_copy", out=cT[:, j * 128:(j + 1) * 128],
                     in_=PS[4][0:8, (j % 4) * 128:(j % 4 + 1) * 128], inc=dv)
            gate_set("ps4", dv, u)
        P.barrier()
        r_es.close()
        wsg = WStream(P, "mg", 2, (128, NDC, 128), es=ph)
        wsu = WStream(P, "mu", 2, (128, NDC, 128), es=ph)
        wsd = WStream(P, "md", GSZM + 1, (128, NDC, 128), es=ph)

        def ffn_branch(eidx, gw_ap, uw_ap, dw_ap, nhc, gbias_key=None, ubias_key=None,
                       first_overall=False, ce=None):
            ngrp = (nhc + GSZM - 1) // GSZM
            for grp in range(ngrp):
                hcs = list(range(grp * GSZM, min((grp + 1) * GSZM, nhc)))
                dts = []
                for gi, hc in enumerate(hcs):
                    tg, wtg = wsg.load(gw_ap[:, :, hc * 128:(hc + 1) * 128])
                    tu, wtu = wsu.load(uw_ap[:, :, hc * 128:(hc + 1) * 128])
                    td, wtd = wsd.load(dw_ap[hc])
                    dts.append((td, wtd))
                    for tb in range(NTB):
                        cs = slice(tb * 512, (tb + 1) * 512)
                        psg, psu = PS[tb], PS[2 + tb]
                        gate_wait("tensor", f"ps{tb}")
                        gate_wait("tensor", f"ps{2 + tb}")
                        wsg.begin("tensor", tg)
                        wsu.begin("tensor", tu)
                        for kk in range(NDC):
                            P.op("tensor", "matmul", psg[:, :], wtg[:, kk, :], xn[:, kk, cs],
                                 start=(kk == 0), stop=(kk == NDC - 1),
                                 inc=(pe if kk == NDC - 1 else None))
                        for kk in range(NDC):
                            P.op("tensor", "matmul", psu[:, :], wtu[:, kk, :], xn[:, kk, cs],
                                 start=(kk == 0), stop=(kk == NDC - 1),
                                 inc=(pe if kk == NDC - 1 else None))
                        v = pe.n
                        if tb == NTB - 1:
                            wsg.done(tg, pe, v - 1)
                            wsu.done(tu, pe, v)
                        P.wait("scalar", pe, v - 1)
                        slot = tb
                        gate_wait("scalar", f"sil1{slot}")
                        if gbias_key is not None:
                            a = P.op("scalar", "activation", silu_sc[slot][:, :], psg[:, :],
                                     AF.Silu, bias=vcol(gbias_key, hc), inc=ac)
                        else:
                            a = P.op("scalar", "activation", silu_sc[slot][:, :], psg[:, :],
                                     AF.Silu, inc=ac)
                        gate_set(f"ps{tb}", ac, a)
                        P.wait("vector", ac, a)
                        P.wait("vector", pe, v)
                        if ubias_key is not None:
                            P.op("vector", "scalar_tensor_tensor", silu_sc[slot][:, :],
                                 psu[:, :], vcol(ubias_key, hc), silu_sc[slot][:, :],
                                 ALU.add, ALU.mult)
                            gate_set(f"ps{2 + tb}", P.last["vector"][0], P.last["vector"][1])
                        else:
                            P.op("vector", "tensor_tensor", silu_sc[slot][:, :], psu[:, :],
                                 silu_sc[slot][:, :], ALU.mult)
                            gate_set(f"ps{2 + tb}", P.last["vector"][0], P.last["vector"][1])
                        if ce is not None:
                            u = P.op("vector", "tensor_tensor", gated[:, gi, cs],
                                     silu_sc[slot][:, :], ce[tb][:, :], ALU.mult, inc=dv)
                        else:
                            u = P.op("vector", "tensor_copy", out=gated[:, gi, cs],
                                     in_=silu_sc[slot][:, :], inc=dv)
                        gate_set(f"sil1{slot}", dv, u)
                gated_done = dv.n
                P.wait("tensor", dv, gated_done)
                for dc in range(NDC):
                    for tb in range(NTB):
                        cs = slice(tb * 512, (tb + 1) * 512)
                        gate_wait("tensor", "ps6")
                        for gi in range(len(hcs)):
                            td, wtd = dts[gi]
                            wsd.begin("tensor", td)
                            v = P.op("tensor", "matmul", PS[6][:, :], wtd[:, dc, :],
                                     gated[:, gi, cs],
                                     start=(gi == 0), stop=(gi == len(hcs) - 1),
                                     inc=(pe if gi == len(hcs) - 1 else None))
                        P.wait("vector", pe, v)
                        if first_overall and grp == 0:
                            u = P.op("vector", "tensor_copy", out=yacc[:, dc, cs],
                                     in_=PS[6][:, :], inc=dv)
                        else:
                            u = P.op("vector", "tensor_add", yacc[:, dc, cs], yacc[:, dc, cs],
                                     PS[6][:, :], inc=dv)
                        gate_set("ps6", dv, u)
                for gi in range(len(hcs)):
                    wsd.done(dts[gi][0], dv, u)

        gw = D["gate_w"]
        uw = D["up_w"]
        dwr = D["down_w"]
        for e in range(NEXP):
            P.wait("tensor", dv, dv.n)
            gate_wait("tensor", "ps4")
            gate_wait("tensor", "ps5")
            v = P.op("tensor", "matmul", PS[4][:, :], msel[:, e * 128:(e + 1) * 128],
                     cT[:, 0:512], start=True, stop=True, inc=pe)
            v = P.op("tensor", "matmul", PS[5][:, :], msel[:, e * 128:(e + 1) * 128],
                     cT[:, 512:1024], start=True, stop=True, inc=pe)
            P.wait("vector", pe, v)
            gwe = gw[e].rearrange("(c p) m -> p c m", p=128)
            uwe = uw[e].rearrange("(c p) m -> p c m", p=128)
            dwe = dwr[e].rearrange("(hc p) (c m) -> hc p c m", p=128, m=128)
            ffn_branch(e, gwe, uwe, dwe, MOE_HC, first_overall=(e == 0),
                       ce=[PS[4], PS[5]])
            gate_set("ps4", dv, dv.n)
            gate_set("ps5", dv, dv.n)
        sgw = D["s_gate_w"].rearrange("(c p) m -> p c m", p=128)
        suw = D["s_up_w"].rearrange("(c p) m -> p c m", p=128)
        sdw = D["s_down_w"].rearrange("(hc p) (c m) -> hc p c m", p=128, m=128)
        ffn_branch(-1, sgw, suw, sdw, SI_C, gbias_key="sgb", ubias_key="sub")
        P.wait("scalar", dv, dv.n)
        for dc in range(NDC):
            a = P.op("scalar", "activation", g_sb[:, dc, :], yacc[:, dc, :], AF.Silu,
                     bias=vcol("sdb", dc), inc=ac)
    P.barrier()
    if stop_after == "moe":
        P.wait("gpsimd", ac, ac.n)
        for c in range(NDC):
            P.op("gpsimd", "dma_start", inc=ost, incv=16,
                 out=out_dram.rearrange("(c p) t -> c p t", p=128)[c], in_=g_sb[:, c, :])
        P.wait("gpsimd", ost, ost.n)
        l1es.close()
        return

    with ExitStack() as ph:
        qh = ph.enter_context(sb("qh1", [128, TPC], BF16))
        kslot = ph.enter_context(sb("kslot1", [128, NSL_E0 + NSL_L0, 512], BF16))
        kown = ph.enter_context(sb("kown1", [128, NTB, 512], BF16))
        vslot = ph.enter_context(sb("vslot1", [128, (NSL_E0 + NSL_L0) * 4, DK], BF16))
        vown = ph.enter_context(sb("vown1", [128, NTB * 4, DK], BF16))
        cm = ph.enter_context(sb("cm1", [128, 8, NSL_E0 + NSL_L0 + 2], F32))
        mall = ph.enter_context(sb("mall1", [128, 8], F32))
        negm = ph.enter_context(sb("negm1", [1, 512], F32))
        mb = ph.enter_context(sb("mb1", [128, 512], F32))
        ssc = [ph.enter_context(sb(f"ssc{i}", [128, 512], F32)) for i in range(2)]
        pT = [ph.enter_context(sb(f"pT1{i}", [128, 512], BF16)) for i in range(3)]
        rbv = ph.enter_context(sb("rbv1", [128, 512], F32))
        attn1 = ph.enter_context(sb("attn1", [128, NDC, TPC], BF16))
        maskA_sb = ph.enter_context(sb("maskA1_sb", [128, 4, 512], F32))
        maskB_sb = ph.enter_context(sb("maskB1_sb", [128, 4, 512], F32))
        padA_b = ph.enter_context(sb("padA_b", [128, 32], F32))
        padB_c = ph.enter_context(sb("padB_c", [128, 32], F32))
        cmadj = ph.enter_context(sb("cmadj", [128, 32], F32))
        NSL = NSL_E0 + NSL_L0
        cld = P.sem("cld")
        P.op("sync", "dma_start", inc=cld, incv=16, out=maskA_sb[:, :, :], in_=D["maskA"])
        P.op("sync", "dma_start", inc=cld, incv=16, out=maskB_sb[:, :, :], in_=D["maskB"])
        P.op("sync", "dma_start", inc=cld, incv=16, out=padB_c[:, 0:27], in_=D["padB1"])
        P.op("sync", "dma_start", inc=cld, incv=16, out=padA_b[:, 0:26], in_=D["padA1"])
        P.wait("vector", cld, cld.n)
        P.wait("tensor", cld, cld.n)
        P.wait("sync", cc, ag_done)
        P.wait("tensor", cc, ag_done)
        agk = D["ag1k_out"].rearrange("(r c p) t -> r c p t", p=128, c=NDC)
        agv = D["ag1v_out"].rearrange("(b p) f -> p b f", p=128)
        agk_own = D["ag1k_in"].rearrange("(c p) t -> c p t", p=128)
        agv_own = D["ag1v_in"].rearrange("(b p) f -> p b f", p=128)
        q1d = D["q1_dram"].rearrange("(c p) t -> c p t", p=128)
        kv1 = XStream(P, "kv1", "sync", 1)

        for h in range(H):
            gate_wait("sync", "heads1")
            P.op("sync", "dma_start", inc=kvld, incv=16, out=qh[:, :], in_=q1d[h])
            for j in range(NSL):
                g = j if j < NSL_E0 else j - NSL_E0
                r, s = BLOCK_OWNER[g]
                P.op("sync", "dma_start", inc=kvld, incv=16,
                     out=kslot[:, j, :], in_=agk[r, h, :, s * 512:(s + 1) * 512])
                P.op("sync", "dma_start", inc=kvld, incv=16,
                     out=vslot[:, 4 * j:4 * j + 4, :],
                     in_=agv[:, 8 * r + 4 * s:8 * r + 4 * s + 4, h * DK:(h + 1) * DK])
            for s in range(NTB):
                P.op("sync", "dma_start", inc=kvld, incv=16,
                     out=kown[:, s, :], in_=agk_own[h][:, s * 512:(s + 1) * 512])
                v = P.op("sync", "dma_start", inc=kvld, incv=16,
                         out=vown[:, 4 * s:4 * s + 4, :],
                         in_=agv_own[:, 4 * s:4 * s + 4, h * DK:(h + 1) * DK])
            ld_done = v
            P.wait("tensor", kvld, ld_done)

            for qtl in range(4):
                for role in range(2):
                    qs = role * 512 + qtl * 128
                    nsl = NSL_E0 if role == 0 else NSL_L0
                    base = 0 if role == 0 else NSL_E0
                    for jj in range(nsl + 1):
                        ps = PS[jj % 2]
                        gate_wait("tensor", f"ps{jj % 2}")
                        rhs = kown[:, role, :] if jj == nsl else kslot[:, base + jj, :]
                        v = P.op("tensor", "matmul", ps[:, :], qh[:, qs:qs + 128], rhs,
                                 start=True, stop=True, inc=pe)
                        P.wait("vector", pe, v)
                        if jj == nsl:
                            P.op("vector", "tensor_add", ps[:, :], ps[:, :],
                                 maskA_sb[:, qtl, :])
                        col = base + jj if jj < nsl else NSL + role
                        u = P.op("vector", "reduce_max",
                                 out=cm[:, role * 4 + qtl, col:col + 1],
                                 in_=ps[:, :], axis=AX.X, inc=dv)
                        gate_set(f"ps{jj % 2}", dv, u)
            for qtl in range(4):
                for role in range(2):
                    qi = role * 4 + qtl
                    if role == 0:
                        P.op("vector", "tensor_add", cmadj[:, 0:NSL_E0],
                             cm[:, qi, 0:NSL_E0], padA_b[:, 0:NSL_E0])
                        P.op("vector", "reduce_max", out=mall[:, qi:qi + 1],
                             in_=cmadj[:, 0:NSL_E0], axis=AX.X)
                        u = P.op("vector", "tensor_max", mall[:, qi:qi + 1],
                                 mall[:, qi:qi + 1], cm[:, qi, NSL:NSL + 1], inc=dv)
                    else:
                        P.op("vector", "tensor_add", cmadj[:, NSL_E0:NSL_E0 + NSL_L0],
                             cm[:, qi, NSL_E0:NSL_E0 + NSL_L0],
                             padA_b[:, NSL_E0:NSL_E0 + NSL_L0])
                        P.op("vector", "reduce_max", out=mall[:, qi:qi + 1],
                             in_=cmadj[:, NSL_E0:NSL_E0 + NSL_L0], axis=AX.X)
                        u = P.op("vector", "tensor_max", mall[:, qi:qi + 1],
                                 mall[:, qi:qi + 1], cm[:, qi, NSL + 1:NSL + 2], inc=dv)
            for role in range(2):
                P.wait("tensor", dv, dv.n)
                gate_wait("tensor", "ps4")
                for qtl in range(4):
                    v = P.op("tensor", "transpose", PS[4][0:1, qtl * 128:(qtl + 1) * 128],
                             mall[:, role * 4 + qtl:role * 4 + qtl + 1], id_f32[:, :],
                             inc=(pe if qtl == 3 else None))
                P.wait("vector", pe, v)
                gate_wait("vector", "negm1")
                u = P.op("vector", "tensor_copy", out=negm[0:1, :], in_=PS[4][0:1, 0:512],
                         inc=dv)
                gate_set("ps4", dv, u)
                P.wait("tensor", dv, u)
                gate_wait("tensor", "ps4")
                v = P.op("tensor", "matmul", PS[4][:, :], ones_row[0:1, :], negm[0:1, :],
                         start=True, stop=True, inc=pe)
                gate_set("negm1", pe, v)
                P.wait("vector", pe, v)
                gate_wait("vector", "mb1")
                u = P.op("vector", "tensor_copy", out=mb[:, :], in_=PS[4][:, :], inc=dv)
                gate_set("ps4", dv, u)
                P.wait("tensor", dv, u)
                mb_ready = u
                nsl = NSL_E0 if role == 0 else NSL_L0
                base = 0 if role == 0 else NSL_E0
                qs = slice(role * 512, (role + 1) * 512)
                first = True
                gate_wait("tensor", "ps6")
                gate_wait("tensor", "ps5")
                nkc = (nsl + 1) * 4
                for t_i in range(nkc):
                    jj, kc = divmod(t_i, 4)
                    own = jj == nsl
                    psb = PS[2 + (t_i % 2)]
                    gate_wait("tensor", f"ps{2 + (t_i % 2)}")
                    if own:
                        lhsT = kown[:, role, kc * 128:(kc + 1) * 128]
                    else:
                        lhsT = kslot[:, base + jj, kc * 128:(kc + 1) * 128]
                    v = P.op("tensor", "matmul", psb[:, :], lhsT, qh[:, qs],
                             start=True, stop=True, inc=pe)
                    P.wait("vector", pe, v)
                    if own:
                        P.op("vector", "tensor_add", psb[:, :], psb[:, :],
                             maskB_sb[:, kc, :])
                    slot_s = t_i % 2
                    gate_wait("vector", f"ssc{slot_s}")
                    padcol = padB_c[:, 26:27] if own else padB_c[:, base + jj:base + jj + 1]
                    u = P.op("vector", "scalar_tensor_tensor", ssc[slot_s][:, :], psb[:, :],
                             padcol, mb[:, :], ALU.add, ALU.subtract, inc=dv)
                    gate_set(f"ps{2 + (t_i % 2)}", dv, u)
                    P.wait("scalar", dv, u)
                    pt = pT[t_i % 3]
                    gate_wait("scalar", f"pT1{t_i % 3}")
                    a = P.op("scalar", "activation", pt[:, :], ssc[slot_s][:, :], AF.Exp,
                             inc=ac)
                    gate_set(f"ssc{slot_s}", ac, a)
                    P.wait("tensor", ac, a)
                    vv = vown[:, 4 * role + kc, :] if own else vslot[:, 4 * (base + jj) + kc, :]
                    P.op("tensor", "matmul", PS[5][0:1, :], shared["ones_col_bf"][:, :],
                         pt[:, :], start=first, stop=(t_i == nkc - 1))
                    u = P.op("tensor", "matmul", PS[6][:, :], vv, pt[:, :],
                             start=first, stop=(t_i == nkc - 1), inc=pe)
                    gate_set(f"pT1{t_i % 3}", pe, u)
                    first = False
                P.wait("vector", pe, pe.n)
                gate_wait("vector", "negm1")
                u = P.op("vector", "reciprocal", inc=dv, out=negm[0:1, :], in_=PS[5][0:1, :])
                gate_set("ps5", dv, u)
                P.wait("tensor", dv, u)
                gate_wait("tensor", "ps4")
                v = P.op("tensor", "matmul", PS[4][:, :], ones_row[0:1, :], negm[0:1, :],
                         start=True, stop=True, inc=pe)
                gate_set("negm1", pe, v)
                P.wait("vector", pe, v)
                P.op("vector", "tensor_copy", out=rbv[:, :], in_=PS[4][:, :])
                gate_set("ps4", P.last["vector"][0], P.last["vector"][1])
                u = P.op("vector", "tensor_tensor", attn1[:, h, qs], PS[6][:, :],
                         rbv[:, :], ALU.mult, inc=dv)
                gate_set("ps6", dv, u)
                gate_set("mb1", dv, u)
            gate_set("heads1", dv, dv.n)
        if stop_after == "attn1":
            P.wait("gpsimd", dv, dv.n)
            for c in range(NDC):
                P.op("gpsimd", "dma_start", inc=ost, incv=16,
                     out=out_dram.rearrange("(c p) t -> c p t", p=128)[c], in_=attn1[:, c, :])
            P.wait("gpsimd", ost, ost.n)
    P.barrier()
    if stop_after == "attn1":
        l1es.close()
        return

    with ExitStack() as ph:
        xbuf = [ph.enter_context(sb(f"xo{i}", [128, TPC], F32)) for i in range(2)]
        x1b = [ph.enter_context(sb(f"xob{i}", [128, TPC], F32)) for i in range(2)]
        tsc = ph.enter_context(sb("tsc1", [128, TPC], F32))
        nm_b = ph.enter_context(sb("nm1b2", [128, TPC], F32))
        rstd_b = ph.enter_context(sb("rstd1b2", [128, TPC], F32))
        ws = WStream(P, "wo", 2, (128, NDC, 128), es=ph)
        wo = D["wo_t"].rearrange("(c p) m -> p c m", p=128)
        xs2 = XStream(P, "xo", "sync", 2)
        st2 = XStream(P, "xos", "sync", 2)
        st_toks = []
        for tb in range(NTB):
            cs = slice(tb * 512, (tb + 1) * 512)
            gate_wait("tensor", "ps0")
            gate_wait("tensor", "ps1")
            P.op("tensor", "matmul", PS[0][:, :], ones_row[0:1, :], nm_row[:, cs],
                 start=True, stop=True)
            v = P.op("tensor", "matmul", PS[1][:, :], ones_row[0:1, :], rstd_row[:, cs],
                     start=True, stop=True, inc=pe)
            P.wait("vector", pe, v)
            u0 = P.op("vector", "tensor_copy", out=nm_b[:, cs], in_=PS[0][:, :], inc=dv)
            u = P.op("vector", "tensor_copy", out=rstd_b[:, cs], in_=PS[1][:, :], inc=dv)
            gate_set("ps0", dv, u0)
            gate_set("ps1", dv, u)
        for dc in range(NDC):
            t, wt = ws.load(wo[:, :, dc * 128:(dc + 1) * 128])
            slot = dc % 2
            gate_wait("sync", f"xo{slot}")
            xtok = xs2.dma(slot, xbuf[slot][:], xr[dc])
            if dc >= 2:
                st2.wait("vector", st_toks[dc - 2])
            for tb in range(NTB):
                cs = slice(tb * 512, (tb + 1) * 512)
                ps = PS[tb]
                gate_wait("tensor", f"ps{tb}")
                ws.begin("tensor", t)
                for k in range(NDC):
                    v = P.op("tensor", "matmul", ps[:, :], wt[:, k, :], attn1[:, k, cs],
                             start=(k == 0), stop=(k == NDC - 1),
                             inc=(pe if k == NDC - 1 else None))
                if tb == NTB - 1:
                    ws.done(t, pe, v)
                P.wait("vector", pe, v)
                xs2.wait("vector", xtok)
                P.op("vector", "tensor_add", tsc[:, cs], xbuf[slot][:, cs], nm_b[:, cs])
                P.op("vector", "tensor_mul", tsc[:, cs], tsc[:, cs], rstd_b[:, cs])
                P.op("vector", "scalar_tensor_tensor", tsc[:, cs], ps[:, :],
                     vcol("bo", dc), tsc[:, cs], ALU.add, ALU.add)
                gate_set(f"ps{tb}", P.last["vector"][0], P.last["vector"][1])
                P.op("vector", "tensor_mul", tsc[:, cs], tsc[:, cs], g_sb[:, dc, cs])
                u = P.op("vector", "tensor_add", x1b[slot][:, cs], tsc[:, cs],
                         xbuf[slot][:, cs], inc=dv)
            gate_set(f"xo{slot}", dv, u)
            P.wait("sync", dv, u)
            st_toks.append(st2.dma(slot, out_dram.rearrange("(c p) t -> c p t", p=128)[dc],
                                   x1b[slot][:]))
        for tok in st_toks[-2:]:
            st2.wait("sync", tok)
    l1es.close()
    P.barrier()


def declare_dram_l1(nc, with_out=True):
    D = {}

    def inp(name, shape, dt=F32):
        D[name] = nc.dram_tensor(name, list(shape), dt, kind="ExternalInput").ap()

    inp("x", (DM, TPC))
    inp("wq_t", (DM, DM))
    inp("wk_t", (DM, DM))
    inp("wv_t", (DM, DM))
    inp("wo_t", (DM, DM))
    inp("bq1", (DM,))
    inp("bk1", (DM,))
    inp("bv1", (DM,))
    inp("bo1", (DM,))
    inp("lng1", (DM,))
    inp("lnb1", (DM,))
    inp("router_t", (DM, 8))
    inp("gate_w", (NEXP, DM, MOE_I))
    inp("up_w", (NEXP, DM, MOE_I))
    inp("down_w", (NEXP, MOE_I, DM))
    inp("s_gate_w", (DM, SI))
    inp("sgb1", (SI,))
    inp("s_up_w", (DM, SI))
    inp("sub1", (SI,))
    inp("s_down_w", (SI, DM))
    inp("sdb1", (DM,))
    inp("id_f32", (128, 128))
    inp("RneoxT", (128, 128), BF16)
    inp("cos1", (128, TPC))
    inp("sin1", (128, TPC))
    inp("cos1q", (128, TPC))
    inp("sin1q", (128, TPC))
    inp("maskA_h", (4, 128, 512))
    inp("maskB_h", (4, 128, 512))
    inp("moe_sel", (8, 8 * 128))
    inp("padA1", (128, 26))
    inp("padB1", (128, 27))
    D["maskA"] = D["maskA_h"].rearrange("d p f -> p d f")
    D["maskB"] = D["maskB_h"].rearrange("d p f -> p d f")
    D["ag1k_in"] = nc.dram_tensor("ag1k_in", [DM, TPC], BF16).ap()
    D["ag1k_out"] = nc.dram_tensor("ag1k_out", [NCORE * DM, TPC], BF16).ap()
    D["ag1v_in"] = nc.dram_tensor("ag1v_in", [TPC, DM], BF16).ap()
    D["ag1v_out"] = nc.dram_tensor("ag1v_out", [SEQ, DM], BF16).ap()
    D["q1_dram"] = nc.dram_tensor("q1_dram", [DM, TPC], BF16).ap()
    if with_out:
        D["xout"] = nc.dram_tensor("xout", [DM, TPC], F32, kind="ExternalOutput").ap()
    return D


def host_stage_l1(core, inputs, cst, x1):
    blocks = CORE_BLOCKS[core]
    xs = np.concatenate([x1[g * 512:(g + 1) * 512] for g in blocks], 0)
    T = lambda a: np.ascontiguousarray(np.asarray(a).T)
    m = {
        "x": np.ascontiguousarray(xs.T.astype(np.float32)),
        "wq_t": T(inputs["L1_attn_wq"]),
        "wk_t": T(inputs["L1_attn_wk"]),
        "wv_t": T(inputs["L1_attn_wv"]),
        "wo_t": T(inputs["L1_attn_wo"]),
        "bq1": inputs["L1_attn_bq"],
        "bk1": inputs["L1_attn_bk"],
        "bv1": inputs["L1_attn_bv"],
        "bo1": inputs["L1_attn_bo"],
        "lng1": inputs["L1_attn_ln_g"],
        "lnb1": inputs["L1_attn_ln_b"],
        "router_t": T(inputs["L1_ffn_router_w"]),
        "gate_w": inputs["L1_ffn_gate_w"],
        "up_w": inputs["L1_ffn_up_w"],
        "down_w": inputs["L1_ffn_down_w"],
        "s_gate_w": inputs["L1_ffn_s_gate_w"],
        "sgb1": inputs["L1_ffn_s_gate_b"],
        "s_up_w": inputs["L1_ffn_s_up_w"],
        "sub1": inputs["L1_ffn_s_up_b"],
        "s_down_w": inputs["L1_ffn_s_down_w"],
        "sdb1": inputs["L1_ffn_s_down_b"],
        "id_f32": cst["id_f32"],
        "RneoxT": cst["RneoxT"],
        "cos1": cst["cos1"],
        "sin1": cst["sin1"],
        "cos1q": cst["cos1q"],
        "sin1q": cst["sin1q"],
        "maskA_h": cst["maskA"],
        "maskB_h": cst["maskB"],
        "moe_sel": cst["moe_sel"],
    }
    padA = np.zeros((1, 26), np.float32)
    padB = np.zeros((1, 26), np.float32)
    for j in range(NSL_E0):
        if j >= blocks[0]:
            padA[0, j] = NEG
            padB[0, j] = NEG
    for j in range(NSL_L0):
        if j >= blocks[1]:
            padA[0, NSL_E0 + j] = NEG
            padB[0, NSL_E0 + j] = NEG
    m["padA1"] = np.broadcast_to(padA, (128, 26)).copy()
    padB27 = np.concatenate([padB, np.zeros((1, 1), np.float32)], 1)
    m["padB1"] = np.broadcast_to(padB27, (128, 27)).copy()
    out = {}
    for k, v in m.items():
        a = np.asarray(v)
        if k == "RneoxT":
            out[k] = np.ascontiguousarray(a.astype(bfloat16))
        else:
            out[k] = np.ascontiguousarray(a.astype(np.float32))
    return out


def build_l1_program(stop_after=None):
    nc = bass.Bass("TRN2", target_bir_lowering=False, debug=False)
    es = ExitStack()
    D = declare_dram_l1(nc)
    P = Prog(nc, es)
    shared = emit_shared(P, nc, D, es, layers=[1])
    emit_l1(P, nc, D, shared, D["xout"], stop_after=stop_after)
    P.finish()
    es.close()
    return nc



_CST = [host_constants(c) for c in range(NCORE)]


def kernel(**inputs):
    nc0 = build_l0_program()
    in_maps0 = [host_stage_l0(c, inputs, _CST[c]) for c in range(NCORE)]
    res0 = run_bass_kernel_spmd(nc0, in_maps0, core_ids=list(range(NCORE)))
    x1 = np.zeros((SEQ, DM), np.float32)
    for c in range(NCORE):
        sh = np.asarray(res0.results[c]["x1"])
        b = CORE_BLOCKS[c]
        x1[b[0] * 512:(b[0] + 1) * 512] = sh.T[:512]
        x1[b[1] * 512:(b[1] + 1) * 512] = sh.T[512:]
    nc1 = build_l1_program()
    in_maps1 = [host_stage_l1(c, inputs, _CST[c], x1) for c in range(NCORE)]
    res1 = run_bass_kernel_spmd(nc1, in_maps1, core_ids=list(range(NCORE)))
    out = np.zeros((1, SEQ, DM), np.float32)
    for c in range(NCORE):
        sh = np.asarray(res1.results[c]["xout"])
        b = CORE_BLOCKS[c]
        out[0, b[0] * 512:(b[0] + 1) * 512] = sh.T[:512]
        out[0, b[1] * 512:(b[1] + 1) * 512] = sh.T[512:]
    return out


# version 02
# speedup vs baseline: 1.0128x; 1.0128x over previous
import numpy as np
import ml_dtypes
from contextlib import ExitStack
import concourse.bass as bass
import concourse.mybir as mybir
from concourse.bass_utils import run_bass_kernel_spmd

F32 = mybir.dt.float32
BF16 = mybir.dt.bfloat16
AF = mybir.ActivationFunctionType
ALU = mybir.AluOpType
AX = mybir.AxisListType
bfloat16 = ml_dtypes.bfloat16

NCORE = 8
SEQ = 8192
DM = 2048
NDC = 16
TPC = 1024
NTB = 2
NGB = SEQ // 512

CORE_BLOCKS = [[c, NGB - 1 - c] for c in range(NCORE)]
BLOCK_OWNER = {}
for _c in range(NCORE):
    for _s, _g in enumerate(CORE_BLOCKS[_c]):
        BLOCK_OWNER[_g] = (_c, _s)

H = 16
NOPE_D, ROPE_D, V_D = 64, 32, 64
Q_RANK, KV_RANK = 768, 256
QR_C, KV_C = Q_RANK // 128, KV_RANK // 128
CAT_D = 97
VA = V_D + 1
SCALE0 = 1.0 / float(np.sqrt(NOPE_D + ROPE_D))
DK = 128
SCALE1 = 1.0 / float(np.sqrt(DK))
NEXP, MOE_I = 8, 1408
MOE_HC = MOE_I // 128
SI = 2816
SI_C = SI // 128
FH = 8192
FH_C = FH // 128

NEG = -1.0e30
DEBUG_PLAN = False
NSL_E0 = 8
NSL_L0 = 16
EPS_LN = 1e-5
EPS_RMS = 1e-6



def host_constants(core):
    cst = {}
    pos = np.concatenate([np.arange(g * 512, g * 512 + 512) for g in CORE_BLOCKS[core]])
    f0 = 1.0 / (10000.0 ** (np.arange(0, ROPE_D, 2, dtype=np.float64) / ROPE_D))
    ang0 = np.outer(pos.astype(np.float64), f0)
    c0 = np.cos(ang0)
    t0 = np.zeros((128, TPC), np.float64)
    for r in range(128):
        t0[r] = c0[:, (r % 32) // 2]
    cst["cos0q"] = (t0 * SCALE0).astype(np.float32)
    cst["cos0k"] = (t0[:32] / H).astype(np.float32)
    f1 = 1.0 / (10000.0 ** (np.arange(0, DK, 2, dtype=np.float64) / DK))
    ang1 = np.outer(pos.astype(np.float64), f1)
    c1, s1 = np.cos(ang1), np.sin(ang1)
    t1c = np.zeros((128, TPC), np.float64)
    t1s = np.zeros((128, TPC), np.float64)
    for r in range(128):
        t1c[r] = c1[:, r % 64]
        t1s[r] = s1[:, r % 64]
    cst["cos1"] = t1c.astype(np.float32)
    cst["sin1"] = t1s.astype(np.float32)
    cst["cos1q"] = (t1c * SCALE1).astype(np.float32)
    cst["sin1q"] = (t1s * SCALE1).astype(np.float32)
    i_ = np.arange(128)[:, None]
    j_ = np.arange(512)[None, :]
    mA = np.stack([np.where(j_ - i_ > d, NEG, 0.0) for d in (0, 128, 256, 384)])
    mB = np.stack([np.where(i_ - j_ > d, NEG, 0.0) for d in (0, -128, -256, -384)])
    cst["maskA"] = mA.astype(np.float32)
    cst["maskB"] = mB.astype(np.float32)
    R32 = np.zeros((32, 32), np.float32)
    for p_ in range(16):
        R32[2 * p_, 2 * p_] = 1.0
        R32[2 * p_, 2 * p_ + 1] = -1.0
        R32[2 * p_ + 1, 2 * p_] = 1.0
        R32[2 * p_ + 1, 2 * p_ + 1] = 1.0
    R4 = np.zeros((128, 128), np.float32)
    for hh in range(4):
        R4[hh * 32:(hh + 1) * 32, hh * 32:(hh + 1) * 32] = R32
    cst["R4T"] = np.ascontiguousarray(R4.T).astype(bfloat16)
    cst["R32T"] = np.ascontiguousarray(R32.T).astype(bfloat16)
    Rn = np.zeros((128, 128), np.float32)
    for i in range(64):
        Rn[i, 64 + i] = -1.0
        Rn[64 + i, i] = 1.0
    cst["RneoxT"] = np.ascontiguousarray(Rn.T).astype(bfloat16)
    cst["id_f32"] = np.eye(128, dtype=np.float32)
    sel = np.zeros((8, 8 * 128), np.float32)
    for e in range(8):
        sel[e, e * 128:(e + 1) * 128] = 1.0
    cst["moe_sel"] = sel
    return cst


MASKA_D = {0: 0, 128: 1, 256: 2, 384: 3}
MASKB_D = {0: 0, -128: 1, -256: 2, -384: 3}



class Sem:
    def __init__(self, h):
        self.h = h
        self.n = 0

    def bump(self, k=1):
        self.n += k
        return self.n


class Prog:
    ENGINES = ("tensor", "vector", "scalar", "gpsimd", "sync")

    def __init__(self, nc, es):
        self.nc = nc
        self.es = es
        self.q = {e: [] for e in self.ENGINES}
        self._sems = {}
        self.taps = {}

    def sem(self, name):
        if name not in self._sems:
            h = self.es.enter_context(self.nc.semaphore(name))
            self._sems[name] = Sem(h)
        return self._sems[name]

    _NOCHAIN = ("nop", "dma_start", "collective_compute", "wait_ge", "sem_inc")

    def op(self, eng, meth, *args, inc=None, incv=1, **kw):
        chained = eng in ("vector", "scalar", "gpsimd") and meth not in self._NOCHAIN
        if chained:
            if not hasattr(self, "last"):
                self.last = {}
            prev = self.last.get(eng)
            if prev is not None:
                self.q[eng].append(lambda e, s=prev[0].h, v=prev[1]: e.wait_ge(s, v))
            if inc is None:
                inc = self.sem("ch_" + eng)
                incv = 1
        post = inc.bump(incv) if inc is not None else None
        if chained:
            self.last[eng] = (inc, post)

        import traceback
        stk = traceback.extract_stack()[-3:-1] if DEBUG_PLAN else None

        def f(e, meth=meth, args=args, kw=kw, inc=inc, incv=incv, stk=stk):
            try:
                ins = getattr(e, meth)(*args, **kw)
            except Exception:
                if stk:
                    print("PLAN SITE:", "".join(traceback.format_list(stk)))
                raise
            if inc is not None:
                ins.then_inc(inc.h, incv)
        self.q[eng].append(f)
        return post

    def wait(self, eng, sem, val):
        self.q[eng].append(lambda e, s=sem.h, v=val: e.wait_ge(s, v))

    def barrier(self, engines=None):
        engines = engines or self.ENGINES
        b = self.sem("bar")
        for eng in ("tensor", "sync"):
            if eng in engines:
                self.op(eng, "nop", inc=b)
        nv = b.n
        lasts = []
        if hasattr(self, "last"):
            for e in ("vector", "scalar", "gpsimd"):
                lp = self.last.get(e)
                if lp is not None:
                    lasts.append(lp)
        for eng in engines:
            self.wait(eng, b, nv)
            for (s, v) in lasts:
                self.wait(eng, s, v)

    def finish(self):
        nc = self.nc
        with nc.Block() as block:
            @block.tensor
            def _(t):
                for f in self.q["tensor"]:
                    f(t)

            @block.vector
            def _(v):
                for f in self.q["vector"]:
                    f(v)

            @block.scalar
            def _(s):
                for f in self.q["scalar"]:
                    f(s)

            @block.gpsimd
            def _(g):
                for f in self.q["gpsimd"]:
                    f(g)

            @block.sync
            def _(s):
                for f in self.q["sync"]:
                    f(s)


class WStream:

    def __init__(self, P, name, nbuf, shape, es=None):
        self.P = P
        self.nbuf = nbuf
        es = es if es is not None else P.es
        self.tiles = [es.enter_context(P.nc.sbuf_tensor(f"{name}{i}", list(shape), BF16))
                      for i in range(nbuf)]
        self.ld = [P.sem(f"{name}_ld{i}") for i in range(nbuf)]
        self.done_pts = {}
        self.nload = 0

    def load(self, in_ap, out_ap_fn=None):
        t = self.nload
        slot = t % self.nbuf
        if t >= self.nbuf:
            s, v = self.done_pts[t - self.nbuf]
            self.P.wait("gpsimd", s, v)
        out = self.tiles[slot]
        out_ap = out_ap_fn(out) if out_ap_fn else out[:]
        self.P.op("gpsimd", "dma_start", inc=self.ld[slot], incv=16, out=out_ap, in_=in_ap)
        self.nload += 1
        return t, out

    def begin(self, eng, t):
        self.P.wait(eng, self.ld[t % self.nbuf], 16 * (t // self.nbuf + 1))

    def done(self, t, sem, val):
        self.done_pts[t] = (sem, val)


class XStream:

    def __init__(self, P, name, queue, nslots):
        self.P = P
        self.queue = queue
        self.nslots = nslots
        self.sems = [P.sem(f"{name}_s{i}") for i in range(nslots)]
        self.cnt = [0] * nslots

    def dma(self, slot, out, in_, **kw):
        self.cnt[slot] += 1
        self.P.op(self.queue, "dma_start", inc=self.sems[slot], incv=16,
                  out=out, in_=in_, **kw)
        return slot, 16 * self.cnt[slot]

    def wait(self, eng, token):
        slot, v = token
        self.P.wait(eng, self.sems[slot], v)


def bcast_row(P, row_ap, width, ps, dst_ap, ones_row_f32, pe, dv, src_ready=None):
    if src_ready is not None:
        P.wait("tensor", src_ready[0], src_ready[1])
    v = P.op("tensor", "matmul", ps[:, 0:width], ones_row_f32[0:1, 0:128], row_ap,
             start=True, stop=True, inc=pe)
    P.wait("vector", pe, v)
    return P.op("vector", "tensor_copy", inc=dv, out=dst_ap, in_=ps[:, 0:width])



def emit_l0(P, nc, D, shared, out_dram, stop_after=None):
    es = P.es
    pe = P.sem("pe")
    dv = P.sem("dv")
    ac = P.sem("ac")
    xld = P.sem("xld")
    kvld = P.sem("kvld")
    agst = P.sem("agst")
    cc = P.sem("cc")
    ost = P.sem("ost")
    sb = P.nc.sbuf_tensor
    ones_col = shared["ones_col"]
    ones_row = shared["ones_row"]
    id_f32 = shared["id_f32"]
    PS = shared["PS"]
    psT = shared["psT"]
    gates = {}

    def gate_wait(eng, res):
        if res in gates:
            s, v = gates[res]
            P.wait(eng, s, v)

    def gate_set(res, sem, val):
        gates[res] = (sem, val)

    l0es = ExitStack()
    xn = l0es.enter_context(sb("xn0", [128, NDC, TPC], BF16))
    g_sb = l0es.enter_context(sb("g0", [128, NDC, TPC], BF16))
    rows = l0es.enter_context(sb("rows0", [1, 4 * TPC], F32))
    nm_row = rows[:, 0:TPC]
    rstd_row = rows[:, TPC:2 * TPC]
    row_sc = rows[:, 2 * TPC:3 * TPC]
    row_sc2 = rows[:, 3 * TPC:4 * TPC]

    with ExitStack() as ph:
        xbuf = [ph.enter_context(sb(f"xb{i}", [128, TPC], F32)) for i in range(2)]
        sqt = [ph.enter_context(sb(f"sq{i}", [128, 512], F32)) for i in range(2)]
        nm_b = ph.enter_context(sb("nm_b", [128, TPC], F32))
        rstd_b = ph.enter_context(sb("rstd_b", [128, TPC], F32))
        xsc = ph.enter_context(sb("xsc", [128, TPC], F32))
        xr = D["x"].rearrange("(c p) t -> c p t", p=128)
        xs = XStream(P, "x0", "sync", 2)
        for c in range(NDC):
            slot = c % 2
            gate_wait("sync", f"xb{slot}")
            tok = xs.dma(slot, xbuf[slot][:], xr[c])
            xs.wait("scalar", tok)
            for tb in range(NTB):
                gate_wait("scalar", f"sq{slot}")
                a = P.op("scalar", "activation", sqt[slot][:, :],
                         xbuf[slot][:, tb * 512:(tb + 1) * 512], AF.Square, inc=ac)
                P.wait("tensor", ac, a)
                P.op("tensor", "matmul", PS[3 + tb][0:1, :], ones_col[:, :],
                     xbuf[slot][:, tb * 512:(tb + 1) * 512],
                     start=(c == 0), stop=(c == NDC - 1))
                u = P.op("tensor", "matmul", PS[5 + tb][0:1, :], ones_col[:, :], sqt[slot][:, :],
                         start=(c == 0), stop=(c == NDC - 1), inc=pe)
                gate_set(f"sq{slot}", pe, u)
            gate_set(f"xb{slot}", pe, u)
        P.wait("vector", pe, pe.n)
        for tb in range(NTB):
            cs = slice(tb * 512, (tb + 1) * 512)
            P.op("vector", "tensor_scalar", nm_row[:, cs], PS[3 + tb][0:1, :], -1.0 / DM, None, ALU.mult)
            P.op("vector", "tensor_scalar", row_sc[:, cs], PS[5 + tb][0:1, :], 1.0 / DM, EPS_LN,
                 ALU.mult, ALU.add)
            P.op("vector", "tensor_mul", row_sc2[:, cs], nm_row[:, cs], nm_row[:, cs])
            v = P.op("vector", "tensor_sub", row_sc[:, cs], row_sc[:, cs], row_sc2[:, cs], inc=dv)
        P.wait("scalar", dv, dv.n)
        P.op("scalar", "activation", row_sc2[:, 0:TPC], row_sc[:, 0:TPC], AF.Ln)
        a = P.op("scalar", "activation", rstd_row[:, 0:TPC], row_sc2[:, 0:TPC], AF.Exp,
                 scale=-0.5, inc=ac)
        P.wait("tensor", ac, a)
        for tb in range(NTB):
            cs = slice(tb * 512, (tb + 1) * 512)
            gate_wait("tensor", "ps0")
            gate_wait("tensor", "ps1")
            v = P.op("tensor", "matmul", PS[0][:, :], ones_row[0:1, :], nm_row[:, cs],
                     start=True, stop=True)
            v = P.op("tensor", "matmul", PS[1][:, :], ones_row[0:1, :], rstd_row[:, cs],
                     start=True, stop=True, inc=pe)
            P.wait("vector", pe, v)
            u0 = P.op("vector", "tensor_copy", out=nm_b[:, cs], in_=PS[0][:, :], inc=dv)
            u = P.op("vector", "tensor_copy", out=rstd_b[:, cs], in_=PS[1][:, :], inc=dv)
            gate_set("ps0", dv, u0)
            gate_set("ps1", dv, u)
        for c in range(NDC):
            slot = c % 2
            gate_wait("sync", f"xb{slot}")
            tok = xs.dma(slot, xbuf[slot][:], xr[c])
            xs.wait("vector", tok)
            P.op("vector", "tensor_add", xsc[:, :], xbuf[slot][:, :], nm_b[:, :])
            u = P.op("vector", "tensor_mul", xn[:, c, :], xsc[:, :], rstd_b[:, :], inc=dv)
            gate_set(f"xb{slot}", dv, u)
    P.barrier()
    if stop_after == "ln":
        P.wait("gpsimd", dv, dv.n)
        for c in range(NDC):
            P.op("gpsimd", "dma_start", inc=ost, incv=16,
                 out=out_dram.rearrange("(c p) t -> c p t", p=128)[c], in_=xn[:, c, :])
        P.wait("gpsimd", ost, ost.n)
        l0es.close()
        return

    with ExitStack() as ph:
        cq = ph.enter_context(sb("cq", [128, QR_C, TPC], BF16))
        ckv = ph.enter_context(sb("ckv", [128, KV_C, TPC], BF16))
        qcat = ph.enter_context(sb("qcat", [CAT_D, H, TPC], BF16))
        kcat = ph.enter_context(sb("kcat", [CAT_D, H, TPC], BF16))
        vaug = ph.enter_context(sb("vaug", [128, 8, H * VA], BF16))
        sqt = [ph.enter_context(sb(f"qsq{i}", [128, 512], F32)) for i in range(2)]
        rb = ph.enter_context(sb("rbq", [128, TPC], F32))
        rsc = [ph.enter_context(sb(f"rsc{i}", [128, 512], BF16)) for i in range(2)]
        ws = WStream(P, "w0", 2, (128, NDC, 128), es=ph)
        cst_l0 = {}
        for cname, shp, cdt in [("R4T", [128, 128], BF16), ("R32T", [32, 32], BF16),
                                ("cos0q", [128, TPC], F32), ("cos0k", [32, TPC], F32)]:
            tt = ph.enter_context(sb(cname + "_s", shp, cdt))
            P.op("sync", "dma_start", inc=P.sem("cld"), incv=16, out=tt[:, :], in_=D[cname])
            cst_l0[cname] = tt
        P.wait("tensor", P.sem("cld"), P.sem("cld").n)
        P.wait("vector", P.sem("cld"), P.sem("cld").n)
        P.op("vector", "memset", qcat[:, :, :], 0.0)
        P.op("vector", "memset", kcat[:, :, :], 1.0)
        P.op("vector", "memset", vaug[:, :, :], 1.0, inc=P.sem("dv"))

        def proj_block(wt_ap, nk, rhs_fn, t, psum, first_last=(True, True)):
            ws.begin("tensor", t)
            for k in range(nk):
                v = P.op("tensor", "matmul", psum, ws.tiles[t % ws.nbuf][:, k, :], rhs_fn(k),
                         start=(k == 0 and first_last[0]), stop=(k == nk - 1 and first_last[1]),
                         inc=(pe if k == nk - 1 else None))
            return v

        wcq = D["w_cq"].rearrange("(c p) m -> p c m", p=128)
        for rc in range(QR_C):
            t, _ = ws.load(wcq[:, :, rc * 128:(rc + 1) * 128])
            for tb in range(NTB):
                ps = PS[rc % 2]
                gate_wait("tensor", f"ps{rc % 2}")
                v = proj_block(None, NDC, lambda k, tb=tb: xn[:, k, tb * 512:(tb + 1) * 512],
                               t, ps[:, :])
                if tb == NTB - 1:
                    ws.done(t, pe, v)
                P.wait("scalar", pe, v)
                P.op("scalar", "activation", cq[:, rc, tb * 512:(tb + 1) * 512], ps[:, :], AF.Copy)
                gate_wait("scalar", f"qsq{rc % 2}")
                a = P.op("scalar", "activation", sqt[rc % 2][:, :], ps[:, :], AF.Square, inc=ac)
                gate_set(f"ps{rc % 2}", ac, a)
                P.wait("tensor", ac, a)
                u = P.op("tensor", "matmul", PS[5 + tb][0:1, :], ones_col[:, :], sqt[rc % 2][:, :],
                         start=(rc == 0), stop=(rc == QR_C - 1), inc=pe)
                gate_set(f"qsq{rc % 2}", pe, u)
        P.wait("vector", pe, pe.n)
        for tb in range(NTB):
            cs = slice(tb * 512, (tb + 1) * 512)
            v = P.op("vector", "tensor_scalar", row_sc[:, cs], PS[5 + tb][0:1, :],
                     1.0 / Q_RANK, EPS_RMS, ALU.mult, ALU.add, inc=dv)
        P.wait("scalar", dv, dv.n)
        P.op("scalar", "activation", row_sc2[:, 0:TPC], row_sc[:, 0:TPC], AF.Ln)
        a = P.op("scalar", "activation", row_sc[:, 0:TPC], row_sc2[:, 0:TPC], AF.Exp,
                 scale=-0.5, inc=ac)
        P.wait("tensor", ac, a)
        for tb in range(NTB):
            cs = slice(tb * 512, (tb + 1) * 512)
            gate_wait("tensor", "ps0")
            v = P.op("tensor", "matmul", PS[0][:, :], ones_row[0:1, :], row_sc[:, cs],
                     start=True, stop=True, inc=pe)
            P.wait("vector", pe, v)
            u = P.op("vector", "tensor_copy", out=rb[:, cs], in_=PS[0][:, :], inc=dv)
            gate_set("ps0", dv, u)
        for rc in range(QR_C):
            u = P.op("vector", "scalar_tensor_tensor", cq[:, rc, :], cq[:, rc, :],
                     shared["vecs"][:, shared["vi"]["qg"] + rc:shared["vi"]["qg"] + rc + 1],
                     rb[:, :], ALU.mult, ALU.mult, inc=dv)
        normq_done = dv.n

        wqn = D["w_q_nope"].rearrange("(c p) m -> p c m", p=128)
        P.wait("tensor", dv, normq_done)
        for oc in range(H * NOPE_D // 128):
            t, wt = ws.load(wqn[:, :, oc * 128:(oc + 1) * 128],
                            out_ap_fn=lambda o: o[:, 0:QR_C, :])
            for tb in range(NTB):
                ps = PS[(oc + tb) % 2]
                gate_wait("tensor", f"ps{(oc + tb) % 2}")
                ws.begin("tensor", t)
                for k in range(QR_C):
                    v = P.op("tensor", "matmul", ps[:, :], wt[:, k, :],
                             cq[:, k, tb * 512:(tb + 1) * 512],
                             start=(k == 0), stop=(k == QR_C - 1),
                             inc=(pe if k == QR_C - 1 else None))
                if tb == NTB - 1:
                    ws.done(t, pe, v)
                P.wait("vector", pe, v)
                cs = slice(tb * 512, (tb + 1) * 512)
                P.op("vector", "tensor_scalar", qcat[0:64, 2 * oc, cs], ps[0:64, :],
                     SCALE0, None, ALU.mult)
                u = P.op("vector", "tensor_scalar", qcat[0:64, 2 * oc + 1, cs], ps[64:128, :],
                         SCALE0, None, ALU.mult, inc=dv)
                gate_set(f"ps{(oc + tb) % 2}", dv, u)

        wqr = D["w_q_rope"].rearrange("(c p) m -> p c m", p=128)
        for oc4 in range(4):
            t, wt = ws.load(wqr[:, :, oc4 * 128:(oc4 + 1) * 128],
                            out_ap_fn=lambda o: o[:, 0:QR_C, :])
            for tb in range(NTB):
                ps = PS[(oc4 + tb) % 2]
                gate_wait("tensor", f"ps{(oc4 + tb) % 2}")
                ws.begin("tensor", t)
                for k in range(QR_C):
                    v = P.op("tensor", "matmul", ps[:, :], wt[:, k, :],
                             cq[:, k, tb * 512:(tb + 1) * 512],
                             start=(k == 0), stop=(k == QR_C - 1),
                             inc=(pe if k == QR_C - 1 else None))
                if tb == NTB - 1:
                    ws.done(t, pe, v)
                P.wait("scalar", pe, v)
                slot = tb % 2
                gate_wait("scalar", f"rsc{slot}")
                a = P.op("scalar", "activation", rsc[slot][:, :], ps[:, :], AF.Copy, inc=ac)
                gate_set(f"ps{(oc4 + tb) % 2}", ac, a)
                P.wait("tensor", ac, a)
                gate_wait("tensor", "ps4")
                v = P.op("tensor", "matmul", PS[4][:, :], cst_l0["R4T"][:, :], rsc[slot][:, :],
                         start=True, stop=True, inc=pe)
                P.wait("vector", pe, v)
                cs = slice(tb * 512, (tb + 1) * 512)
                u = P.op("vector", "tensor_tensor", rsc[slot][:, :], PS[4][:, :],
                         cst_l0["cos0q"][:, cs], ALU.mult, inc=dv)
                gate_set("ps4", dv, u)
                for hh in range(4):
                    u = P.op("vector", "tensor_copy",
                             out=qcat[64:96, oc4 * 4 + hh, cs],
                             in_=rsc[slot][hh * 32:(hh + 1) * 32, :], inc=dv)
                gate_set(f"rsc{slot}", dv, u)

        wckv = D["w_ckv"].rearrange("(c p) m -> p c m", p=128)
        for rc in range(KV_C):
            t, wt = ws.load(wckv[:, :, rc * 128:(rc + 1) * 128])
            for tb in range(NTB):
                ps = PS[(rc + tb) % 2]
                gate_wait("tensor", f"ps{(rc + tb) % 2}")
                ws.begin("tensor", t)
                for k in range(NDC):
                    v = P.op("tensor", "matmul", ps[:, :], wt[:, k, :],
                             xn[:, k, tb * 512:(tb + 1) * 512],
                             start=(k == 0), stop=(k == NDC - 1),
                             inc=(pe if k == NDC - 1 else None))
                if tb == NTB - 1:
                    ws.done(t, pe, v)
                P.wait("scalar", pe, v)
                P.op("scalar", "activation", ckv[:, rc, tb * 512:(tb + 1) * 512], ps[:, :], AF.Copy)
                gate_wait("scalar", f"qsq{rc % 2}")
                a = P.op("scalar", "activation", sqt[rc % 2][:, :], ps[:, :], AF.Square, inc=ac)
                gate_set(f"ps{(rc + tb) % 2}", ac, a)
                P.wait("tensor", ac, a)
                u = P.op("tensor", "matmul", PS[5 + tb][0:1, :], ones_col[:, :], sqt[rc % 2][:, :],
                         start=(rc == 0), stop=(rc == KV_C - 1), inc=pe)
                gate_set(f"qsq{rc % 2}", pe, u)
        P.wait("vector", pe, pe.n)
        for tb in range(NTB):
            cs = slice(tb * 512, (tb + 1) * 512)
            v = P.op("vector", "tensor_scalar", row_sc[:, cs], PS[5 + tb][0:1, :],
                     1.0 / KV_RANK, EPS_RMS, ALU.mult, ALU.add, inc=dv)
        P.wait("scalar", dv, dv.n)
        P.op("scalar", "activation", row_sc2[:, 0:TPC], row_sc[:, 0:TPC], AF.Ln)
        a = P.op("scalar", "activation", row_sc[:, 0:TPC], row_sc2[:, 0:TPC], AF.Exp,
                 scale=-0.5, inc=ac)
        P.wait("tensor", ac, a)
        for tb in range(NTB):
            cs = slice(tb * 512, (tb + 1) * 512)
            gate_wait("tensor", "ps0")
            v = P.op("tensor", "matmul", PS[0][:, :], ones_row[0:1, :], row_sc[:, cs],
                     start=True, stop=True, inc=pe)
            P.wait("vector", pe, v)
            u = P.op("vector", "tensor_copy", out=rb[:, cs], in_=PS[0][:, :], inc=dv)
            gate_set("ps0", dv, u)
        for rc in range(KV_C):
            u = P.op("vector", "scalar_tensor_tensor", ckv[:, rc, :], ckv[:, rc, :],
                     shared["vecs"][:, shared["vi"]["kvg"] + rc:shared["vi"]["kvg"] + rc + 1],
                     rb[:, :], ALU.mult, ALU.mult, inc=dv)
        normkv_done = dv.n

        wkn = D["w_k_nope"].rearrange("(c p) m -> p c m", p=128)
        P.wait("tensor", dv, normkv_done)
        for oc in range(H * NOPE_D // 128):
            t, wt = ws.load(wkn[:, :, oc * 128:(oc + 1) * 128],
                            out_ap_fn=lambda o: o[:, 0:KV_C, :])
            for tb in range(NTB):
                ps = PS[(oc + tb) % 2]
                gate_wait("tensor", f"ps{(oc + tb) % 2}")
                ws.begin("tensor", t)
                for k in range(KV_C):
                    v = P.op("tensor", "matmul", ps[:, :], wt[:, k, :],
                             ckv[:, k, tb * 512:(tb + 1) * 512],
                             start=(k == 0), stop=(k == KV_C - 1),
                             inc=(pe if k == KV_C - 1 else None))
                if tb == NTB - 1:
                    ws.done(t, pe, v)
                P.wait("vector", pe, v)
                cs = slice(tb * 512, (tb + 1) * 512)
                P.op("vector", "tensor_copy", out=kcat[0:64, 2 * oc, cs], in_=ps[0:64, :])
                u = P.op("vector", "tensor_copy", out=kcat[0:64, 2 * oc + 1, cs],
                         in_=ps[64:128, :], inc=dv)
                gate_set(f"ps{(oc + tb) % 2}", dv, u)

        wkr = D["w_k_rope"].rearrange("(c p) m -> p c m", p=128)
        t, wt = ws.load(wkr[:, :, :], out_ap_fn=lambda o: o[:, :, 0:32])
        for tb in range(NTB):
            ps = PS[2]
            gate_wait("tensor", "ps2")
            ws.begin("tensor", t)
            for k in range(NDC):
                v = P.op("tensor", "matmul", ps[0:32, :], wt[:, k, 0:32],
                         xn[:, k, tb * 512:(tb + 1) * 512],
                         start=(k == 0), stop=(k == NDC - 1),
                         inc=(pe if k == NDC - 1 else None))
            if tb == NTB - 1:
                ws.done(t, pe, v)
            P.wait("scalar", pe, v)
            slot = tb % 2
            gate_wait("scalar", f"rsc{slot}")
            a = P.op("scalar", "activation", rsc[slot][0:32, :], ps[0:32, :], AF.Copy, inc=ac)
            gate_set("ps2", ac, a)
            P.wait("tensor", ac, a)
            gate_wait("tensor", "ps4")
            v = P.op("tensor", "matmul", PS[4][0:32, :], cst_l0["R32T"][:, :], rsc[slot][0:32, :],
                     start=True, stop=True, inc=pe)
            P.wait("vector", pe, v)
            cs = slice(tb * 512, (tb + 1) * 512)
            u = P.op("vector", "tensor_tensor", rsc[slot][0:32, :], PS[4][0:32, :],
                     cst_l0["cos0k"][:, cs], ALU.mult, inc=dv)
            gate_set("ps4", dv, u)
            for h in range(H):
                u = P.op("vector", "tensor_copy", out=kcat[64:96, h, cs],
                         in_=rsc[slot][0:32, :], inc=dv)
            gate_set(f"rsc{slot}", dv, u)

        wv = D["w_v"].rearrange("(c p) m -> p c m", p=128)
        for oc in range(H * V_D // 128):
            t, wt = ws.load(wv[:, :, oc * 128:(oc + 1) * 128],
                            out_ap_fn=lambda o: o[:, 0:KV_C, :])
            for tb in range(NTB):
                ps = PS[(oc + tb) % 2]
                gate_wait("tensor", f"ps{(oc + tb) % 2}")
                ws.begin("tensor", t)
                for k in range(KV_C):
                    v = P.op("tensor", "matmul", ps[:, :], wt[:, k, :],
                             ckv[:, k, tb * 512:(tb + 1) * 512],
                             start=(k == 0), stop=(k == KV_C - 1),
                             inc=(pe if k == KV_C - 1 else None))
                if tb == NTB - 1:
                    ws.done(t, pe, v)
                P.wait("vector", pe, v)
                slot = tb % 2
                gate_wait("vector", f"rsc{slot}")
                u = P.op("vector", "tensor_copy", out=rsc[slot][:, :], in_=ps[:, :], inc=dv)
                gate_set(f"ps{(oc + tb) % 2}", dv, u)
                P.wait("tensor", dv, u)
                gate_wait("tensor", "psT")
                for j in range(4):
                    v = P.op("tensor", "transpose", psT[:, j * 128:(j + 1) * 128],
                             rsc[slot][:, j * 128:(j + 1) * 128], shared["id_bf"][:, :],
                             inc=(pe if j == 3 else None))
                P.wait("vector", pe, v)
                for j in range(4):
                    for half in range(2):
                        h = 2 * oc + half
                        u = P.op("vector", "tensor_copy",
                                 out=vaug[:, tb * 4 + j, h * VA:h * VA + V_D],
                                 in_=psT[:, j * 128 + half * 64: j * 128 + half * 64 + 64],
                                 inc=dv)
                gate_set("psT", dv, u)
                gate_set(f"rsc{slot}", dv, u)

        evacs_done = dv.n
        P.wait("scalar", dv, evacs_done)
        P.op("scalar", "dma_start", inc=agst, incv=16, out=D["ag0k_in"], in_=kcat[:, :, :])
        for j in range(8):
            P.op("scalar", "dma_start", inc=agst, incv=16,
                 out=D["ag0v_in_r"][j], in_=vaug[:, j, :])
        P.op("scalar", "dma_start", inc=agst, incv=16, out=D["qcat_dram"], in_=qcat[:, :, :])
        P.wait("gpsimd", agst, agst.n)
        P.op("gpsimd", "collective_compute", "AllGather", ALU.bypass,
             replica_groups=[list(range(NCORE))],
             ins=[D["ag0k_in"].opt()], outs=[D["ag0k_out"].opt()], inc=cc)
        P.op("gpsimd", "collective_compute", "AllGather", ALU.bypass,
             replica_groups=[list(range(NCORE))],
             ins=[D["ag0v_in"].opt()], outs=[D["ag0v_out"].opt()], inc=cc)
        ag_done = cc.n
        if stop_after == "qkv":
            P.wait("sync", cc, ag_done)
            for c in range(NDC):
                P.op("gpsimd", "dma_start", inc=ost, incv=16,
                     out=out_dram.rearrange("(c p) t -> c p t", p=128)[c], in_=xn[:, c, :])
            P.wait("gpsimd", ost, ost.n)
    P.barrier()
    if stop_after == "qkv":
        l0es.close()
        return

    GSZ = 4
    with ExitStack() as ph:
        yacc = ph.enter_context(sb("yacc", [128, NDC, TPC], F32))
        gated = ph.enter_context(sb("gated", [128, GSZ, TPC], BF16))
        silu_sc = [ph.enter_context(sb(f"sil{i}", [128, 512], F32)) for i in range(2)]
        wsg = WStream(P, "wg", 2, (128, NDC, 128), es=ph)
        wsu = WStream(P, "wu", 2, (128, NDC, 128), es=ph)
        wsd = WStream(P, "wd", GSZ + 1, (128, NDC, 128), es=ph)
        wg_r = D["wg_t"].rearrange("(c p) m -> p c m", p=128)
        wu_r = D["wu_t"].rearrange("(c p) m -> p c m", p=128)
        wd_r = D["wd_t"].rearrange("(hc p) (c m) -> hc p c m", p=128, m=128)
        for grp in range(FH_C // GSZ):
            dts = []
            for hcl in range(GSZ):
                hc = grp * GSZ + hcl
                tg, wtg = wsg.load(wg_r[:, :, hc * 128:(hc + 1) * 128])
                tu, wtu = wsu.load(wu_r[:, :, hc * 128:(hc + 1) * 128])
                td, wtd = wsd.load(wd_r[hc])
                dts.append((td, wtd))
                for tb in range(NTB):
                    cs = slice(tb * 512, (tb + 1) * 512)
                    psg, psu = PS[tb], PS[4 + tb]
                    gate_wait("tensor", f"ps{tb}")
                    gate_wait("tensor", f"ps{4 + tb}")
                    wsg.begin("tensor", tg)
                    wsu.begin("tensor", tu)
                    for k in range(NDC):
                        P.op("tensor", "matmul", psg[:, :], wtg[:, k, :], xn[:, k, cs],
                             start=(k == 0), stop=(k == NDC - 1),
                             inc=(pe if k == NDC - 1 else None))
                    for k in range(NDC):
                        P.op("tensor", "matmul", psu[:, :], wtu[:, k, :], xn[:, k, cs],
                             start=(k == 0), stop=(k == NDC - 1),
                             inc=(pe if k == NDC - 1 else None))
                    v = pe.n
                    if tb == NTB - 1:
                        wsg.done(tg, pe, v - 1)
                        wsu.done(tu, pe, v)
                    P.wait("scalar", pe, v - 1)
                    slot = tb
                    gate_wait("scalar", f"sil{slot}")
                    a = P.op("scalar", "activation", silu_sc[slot][:, :], psg[:, :], AF.Silu,
                             inc=ac)
                    gate_set(f"ps{tb}", ac, a)
                    P.wait("vector", ac, a)
                    P.wait("vector", pe, v)
                    u = P.op("vector", "tensor_tensor", gated[:, hcl, cs], psu[:, :],
                             silu_sc[slot][:, :], ALU.mult, inc=dv)
                    gate_set(f"ps{4 + tb}", dv, u)
                    gate_set(f"sil{slot}", dv, u)
            gated_done = dv.n
            P.wait("tensor", dv, gated_done)
            for dc in range(NDC):
                for tb in range(NTB):
                    cs = slice(tb * 512, (tb + 1) * 512)
                    gate_wait("tensor", "ps6")
                    for hcl in range(GSZ):
                        td, wtd = dts[hcl]
                        wsd.begin("tensor", td)
                        v = P.op("tensor", "matmul", PS[6][:, :], wtd[:, dc, :],
                                 gated[:, hcl, cs],
                                 start=(hcl == 0), stop=(hcl == GSZ - 1),
                                 inc=(pe if hcl == GSZ - 1 else None))
                    P.wait("vector", pe, v)
                    if grp == 0:
                        u = P.op("vector", "tensor_copy", out=yacc[:, dc, cs],
                                 in_=PS[6][:, :], inc=dv)
                    else:
                        u = P.op("vector", "tensor_add", yacc[:, dc, cs], yacc[:, dc, cs],
                                 PS[6][:, :], inc=dv)
                    gate_set("ps6", dv, u)
            for hcl in range(GSZ):
                wsd.done(dts[hcl][0], dv, u)
        P.wait("scalar", dv, dv.n)
        for dc in range(NDC):
            a = P.op("scalar", "activation", g_sb[:, dc, :], yacc[:, dc, :], AF.Silu, inc=ac)
    P.barrier()
    if stop_after == "ffn":
        P.wait("sync", cc, ag_done)
        P.wait("gpsimd", ac, ac.n)
        for c in range(NDC):
            P.op("gpsimd", "dma_start", inc=ost, incv=16,
                 out=out_dram.rearrange("(c p) t -> c p t", p=128)[c], in_=g_sb[:, c, :])
        P.wait("gpsimd", ost, ost.n)
        l0es.close()
        return

    CAT2 = CAT_D + 1
    NSL = 24
    with ExitStack() as ph:
        qh = ph.enter_context(sb("qh", [CAT2, TPC], BF16))
        kslot = ph.enter_context(sb("kslot", [CAT2, NSL, 512], BF16))
        kown = ph.enter_context(sb("kown", [CAT2, NTB, 512], BF16))
        vslot = ph.enter_context(sb("vslot", [128, NSL * 4, VA], BF16))
        vown = ph.enter_context(sb("vown", [128, NTB * 4, VA], BF16))
        cm = ph.enter_context(sb("cm", [128, 8, NSL + 2], F32))
        mall = ph.enter_context(sb("mall", [128, 8], F32))
        negm = ph.enter_context(sb("negm", [1, 512], F32))
        negm_bf = ph.enter_context(sb("negm_bf", [1, 512], BF16))
        pT = [ph.enter_context(sb(f"pT{i}", [128, 512], BF16)) for i in range(3)]
        rbv = ph.enter_context(sb("rbv", [64, 512], F32))
        attn0 = ph.enter_context(sb("attn0", [128, 8, TPC], BF16))
        maskA_sb = ph.enter_context(sb("maskA_sb", [128, 4, 512], F32))
        maskB_sb = ph.enter_context(sb("maskB_sb", [128, 4, 512], F32))
        P.op("sync", "dma_start", inc=kvld, incv=16, out=maskA_sb[:, :, :], in_=D["maskA"])
        P.op("sync", "dma_start", inc=kvld, incv=16, out=maskB_sb[:, :, :], in_=D["maskB"])
        P.op("sync", "dma_start", inc=kvld, incv=16, out=kslot[97:98, :, :], in_=D["padk0"])
        P.op("sync", "dma_start", inc=kvld, incv=16, out=qh[97:98, :], in_=D["ones_row_tpc"])
        P.op("sync", "dma_start", inc=kvld, incv=16, out=kown[97:98, :, :], in_=D["zero_row_2x512"])
        P.wait("sync", cc, ag_done)
        P.wait("tensor", cc, ag_done)
        agk = D["ag0k_out"].rearrange("(r p) (h t) -> r p h t", p=CAT_D, t=TPC)
        agv = D["ag0v_out"].rearrange("(b p) f -> p b f", p=128)
        agk_own = D["ag0k_in"].rearrange("p (h t) -> p h t", t=TPC)
        agv_own = D["ag0v_in"].rearrange("(b p) f -> p b f", p=128)

        for h in range(H):
            gate_wait("sync", "heads")
            P.op("sync", "dma_start", inc=kvld, incv=16,
                 out=qh[0:CAT_D, :], in_=D["qcat_dram_r"][:, h, :])
            for j in range(NSL):
                g = j if j < NSL_E0 else j - NSL_E0
                r, s = BLOCK_OWNER[g]
                P.op("sync", "dma_start", inc=kvld, incv=16,
                     out=kslot[0:CAT_D, j, :], in_=agk[r, :, h, s * 512:(s + 1) * 512])
                P.op("sync", "dma_start", inc=kvld, incv=16,
                     out=vslot[:, 4 * j:4 * j + 4, :],
                     in_=agv[:, 8 * r + 4 * s:8 * r + 4 * s + 4, h * VA:(h + 1) * VA])
            for s in range(NTB):
                P.op("sync", "dma_start", inc=kvld, incv=16,
                     out=kown[0:CAT_D, s, :], in_=agk_own[:, h, s * 512:(s + 1) * 512])
                v = P.op("sync", "dma_start", inc=kvld, incv=16,
                         out=vown[:, 4 * s:4 * s + 4, :],
                         in_=agv_own[:, 4 * s:4 * s + 4, h * VA:(h + 1) * VA])
            ld_done = v
            P.wait("tensor", kvld, ld_done)

            for qtl in range(4):
                for role in range(2):
                    qs = role * 512 + qtl * 128
                    nsl = NSL_E0 if role == 0 else NSL_L0
                    base = 0 if role == 0 else NSL_E0
                    for jj in range(nsl + 1):
                        ps = PS[jj % 2]
                        gate_wait("tensor", f"ps{jj % 2}")
                        if jj < nsl:
                            rhs = kslot[:, base + jj, :]
                        else:
                            rhs = kown[:, role, :]
                        v = P.op("tensor", "matmul", ps[:, :], qh[:, qs:qs + 128], rhs,
                                 start=True, stop=True, inc=pe)
                        P.wait("vector", pe, v)
                        if jj == nsl:
                            P.op("vector", "tensor_add", ps[:, :], ps[:, :],
                                 maskA_sb[:, qtl, :])
                        u = P.op("vector", "reduce_max",
                                 out=cm[:, role * 4 + qtl, (base + jj if jj < nsl else NSL + role):(base + jj if jj < nsl else NSL + role) + 1],
                                 in_=ps[:, :], axis=AX.X, inc=dv)
                        gate_set(f"ps{jj % 2}", dv, u)
            P.wait("vector", dv, dv.n)
            for qtl in range(4):
                for role in range(2):
                    qi = role * 4 + qtl
                    if role == 0:
                        P.op("vector", "reduce_max", out=mall[:, qi:qi + 1],
                             in_=cm[:, qi, 0:NSL_E0], axis=AX.X)
                        u = P.op("vector", "tensor_max", mall[:, qi:qi + 1], mall[:, qi:qi + 1],
                                 cm[:, qi, NSL:NSL + 1], inc=dv)
                    else:
                        P.op("vector", "reduce_max", out=mall[:, qi:qi + 1],
                             in_=cm[:, qi, NSL_E0:NSL_E0 + NSL_L0], axis=AX.X)
                        u = P.op("vector", "tensor_max", mall[:, qi:qi + 1], mall[:, qi:qi + 1],
                                 cm[:, qi, NSL + 1:NSL + 2], inc=dv)
            for role in range(2):
                P.wait("tensor", dv, dv.n)
                gate_wait("tensor", "ps4")
                for qtl in range(4):
                    v = P.op("tensor", "transpose", PS[4][0:1, qtl * 128:(qtl + 1) * 128],
                             mall[:, role * 4 + qtl:role * 4 + qtl + 1], id_f32[:, :],
                             inc=(pe if qtl == 3 else None))
                P.wait("vector", pe, v)
                gate_wait("vector", "negm")
                P.op("vector", "tensor_scalar", negm[0:1, :], PS[4][0:1, 0:512], -1.0, None, ALU.mult)
                u = P.op("vector", "tensor_copy", out=negm_bf[0:1, :], in_=negm[0:1, :], inc=dv)
                gate_set("ps4", dv, u)
                P.wait("sync", dv, u)
                v = P.op("sync", "dma_start", inc=kvld, incv=16,
                         out=qh[96:97, role * 512:(role + 1) * 512], in_=negm_bf[0:1, :])
                gate_set("negm", kvld, v)
                P.wait("tensor", kvld, v)
                nsl = NSL_E0 if role == 0 else NSL_L0
                base = 0 if role == 0 else NSL_E0
                qs = slice(role * 512, (role + 1) * 512)
                first = True
                gate_wait("tensor", "ps6")
                nkc = (nsl + 1) * 4
                for t_i in range(nkc):
                    jj, kc = divmod(t_i, 4)
                    own = jj == nsl
                    psb = PS[2 + (t_i % 2)]
                    gate_wait("tensor", f"ps{2 + (t_i % 2)}")
                    if own:
                        lhsT = kown[:, role, kc * 128:(kc + 1) * 128]
                    else:
                        lhsT = kslot[:, base + jj, kc * 128:(kc + 1) * 128]
                    v = P.op("tensor", "matmul", psb[:, :], lhsT, qh[:, qs],
                             start=True, stop=True, inc=pe)
                    if own:
                        P.wait("vector", pe, v)
                        v2 = P.op("vector", "tensor_add", psb[:, :], psb[:, :],
                                  maskB_sb[:, kc, :], inc=dv)
                        P.wait("scalar", dv, v2)
                    else:
                        P.wait("scalar", pe, v)
                    pt = pT[t_i % 3]
                    gate_wait("scalar", f"pT{t_i % 3}")
                    a = P.op("scalar", "activation", pt[:, :], psb[:, :], AF.Exp, inc=ac)
                    gate_set(f"ps{2 + (t_i % 2)}", ac, a)
                    P.wait("tensor", ac, a)
                    vv = vown[:, 4 * role + kc, :] if own else vslot[:, 4 * (base + jj) + kc, :]
                    u = P.op("tensor", "matmul", PS[6][0:VA, :], vv, pt[:, :],
                             start=first, stop=(t_i == nkc - 1), inc=pe)
                    gate_set(f"pT{t_i % 3}", pe, u)
                    first = False
                P.wait("vector", pe, pe.n)
                gate_wait("vector", "negm")
                u = P.op("vector", "reciprocal", inc=dv, out=negm[0:1, :], in_=PS[6][64:65, :])
                P.wait("tensor", dv, u)
                gate_wait("tensor", "ps4")
                v = P.op("tensor", "matmul", PS[4][0:64, :], ones_row[0:1, 0:64], negm[0:1, :],
                         start=True, stop=True, inc=pe)
                P.wait("vector", pe, v)
                P.op("vector", "tensor_copy", out=rbv[:, :], in_=PS[4][0:64, :])
                u = P.op("vector", "tensor_tensor",
                         attn0[(h % 2) * 64:(h % 2) * 64 + 64, h // 2, role * 512:(role + 1) * 512],
                         PS[6][0:64, :], rbv[:, :], ALU.mult, inc=dv)
                gate_set("ps6", dv, u)
                gate_set("ps4", dv, u)
            gate_set("heads", dv, dv.n)
        if stop_after == "attn":
            P.wait("gpsimd", dv, dv.n)
            for c in range(8):
                P.op("gpsimd", "dma_start", inc=ost, incv=16,
                     out=out_dram.rearrange("(c p) t -> c p t", p=128)[c], in_=attn0[:, c, :])
            P.wait("gpsimd", ost, ost.n)
    P.barrier()
    if stop_after == "attn":
        l0es.close()
        return

    with ExitStack() as ph:
        xbuf = [ph.enter_context(sb(f"xc{i}", [128, TPC], F32)) for i in range(2)]
        x1b = [ph.enter_context(sb(f"x1b{i}", [128, TPC], F32)) for i in range(2)]
        tsc = ph.enter_context(sb("tsc", [128, 512], F32))
        st_toks = []
        ws = WStream(P, "wp", 2, (128, 8, 128), es=ph)
        wp = D["w_proj"].rearrange("(c p) m -> p c m", p=128)
        xr = D["x"].rearrange("(c p) t -> c p t", p=128)
        xs2 = XStream(P, "xc", "sync", 2)
        st2 = XStream(P, "x1s", "sync", 2)
        for dc in range(NDC):
            t, wt = ws.load(wp[:, :, dc * 128:(dc + 1) * 128])
            slot = dc % 2
            gate_wait("sync", f"xc{slot}")
            xtok = xs2.dma(slot, xbuf[slot][:], xr[dc])
            if dc >= 2:
                st2.wait("vector", st_toks[dc - 2])
            for tb in range(NTB):
                cs = slice(tb * 512, (tb + 1) * 512)
                ps = PS[tb]
                gate_wait("tensor", f"ps{tb}")
                ws.begin("tensor", t)
                for k in range(8):
                    v = P.op("tensor", "matmul", ps[:, :], wt[:, k, :], attn0[:, k, cs],
                             start=(k == 0), stop=(k == 7), inc=(pe if k == 7 else None))
                if tb == NTB - 1:
                    ws.done(t, pe, v)
                P.wait("vector", pe, v)
                xs2.wait("vector", xtok)
                P.op("vector", "tensor_tensor", tsc[:, :], ps[:, :], g_sb[:, dc, cs], ALU.mult)
                u = P.op("vector", "tensor_add", x1b[slot][:, cs], tsc[:, :], xbuf[slot][:, cs],
                         inc=dv)
                gate_set(f"ps{tb}", dv, u)
            gate_set(f"xc{slot}", dv, u)
            P.wait("sync", dv, u)
            st_toks.append(st2.dma(slot, out_dram.rearrange("(c p) t -> c p t", p=128)[dc],
                                   x1b[slot][:]))
        for tok in st_toks[-2:]:
            st2.wait("sync", tok)
    l0es.close()
    P.barrier()



VI_L0 = {"qg": 0, "kvg": 6}
NV_L0 = 8
VI_L1 = {"bq": 0, "bk": 16, "bv": 32, "bo": 48, "lng": 64, "lnb": 80,
         "sgb": 96, "sub": 118, "sdb": 140}
NV_L1 = 156


def emit_shared(P, nc, D, es, layers):
    sb = nc.sbuf_tensor
    shared = {}
    shared["PS"] = [es.enter_context(nc.psum_tensor(f"ps{i}", [128, 512], F32))
                    for i in range(7)]
    shared["psT"] = es.enter_context(nc.psum_tensor("psT", [128, 1024], BF16))
    ld = P.sem("cld")
    dv = P.sem("dv")
    pe = P.sem("pe")
    ones_col = es.enter_context(sb("ones_col", [128, 1], F32))
    ones_row = es.enter_context(sb("ones_row", [1, 128], F32))
    id_f32 = es.enter_context(sb("id_f32s", [128, 128], F32))
    id_bf = es.enter_context(sb("id_bfs", [128, 128], BF16))
    ones_col_bf = es.enter_context(sb("ones_col_bf", [128, 1], BF16))
    P.op("vector", "memset", ones_col[:, :], 1.0)
    P.op("vector", "memset", ones_col_bf[:, :], 1.0)
    P.op("vector", "memset", ones_row[:, :], 1.0, inc=dv)
    shared["ones_col_bf"] = ones_col_bf
    gld = P.sem("gld")
    P.op("sync", "dma_start", inc=ld, incv=16, out=id_f32[:, :], in_=D["id_f32"])
    P.op("gpsimd", "dma_start", inc=gld, incv=16, out=id_bf[:, :], in_=D["id_f32"])
    shared.update(ones_col=ones_col, ones_row=ones_row, id_f32=id_f32, id_bf=id_bf)
    nv = (NV_L0 if 0 in layers else 0) + (NV_L1 if 1 in layers else 0)
    vecs = es.enter_context(sb("vecs", [128, nv], F32))
    row_es = ExitStack()
    rowbuf = row_es.enter_context(sb("rowbuf", [1, 4096], F32))
    vi = {}
    off = 0
    specs = []
    if 0 in layers:
        specs += [("q_g", "qg", QR_C), ("kv_g", "kvg", KV_C)]
    if 1 in layers:
        specs += [("bq1", "bq", 16), ("bk1", "bk", 16), ("bv1", "bv", 16), ("bo1", "bo", 16),
                  ("lng1", "lng", 16), ("lnb1", "lnb", 16), ("sgb1", "sgb", SI_C),
                  ("sub1", "sub", SI_C), ("sdb1", "sdb", 16)]
    P.wait("vector", ld, ld.n)
    P.wait("vector", gld, gld.n)
    P.wait("tensor", dv, dv.n)
    rld = P.sem("rld")
    for dname, key, ncol in specs:
        vi[key] = off
        P.wait("sync", dv, dv.n)
        v = P.op("sync", "dma_start", inc=rld, incv=16,
                 out=rowbuf[0:1, 0:ncol * 128], in_=D[dname])
        P.wait("tensor", rld, v)
        for c in range(ncol):
            pv = P.op("tensor", "matmul", shared["PS"][0][:, c % 4:c % 4 + 1],
                      rowbuf[0:1, c * 128:(c + 1) * 128], ones_row[0:1, 0:1],
                      start=True, stop=True, inc=pe)
            P.wait("vector", pe, pv)
            u = P.op("vector", "tensor_copy", out=vecs[:, off + c:off + c + 1],
                     in_=shared["PS"][0][:, c % 4:c % 4 + 1], inc=dv)
            P.wait("tensor", dv, u)
        off += ncol
    shared["vecs"] = vecs
    shared["vi"] = vi
    P.barrier()
    row_es.close()
    return shared


def declare_dram_l0(nc, with_out=True):
    D = {}

    def inp(name, shape, dt=F32):
        D[name] = nc.dram_tensor(name, list(shape), dt, kind="ExternalInput").ap()

    inp("x", (DM, TPC))
    inp("w_cq", (DM, Q_RANK))
    inp("w_q_nope", (Q_RANK, 1024))
    inp("w_q_rope", (Q_RANK, 512))
    inp("q_g", (Q_RANK,))
    inp("w_ckv", (DM, KV_RANK))
    inp("w_k_nope", (KV_RANK, 1024))
    inp("w_v", (KV_RANK, 1024))
    inp("kv_g", (KV_RANK,))
    inp("w_k_rope", (DM, 32))
    inp("w_proj", (1024, DM))
    inp("wg_t", (DM, FH))
    inp("wu_t", (DM, FH))
    inp("wd_t", (FH, DM))
    inp("id_f32", (128, 128))
    inp("R4T", (128, 128), BF16)
    inp("R32T", (32, 32), BF16)
    inp("cos0q", (128, TPC))
    inp("cos0k", (32, TPC))
    inp("maskA_h", (4, 128, 512))
    inp("maskB_h", (4, 128, 512))
    inp("padk0", (1, NSL_E0 + NSL_L0, 512), BF16)
    inp("ones_row_tpc", (1, TPC), BF16)
    inp("ones_row_2x512", (1, 2, 512), BF16)
    inp("zero_row_2x512", (1, 2, 512), BF16)
    D["maskA"] = D["maskA_h"].rearrange("d p f -> p d f")
    D["maskB"] = D["maskB_h"].rearrange("d p f -> p d f")
    D["qcat_dram_t"] = nc.dram_tensor("qcat_dram", [CAT_D, H * TPC], BF16)
    D["qcat_dram"] = D["qcat_dram_t"].ap().rearrange("p (h t) -> p h t", t=TPC)
    D["qcat_dram_r"] = D["qcat_dram"]
    a = nc.dram_tensor("ag0k_in", [CAT_D, H * TPC], BF16).ap()
    D["ag0k_in"] = a
    D["ag0k_in_r"] = a.rearrange("p (h t) -> p h t", t=TPC)
    D["ag0k_out"] = nc.dram_tensor("ag0k_out", [NCORE * CAT_D, H * TPC], BF16).ap()
    av = nc.dram_tensor("ag0v_in", [TPC, H * VA], BF16).ap()
    D["ag0v_in"] = av
    D["ag0v_in_r"] = [av.rearrange("(b p) f -> b p f", p=128)[j] for j in range(8)]
    D["ag0v_out"] = nc.dram_tensor("ag0v_out", [SEQ, H * VA], BF16).ap()
    if with_out:
        D["x1"] = nc.dram_tensor("x1", [DM, TPC], F32, kind="ExternalOutput").ap()
    return D


def host_stage_l0(core, inputs, cst):
    x = np.asarray(inputs["L0_attn_x"])[0]
    blocks = CORE_BLOCKS[core]
    xs = np.concatenate([x[g * 512:(g + 1) * 512] for g in blocks], 0)
    m = {
        "x": np.ascontiguousarray(xs.T),
        "w_cq": inputs["L0_attn_w_cq"],
        "w_q_nope": inputs["L0_attn_w_q_nope"],
        "w_q_rope": inputs["L0_attn_w_q_rope"],
        "q_g": inputs["L0_attn_q_g"],
        "w_ckv": inputs["L0_attn_w_ckv"],
        "w_k_nope": inputs["L0_attn_w_k_nope"],
        "w_v": inputs["L0_attn_w_v"],
        "kv_g": inputs["L0_attn_kv_g"],
        "w_k_rope": inputs["L0_attn_w_k_rope"],
        "w_proj": inputs["L0_attn_w_proj"],
        "wg_t": np.ascontiguousarray(np.asarray(inputs["L0_ffn_w_gate"]).T),
        "wu_t": np.ascontiguousarray(np.asarray(inputs["L0_ffn_w_up"]).T),
        "wd_t": np.ascontiguousarray(np.asarray(inputs["L0_ffn_w_down"]).T),
        "id_f32": cst["id_f32"],
        "R4T": cst["R4T"],
        "R32T": cst["R32T"],
        "cos0q": cst["cos0q"],
        "cos0k": cst["cos0k"],
        "maskA_h": cst["maskA"],
        "maskB_h": cst["maskB"],
        "ones_row_tpc": np.ones((1, TPC), bfloat16),
        "ones_row_2x512": np.ones((1, 2, 512), bfloat16),
        "zero_row_2x512": np.zeros((1, 2, 512), bfloat16),
    }
    c = core
    pad = np.zeros((NSL_E0 + NSL_L0, 512), np.float32)
    for j in range(NSL_E0):
        if j >= blocks[0]:
            pad[j] = NEG
    for j in range(NSL_L0):
        if j >= blocks[1]:
            pad[NSL_E0 + j] = NEG
    m["padk0"] = pad[None].astype(bfloat16)
    m = {k: np.ascontiguousarray(np.asarray(v, dtype=(bfloat16 if k in
         ("R4T", "R32T", "padk0", "ones_row_tpc", "ones_row_2x512", "zero_row_2x512")
         else np.float32)))
         for k, v in m.items()}
    return m


def build_l0_program(stop_after=None):
    nc = bass.Bass("TRN2", target_bir_lowering=False, debug=False)
    es = ExitStack()
    D = declare_dram_l0(nc)
    P = Prog(nc, es)
    shared = emit_shared(P, nc, D, es, layers=[0])
    if stop_after == "shared":
        ost = P.sem("ost")
        for c in range(NDC):
            P.op("gpsimd", "dma_start", inc=ost, incv=16,
                 out=D["x1"].rearrange("(c p) t -> c p t", p=128)[c][:, 0:1],
                 in_=shared["vecs"][:, 0:1], allow_slow_non_contiguous=True)
        P.wait("gpsimd", ost, ost.n)
    else:
        emit_l0(P, nc, D, shared, D["x1"], stop_after=stop_after)
    P.finish()
    es.close()
    return nc



def emit_l1(P, nc, D, shared, out_dram, stop_after=None):
    es = P.es
    pe = P.sem("pe")
    dv = P.sem("dv")
    ac = P.sem("ac")
    kvld = P.sem("kvld1")
    agst = P.sem("agst1")
    cc = P.sem("cc")
    ost = P.sem("ost1")
    sb = P.nc.sbuf_tensor
    ones_col = shared["ones_col"]
    ones_row = shared["ones_row"]
    id_f32 = shared["id_f32"]
    id_bf = shared["id_bf"]
    PS = shared["PS"]
    psT = shared["psT"]
    V = shared["vecs"]
    vi = shared["vi"]
    gates = {}

    def gate_wait(eng, res):
        if res in gates:
            s, v = gates[res]
            P.wait(eng, s, v)

    def gate_set(res, sem, val):
        gates[res] = (sem, val)

    def vcol(key, i):
        return V[:, vi[key] + i:vi[key] + i + 1]

    l1es = ExitStack()
    xn = l1es.enter_context(sb("xn1", [128, NDC, TPC], BF16))
    rows = l1es.enter_context(sb("rows1", [1, 5 * TPC], F32))
    nm_row = rows[:, 0:TPC]
    rstd_row = rows[:, TPC:2 * TPC]
    row_sc = rows[:, 2 * TPC:3 * TPC]
    row_sc2 = rows[:, 3 * TPC:4 * TPC]
    row_s3 = rows[:, 4 * TPC:5 * TPC]
    xr = D["x"].rearrange("(c p) t -> c p t", p=128)

    with ExitStack() as ph:
        xbuf = [ph.enter_context(sb(f"xb1{i}", [128, TPC], F32)) for i in range(2)]
        sqt = [ph.enter_context(sb(f"sq1{i}", [128, 512], F32)) for i in range(2)]
        nm_b = ph.enter_context(sb("nm1_b", [128, TPC], F32))
        rstd_b = ph.enter_context(sb("rstd1_b", [128, TPC], F32))
        xsc = ph.enter_context(sb("xsc1", [128, TPC], F32))
        xs = XStream(P, "x1ld", "sync", 2)
        for c in range(NDC):
            slot = c % 2
            gate_wait("sync", f"xb{slot}")
            tok = xs.dma(slot, xbuf[slot][:], xr[c])
            xs.wait("scalar", tok)
            for tb in range(NTB):
                gate_wait("scalar", f"sq{slot}")
                a = P.op("scalar", "activation", sqt[slot][:, :],
                         xbuf[slot][:, tb * 512:(tb + 1) * 512], AF.Square, inc=ac)
                P.wait("tensor", ac, a)
                P.op("tensor", "matmul", PS[3 + tb][0:1, :], ones_col[:, :],
                     xbuf[slot][:, tb * 512:(tb + 1) * 512],
                     start=(c == 0), stop=(c == NDC - 1))
                u = P.op("tensor", "matmul", PS[5 + tb][0:1, :], ones_col[:, :], sqt[slot][:, :],
                         start=(c == 0), stop=(c == NDC - 1), inc=pe)
                gate_set(f"sq{slot}", pe, u)
            gate_set(f"xb{slot}", pe, u)
        P.wait("vector", pe, pe.n)
        for tb in range(NTB):
            cs = slice(tb * 512, (tb + 1) * 512)
            P.op("vector", "tensor_scalar", nm_row[:, cs], PS[3 + tb][0:1, :], -1.0 / DM, None, ALU.mult)
            P.op("vector", "tensor_scalar", row_sc[:, cs], PS[5 + tb][0:1, :], 1.0 / DM, EPS_LN,
                 ALU.mult, ALU.add)
            P.op("vector", "tensor_mul", row_sc2[:, cs], nm_row[:, cs], nm_row[:, cs])
            v = P.op("vector", "tensor_sub", row_sc[:, cs], row_sc[:, cs], row_sc2[:, cs], inc=dv)
        P.wait("scalar", dv, dv.n)
        P.op("scalar", "activation", row_sc2[:, 0:TPC], row_sc[:, 0:TPC], AF.Ln)
        a = P.op("scalar", "activation", rstd_row[:, 0:TPC], row_sc2[:, 0:TPC], AF.Exp,
                 scale=-0.5, inc=ac)
        P.wait("tensor", ac, a)
        for tb in range(NTB):
            cs = slice(tb * 512, (tb + 1) * 512)
            gate_wait("tensor", "ps0")
            gate_wait("tensor", "ps1")
            P.op("tensor", "matmul", PS[0][:, :], ones_row[0:1, :], nm_row[:, cs],
                 start=True, stop=True)
            v = P.op("tensor", "matmul", PS[1][:, :], ones_row[0:1, :], rstd_row[:, cs],
                     start=True, stop=True, inc=pe)
            P.wait("vector", pe, v)
            u0 = P.op("vector", "tensor_copy", out=nm_b[:, cs], in_=PS[0][:, :], inc=dv)
            u = P.op("vector", "tensor_copy", out=rstd_b[:, cs], in_=PS[1][:, :], inc=dv)
            gate_set("ps0", dv, u0)
            gate_set("ps1", dv, u)
        for c in range(NDC):
            slot = c % 2
            gate_wait("sync", f"xb{slot}")
            tok = xs.dma(slot, xbuf[slot][:], xr[c])
            xs.wait("vector", tok)
            P.op("vector", "tensor_add", xsc[:, :], xbuf[slot][:, :], nm_b[:, :])
            u = P.op("vector", "tensor_mul", xn[:, c, :], xsc[:, :], rstd_b[:, :], inc=dv)
            gate_set(f"xb{slot}", dv, u)
    P.barrier()
    if stop_after == "ln1":
        P.wait("gpsimd", dv, dv.n)
        for c in range(NDC):
            P.op("gpsimd", "dma_start", inc=ost, incv=16,
                 out=out_dram.rearrange("(c p) t -> c p t", p=128)[c], in_=xn[:, c, :])
        P.wait("gpsimd", ost, ost.n)
        l1es.close()
        return

    with ExitStack() as ph:
        xa = ph.enter_context(sb("xa1", [128, NDC, TPC], BF16))
        q1 = ph.enter_context(sb("q1", [128, NDC, TPC], BF16))
        k1 = ph.enter_context(sb("k1b", [128, NDC, 512], BF16))
        v1 = ph.enter_context(sb("v1b", [128, 4, DM], BF16))
        sqt = [ph.enter_context(sb(f"sqa{i}", [128, 512], F32)) for i in range(2)]
        nm2_b = ph.enter_context(sb("nm2_b", [128, TPC], F32))
        rstd2_b = ph.enter_context(sb("rstd2_b", [128, TPC], F32))
        xsc2 = ph.enter_context(sb("xsc2", [128, TPC], F32))
        rsc = [ph.enter_context(sb(f"rsc1{i}", [128, 512], BF16)) for i in range(2)]
        cst = {}
        cld = P.sem("cld")
        for cname in ("RneoxT", "cos1", "sin1", "cos1q", "sin1q"):
            shp = [128, 128] if cname == "RneoxT" else [128, TPC]
            cdt = BF16 if cname == "RneoxT" else F32
            tt = ph.enter_context(sb(cname + "_s", shp, cdt))
            P.op("sync", "dma_start", inc=cld, incv=16, out=tt[:, :], in_=D[cname])
            cst[cname] = tt
        P.wait("tensor", cld, cld.n)
        P.wait("vector", cld, cld.n)
        for c in range(NDC):
            for tb in range(NTB):
                gate_wait("scalar", f"sqa{c % 2}")
                a = P.op("scalar", "activation", sqt[c % 2][:, :],
                         xn[:, c, tb * 512:(tb + 1) * 512], AF.Square, inc=ac)
                P.wait("tensor", ac, a)
                P.op("tensor", "matmul", PS[3 + tb][0:1, :], shared["ones_col_bf"][:, :],
                     xn[:, c, tb * 512:(tb + 1) * 512],
                     start=(c == 0), stop=(c == NDC - 1))
                u = P.op("tensor", "matmul", PS[5 + tb][0:1, :], ones_col[:, :],
                         sqt[c % 2][:, :],
                         start=(c == 0), stop=(c == NDC - 1), inc=pe)
                gate_set(f"sqa{c % 2}", pe, u)
        P.wait("vector", pe, pe.n)
        for tb in range(NTB):
            cs = slice(tb * 512, (tb + 1) * 512)
            P.op("vector", "tensor_scalar", row_s3[:, cs], PS[3 + tb][0:1, :], -1.0 / DM, None, ALU.mult)
            P.op("vector", "tensor_scalar", row_sc[:, cs], PS[5 + tb][0:1, :], 1.0 / DM, EPS_LN,
                 ALU.mult, ALU.add)
            P.op("vector", "tensor_mul", row_sc2[:, cs], row_s3[:, cs], row_s3[:, cs])
            v = P.op("vector", "tensor_sub", row_sc[:, cs], row_sc[:, cs], row_sc2[:, cs], inc=dv)
        P.wait("scalar", dv, dv.n)
        P.op("scalar", "activation", row_sc2[:, 0:TPC], row_sc[:, 0:TPC], AF.Ln)
        a = P.op("scalar", "activation", row_sc[:, 0:TPC], row_sc2[:, 0:TPC], AF.Exp,
                 scale=-0.5, inc=ac)
        P.wait("tensor", ac, a)
        for tb in range(NTB):
            cs = slice(tb * 512, (tb + 1) * 512)
            gate_wait("tensor", "ps0")
            gate_wait("tensor", "ps1")
            P.op("tensor", "matmul", PS[0][:, :], ones_row[0:1, :], row_s3[:, cs],
                 start=True, stop=True)
            v = P.op("tensor", "matmul", PS[1][:, :], ones_row[0:1, :], row_sc[:, cs],
                     start=True, stop=True, inc=pe)
            P.wait("vector", pe, v)
            u0 = P.op("vector", "tensor_copy", out=nm2_b[:, cs], in_=PS[0][:, :], inc=dv)
            u = P.op("vector", "tensor_copy", out=rstd2_b[:, cs], in_=PS[1][:, :], inc=dv)
            gate_set("ps0", dv, u0)
            gate_set("ps1", dv, u)
        for c in range(NDC):
            P.op("vector", "tensor_add", xsc2[:, :], xn[:, c, :], nm2_b[:, :])
            P.op("vector", "tensor_mul", xsc2[:, :], xsc2[:, :], rstd2_b[:, :])
            u = P.op("vector", "tensor_scalar", xa[:, c, :], xsc2[:, :],
                     vcol("lng", c), vcol("lnb", c), ALU.mult, ALU.add, inc=dv)
        xa_done = dv.n
        ws = WStream(P, "w1", 2, (128, NDC, 128), es=ph)
        agk1 = D["ag1k_in"].rearrange("(c p) t -> c p t", p=128)
        agv1 = D["ag1v_in"].rearrange("(b p) f -> b p f", p=128)
        wq = D["wq_t"].rearrange("(c p) m -> p c m", p=128)
        wk = D["wk_t"].rearrange("(c p) m -> p c m", p=128)
        wv = D["wv_t"].rearrange("(c p) m -> p c m", p=128)
        P.wait("tensor", dv, xa_done)
        for tb in range(NTB):
            cs = slice(tb * 512, (tb + 1) * 512)
            for hc in range(NDC):
                t, wt = ws.load(wq[:, :, hc * 128:(hc + 1) * 128])
                ps = PS[hc % 2]
                gate_wait("tensor", f"ps{hc % 2}")
                ws.begin("tensor", t)
                for kk in range(NDC):
                    v = P.op("tensor", "matmul", ps[:, :], wt[:, kk, :], xa[:, kk, cs],
                             start=(kk == 0), stop=(kk == NDC - 1),
                             inc=(pe if kk == NDC - 1 else None))
                ws.done(t, pe, v)
                P.wait("vector", pe, v)
                slot = hc % 2
                gate_wait("vector", f"rsc1{slot}")
                a = P.op("vector", "tensor_scalar", rsc[slot][:, :], ps[:, :],
                         vcol("bq", hc), None, ALU.add, inc=ac)
                P.wait("tensor", ac, a)
                gate_wait("tensor", "ps4")
                v = P.op("tensor", "matmul", PS[4][:, :], cst["RneoxT"][:, :], rsc[slot][:, :],
                         start=True, stop=True, inc=pe)
                P.wait("vector", pe, v)
                P.wait("vector", ac, a)
                P.op("vector", "tensor_tensor", xsc2[:, 0:512], PS[4][:, :],
                     cst["sin1q"][:, cs], ALU.mult)
                gate_set("ps4", P.last["vector"][0], P.last["vector"][1])
                P.op("vector", "tensor_tensor", xsc2[:, 512:1024], rsc[slot][:, :],
                     cst["cos1q"][:, cs], ALU.mult)
                u = P.op("vector", "tensor_add", q1[:, hc, cs], xsc2[:, 0:512],
                         xsc2[:, 512:1024], inc=dv)
                gate_set(f"ps{hc % 2}", ac, a)
                gate_set(f"rsc1{slot}", dv, u)
                t, wt = ws.load(wk[:, :, hc * 128:(hc + 1) * 128])
                ps = PS[2 + hc % 2]
                gate_wait("tensor", f"ps{2 + hc % 2}")
                ws.begin("tensor", t)
                for kk in range(NDC):
                    v = P.op("tensor", "matmul", ps[:, :], wt[:, kk, :], xa[:, kk, cs],
                             start=(kk == 0), stop=(kk == NDC - 1),
                             inc=(pe if kk == NDC - 1 else None))
                ws.done(t, pe, v)
                P.wait("vector", pe, v)
                slot = hc % 2
                gate_wait("vector", f"rsc1{slot}")
                a = P.op("vector", "tensor_scalar", rsc[slot][:, :], ps[:, :],
                         vcol("bk", hc), None, ALU.add, inc=ac)
                P.wait("tensor", ac, a)
                gate_wait("tensor", "ps4")
                v = P.op("tensor", "matmul", PS[4][:, :], cst["RneoxT"][:, :], rsc[slot][:, :],
                         start=True, stop=True, inc=pe)
                P.wait("vector", pe, v)
                P.wait("vector", ac, a)
                P.op("vector", "tensor_tensor", xsc2[:, 0:512], PS[4][:, :],
                     cst["sin1"][:, cs], ALU.mult)
                gate_set("ps4", P.last["vector"][0], P.last["vector"][1])
                P.op("vector", "tensor_tensor", xsc2[:, 512:1024], rsc[slot][:, :],
                     cst["cos1"][:, cs], ALU.mult)
                gate_wait("vector", "k1buf")
                u = P.op("vector", "tensor_add", k1[:, hc, :], xsc2[:, 0:512],
                         xsc2[:, 512:1024], inc=dv)
                gate_set(f"ps{2 + hc % 2}", ac, a)
                gate_set(f"rsc1{slot}", dv, u)
                t, wt = ws.load(wv[:, :, hc * 128:(hc + 1) * 128])
                ps = PS[hc % 2]
                gate_wait("tensor", f"ps{hc % 2}")
                ws.begin("tensor", t)
                for kk in range(NDC):
                    v = P.op("tensor", "matmul", ps[:, :], wt[:, kk, :], xa[:, kk, cs],
                             start=(kk == 0), stop=(kk == NDC - 1),
                             inc=(pe if kk == NDC - 1 else None))
                ws.done(t, pe, v)
                P.wait("vector", pe, v)
                slot = hc % 2
                gate_wait("vector", f"rsc1{slot}")
                u = P.op("vector", "tensor_scalar", rsc[slot][:, :], ps[:, :],
                         vcol("bv", hc), None, ALU.add, inc=dv)
                gate_set(f"ps{hc % 2}", dv, u)
                P.wait("tensor", dv, u)
                gate_wait("tensor", "psT")
                for j in range(4):
                    v = P.op("tensor", "transpose", psT[:, j * 128:(j + 1) * 128],
                             rsc[slot][:, j * 128:(j + 1) * 128], id_bf[:, :],
                             inc=(pe if j == 3 else None))
                P.wait("vector", pe, v)
                gate_wait("vector", "k1buf")
                for j in range(4):
                    u = P.op("vector", "tensor_copy",
                             out=v1[:, j, hc * 128:(hc + 1) * 128],
                             in_=psT[:, j * 128:(j + 1) * 128], inc=dv)
                gate_set("psT", dv, u)
                gate_set(f"rsc1{slot}", dv, u)
            P.wait("scalar", dv, dv.n)
            for c in range(NDC):
                P.op("scalar", "dma_start", inc=agst, incv=16,
                     out=agk1[c][:, tb * 512:(tb + 1) * 512], in_=k1[:, c, :])
            for j in range(4):
                P.op("scalar", "dma_start", inc=agst, incv=16,
                     out=agv1[tb * 4 + j], in_=v1[:, j, :])
            gate_set("k1buf", agst, agst.n)
        P.wait("scalar", dv, dv.n)
        P.op("scalar", "dma_start", inc=agst, incv=16,
             out=D["q1_dram"].rearrange("(c p) t -> p c t", p=128), in_=q1[:, :, :])
        P.wait("gpsimd", agst, agst.n)
        P.op("gpsimd", "collective_compute", "AllGather", ALU.bypass,
             replica_groups=[list(range(NCORE))],
             ins=[D["ag1k_in"].opt()], outs=[D["ag1k_out"].opt()], inc=cc)
        P.op("gpsimd", "collective_compute", "AllGather", ALU.bypass,
             replica_groups=[list(range(NCORE))],
             ins=[D["ag1v_in"].opt()], outs=[D["ag1v_out"].opt()], inc=cc)
        ag_done = cc.n
    P.barrier()

    GSZM = 4
    g_sb = l1es.enter_context(sb("g1", [128, NDC, TPC], BF16))
    with ExitStack() as ph:
        yacc = ph.enter_context(sb("yacc1", [128, NDC, TPC], F32))
        gated = ph.enter_context(sb("gated1", [128, GSZM, TPC], BF16))
        silu_sc = [ph.enter_context(sb(f"sil1{i}", [128, 512], F32)) for i in range(2)]
        cT = ph.enter_context(sb("cT", [8, TPC], F32))
        msel = ph.enter_context(sb("msel", [8, 8 * 128], F32))
        r_es = ExitStack()
        lgT = r_es.enter_context(sb("lgT", [8, TPC], F32))
        lgt = r_es.enter_context(sb("lgt", [128, 8 * 8], F32))
        comb = r_es.enter_context(sb("comb", [128, 8 * 8], F32))
        stat = r_es.enter_context(sb("stat1", [128, 8], F32))
        wsr = WStream(P, "wr", 1, (128, NDC, 8), es=r_es)
        cld = P.sem("cld")
        P.op("sync", "dma_start", inc=cld, incv=16, out=msel[:, :], in_=D["moe_sel"])
        rt = D["router_t"].rearrange("(c p) m -> p c m", p=128)
        t, wt = wsr.load(rt[:, :, :], out_ap_fn=lambda o: o[:, :, 0:8])
        for tb in range(NTB):
            ps = PS[2]
            gate_wait("tensor", "ps2")
            wsr.begin("tensor", t)
            for kk in range(NDC):
                v = P.op("tensor", "matmul", ps[0:8, :], wt[:, kk, 0:8],
                         xn[:, kk, tb * 512:(tb + 1) * 512],
                         start=(kk == 0), stop=(kk == NDC - 1),
                         inc=(pe if kk == NDC - 1 else None))
            P.wait("vector", pe, v)
            u = P.op("vector", "tensor_copy", out=lgT[:, tb * 512:(tb + 1) * 512],
                     in_=ps[0:8, :], inc=dv)
            gate_set("ps2", dv, u)
        P.wait("tensor", dv, dv.n)
        P.wait("tensor", cld, cld.n)
        for j in range(8):
            gate_wait("tensor", "ps4")
            v = P.op("tensor", "transpose", PS[4][0:128, 0:8],
                     lgT[:, j * 128:(j + 1) * 128], id_f32[0:8, 0:8], inc=pe)
            P.wait("vector", pe, v)
            u = P.op("vector", "tensor_copy", out=lgt[:, j * 8:(j + 1) * 8],
                     in_=PS[4][0:128, 0:8], inc=dv)
            gate_set("ps4", dv, u)
        for j in range(8):
            tl = lgt[:, j * 8:(j + 1) * 8]
            P.op("vector", "reduce_max", out=stat[:, 0:1], in_=tl, axis=AX.X)
            P.op("vector", "tensor_scalar", comb[:, j * 8:(j + 1) * 8], tl, stat[:, 0:1],
                 None, ALU.is_equal)
            P.op("vector", "scalar_tensor_tensor", comb[:, j * 8:(j + 1) * 8],
                 comb[:, j * 8:(j + 1) * 8], NEG, tl, ALU.mult, ALU.add)
            P.op("vector", "reduce_max", out=stat[:, 1:2], in_=comb[:, j * 8:(j + 1) * 8],
                 axis=AX.X)
            P.op("vector", "tensor_add", stat[:, 2:3], stat[:, 0:1], stat[:, 1:2])
            P.op("vector", "tensor_scalar", stat[:, 2:3], stat[:, 2:3], 0.5, None, ALU.mult)
            v = P.op("vector", "tensor_scalar", comb[:, j * 8:(j + 1) * 8], tl, stat[:, 2:3],
                     None, ALU.subtract, inc=dv)
            P.wait("scalar", dv, v)
            a = P.op("scalar", "activation", comb[:, j * 8:(j + 1) * 8],
                     comb[:, j * 8:(j + 1) * 8], AF.Sigmoid, scale=2.0, inc=ac)
            P.wait("vector", ac, a)
            P.op("vector", "tensor_scalar", lgt[:, j * 8:(j + 1) * 8], tl, stat[:, 1:2],
                 None, ALU.is_ge)
            u = P.op("vector", "tensor_mul", comb[:, j * 8:(j + 1) * 8],
                     comb[:, j * 8:(j + 1) * 8], lgt[:, j * 8:(j + 1) * 8], inc=dv)
        P.wait("tensor", dv, dv.n)
        for j in range(8):
            gate_wait("tensor", "ps4")
            v = P.op("tensor", "transpose", PS[4][0:8, (j % 4) * 128:(j % 4 + 1) * 128],
                     comb[:, j * 8:(j + 1) * 8], id_f32[:, :], inc=pe)
            P.wait("vector", pe, v)
            u = P.op("vector", "tensor_copy", out=cT[:, j * 128:(j + 1) * 128],
                     in_=PS[4][0:8, (j % 4) * 128:(j % 4 + 1) * 128], inc=dv)
            gate_set("ps4", dv, u)
        P.barrier()
        r_es.close()
        wsg = WStream(P, "mg", 2, (128, NDC, 128), es=ph)
        wsu = WStream(P, "mu", 2, (128, NDC, 128), es=ph)
        wsd = WStream(P, "md", GSZM + 1, (128, NDC, 128), es=ph)

        def ffn_branch(eidx, gw_ap, uw_ap, dw_ap, nhc, gbias_key=None, ubias_key=None,
                       first_overall=False, ce=None):
            ngrp = (nhc + GSZM - 1) // GSZM
            for grp in range(ngrp):
                hcs = list(range(grp * GSZM, min((grp + 1) * GSZM, nhc)))
                dts = []
                for gi, hc in enumerate(hcs):
                    tg, wtg = wsg.load(gw_ap[:, :, hc * 128:(hc + 1) * 128])
                    tu, wtu = wsu.load(uw_ap[:, :, hc * 128:(hc + 1) * 128])
                    td, wtd = wsd.load(dw_ap[hc])
                    dts.append((td, wtd))
                    for tb in range(NTB):
                        cs = slice(tb * 512, (tb + 1) * 512)
                        psg, psu = PS[tb], PS[2 + tb]
                        gate_wait("tensor", f"ps{tb}")
                        gate_wait("tensor", f"ps{2 + tb}")
                        wsg.begin("tensor", tg)
                        wsu.begin("tensor", tu)
                        for kk in range(NDC):
                            P.op("tensor", "matmul", psg[:, :], wtg[:, kk, :], xn[:, kk, cs],
                                 start=(kk == 0), stop=(kk == NDC - 1),
                                 inc=(pe if kk == NDC - 1 else None))
                        for kk in range(NDC):
                            P.op("tensor", "matmul", psu[:, :], wtu[:, kk, :], xn[:, kk, cs],
                                 start=(kk == 0), stop=(kk == NDC - 1),
                                 inc=(pe if kk == NDC - 1 else None))
                        v = pe.n
                        if tb == NTB - 1:
                            wsg.done(tg, pe, v - 1)
                            wsu.done(tu, pe, v)
                        P.wait("scalar", pe, v - 1)
                        slot = tb
                        gate_wait("scalar", f"sil1{slot}")
                        if gbias_key is not None:
                            a = P.op("scalar", "activation", silu_sc[slot][:, :], psg[:, :],
                                     AF.Silu, bias=vcol(gbias_key, hc), inc=ac)
                        else:
                            a = P.op("scalar", "activation", silu_sc[slot][:, :], psg[:, :],
                                     AF.Silu, inc=ac)
                        gate_set(f"ps{tb}", ac, a)
                        P.wait("vector", ac, a)
                        P.wait("vector", pe, v)
                        if ubias_key is not None:
                            P.op("vector", "scalar_tensor_tensor", silu_sc[slot][:, :],
                                 psu[:, :], vcol(ubias_key, hc), silu_sc[slot][:, :],
                                 ALU.add, ALU.mult)
                            gate_set(f"ps{2 + tb}", P.last["vector"][0], P.last["vector"][1])
                        else:
                            P.op("vector", "tensor_tensor", silu_sc[slot][:, :], psu[:, :],
                                 silu_sc[slot][:, :], ALU.mult)
                            gate_set(f"ps{2 + tb}", P.last["vector"][0], P.last["vector"][1])
                        if ce is not None:
                            u = P.op("vector", "tensor_tensor", gated[:, gi, cs],
                                     silu_sc[slot][:, :], ce[tb][:, :], ALU.mult, inc=dv)
                        else:
                            u = P.op("vector", "tensor_copy", out=gated[:, gi, cs],
                                     in_=silu_sc[slot][:, :], inc=dv)
                        gate_set(f"sil1{slot}", dv, u)
                gated_done = dv.n
                P.wait("tensor", dv, gated_done)
                for dc in range(NDC):
                    for tb in range(NTB):
                        cs = slice(tb * 512, (tb + 1) * 512)
                        gate_wait("tensor", "ps6")
                        for gi in range(len(hcs)):
                            td, wtd = dts[gi]
                            wsd.begin("tensor", td)
                            v = P.op("tensor", "matmul", PS[6][:, :], wtd[:, dc, :],
                                     gated[:, gi, cs],
                                     start=(gi == 0), stop=(gi == len(hcs) - 1),
                                     inc=(pe if gi == len(hcs) - 1 else None))
                        P.wait("vector", pe, v)
                        if first_overall and grp == 0:
                            u = P.op("vector", "tensor_copy", out=yacc[:, dc, cs],
                                     in_=PS[6][:, :], inc=dv)
                        else:
                            u = P.op("vector", "tensor_add", yacc[:, dc, cs], yacc[:, dc, cs],
                                     PS[6][:, :], inc=dv)
                        gate_set("ps6", dv, u)
                for gi in range(len(hcs)):
                    wsd.done(dts[gi][0], dv, u)

        gw = D["gate_w"]
        uw = D["up_w"]
        dwr = D["down_w"]
        for e in range(NEXP):
            P.wait("tensor", dv, dv.n)
            gate_wait("tensor", "ps4")
            gate_wait("tensor", "ps5")
            v = P.op("tensor", "matmul", PS[4][:, :], msel[:, e * 128:(e + 1) * 128],
                     cT[:, 0:512], start=True, stop=True, inc=pe)
            v = P.op("tensor", "matmul", PS[5][:, :], msel[:, e * 128:(e + 1) * 128],
                     cT[:, 512:1024], start=True, stop=True, inc=pe)
            P.wait("vector", pe, v)
            gwe = gw[e].rearrange("(c p) m -> p c m", p=128)
            uwe = uw[e].rearrange("(c p) m -> p c m", p=128)
            dwe = dwr[e].rearrange("(hc p) (c m) -> hc p c m", p=128, m=128)
            ffn_branch(e, gwe, uwe, dwe, MOE_HC, first_overall=(e == 0),
                       ce=[PS[4], PS[5]])
            gate_set("ps4", dv, dv.n)
            gate_set("ps5", dv, dv.n)
        sgw = D["s_gate_w"].rearrange("(c p) m -> p c m", p=128)
        suw = D["s_up_w"].rearrange("(c p) m -> p c m", p=128)
        sdw = D["s_down_w"].rearrange("(hc p) (c m) -> hc p c m", p=128, m=128)
        ffn_branch(-1, sgw, suw, sdw, SI_C, gbias_key="sgb", ubias_key="sub")
        P.wait("scalar", dv, dv.n)
        for dc in range(NDC):
            a = P.op("scalar", "activation", g_sb[:, dc, :], yacc[:, dc, :], AF.Silu,
                     bias=vcol("sdb", dc), inc=ac)
    P.barrier()
    if stop_after == "moe":
        P.wait("gpsimd", ac, ac.n)
        for c in range(NDC):
            P.op("gpsimd", "dma_start", inc=ost, incv=16,
                 out=out_dram.rearrange("(c p) t -> c p t", p=128)[c], in_=g_sb[:, c, :])
        P.wait("gpsimd", ost, ost.n)
        l1es.close()
        return

    with ExitStack() as ph:
        qh = ph.enter_context(sb("qh1", [128, TPC], BF16))
        kslot = ph.enter_context(sb("kslot1", [128, NSL_E0 + NSL_L0, 512], BF16))
        kown = ph.enter_context(sb("kown1", [128, NTB, 512], BF16))
        vslot = ph.enter_context(sb("vslot1", [128, (NSL_E0 + NSL_L0) * 4, DK], BF16))
        vown = ph.enter_context(sb("vown1", [128, NTB * 4, DK], BF16))
        cm = ph.enter_context(sb("cm1", [128, 8, NSL_E0 + NSL_L0 + 2], F32))
        mall = ph.enter_context(sb("mall1", [128, 8], F32))
        negm = ph.enter_context(sb("negm1", [1, 512], F32))
        mb = ph.enter_context(sb("mb1", [128, 512], F32))
        ssc = [ph.enter_context(sb(f"ssc{i}", [128, 512], F32)) for i in range(2)]
        pT = [ph.enter_context(sb(f"pT1{i}", [128, 512], BF16)) for i in range(3)]
        rbv = ph.enter_context(sb("rbv1", [128, 512], F32))
        attn1 = ph.enter_context(sb("attn1", [128, NDC, TPC], BF16))
        maskA_sb = ph.enter_context(sb("maskA1_sb", [128, 4, 512], F32))
        maskB_sb = ph.enter_context(sb("maskB1_sb", [128, 4, 512], F32))
        padA_b = ph.enter_context(sb("padA_b", [128, 32], F32))
        padB_c = ph.enter_context(sb("padB_c", [128, 32], F32))
        cmadj = ph.enter_context(sb("cmadj", [128, 32], F32))
        NSL = NSL_E0 + NSL_L0
        cld = P.sem("cld")
        P.op("sync", "dma_start", inc=cld, incv=16, out=maskA_sb[:, :, :], in_=D["maskA"])
        P.op("sync", "dma_start", inc=cld, incv=16, out=maskB_sb[:, :, :], in_=D["maskB"])
        P.op("sync", "dma_start", inc=cld, incv=16, out=padB_c[:, 0:27], in_=D["padB1"])
        P.op("sync", "dma_start", inc=cld, incv=16, out=padA_b[:, 0:26], in_=D["padA1"])
        P.wait("vector", cld, cld.n)
        P.wait("tensor", cld, cld.n)
        P.wait("sync", cc, ag_done)
        P.wait("tensor", cc, ag_done)
        agk = D["ag1k_out"].rearrange("(r c p) t -> r c p t", p=128, c=NDC)
        agv = D["ag1v_out"].rearrange("(b p) f -> p b f", p=128)
        agk_own = D["ag1k_in"].rearrange("(c p) t -> c p t", p=128)
        agv_own = D["ag1v_in"].rearrange("(b p) f -> p b f", p=128)
        q1d = D["q1_dram"].rearrange("(c p) t -> c p t", p=128)
        kv1 = XStream(P, "kv1", "sync", 1)

        for h in range(H):
            gate_wait("sync", "heads1")
            P.op("sync", "dma_start", inc=kvld, incv=16, out=qh[:, :], in_=q1d[h])
            for j in range(NSL):
                g = j if j < NSL_E0 else j - NSL_E0
                r, s = BLOCK_OWNER[g]
                P.op("sync", "dma_start", inc=kvld, incv=16,
                     out=kslot[:, j, :], in_=agk[r, h, :, s * 512:(s + 1) * 512])
                P.op("sync", "dma_start", inc=kvld, incv=16,
                     out=vslot[:, 4 * j:4 * j + 4, :],
                     in_=agv[:, 8 * r + 4 * s:8 * r + 4 * s + 4, h * DK:(h + 1) * DK])
            for s in range(NTB):
                P.op("sync", "dma_start", inc=kvld, incv=16,
                     out=kown[:, s, :], in_=agk_own[h][:, s * 512:(s + 1) * 512])
                v = P.op("sync", "dma_start", inc=kvld, incv=16,
                         out=vown[:, 4 * s:4 * s + 4, :],
                         in_=agv_own[:, 4 * s:4 * s + 4, h * DK:(h + 1) * DK])
            ld_done = v
            P.wait("tensor", kvld, ld_done)

            for qtl in range(4):
                for role in range(2):
                    qs = role * 512 + qtl * 128
                    nsl = NSL_E0 if role == 0 else NSL_L0
                    base = 0 if role == 0 else NSL_E0
                    for jj in range(nsl + 1):
                        ps = PS[jj % 2]
                        gate_wait("tensor", f"ps{jj % 2}")
                        rhs = kown[:, role, :] if jj == nsl else kslot[:, base + jj, :]
                        v = P.op("tensor", "matmul", ps[:, :], qh[:, qs:qs + 128], rhs,
                                 start=True, stop=True, inc=pe)
                        P.wait("vector", pe, v)
                        if jj == nsl:
                            P.op("vector", "tensor_add", ps[:, :], ps[:, :],
                                 maskA_sb[:, qtl, :])
                        col = base + jj if jj < nsl else NSL + role
                        u = P.op("vector", "reduce_max",
                                 out=cm[:, role * 4 + qtl, col:col + 1],
                                 in_=ps[:, :], axis=AX.X, inc=dv)
                        gate_set(f"ps{jj % 2}", dv, u)
            for qtl in range(4):
                for role in range(2):
                    qi = role * 4 + qtl
                    if role == 0:
                        P.op("vector", "tensor_add", cmadj[:, 0:NSL_E0],
                             cm[:, qi, 0:NSL_E0], padA_b[:, 0:NSL_E0])
                        P.op("vector", "reduce_max", out=mall[:, qi:qi + 1],
                             in_=cmadj[:, 0:NSL_E0], axis=AX.X)
                        u = P.op("vector", "tensor_max", mall[:, qi:qi + 1],
                                 mall[:, qi:qi + 1], cm[:, qi, NSL:NSL + 1], inc=dv)
                    else:
                        P.op("vector", "tensor_add", cmadj[:, NSL_E0:NSL_E0 + NSL_L0],
                             cm[:, qi, NSL_E0:NSL_E0 + NSL_L0],
                             padA_b[:, NSL_E0:NSL_E0 + NSL_L0])
                        P.op("vector", "reduce_max", out=mall[:, qi:qi + 1],
                             in_=cmadj[:, NSL_E0:NSL_E0 + NSL_L0], axis=AX.X)
                        u = P.op("vector", "tensor_max", mall[:, qi:qi + 1],
                                 mall[:, qi:qi + 1], cm[:, qi, NSL + 1:NSL + 2], inc=dv)
            for role in range(2):
                P.wait("tensor", dv, dv.n)
                gate_wait("tensor", "ps4")
                for qtl in range(4):
                    v = P.op("tensor", "transpose", PS[4][0:1, qtl * 128:(qtl + 1) * 128],
                             mall[:, role * 4 + qtl:role * 4 + qtl + 1], id_f32[:, :],
                             inc=(pe if qtl == 3 else None))
                P.wait("vector", pe, v)
                gate_wait("vector", "negm1")
                u = P.op("vector", "tensor_copy", out=negm[0:1, :], in_=PS[4][0:1, 0:512],
                         inc=dv)
                gate_set("ps4", dv, u)
                P.wait("tensor", dv, u)
                gate_wait("tensor", "ps4")
                v = P.op("tensor", "matmul", PS[4][:, :], ones_row[0:1, :], negm[0:1, :],
                         start=True, stop=True, inc=pe)
                gate_set("negm1", pe, v)
                P.wait("vector", pe, v)
                gate_wait("vector", "mb1")
                u = P.op("vector", "tensor_copy", out=mb[:, :], in_=PS[4][:, :], inc=dv)
                gate_set("ps4", dv, u)
                P.wait("tensor", dv, u)
                mb_ready = u
                nsl = NSL_E0 if role == 0 else NSL_L0
                base = 0 if role == 0 else NSL_E0
                qs = slice(role * 512, (role + 1) * 512)
                first = True
                gate_wait("tensor", "ps6")
                gate_wait("tensor", "ps5")
                nkc = (nsl + 1) * 4
                for t_i in range(nkc):
                    jj, kc = divmod(t_i, 4)
                    own = jj == nsl
                    psb = PS[2 + (t_i % 2)]
                    gate_wait("tensor", f"ps{2 + (t_i % 2)}")
                    if own:
                        lhsT = kown[:, role, kc * 128:(kc + 1) * 128]
                    else:
                        lhsT = kslot[:, base + jj, kc * 128:(kc + 1) * 128]
                    v = P.op("tensor", "matmul", psb[:, :], lhsT, qh[:, qs],
                             start=True, stop=True, inc=pe)
                    P.wait("vector", pe, v)
                    if own:
                        P.op("vector", "tensor_add", psb[:, :], psb[:, :],
                             maskB_sb[:, kc, :])
                    slot_s = t_i % 2
                    gate_wait("vector", f"ssc{slot_s}")
                    padcol = padB_c[:, 26:27] if own else padB_c[:, base + jj:base + jj + 1]
                    u = P.op("vector", "scalar_tensor_tensor", ssc[slot_s][:, :], psb[:, :],
                             padcol, mb[:, :], ALU.add, ALU.subtract, inc=dv)
                    gate_set(f"ps{2 + (t_i % 2)}", dv, u)
                    P.wait("scalar", dv, u)
                    pt = pT[t_i % 3]
                    gate_wait("scalar", f"pT1{t_i % 3}")
                    a = P.op("scalar", "activation", pt[:, :], ssc[slot_s][:, :], AF.Exp,
                             inc=ac)
                    gate_set(f"ssc{slot_s}", ac, a)
                    P.wait("tensor", ac, a)
                    vv = vown[:, 4 * role + kc, :] if own else vslot[:, 4 * (base + jj) + kc, :]
                    P.op("tensor", "matmul", PS[5][0:1, :], shared["ones_col_bf"][:, :],
                         pt[:, :], start=first, stop=(t_i == nkc - 1))
                    u = P.op("tensor", "matmul", PS[6][:, :], vv, pt[:, :],
                             start=first, stop=(t_i == nkc - 1), inc=pe)
                    gate_set(f"pT1{t_i % 3}", pe, u)
                    first = False
                P.wait("vector", pe, pe.n)
                gate_wait("vector", "negm1")
                u = P.op("vector", "reciprocal", inc=dv, out=negm[0:1, :], in_=PS[5][0:1, :])
                gate_set("ps5", dv, u)
                P.wait("tensor", dv, u)
                gate_wait("tensor", "ps4")
                v = P.op("tensor", "matmul", PS[4][:, :], ones_row[0:1, :], negm[0:1, :],
                         start=True, stop=True, inc=pe)
                gate_set("negm1", pe, v)
                P.wait("vector", pe, v)
                P.op("vector", "tensor_copy", out=rbv[:, :], in_=PS[4][:, :])
                gate_set("ps4", P.last["vector"][0], P.last["vector"][1])
                u = P.op("vector", "tensor_tensor", attn1[:, h, qs], PS[6][:, :],
                         rbv[:, :], ALU.mult, inc=dv)
                gate_set("ps6", dv, u)
                gate_set("mb1", dv, u)
            gate_set("heads1", dv, dv.n)
        if stop_after == "attn1":
            P.wait("gpsimd", dv, dv.n)
            for c in range(NDC):
                P.op("gpsimd", "dma_start", inc=ost, incv=16,
                     out=out_dram.rearrange("(c p) t -> c p t", p=128)[c], in_=attn1[:, c, :])
            P.wait("gpsimd", ost, ost.n)
    P.barrier()
    if stop_after == "attn1":
        l1es.close()
        return

    with ExitStack() as ph:
        xbuf = [ph.enter_context(sb(f"xo{i}", [128, TPC], F32)) for i in range(2)]
        x1b = [ph.enter_context(sb(f"xob{i}", [128, TPC], F32)) for i in range(2)]
        tsc = ph.enter_context(sb("tsc1", [128, TPC], F32))
        nm_b = ph.enter_context(sb("nm1b2", [128, TPC], F32))
        rstd_b = ph.enter_context(sb("rstd1b2", [128, TPC], F32))
        ws = WStream(P, "wo", 2, (128, NDC, 128), es=ph)
        wo = D["wo_t"].rearrange("(c p) m -> p c m", p=128)
        xs2 = XStream(P, "xo", "sync", 2)
        st2 = XStream(P, "xos", "sync", 2)
        st_toks = []
        for tb in range(NTB):
            cs = slice(tb * 512, (tb + 1) * 512)
            gate_wait("tensor", "ps0")
            gate_wait("tensor", "ps1")
            P.op("tensor", "matmul", PS[0][:, :], ones_row[0:1, :], nm_row[:, cs],
                 start=True, stop=True)
            v = P.op("tensor", "matmul", PS[1][:, :], ones_row[0:1, :], rstd_row[:, cs],
                     start=True, stop=True, inc=pe)
            P.wait("vector", pe, v)
            u0 = P.op("vector", "tensor_copy", out=nm_b[:, cs], in_=PS[0][:, :], inc=dv)
            u = P.op("vector", "tensor_copy", out=rstd_b[:, cs], in_=PS[1][:, :], inc=dv)
            gate_set("ps0", dv, u0)
            gate_set("ps1", dv, u)
        for dc in range(NDC):
            t, wt = ws.load(wo[:, :, dc * 128:(dc + 1) * 128])
            slot = dc % 2
            gate_wait("sync", f"xo{slot}")
            xtok = xs2.dma(slot, xbuf[slot][:], xr[dc])
            if dc >= 2:
                st2.wait("vector", st_toks[dc - 2])
            for tb in range(NTB):
                cs = slice(tb * 512, (tb + 1) * 512)
                ps = PS[tb]
                gate_wait("tensor", f"ps{tb}")
                ws.begin("tensor", t)
                for k in range(NDC):
                    v = P.op("tensor", "matmul", ps[:, :], wt[:, k, :], attn1[:, k, cs],
                             start=(k == 0), stop=(k == NDC - 1),
                             inc=(pe if k == NDC - 1 else None))
                if tb == NTB - 1:
                    ws.done(t, pe, v)
                P.wait("vector", pe, v)
                xs2.wait("vector", xtok)
                P.op("vector", "tensor_add", tsc[:, cs], xbuf[slot][:, cs], nm_b[:, cs])
                P.op("vector", "tensor_mul", tsc[:, cs], tsc[:, cs], rstd_b[:, cs])
                P.op("vector", "scalar_tensor_tensor", tsc[:, cs], ps[:, :],
                     vcol("bo", dc), tsc[:, cs], ALU.add, ALU.add)
                gate_set(f"ps{tb}", P.last["vector"][0], P.last["vector"][1])
                P.op("vector", "tensor_mul", tsc[:, cs], tsc[:, cs], g_sb[:, dc, cs])
                u = P.op("vector", "tensor_add", x1b[slot][:, cs], tsc[:, cs],
                         xbuf[slot][:, cs], inc=dv)
            gate_set(f"xo{slot}", dv, u)
            P.wait("sync", dv, u)
            st_toks.append(st2.dma(slot, out_dram.rearrange("(c p) t -> c p t", p=128)[dc],
                                   x1b[slot][:]))
        for tok in st_toks[-2:]:
            st2.wait("sync", tok)
    l1es.close()
    P.barrier()


def declare_dram_l1(nc, with_out=True, x_ap=None, skip=()):
    D = {}

    def inp(name, shape, dt=F32):
        if name in skip:
            return
        D[name] = nc.dram_tensor(name, list(shape), dt, kind="ExternalInput").ap()

    if x_ap is None:
        inp("x", (DM, TPC))
    else:
        D["x"] = x_ap
    inp("wq_t", (DM, DM))
    inp("wk_t", (DM, DM))
    inp("wv_t", (DM, DM))
    inp("wo_t", (DM, DM))
    inp("bq1", (DM,))
    inp("bk1", (DM,))
    inp("bv1", (DM,))
    inp("bo1", (DM,))
    inp("lng1", (DM,))
    inp("lnb1", (DM,))
    inp("router_t", (DM, 8))
    inp("gate_w", (NEXP, DM, MOE_I))
    inp("up_w", (NEXP, DM, MOE_I))
    inp("down_w", (NEXP, MOE_I, DM))
    inp("s_gate_w", (DM, SI))
    inp("sgb1", (SI,))
    inp("s_up_w", (DM, SI))
    inp("sub1", (SI,))
    inp("s_down_w", (SI, DM))
    inp("sdb1", (DM,))
    inp("id_f32", (128, 128))
    inp("RneoxT", (128, 128), BF16)
    inp("cos1", (128, TPC))
    inp("sin1", (128, TPC))
    inp("cos1q", (128, TPC))
    inp("sin1q", (128, TPC))
    inp("maskA_h", (4, 128, 512))
    inp("maskB_h", (4, 128, 512))
    inp("moe_sel", (8, 8 * 128))
    inp("padA1", (128, 26))
    inp("padB1", (128, 27))
    if "maskA_h" in D:
        D["maskA"] = D["maskA_h"].rearrange("d p f -> p d f")
        D["maskB"] = D["maskB_h"].rearrange("d p f -> p d f")
    D["ag1k_in"] = nc.dram_tensor("ag1k_in", [DM, TPC], BF16).ap()
    D["ag1k_out"] = nc.dram_tensor("ag1k_out", [NCORE * DM, TPC], BF16).ap()
    D["ag1v_in"] = nc.dram_tensor("ag1v_in", [TPC, DM], BF16).ap()
    D["ag1v_out"] = nc.dram_tensor("ag1v_out", [SEQ, DM], BF16).ap()
    D["q1_dram"] = nc.dram_tensor("q1_dram", [DM, TPC], BF16).ap()
    if with_out:
        D["xout"] = nc.dram_tensor("xout", [DM, TPC], F32, kind="ExternalOutput").ap()
    return D


def host_stage_l1(core, inputs, cst, x1):
    blocks = CORE_BLOCKS[core]
    T = lambda a: np.ascontiguousarray(np.asarray(a).T)
    m = {}
    if x1 is not None:
        xs = np.concatenate([x1[g * 512:(g + 1) * 512] for g in blocks], 0)
        m["x"] = np.ascontiguousarray(xs.T.astype(np.float32))
    m.update({
        "wq_t": T(inputs["L1_attn_wq"]),
        "wk_t": T(inputs["L1_attn_wk"]),
        "wv_t": T(inputs["L1_attn_wv"]),
        "wo_t": T(inputs["L1_attn_wo"]),
        "bq1": inputs["L1_attn_bq"],
        "bk1": inputs["L1_attn_bk"],
        "bv1": inputs["L1_attn_bv"],
        "bo1": inputs["L1_attn_bo"],
        "lng1": inputs["L1_attn_ln_g"],
        "lnb1": inputs["L1_attn_ln_b"],
        "router_t": T(inputs["L1_ffn_router_w"]),
        "gate_w": inputs["L1_ffn_gate_w"],
        "up_w": inputs["L1_ffn_up_w"],
        "down_w": inputs["L1_ffn_down_w"],
        "s_gate_w": inputs["L1_ffn_s_gate_w"],
        "sgb1": inputs["L1_ffn_s_gate_b"],
        "s_up_w": inputs["L1_ffn_s_up_w"],
        "sub1": inputs["L1_ffn_s_up_b"],
        "s_down_w": inputs["L1_ffn_s_down_w"],
        "sdb1": inputs["L1_ffn_s_down_b"],
        "id_f32": cst["id_f32"],
        "RneoxT": cst["RneoxT"],
        "cos1": cst["cos1"],
        "sin1": cst["sin1"],
        "cos1q": cst["cos1q"],
        "sin1q": cst["sin1q"],
        "maskA_h": cst["maskA"],
        "maskB_h": cst["maskB"],
        "moe_sel": cst["moe_sel"],
    })
    padA = np.zeros((1, 26), np.float32)
    padB = np.zeros((1, 26), np.float32)
    for j in range(NSL_E0):
        if j >= blocks[0]:
            padA[0, j] = NEG
            padB[0, j] = NEG
    for j in range(NSL_L0):
        if j >= blocks[1]:
            padA[0, NSL_E0 + j] = NEG
            padB[0, NSL_E0 + j] = NEG
    m["padA1"] = np.broadcast_to(padA, (128, 26)).copy()
    padB27 = np.concatenate([padB, np.zeros((1, 1), np.float32)], 1)
    m["padB1"] = np.broadcast_to(padB27, (128, 27)).copy()
    out = {}
    for k, v in m.items():
        a = np.asarray(v)
        if k == "RneoxT":
            out[k] = np.ascontiguousarray(a.astype(bfloat16))
        else:
            out[k] = np.ascontiguousarray(a.astype(np.float32))
    return out


def build_fused_program():
    nc = bass.Bass("TRN2", target_bir_lowering=False, debug=False)
    es = ExitStack()
    D0 = declare_dram_l0(nc, with_out=False)
    x1d = nc.dram_tensor("x1_mid", [DM, TPC], F32)
    D1 = declare_dram_l1(nc, with_out=True, x_ap=x1d.ap(), skip=("id_f32", "maskA_h", "maskB_h"))
    D1["id_f32"] = D0["id_f32"]
    D1["maskA"] = D0["maskA"]
    D1["maskB"] = D0["maskB"]
    DD = dict(D0)
    DD.update({k: v for k, v in D1.items() if k not in DD})
    P = Prog(nc, es)
    shared = emit_shared(P, nc, DD, es, layers=[0, 1])
    emit_l0(P, nc, D0, shared, x1d.ap())
    emit_l1(P, nc, D1, shared, D1["xout"])
    P.finish()
    es.close()
    return nc


def build_l1_program(stop_after=None):
    nc = bass.Bass("TRN2", target_bir_lowering=False, debug=False)
    es = ExitStack()
    D = declare_dram_l1(nc)
    P = Prog(nc, es)
    shared = emit_shared(P, nc, D, es, layers=[1])
    emit_l1(P, nc, D, shared, D["xout"], stop_after=stop_after)
    P.finish()
    es.close()
    return nc



_CST = [host_constants(c) for c in range(NCORE)]


def kernel(**inputs):
    nc = build_fused_program()
    in_maps = []
    for c in range(NCORE):
        m = host_stage_l0(c, inputs, _CST[c])
        m1 = host_stage_l1(c, inputs, _CST[c], None)
        for k, v in m1.items():
            if k not in m and k != "x":
                m[k] = v
        in_maps.append(m)
    res = run_bass_kernel_spmd(nc, in_maps, core_ids=list(range(NCORE)))
    out = np.zeros((1, SEQ, DM), np.float32)
    for c in range(NCORE):
        sh = np.asarray(res.results[c]["xout"])
        b = CORE_BLOCKS[c]
        out[0, b[0] * 512:(b[0] + 1) * 512] = sh.T[:512]
        out[0, b[1] * 512:(b[1] + 1) * 512] = sh.T[512:]
    return out


# version 03
# speedup vs baseline: 1.0431x; 1.0299x over previous
import numpy as np
import ml_dtypes
from contextlib import ExitStack
import concourse.bass as bass
import concourse.mybir as mybir
from concourse.bass_utils import run_bass_kernel_spmd

F32 = mybir.dt.float32
BF16 = mybir.dt.bfloat16
AF = mybir.ActivationFunctionType
ALU = mybir.AluOpType
AX = mybir.AxisListType
bfloat16 = ml_dtypes.bfloat16

NCORE = 8
SEQ = 8192
DM = 2048
NDC = 16
TPC = 1024
NTB = 2
NGB = SEQ // 512

CORE_BLOCKS = [[c, NGB - 1 - c] for c in range(NCORE)]
BLOCK_OWNER = {}
for _c in range(NCORE):
    for _s, _g in enumerate(CORE_BLOCKS[_c]):
        BLOCK_OWNER[_g] = (_c, _s)

H = 16
NOPE_D, ROPE_D, V_D = 64, 32, 64
Q_RANK, KV_RANK = 768, 256
QR_C, KV_C = Q_RANK // 128, KV_RANK // 128
CAT_D = 97
VA = V_D + 1
SCALE0 = 1.0 / float(np.sqrt(NOPE_D + ROPE_D))
DK = 128
SCALE1 = 1.0 / float(np.sqrt(DK))
NEXP, MOE_I = 8, 1408
MOE_HC = MOE_I // 128
SI = 2816
SI_C = SI // 128
FH = 8192
FH_C = FH // 128

NEG = -1.0e30
DEBUG_PLAN = False
NSL_E0 = 8
NSL_L0 = 16
EPS_LN = 1e-5
EPS_RMS = 1e-6



def host_constants(core):
    cst = {}
    pos = np.concatenate([np.arange(g * 512, g * 512 + 512) for g in CORE_BLOCKS[core]])
    f0 = 1.0 / (10000.0 ** (np.arange(0, ROPE_D, 2, dtype=np.float64) / ROPE_D))
    ang0 = np.outer(pos.astype(np.float64), f0)
    c0 = np.cos(ang0)
    t0 = np.zeros((128, TPC), np.float64)
    for r in range(128):
        t0[r] = c0[:, (r % 32) // 2]
    cst["cos0q"] = (t0 * SCALE0).astype(np.float32)
    cst["cos0k"] = (t0[:32] / H).astype(np.float32)
    f1 = 1.0 / (10000.0 ** (np.arange(0, DK, 2, dtype=np.float64) / DK))
    ang1 = np.outer(pos.astype(np.float64), f1)
    c1, s1 = np.cos(ang1), np.sin(ang1)
    t1c = np.zeros((128, TPC), np.float64)
    t1s = np.zeros((128, TPC), np.float64)
    for r in range(128):
        t1c[r] = c1[:, r % 64]
        t1s[r] = s1[:, r % 64]
    cst["cos1"] = t1c.astype(np.float32)
    cst["sin1"] = t1s.astype(np.float32)
    cst["cos1q"] = (t1c * SCALE1).astype(np.float32)
    cst["sin1q"] = (t1s * SCALE1).astype(np.float32)
    i_ = np.arange(128)[:, None]
    j_ = np.arange(512)[None, :]
    mA = np.stack([np.where(j_ - i_ > d, NEG, 0.0) for d in (0, 128, 256, 384)])
    mB = np.stack([np.where(i_ - j_ > d, NEG, 0.0) for d in (0, -128, -256, -384)])
    cst["maskA"] = mA.astype(np.float32)
    cst["maskB"] = mB.astype(np.float32)
    R32 = np.zeros((32, 32), np.float32)
    for p_ in range(16):
        R32[2 * p_, 2 * p_] = 1.0
        R32[2 * p_, 2 * p_ + 1] = -1.0
        R32[2 * p_ + 1, 2 * p_] = 1.0
        R32[2 * p_ + 1, 2 * p_ + 1] = 1.0
    R4 = np.zeros((128, 128), np.float32)
    for hh in range(4):
        R4[hh * 32:(hh + 1) * 32, hh * 32:(hh + 1) * 32] = R32
    cst["R4T"] = np.ascontiguousarray(R4.T).astype(bfloat16)
    cst["R32T"] = np.ascontiguousarray(R32.T).astype(bfloat16)
    Rn = np.zeros((128, 128), np.float32)
    for i in range(64):
        Rn[i, 64 + i] = -1.0
        Rn[64 + i, i] = 1.0
    cst["RneoxT"] = np.ascontiguousarray(Rn.T).astype(bfloat16)
    cst["id_f32"] = np.eye(128, dtype=np.float32)
    sel = np.zeros((8, 8 * 128), np.float32)
    for e in range(8):
        sel[e, e * 128:(e + 1) * 128] = 1.0
    cst["moe_sel"] = sel
    return cst


MASKA_D = {0: 0, 128: 1, 256: 2, 384: 3}
MASKB_D = {0: 0, -128: 1, -256: 2, -384: 3}



class Sem:
    def __init__(self, h):
        self.h = h
        self.n = 0

    def bump(self, k=1):
        self.n += k
        return self.n


class Prog:
    ENGINES = ("tensor", "vector", "scalar", "gpsimd", "sync")

    def __init__(self, nc, es):
        self.nc = nc
        self.es = es
        self.q = {e: [] for e in self.ENGINES}
        self._sems = {}
        self.taps = {}

    def sem(self, name):
        if name not in self._sems:
            h = self.es.enter_context(self.nc.semaphore(name))
            self._sems[name] = Sem(h)
        return self._sems[name]

    _NOCHAIN = ("nop", "dma_start", "collective_compute", "wait_ge", "sem_inc")

    def op(self, eng, meth, *args, inc=None, incv=1, **kw):
        chained = eng in ("vector", "scalar", "gpsimd") and meth not in self._NOCHAIN
        if chained:
            if not hasattr(self, "last"):
                self.last = {}
            prev = self.last.get(eng)
            if prev is not None:
                self.q[eng].append(lambda e, s=prev[0].h, v=prev[1]: e.wait_ge(s, v))
            if inc is None:
                inc = self.sem("ch_" + eng)
                incv = 1
        post = inc.bump(incv) if inc is not None else None
        if chained:
            self.last[eng] = (inc, post)

        import traceback
        stk = traceback.extract_stack()[-3:-1] if DEBUG_PLAN else None

        def f(e, meth=meth, args=args, kw=kw, inc=inc, incv=incv, stk=stk):
            try:
                ins = getattr(e, meth)(*args, **kw)
            except Exception:
                if stk:
                    print("PLAN SITE:", "".join(traceback.format_list(stk)))
                raise
            if inc is not None:
                ins.then_inc(inc.h, incv)
        self.q[eng].append(f)
        return post

    def wait(self, eng, sem, val):
        self.q[eng].append(lambda e, s=sem.h, v=val: e.wait_ge(s, v))

    def barrier(self, engines=None):
        engines = engines or self.ENGINES
        b = self.sem("bar")
        for eng in ("tensor", "sync"):
            if eng in engines:
                self.op(eng, "nop", inc=b)
        nv = b.n
        lasts = []
        if hasattr(self, "last"):
            for e in ("vector", "scalar", "gpsimd"):
                lp = self.last.get(e)
                if lp is not None:
                    lasts.append(lp)
        for eng in engines:
            self.wait(eng, b, nv)
            for (s, v) in lasts:
                self.wait(eng, s, v)

    def finish(self):
        nc = self.nc
        with nc.Block() as block:
            @block.tensor
            def _(t):
                for f in self.q["tensor"]:
                    f(t)

            @block.vector
            def _(v):
                for f in self.q["vector"]:
                    f(v)

            @block.scalar
            def _(s):
                for f in self.q["scalar"]:
                    f(s)

            @block.gpsimd
            def _(g):
                for f in self.q["gpsimd"]:
                    f(g)

            @block.sync
            def _(s):
                for f in self.q["sync"]:
                    f(s)


class WStream:

    def __init__(self, P, name, nbuf, shape, es=None):
        self.P = P
        self.nbuf = nbuf
        es = es if es is not None else P.es
        self.tiles = [es.enter_context(P.nc.sbuf_tensor(f"{name}{i}", list(shape), BF16))
                      for i in range(nbuf)]
        self.ld = [P.sem(f"{name}_ld{i}") for i in range(nbuf)]
        self.done_pts = {}
        self.nload = 0

    def load(self, in_ap, out_ap_fn=None):
        t = self.nload
        slot = t % self.nbuf
        if t >= self.nbuf:
            s, v = self.done_pts[t - self.nbuf]
            self.P.wait("gpsimd", s, v)
        out = self.tiles[slot]
        out_ap = out_ap_fn(out) if out_ap_fn else out[:]
        self.P.op("gpsimd", "dma_start", inc=self.ld[slot], incv=16, out=out_ap, in_=in_ap)
        self.nload += 1
        return t, out

    def begin(self, eng, t):
        self.P.wait(eng, self.ld[t % self.nbuf], 16 * (t // self.nbuf + 1))

    def done(self, t, sem, val):
        self.done_pts[t] = (sem, val)


class XStream:

    def __init__(self, P, name, queue, nslots):
        self.P = P
        self.queue = queue
        self.nslots = nslots
        self.sems = [P.sem(f"{name}_s{i}") for i in range(nslots)]
        self.cnt = [0] * nslots

    def dma(self, slot, out, in_, **kw):
        self.cnt[slot] += 1
        self.P.op(self.queue, "dma_start", inc=self.sems[slot], incv=16,
                  out=out, in_=in_, **kw)
        return slot, 16 * self.cnt[slot]

    def wait(self, eng, token):
        slot, v = token
        self.P.wait(eng, self.sems[slot], v)


def bcast_row(P, row_ap, width, ps, dst_ap, ones_row_f32, pe, dv, src_ready=None):
    if src_ready is not None:
        P.wait("tensor", src_ready[0], src_ready[1])
    v = P.op("tensor", "matmul", ps[:, 0:width], ones_row_f32[0:1, 0:128], row_ap,
             start=True, stop=True, inc=pe)
    P.wait("vector", pe, v)
    return P.op("vector", "tensor_copy", inc=dv, out=dst_ap, in_=ps[:, 0:width])



def emit_l0(P, nc, D, shared, out_dram, stop_after=None):
    es = P.es
    pe = P.sem("pe")
    dv = P.sem("dv")
    ac = P.sem("ac")
    xld = P.sem("xld")
    kvld = P.sem("kvld")
    agst = P.sem("agst")
    cc = P.sem("cc")
    ost = P.sem("ost")
    sb = P.nc.sbuf_tensor
    ones_col = shared["ones_col"]
    ones_row = shared["ones_row"]
    id_f32 = shared["id_f32"]
    PS = shared["PS"]
    psT = shared["psT"]
    gates = {}

    def gate_wait(eng, res):
        if res in gates:
            s, v = gates[res]
            P.wait(eng, s, v)

    def gate_set(res, sem, val):
        gates[res] = (sem, val)

    l0es = ExitStack()
    xn = l0es.enter_context(sb("xn0", [128, NDC, TPC], BF16))
    g_sb = l0es.enter_context(sb("g0", [128, NDC, TPC], BF16))
    rows = l0es.enter_context(sb("rows0", [1, 4 * TPC], F32))
    nm_row = rows[:, 0:TPC]
    rstd_row = rows[:, TPC:2 * TPC]
    row_sc = rows[:, 2 * TPC:3 * TPC]
    row_sc2 = rows[:, 3 * TPC:4 * TPC]

    with ExitStack() as ph:
        xbuf = [ph.enter_context(sb(f"xb{i}", [128, TPC], F32)) for i in range(2)]
        sqt = [ph.enter_context(sb(f"sq{i}", [128, 512], F32)) for i in range(2)]
        nm_b = ph.enter_context(sb("nm_b", [128, TPC], F32))
        rstd_b = ph.enter_context(sb("rstd_b", [128, TPC], F32))
        xsc = ph.enter_context(sb("xsc", [128, TPC], F32))
        xr = D["x"].rearrange("(c p) t -> c p t", p=128)
        xs = XStream(P, "x0", "sync", 2)
        for c in range(NDC):
            slot = c % 2
            gate_wait("sync", f"xb{slot}")
            tok = xs.dma(slot, xbuf[slot][:], xr[c])
            xs.wait("scalar", tok)
            for tb in range(NTB):
                gate_wait("scalar", f"sq{slot}")
                a = P.op("scalar", "activation", sqt[slot][:, :],
                         xbuf[slot][:, tb * 512:(tb + 1) * 512], AF.Square, inc=ac)
                P.wait("tensor", ac, a)
                P.op("tensor", "matmul", PS[3 + tb][0:1, :], ones_col[:, :],
                     xbuf[slot][:, tb * 512:(tb + 1) * 512],
                     start=(c == 0), stop=(c == NDC - 1))
                u = P.op("tensor", "matmul", PS[5 + tb][0:1, :], ones_col[:, :], sqt[slot][:, :],
                         start=(c == 0), stop=(c == NDC - 1), inc=pe)
                gate_set(f"sq{slot}", pe, u)
            gate_set(f"xb{slot}", pe, u)
        P.wait("vector", pe, pe.n)
        for tb in range(NTB):
            cs = slice(tb * 512, (tb + 1) * 512)
            P.op("vector", "tensor_scalar", nm_row[:, cs], PS[3 + tb][0:1, :], -1.0 / DM, None, ALU.mult)
            P.op("vector", "tensor_scalar", row_sc[:, cs], PS[5 + tb][0:1, :], 1.0 / DM, EPS_LN,
                 ALU.mult, ALU.add)
            P.op("vector", "tensor_mul", row_sc2[:, cs], nm_row[:, cs], nm_row[:, cs])
            v = P.op("vector", "tensor_sub", row_sc[:, cs], row_sc[:, cs], row_sc2[:, cs], inc=dv)
        P.wait("scalar", dv, dv.n)
        P.op("scalar", "activation", row_sc2[:, 0:TPC], row_sc[:, 0:TPC], AF.Ln)
        a = P.op("scalar", "activation", rstd_row[:, 0:TPC], row_sc2[:, 0:TPC], AF.Exp,
                 scale=-0.5, inc=ac)
        P.wait("tensor", ac, a)
        for tb in range(NTB):
            cs = slice(tb * 512, (tb + 1) * 512)
            gate_wait("tensor", "ps0")
            gate_wait("tensor", "ps1")
            v = P.op("tensor", "matmul", PS[0][:, :], ones_row[0:1, :], nm_row[:, cs],
                     start=True, stop=True)
            v = P.op("tensor", "matmul", PS[1][:, :], ones_row[0:1, :], rstd_row[:, cs],
                     start=True, stop=True, inc=pe)
            P.wait("vector", pe, v)
            u0 = P.op("vector", "tensor_copy", out=nm_b[:, cs], in_=PS[0][:, :], inc=dv)
            u = P.op("vector", "tensor_copy", out=rstd_b[:, cs], in_=PS[1][:, :], inc=dv)
            gate_set("ps0", dv, u0)
            gate_set("ps1", dv, u)
        for c in range(NDC):
            slot = c % 2
            gate_wait("sync", f"xb{slot}")
            tok = xs.dma(slot, xbuf[slot][:], xr[c])
            xs.wait("vector", tok)
            P.op("vector", "tensor_add", xsc[:, :], xbuf[slot][:, :], nm_b[:, :])
            u = P.op("vector", "tensor_mul", xn[:, c, :], xsc[:, :], rstd_b[:, :], inc=dv)
            gate_set(f"xb{slot}", dv, u)
    P.barrier()
    if stop_after == "ln":
        P.wait("gpsimd", dv, dv.n)
        for c in range(NDC):
            P.op("gpsimd", "dma_start", inc=ost, incv=16,
                 out=out_dram.rearrange("(c p) t -> c p t", p=128)[c], in_=xn[:, c, :])
        P.wait("gpsimd", ost, ost.n)
        l0es.close()
        return

    with ExitStack() as ph:
        cq = ph.enter_context(sb("cq", [128, QR_C, TPC], BF16))
        ckv = ph.enter_context(sb("ckv", [128, KV_C, TPC], BF16))
        qcat = ph.enter_context(sb("qcat", [CAT_D, H, TPC], BF16))
        kcat = ph.enter_context(sb("kcat", [CAT_D, H, TPC], BF16))
        vaug = ph.enter_context(sb("vaug", [128, 8, H * VA], BF16))
        sqt = [ph.enter_context(sb(f"qsq{i}", [128, 512], F32)) for i in range(2)]
        rb = ph.enter_context(sb("rbq", [128, TPC], F32))
        rsc = [ph.enter_context(sb(f"rsc{i}", [128, 512], BF16)) for i in range(2)]
        ws = WStream(P, "w0", 2, (128, NDC, 128), es=ph)
        cst_l0 = {}
        for cname, shp, cdt in [("R4T", [128, 128], BF16), ("R32T", [32, 32], BF16),
                                ("cos0q", [128, TPC], F32), ("cos0k", [32, TPC], F32)]:
            tt = ph.enter_context(sb(cname + "_s", shp, cdt))
            P.op("sync", "dma_start", inc=P.sem("cld"), incv=16, out=tt[:, :], in_=D[cname])
            cst_l0[cname] = tt
        P.wait("tensor", P.sem("cld"), P.sem("cld").n)
        P.wait("vector", P.sem("cld"), P.sem("cld").n)
        P.op("vector", "memset", qcat[:, :, :], 0.0)
        P.op("vector", "memset", kcat[:, :, :], 1.0)
        P.op("vector", "memset", vaug[:, :, :], 1.0, inc=P.sem("dv"))

        def proj_block(wt_ap, nk, rhs_fn, t, psum, first_last=(True, True)):
            ws.begin("tensor", t)
            for k in range(nk):
                v = P.op("tensor", "matmul", psum, ws.tiles[t % ws.nbuf][:, k, :], rhs_fn(k),
                         start=(k == 0 and first_last[0]), stop=(k == nk - 1 and first_last[1]),
                         inc=(pe if k == nk - 1 else None))
            return v

        wcq = D["w_cq"].rearrange("(c p) m -> p c m", p=128)
        for rc in range(QR_C):
            t, _ = ws.load(wcq[:, :, rc * 128:(rc + 1) * 128])
            for tb in range(NTB):
                ps = PS[rc % 2]
                gate_wait("tensor", f"ps{rc % 2}")
                v = proj_block(None, NDC, lambda k, tb=tb: xn[:, k, tb * 512:(tb + 1) * 512],
                               t, ps[:, :])
                if tb == NTB - 1:
                    ws.done(t, pe, v)
                P.wait("scalar", pe, v)
                P.op("scalar", "activation", cq[:, rc, tb * 512:(tb + 1) * 512], ps[:, :], AF.Copy)
                gate_wait("scalar", f"qsq{rc % 2}")
                a = P.op("scalar", "activation", sqt[rc % 2][:, :], ps[:, :], AF.Square, inc=ac)
                gate_set(f"ps{rc % 2}", ac, a)
                P.wait("tensor", ac, a)
                u = P.op("tensor", "matmul", PS[5 + tb][0:1, :], ones_col[:, :], sqt[rc % 2][:, :],
                         start=(rc == 0), stop=(rc == QR_C - 1), inc=pe)
                gate_set(f"qsq{rc % 2}", pe, u)
        P.wait("vector", pe, pe.n)
        for tb in range(NTB):
            cs = slice(tb * 512, (tb + 1) * 512)
            v = P.op("vector", "tensor_scalar", row_sc[:, cs], PS[5 + tb][0:1, :],
                     1.0 / Q_RANK, EPS_RMS, ALU.mult, ALU.add, inc=dv)
        P.wait("scalar", dv, dv.n)
        P.op("scalar", "activation", row_sc2[:, 0:TPC], row_sc[:, 0:TPC], AF.Ln)
        a = P.op("scalar", "activation", row_sc[:, 0:TPC], row_sc2[:, 0:TPC], AF.Exp,
                 scale=-0.5, inc=ac)
        P.wait("tensor", ac, a)
        for tb in range(NTB):
            cs = slice(tb * 512, (tb + 1) * 512)
            gate_wait("tensor", "ps0")
            v = P.op("tensor", "matmul", PS[0][:, :], ones_row[0:1, :], row_sc[:, cs],
                     start=True, stop=True, inc=pe)
            P.wait("vector", pe, v)
            u = P.op("vector", "tensor_copy", out=rb[:, cs], in_=PS[0][:, :], inc=dv)
            gate_set("ps0", dv, u)
        for rc in range(QR_C):
            u = P.op("vector", "scalar_tensor_tensor", cq[:, rc, :], cq[:, rc, :],
                     shared["vecs"][:, shared["vi"]["qg"] + rc:shared["vi"]["qg"] + rc + 1],
                     rb[:, :], ALU.mult, ALU.mult, inc=dv)
        normq_done = dv.n

        wqn = D["w_q_nope"].rearrange("(c p) m -> p c m", p=128)
        P.wait("tensor", dv, normq_done)
        for oc in range(H * NOPE_D // 128):
            t, wt = ws.load(wqn[:, :, oc * 128:(oc + 1) * 128],
                            out_ap_fn=lambda o: o[:, 0:QR_C, :])
            for tb in range(NTB):
                ps = PS[(oc + tb) % 2]
                gate_wait("tensor", f"ps{(oc + tb) % 2}")
                ws.begin("tensor", t)
                for k in range(QR_C):
                    v = P.op("tensor", "matmul", ps[:, :], wt[:, k, :],
                             cq[:, k, tb * 512:(tb + 1) * 512],
                             start=(k == 0), stop=(k == QR_C - 1),
                             inc=(pe if k == QR_C - 1 else None))
                if tb == NTB - 1:
                    ws.done(t, pe, v)
                P.wait("vector", pe, v)
                cs = slice(tb * 512, (tb + 1) * 512)
                P.op("vector", "tensor_scalar", qcat[0:64, 2 * oc, cs], ps[0:64, :],
                     SCALE0, None, ALU.mult)
                u = P.op("vector", "tensor_scalar", qcat[0:64, 2 * oc + 1, cs], ps[64:128, :],
                         SCALE0, None, ALU.mult, inc=dv)
                gate_set(f"ps{(oc + tb) % 2}", dv, u)

        wqr = D["w_q_rope"].rearrange("(c p) m -> p c m", p=128)
        for oc4 in range(4):
            t, wt = ws.load(wqr[:, :, oc4 * 128:(oc4 + 1) * 128],
                            out_ap_fn=lambda o: o[:, 0:QR_C, :])
            for tb in range(NTB):
                ps = PS[(oc4 + tb) % 2]
                gate_wait("tensor", f"ps{(oc4 + tb) % 2}")
                ws.begin("tensor", t)
                for k in range(QR_C):
                    v = P.op("tensor", "matmul", ps[:, :], wt[:, k, :],
                             cq[:, k, tb * 512:(tb + 1) * 512],
                             start=(k == 0), stop=(k == QR_C - 1),
                             inc=(pe if k == QR_C - 1 else None))
                if tb == NTB - 1:
                    ws.done(t, pe, v)
                P.wait("scalar", pe, v)
                slot = tb % 2
                gate_wait("scalar", f"rsc{slot}")
                a = P.op("scalar", "activation", rsc[slot][:, :], ps[:, :], AF.Copy, inc=ac)
                gate_set(f"ps{(oc4 + tb) % 2}", ac, a)
                P.wait("tensor", ac, a)
                gate_wait("tensor", "ps4")
                v = P.op("tensor", "matmul", PS[4][:, :], cst_l0["R4T"][:, :], rsc[slot][:, :],
                         start=True, stop=True, inc=pe)
                P.wait("vector", pe, v)
                cs = slice(tb * 512, (tb + 1) * 512)
                u = P.op("vector", "tensor_tensor", rsc[slot][:, :], PS[4][:, :],
                         cst_l0["cos0q"][:, cs], ALU.mult, inc=dv)
                gate_set("ps4", dv, u)
                for hh in range(4):
                    u = P.op("vector", "tensor_copy",
                             out=qcat[64:96, oc4 * 4 + hh, cs],
                             in_=rsc[slot][hh * 32:(hh + 1) * 32, :], inc=dv)
                gate_set(f"rsc{slot}", dv, u)

        wckv = D["w_ckv"].rearrange("(c p) m -> p c m", p=128)
        for rc in range(KV_C):
            t, wt = ws.load(wckv[:, :, rc * 128:(rc + 1) * 128])
            for tb in range(NTB):
                ps = PS[(rc + tb) % 2]
                gate_wait("tensor", f"ps{(rc + tb) % 2}")
                ws.begin("tensor", t)
                for k in range(NDC):
                    v = P.op("tensor", "matmul", ps[:, :], wt[:, k, :],
                             xn[:, k, tb * 512:(tb + 1) * 512],
                             start=(k == 0), stop=(k == NDC - 1),
                             inc=(pe if k == NDC - 1 else None))
                if tb == NTB - 1:
                    ws.done(t, pe, v)
                P.wait("scalar", pe, v)
                P.op("scalar", "activation", ckv[:, rc, tb * 512:(tb + 1) * 512], ps[:, :], AF.Copy)
                gate_wait("scalar", f"qsq{rc % 2}")
                a = P.op("scalar", "activation", sqt[rc % 2][:, :], ps[:, :], AF.Square, inc=ac)
                gate_set(f"ps{(rc + tb) % 2}", ac, a)
                P.wait("tensor", ac, a)
                u = P.op("tensor", "matmul", PS[5 + tb][0:1, :], ones_col[:, :], sqt[rc % 2][:, :],
                         start=(rc == 0), stop=(rc == KV_C - 1), inc=pe)
                gate_set(f"qsq{rc % 2}", pe, u)
        P.wait("vector", pe, pe.n)
        for tb in range(NTB):
            cs = slice(tb * 512, (tb + 1) * 512)
            v = P.op("vector", "tensor_scalar", row_sc[:, cs], PS[5 + tb][0:1, :],
                     1.0 / KV_RANK, EPS_RMS, ALU.mult, ALU.add, inc=dv)
        P.wait("scalar", dv, dv.n)
        P.op("scalar", "activation", row_sc2[:, 0:TPC], row_sc[:, 0:TPC], AF.Ln)
        a = P.op("scalar", "activation", row_sc[:, 0:TPC], row_sc2[:, 0:TPC], AF.Exp,
                 scale=-0.5, inc=ac)
        P.wait("tensor", ac, a)
        for tb in range(NTB):
            cs = slice(tb * 512, (tb + 1) * 512)
            gate_wait("tensor", "ps0")
            v = P.op("tensor", "matmul", PS[0][:, :], ones_row[0:1, :], row_sc[:, cs],
                     start=True, stop=True, inc=pe)
            P.wait("vector", pe, v)
            u = P.op("vector", "tensor_copy", out=rb[:, cs], in_=PS[0][:, :], inc=dv)
            gate_set("ps0", dv, u)
        for rc in range(KV_C):
            u = P.op("vector", "scalar_tensor_tensor", ckv[:, rc, :], ckv[:, rc, :],
                     shared["vecs"][:, shared["vi"]["kvg"] + rc:shared["vi"]["kvg"] + rc + 1],
                     rb[:, :], ALU.mult, ALU.mult, inc=dv)
        normkv_done = dv.n

        wkn = D["w_k_nope"].rearrange("(c p) m -> p c m", p=128)
        P.wait("tensor", dv, normkv_done)
        for oc in range(H * NOPE_D // 128):
            t, wt = ws.load(wkn[:, :, oc * 128:(oc + 1) * 128],
                            out_ap_fn=lambda o: o[:, 0:KV_C, :])
            for tb in range(NTB):
                ps = PS[(oc + tb) % 2]
                gate_wait("tensor", f"ps{(oc + tb) % 2}")
                ws.begin("tensor", t)
                for k in range(KV_C):
                    v = P.op("tensor", "matmul", ps[:, :], wt[:, k, :],
                             ckv[:, k, tb * 512:(tb + 1) * 512],
                             start=(k == 0), stop=(k == KV_C - 1),
                             inc=(pe if k == KV_C - 1 else None))
                if tb == NTB - 1:
                    ws.done(t, pe, v)
                P.wait("vector", pe, v)
                cs = slice(tb * 512, (tb + 1) * 512)
                P.op("vector", "tensor_copy", out=kcat[0:64, 2 * oc, cs], in_=ps[0:64, :])
                u = P.op("vector", "tensor_copy", out=kcat[0:64, 2 * oc + 1, cs],
                         in_=ps[64:128, :], inc=dv)
                gate_set(f"ps{(oc + tb) % 2}", dv, u)

        wkr = D["w_k_rope"].rearrange("(c p) m -> p c m", p=128)
        t, wt = ws.load(wkr[:, :, :], out_ap_fn=lambda o: o[:, :, 0:32])
        for tb in range(NTB):
            ps = PS[2]
            gate_wait("tensor", "ps2")
            ws.begin("tensor", t)
            for k in range(NDC):
                v = P.op("tensor", "matmul", ps[0:32, :], wt[:, k, 0:32],
                         xn[:, k, tb * 512:(tb + 1) * 512],
                         start=(k == 0), stop=(k == NDC - 1),
                         inc=(pe if k == NDC - 1 else None))
            if tb == NTB - 1:
                ws.done(t, pe, v)
            P.wait("scalar", pe, v)
            slot = tb % 2
            gate_wait("scalar", f"rsc{slot}")
            a = P.op("scalar", "activation", rsc[slot][0:32, :], ps[0:32, :], AF.Copy, inc=ac)
            gate_set("ps2", ac, a)
            P.wait("tensor", ac, a)
            gate_wait("tensor", "ps4")
            v = P.op("tensor", "matmul", PS[4][0:32, :], cst_l0["R32T"][:, :], rsc[slot][0:32, :],
                     start=True, stop=True, inc=pe)
            P.wait("vector", pe, v)
            cs = slice(tb * 512, (tb + 1) * 512)
            u = P.op("vector", "tensor_tensor", rsc[slot][0:32, :], PS[4][0:32, :],
                     cst_l0["cos0k"][:, cs], ALU.mult, inc=dv)
            gate_set("ps4", dv, u)
            for h in range(H):
                u = P.op("vector", "tensor_copy", out=kcat[64:96, h, cs],
                         in_=rsc[slot][0:32, :], inc=dv)
            gate_set(f"rsc{slot}", dv, u)

        wv = D["w_v"].rearrange("(c p) m -> p c m", p=128)
        for oc in range(H * V_D // 128):
            t, wt = ws.load(wv[:, :, oc * 128:(oc + 1) * 128],
                            out_ap_fn=lambda o: o[:, 0:KV_C, :])
            for tb in range(NTB):
                ps = PS[(oc + tb) % 2]
                gate_wait("tensor", f"ps{(oc + tb) % 2}")
                ws.begin("tensor", t)
                for k in range(KV_C):
                    v = P.op("tensor", "matmul", ps[:, :], wt[:, k, :],
                             ckv[:, k, tb * 512:(tb + 1) * 512],
                             start=(k == 0), stop=(k == KV_C - 1),
                             inc=(pe if k == KV_C - 1 else None))
                if tb == NTB - 1:
                    ws.done(t, pe, v)
                P.wait("vector", pe, v)
                slot = tb % 2
                gate_wait("vector", f"rsc{slot}")
                u = P.op("vector", "tensor_copy", out=rsc[slot][:, :], in_=ps[:, :], inc=dv)
                gate_set(f"ps{(oc + tb) % 2}", dv, u)
                P.wait("tensor", dv, u)
                gate_wait("tensor", "psT")
                for j in range(4):
                    v = P.op("tensor", "transpose", psT[:, j * 128:(j + 1) * 128],
                             rsc[slot][:, j * 128:(j + 1) * 128], shared["id_bf"][:, :],
                             inc=(pe if j == 3 else None))
                P.wait("vector", pe, v)
                for j in range(4):
                    for half in range(2):
                        h = 2 * oc + half
                        u = P.op("vector", "tensor_copy",
                                 out=vaug[:, tb * 4 + j, h * VA:h * VA + V_D],
                                 in_=psT[:, j * 128 + half * 64: j * 128 + half * 64 + 64],
                                 inc=dv)
                gate_set("psT", dv, u)
                gate_set(f"rsc{slot}", dv, u)

        evacs_done = dv.n
        P.wait("scalar", dv, evacs_done)
        P.op("scalar", "dma_start", inc=agst, incv=16, out=D["ag0k_in"], in_=kcat[:, :, :])
        for j in range(8):
            P.op("scalar", "dma_start", inc=agst, incv=16,
                 out=D["ag0v_in_r"][j], in_=vaug[:, j, :])
        P.op("scalar", "dma_start", inc=agst, incv=16, out=D["qcat_dram"], in_=qcat[:, :, :])
        P.wait("gpsimd", agst, agst.n)
        P.op("gpsimd", "collective_compute", "AllGather", ALU.bypass,
             replica_groups=[list(range(NCORE))],
             ins=[D["ag0k_in"].opt()], outs=[D["ag0k_out"].opt()], inc=cc)
        P.op("gpsimd", "collective_compute", "AllGather", ALU.bypass,
             replica_groups=[list(range(NCORE))],
             ins=[D["ag0v_in"].opt()], outs=[D["ag0v_out"].opt()], inc=cc)
        ag_done = cc.n
        if stop_after == "qkv":
            P.wait("sync", cc, ag_done)
            for c in range(NDC):
                P.op("gpsimd", "dma_start", inc=ost, incv=16,
                     out=out_dram.rearrange("(c p) t -> c p t", p=128)[c], in_=xn[:, c, :])
            P.wait("gpsimd", ost, ost.n)
    P.barrier()
    if stop_after == "qkv":
        l0es.close()
        return

    GSZ = 4
    with ExitStack() as ph:
        yacc = ph.enter_context(sb("yacc", [128, NDC, TPC], F32))
        gated = ph.enter_context(sb("gated", [128, GSZ, TPC], BF16))
        silu_sc = [ph.enter_context(sb(f"sil{i}", [128, 512], F32)) for i in range(2)]
        wsg = WStream(P, "wg", 2, (128, NDC, 128), es=ph)
        wsu = WStream(P, "wu", 2, (128, NDC, 128), es=ph)
        wsd = WStream(P, "wd", GSZ + 1, (128, NDC, 128), es=ph)
        wg_r = D["wg_t"].rearrange("(c p) m -> p c m", p=128)
        wu_r = D["wu_t"].rearrange("(c p) m -> p c m", p=128)
        wd_r = D["wd_t"].rearrange("(hc p) (c m) -> hc p c m", p=128, m=128)
        for grp in range(FH_C // GSZ):
            dts = []
            for hcl in range(GSZ):
                hc = grp * GSZ + hcl
                tg, wtg = wsg.load(wg_r[:, :, hc * 128:(hc + 1) * 128])
                tu, wtu = wsu.load(wu_r[:, :, hc * 128:(hc + 1) * 128])
                td, wtd = wsd.load(wd_r[hc])
                dts.append((td, wtd))
                for tb in range(NTB):
                    cs = slice(tb * 512, (tb + 1) * 512)
                    psg, psu = PS[tb], PS[4 + tb]
                    gate_wait("tensor", f"ps{tb}")
                    gate_wait("tensor", f"ps{4 + tb}")
                    wsg.begin("tensor", tg)
                    wsu.begin("tensor", tu)
                    for k in range(NDC):
                        P.op("tensor", "matmul", psg[:, :], wtg[:, k, :], xn[:, k, cs],
                             start=(k == 0), stop=(k == NDC - 1),
                             inc=(pe if k == NDC - 1 else None))
                    for k in range(NDC):
                        P.op("tensor", "matmul", psu[:, :], wtu[:, k, :], xn[:, k, cs],
                             start=(k == 0), stop=(k == NDC - 1),
                             inc=(pe if k == NDC - 1 else None))
                    v = pe.n
                    if tb == NTB - 1:
                        wsg.done(tg, pe, v - 1)
                        wsu.done(tu, pe, v)
                    P.wait("scalar", pe, v - 1)
                    slot = tb
                    gate_wait("scalar", f"sil{slot}")
                    a = P.op("scalar", "activation", silu_sc[slot][:, :], psg[:, :], AF.Silu,
                             inc=ac)
                    gate_set(f"ps{tb}", ac, a)
                    P.wait("vector", ac, a)
                    P.wait("vector", pe, v)
                    u = P.op("vector", "tensor_tensor", gated[:, hcl, cs], psu[:, :],
                             silu_sc[slot][:, :], ALU.mult, inc=dv)
                    gate_set(f"ps{4 + tb}", dv, u)
                    gate_set(f"sil{slot}", dv, u)
            gated_done = dv.n
            P.wait("tensor", dv, gated_done)
            for dc in range(NDC):
                for tb in range(NTB):
                    cs = slice(tb * 512, (tb + 1) * 512)
                    gate_wait("tensor", "ps6")
                    for hcl in range(GSZ):
                        td, wtd = dts[hcl]
                        wsd.begin("tensor", td)
                        v = P.op("tensor", "matmul", PS[6][:, :], wtd[:, dc, :],
                                 gated[:, hcl, cs],
                                 start=(hcl == 0), stop=(hcl == GSZ - 1),
                                 inc=(pe if hcl == GSZ - 1 else None))
                    P.wait("vector", pe, v)
                    if grp == 0:
                        u = P.op("vector", "tensor_copy", out=yacc[:, dc, cs],
                                 in_=PS[6][:, :], inc=dv)
                    else:
                        u = P.op("vector", "tensor_add", yacc[:, dc, cs], yacc[:, dc, cs],
                                 PS[6][:, :], inc=dv)
                    gate_set("ps6", dv, u)
            for hcl in range(GSZ):
                wsd.done(dts[hcl][0], dv, u)
        P.wait("scalar", dv, dv.n)
        for dc in range(NDC):
            a = P.op("scalar", "activation", g_sb[:, dc, :], yacc[:, dc, :], AF.Silu, inc=ac)
    P.barrier()
    if stop_after == "ffn":
        P.wait("sync", cc, ag_done)
        P.wait("gpsimd", ac, ac.n)
        for c in range(NDC):
            P.op("gpsimd", "dma_start", inc=ost, incv=16,
                 out=out_dram.rearrange("(c p) t -> c p t", p=128)[c], in_=g_sb[:, c, :])
        P.wait("gpsimd", ost, ost.n)
        l0es.close()
        return

    CAT2 = CAT_D + 1
    NSL = 24
    with ExitStack() as ph:
        qh = ph.enter_context(sb("qh", [CAT2, TPC], BF16))
        kslot = ph.enter_context(sb("kslot", [CAT2, NSL, 512], BF16))
        kown = ph.enter_context(sb("kown", [CAT2, NTB, 512], BF16))
        vslot = ph.enter_context(sb("vslot", [128, NSL * 4, VA], BF16))
        vown = ph.enter_context(sb("vown", [128, NTB * 4, VA], BF16))
        cm = ph.enter_context(sb("cm", [128, 8, NSL + 2], F32))
        mall = ph.enter_context(sb("mall", [128, 8], F32))
        negm = ph.enter_context(sb("negm", [1, 512], F32))
        negm_bf = ph.enter_context(sb("negm_bf", [1, 512], BF16))
        pT = [ph.enter_context(sb(f"pT{i}", [128, 512], BF16)) for i in range(3)]
        rbv = ph.enter_context(sb("rbv", [64, 512], F32))
        attn0 = ph.enter_context(sb("attn0", [128, 8, TPC], BF16))
        maskA_sb = ph.enter_context(sb("maskA_sb", [128, 4, 512], F32))
        maskB_sb = ph.enter_context(sb("maskB_sb", [128, 4, 512], F32))
        P.op("sync", "dma_start", inc=kvld, incv=16, out=maskA_sb[:, :, :], in_=D["maskA"])
        P.op("sync", "dma_start", inc=kvld, incv=16, out=maskB_sb[:, :, :], in_=D["maskB"])
        P.op("sync", "dma_start", inc=kvld, incv=16, out=kslot[97:98, :, :], in_=D["padk0"])
        P.op("sync", "dma_start", inc=kvld, incv=16, out=qh[97:98, :], in_=D["ones_row_tpc"])
        P.op("sync", "dma_start", inc=kvld, incv=16, out=kown[97:98, :, :], in_=D["zero_row_2x512"])
        P.wait("sync", cc, ag_done)
        P.wait("tensor", cc, ag_done)
        agk = D["ag0k_out"].rearrange("(r p) (h t) -> r p h t", p=CAT_D, t=TPC)
        agv = D["ag0v_out"].rearrange("(b p) f -> p b f", p=128)
        agk_own = D["ag0k_in"].rearrange("p (h t) -> p h t", t=TPC)
        agv_own = D["ag0v_in"].rearrange("(b p) f -> p b f", p=128)

        kvE = P.sem("kv0E")
        kvL = P.sem("kv0L")
        for h in range(H):
            gate_wait("sync", "heads")
            P.op("sync", "dma_start", inc=kvE, incv=16,
                 out=qh[0:CAT_D, :], in_=D["qcat_dram_r"][:, h, :])
            for s in range(NTB):
                P.op("sync", "dma_start", inc=kvE, incv=16,
                     out=kown[0:CAT_D, s, :], in_=agk_own[:, h, s * 512:(s + 1) * 512])
                P.op("sync", "dma_start", inc=kvE, incv=16,
                     out=vown[:, 4 * s:4 * s + 4, :],
                     in_=agv_own[:, 4 * s:4 * s + 4, h * VA:(h + 1) * VA])
            for j in range(NSL_E0):
                g = j
                r, s = BLOCK_OWNER[g]
                P.op("sync", "dma_start", inc=kvE, incv=16,
                     out=kslot[0:CAT_D, j, :], in_=agk[r, :, h, s * 512:(s + 1) * 512])
                P.op("sync", "dma_start", inc=kvE, incv=16,
                     out=vslot[:, 4 * j:4 * j + 4, :],
                     in_=agv[:, 8 * r + 4 * s:8 * r + 4 * s + 4, h * VA:(h + 1) * VA])
            totE = kvE.n
            for j in range(NSL_E0, NSL):
                g = j - NSL_E0
                r, s = BLOCK_OWNER[g]
                P.op("sync", "dma_start", inc=kvL, incv=16,
                     out=kslot[0:CAT_D, j, :], in_=agk[r, :, h, s * 512:(s + 1) * 512])
                P.op("sync", "dma_start", inc=kvL, incv=16,
                     out=vslot[:, 4 * j:4 * j + 4, :],
                     in_=agv[:, 8 * r + 4 * s:8 * r + 4 * s + 4, h * VA:(h + 1) * VA])
            totL = kvL.n

            P.wait("tensor", kvE, totE)
            for role in range(2):
                if role == 1:
                    P.wait("tensor", kvL, totL)
                for qtl in range(4):
                    qs = role * 512 + qtl * 128
                    nsl = NSL_E0 if role == 0 else NSL_L0
                    base = 0 if role == 0 else NSL_E0
                    for jj in range(nsl + 1):
                        ps = PS[jj % 2]
                        gate_wait("tensor", f"ps{jj % 2}")
                        if jj < nsl:
                            rhs = kslot[:, base + jj, :]
                        else:
                            rhs = kown[:, role, :]
                        v = P.op("tensor", "matmul", ps[:, :], qh[:, qs:qs + 128], rhs,
                                 start=True, stop=True, inc=pe)
                        P.wait("vector", pe, v)
                        if jj == nsl:
                            P.op("vector", "tensor_add", ps[:, :], ps[:, :],
                                 maskA_sb[:, qtl, :])
                        u = P.op("vector", "reduce_max",
                                 out=cm[:, role * 4 + qtl, (base + jj if jj < nsl else NSL + role):(base + jj if jj < nsl else NSL + role) + 1],
                                 in_=ps[:, :], axis=AX.X, inc=dv)
                        gate_set(f"ps{jj % 2}", dv, u)
            P.wait("vector", dv, dv.n)
            for qtl in range(4):
                for role in range(2):
                    qi = role * 4 + qtl
                    if role == 0:
                        P.op("vector", "reduce_max", out=mall[:, qi:qi + 1],
                             in_=cm[:, qi, 0:NSL_E0], axis=AX.X)
                        u = P.op("vector", "tensor_max", mall[:, qi:qi + 1], mall[:, qi:qi + 1],
                                 cm[:, qi, NSL:NSL + 1], inc=dv)
                    else:
                        P.op("vector", "reduce_max", out=mall[:, qi:qi + 1],
                             in_=cm[:, qi, NSL_E0:NSL_E0 + NSL_L0], axis=AX.X)
                        u = P.op("vector", "tensor_max", mall[:, qi:qi + 1], mall[:, qi:qi + 1],
                                 cm[:, qi, NSL + 1:NSL + 2], inc=dv)
            for role in range(2):
                P.wait("tensor", dv, dv.n)
                gate_wait("tensor", "ps4")
                for qtl in range(4):
                    v = P.op("tensor", "transpose", PS[4][0:1, qtl * 128:(qtl + 1) * 128],
                             mall[:, role * 4 + qtl:role * 4 + qtl + 1], id_f32[:, :],
                             inc=(pe if qtl == 3 else None))
                P.wait("vector", pe, v)
                gate_wait("vector", "negm")
                P.op("vector", "tensor_scalar", negm[0:1, :], PS[4][0:1, 0:512], -1.0, None, ALU.mult)
                u = P.op("vector", "tensor_copy", out=negm_bf[0:1, :], in_=negm[0:1, :], inc=dv)
                gate_set("ps4", dv, u)
                P.wait("sync", dv, u)
                v = P.op("sync", "dma_start", inc=kvld, incv=16,
                         out=qh[96:97, role * 512:(role + 1) * 512], in_=negm_bf[0:1, :])
                gate_set("negm", kvld, v)
                P.wait("tensor", kvld, v)
                nsl = NSL_E0 if role == 0 else NSL_L0
                base = 0 if role == 0 else NSL_E0
                qs = slice(role * 512, (role + 1) * 512)
                first = True
                gate_wait("tensor", "ps6")
                nkc = (nsl + 1) * 4
                for t_i in range(nkc):
                    jj, kc = divmod(t_i, 4)
                    own = jj == nsl
                    psb = PS[2 + (t_i % 2)]
                    gate_wait("tensor", f"ps{2 + (t_i % 2)}")
                    if own:
                        lhsT = kown[:, role, kc * 128:(kc + 1) * 128]
                    else:
                        lhsT = kslot[:, base + jj, kc * 128:(kc + 1) * 128]
                    v = P.op("tensor", "matmul", psb[:, :], lhsT, qh[:, qs],
                             start=True, stop=True, inc=pe)
                    if own:
                        P.wait("vector", pe, v)
                        v2 = P.op("vector", "tensor_add", psb[:, :], psb[:, :],
                                  maskB_sb[:, kc, :], inc=dv)
                        P.wait("scalar", dv, v2)
                    else:
                        P.wait("scalar", pe, v)
                    pt = pT[t_i % 3]
                    gate_wait("scalar", f"pT{t_i % 3}")
                    a = P.op("scalar", "activation", pt[:, :], psb[:, :], AF.Exp, inc=ac)
                    gate_set(f"ps{2 + (t_i % 2)}", ac, a)
                    P.wait("tensor", ac, a)
                    vv = vown[:, 4 * role + kc, :] if own else vslot[:, 4 * (base + jj) + kc, :]
                    u = P.op("tensor", "matmul", PS[6][0:VA, :], vv, pt[:, :],
                             start=first, stop=(t_i == nkc - 1), inc=pe)
                    gate_set(f"pT{t_i % 3}", pe, u)
                    first = False
                P.wait("vector", pe, pe.n)
                gate_wait("vector", "negm")
                u = P.op("vector", "reciprocal", inc=dv, out=negm[0:1, :], in_=PS[6][64:65, :])
                P.wait("tensor", dv, u)
                gate_wait("tensor", "ps4")
                v = P.op("tensor", "matmul", PS[4][0:64, :], ones_row[0:1, 0:64], negm[0:1, :],
                         start=True, stop=True, inc=pe)
                P.wait("vector", pe, v)
                P.op("vector", "tensor_copy", out=rbv[:, :], in_=PS[4][0:64, :])
                u = P.op("vector", "tensor_tensor",
                         attn0[(h % 2) * 64:(h % 2) * 64 + 64, h // 2, role * 512:(role + 1) * 512],
                         PS[6][0:64, :], rbv[:, :], ALU.mult, inc=dv)
                gate_set("ps6", dv, u)
                gate_set("ps4", dv, u)
            gate_set("heads", dv, dv.n)
        if stop_after == "attn":
            P.wait("gpsimd", dv, dv.n)
            for c in range(8):
                P.op("gpsimd", "dma_start", inc=ost, incv=16,
                     out=out_dram.rearrange("(c p) t -> c p t", p=128)[c], in_=attn0[:, c, :])
            P.wait("gpsimd", ost, ost.n)
    P.barrier()
    if stop_after == "attn":
        l0es.close()
        return

    with ExitStack() as ph:
        xbuf = [ph.enter_context(sb(f"xc{i}", [128, TPC], F32)) for i in range(2)]
        x1b = [ph.enter_context(sb(f"x1b{i}", [128, TPC], F32)) for i in range(2)]
        tsc = ph.enter_context(sb("tsc", [128, 512], F32))
        st_toks = []
        ws = WStream(P, "wp", 2, (128, 8, 128), es=ph)
        wp = D["w_proj"].rearrange("(c p) m -> p c m", p=128)
        xr = D["x"].rearrange("(c p) t -> c p t", p=128)
        xs2 = XStream(P, "xc", "sync", 2)
        st2 = XStream(P, "x1s", "sync", 2)
        for dc in range(NDC):
            t, wt = ws.load(wp[:, :, dc * 128:(dc + 1) * 128])
            slot = dc % 2
            gate_wait("sync", f"xc{slot}")
            xtok = xs2.dma(slot, xbuf[slot][:], xr[dc])
            if dc >= 2:
                st2.wait("vector", st_toks[dc - 2])
            for tb in range(NTB):
                cs = slice(tb * 512, (tb + 1) * 512)
                ps = PS[tb]
                gate_wait("tensor", f"ps{tb}")
                ws.begin("tensor", t)
                for k in range(8):
                    v = P.op("tensor", "matmul", ps[:, :], wt[:, k, :], attn0[:, k, cs],
                             start=(k == 0), stop=(k == 7), inc=(pe if k == 7 else None))
                if tb == NTB - 1:
                    ws.done(t, pe, v)
                P.wait("vector", pe, v)
                xs2.wait("vector", xtok)
                P.op("vector", "tensor_tensor", tsc[:, :], ps[:, :], g_sb[:, dc, cs], ALU.mult)
                u = P.op("vector", "tensor_add", x1b[slot][:, cs], tsc[:, :], xbuf[slot][:, cs],
                         inc=dv)
                gate_set(f"ps{tb}", dv, u)
            gate_set(f"xc{slot}", dv, u)
            P.wait("sync", dv, u)
            st_toks.append(st2.dma(slot, out_dram.rearrange("(c p) t -> c p t", p=128)[dc],
                                   x1b[slot][:]))
        for tok in st_toks[-2:]:
            st2.wait("sync", tok)
    l0es.close()
    P.barrier()



VI_L0 = {"qg": 0, "kvg": 6}
NV_L0 = 8
VI_L1 = {"bq": 0, "bk": 16, "bv": 32, "bo": 48, "lng": 64, "lnb": 80,
         "sgb": 96, "sub": 118, "sdb": 140}
NV_L1 = 156


def emit_shared(P, nc, D, es, layers):
    sb = nc.sbuf_tensor
    shared = {}
    shared["PS"] = [es.enter_context(nc.psum_tensor(f"ps{i}", [128, 512], F32))
                    for i in range(7)]
    shared["psT"] = es.enter_context(nc.psum_tensor("psT", [128, 1024], BF16))
    ld = P.sem("cld")
    dv = P.sem("dv")
    pe = P.sem("pe")
    ones_col = es.enter_context(sb("ones_col", [128, 1], F32))
    ones_row = es.enter_context(sb("ones_row", [1, 128], F32))
    id_f32 = es.enter_context(sb("id_f32s", [128, 128], F32))
    id_bf = es.enter_context(sb("id_bfs", [128, 128], BF16))
    ones_col_bf = es.enter_context(sb("ones_col_bf", [128, 1], BF16))
    P.op("vector", "memset", ones_col[:, :], 1.0)
    P.op("vector", "memset", ones_col_bf[:, :], 1.0)
    P.op("vector", "memset", ones_row[:, :], 1.0, inc=dv)
    shared["ones_col_bf"] = ones_col_bf
    gld = P.sem("gld")
    P.op("sync", "dma_start", inc=ld, incv=16, out=id_f32[:, :], in_=D["id_f32"])
    P.op("gpsimd", "dma_start", inc=gld, incv=16, out=id_bf[:, :], in_=D["id_f32"])
    shared.update(ones_col=ones_col, ones_row=ones_row, id_f32=id_f32, id_bf=id_bf)
    nv = (NV_L0 if 0 in layers else 0) + (NV_L1 if 1 in layers else 0)
    vecs = es.enter_context(sb("vecs", [128, nv], F32))
    row_es = ExitStack()
    rowbuf = row_es.enter_context(sb("rowbuf", [1, 4096], F32))
    vi = {}
    off = 0
    specs = []
    if 0 in layers:
        specs += [("q_g", "qg", QR_C), ("kv_g", "kvg", KV_C)]
    if 1 in layers:
        specs += [("bq1", "bq", 16), ("bk1", "bk", 16), ("bv1", "bv", 16), ("bo1", "bo", 16),
                  ("lng1", "lng", 16), ("lnb1", "lnb", 16), ("sgb1", "sgb", SI_C),
                  ("sub1", "sub", SI_C), ("sdb1", "sdb", 16)]
    P.wait("vector", ld, ld.n)
    P.wait("vector", gld, gld.n)
    P.wait("tensor", dv, dv.n)
    rld = P.sem("rld")
    for dname, key, ncol in specs:
        vi[key] = off
        P.wait("sync", dv, dv.n)
        v = P.op("sync", "dma_start", inc=rld, incv=16,
                 out=rowbuf[0:1, 0:ncol * 128], in_=D[dname])
        P.wait("tensor", rld, v)
        for c in range(ncol):
            pv = P.op("tensor", "matmul", shared["PS"][0][:, c % 4:c % 4 + 1],
                      rowbuf[0:1, c * 128:(c + 1) * 128], ones_row[0:1, 0:1],
                      start=True, stop=True, inc=pe)
            P.wait("vector", pe, pv)
            u = P.op("vector", "tensor_copy", out=vecs[:, off + c:off + c + 1],
                     in_=shared["PS"][0][:, c % 4:c % 4 + 1], inc=dv)
            P.wait("tensor", dv, u)
        off += ncol
    shared["vecs"] = vecs
    shared["vi"] = vi
    P.barrier()
    row_es.close()
    return shared


def declare_dram_l0(nc, with_out=True):
    D = {}

    def inp(name, shape, dt=F32):
        D[name] = nc.dram_tensor(name, list(shape), dt, kind="ExternalInput").ap()

    inp("x", (DM, TPC))
    inp("w_cq", (DM, Q_RANK))
    inp("w_q_nope", (Q_RANK, 1024))
    inp("w_q_rope", (Q_RANK, 512))
    inp("q_g", (Q_RANK,))
    inp("w_ckv", (DM, KV_RANK))
    inp("w_k_nope", (KV_RANK, 1024))
    inp("w_v", (KV_RANK, 1024))
    inp("kv_g", (KV_RANK,))
    inp("w_k_rope", (DM, 32))
    inp("w_proj", (1024, DM))
    inp("wg_t", (DM, FH))
    inp("wu_t", (DM, FH))
    inp("wd_t", (FH, DM))
    inp("id_f32", (128, 128))
    inp("R4T", (128, 128), BF16)
    inp("R32T", (32, 32), BF16)
    inp("cos0q", (128, TPC))
    inp("cos0k", (32, TPC))
    inp("maskA_h", (4, 128, 512))
    inp("maskB_h", (4, 128, 512))
    inp("padk0", (1, NSL_E0 + NSL_L0, 512), BF16)
    inp("ones_row_tpc", (1, TPC), BF16)
    inp("ones_row_2x512", (1, 2, 512), BF16)
    inp("zero_row_2x512", (1, 2, 512), BF16)
    D["maskA"] = D["maskA_h"].rearrange("d p f -> p d f")
    D["maskB"] = D["maskB_h"].rearrange("d p f -> p d f")
    D["qcat_dram_t"] = nc.dram_tensor("qcat_dram", [CAT_D, H * TPC], BF16)
    D["qcat_dram"] = D["qcat_dram_t"].ap().rearrange("p (h t) -> p h t", t=TPC)
    D["qcat_dram_r"] = D["qcat_dram"]
    a = nc.dram_tensor("ag0k_in", [CAT_D, H * TPC], BF16).ap()
    D["ag0k_in"] = a
    D["ag0k_in_r"] = a.rearrange("p (h t) -> p h t", t=TPC)
    D["ag0k_out"] = nc.dram_tensor("ag0k_out", [NCORE * CAT_D, H * TPC], BF16).ap()
    av = nc.dram_tensor("ag0v_in", [TPC, H * VA], BF16).ap()
    D["ag0v_in"] = av
    D["ag0v_in_r"] = [av.rearrange("(b p) f -> b p f", p=128)[j] for j in range(8)]
    D["ag0v_out"] = nc.dram_tensor("ag0v_out", [SEQ, H * VA], BF16).ap()
    if with_out:
        D["x1"] = nc.dram_tensor("x1", [DM, TPC], F32, kind="ExternalOutput").ap()
    return D


def host_stage_l0(core, inputs, cst):
    x = np.asarray(inputs["L0_attn_x"])[0]
    blocks = CORE_BLOCKS[core]
    xs = np.concatenate([x[g * 512:(g + 1) * 512] for g in blocks], 0)
    m = {
        "x": np.ascontiguousarray(xs.T),
        "w_cq": inputs["L0_attn_w_cq"],
        "w_q_nope": inputs["L0_attn_w_q_nope"],
        "w_q_rope": inputs["L0_attn_w_q_rope"],
        "q_g": inputs["L0_attn_q_g"],
        "w_ckv": inputs["L0_attn_w_ckv"],
        "w_k_nope": inputs["L0_attn_w_k_nope"],
        "w_v": inputs["L0_attn_w_v"],
        "kv_g": inputs["L0_attn_kv_g"],
        "w_k_rope": inputs["L0_attn_w_k_rope"],
        "w_proj": inputs["L0_attn_w_proj"],
        "wg_t": np.ascontiguousarray(np.asarray(inputs["L0_ffn_w_gate"]).T),
        "wu_t": np.ascontiguousarray(np.asarray(inputs["L0_ffn_w_up"]).T),
        "wd_t": np.ascontiguousarray(np.asarray(inputs["L0_ffn_w_down"]).T),
        "id_f32": cst["id_f32"],
        "R4T": cst["R4T"],
        "R32T": cst["R32T"],
        "cos0q": cst["cos0q"],
        "cos0k": cst["cos0k"],
        "maskA_h": cst["maskA"],
        "maskB_h": cst["maskB"],
        "ones_row_tpc": np.ones((1, TPC), bfloat16),
        "ones_row_2x512": np.ones((1, 2, 512), bfloat16),
        "zero_row_2x512": np.zeros((1, 2, 512), bfloat16),
    }
    c = core
    pad = np.zeros((NSL_E0 + NSL_L0, 512), np.float32)
    for j in range(NSL_E0):
        if j >= blocks[0]:
            pad[j] = NEG
    for j in range(NSL_L0):
        if j >= blocks[1]:
            pad[NSL_E0 + j] = NEG
    m["padk0"] = pad[None].astype(bfloat16)
    m = {k: np.ascontiguousarray(np.asarray(v, dtype=(bfloat16 if k in
         ("R4T", "R32T", "padk0", "ones_row_tpc", "ones_row_2x512", "zero_row_2x512")
         else np.float32)))
         for k, v in m.items()}
    return m


def build_l0_program(stop_after=None):
    nc = bass.Bass("TRN2", target_bir_lowering=False, debug=False)
    es = ExitStack()
    D = declare_dram_l0(nc)
    P = Prog(nc, es)
    shared = emit_shared(P, nc, D, es, layers=[0])
    if stop_after == "shared":
        ost = P.sem("ost")
        for c in range(NDC):
            P.op("gpsimd", "dma_start", inc=ost, incv=16,
                 out=D["x1"].rearrange("(c p) t -> c p t", p=128)[c][:, 0:1],
                 in_=shared["vecs"][:, 0:1], allow_slow_non_contiguous=True)
        P.wait("gpsimd", ost, ost.n)
    else:
        emit_l0(P, nc, D, shared, D["x1"], stop_after=stop_after)
    P.finish()
    es.close()
    return nc



def emit_l1(P, nc, D, shared, out_dram, stop_after=None):
    es = P.es
    pe = P.sem("pe")
    dv = P.sem("dv")
    ac = P.sem("ac")
    kvld = P.sem("kvld1")
    agst = P.sem("agst1")
    cc = P.sem("cc")
    ost = P.sem("ost1")
    sb = P.nc.sbuf_tensor
    ones_col = shared["ones_col"]
    ones_row = shared["ones_row"]
    id_f32 = shared["id_f32"]
    id_bf = shared["id_bf"]
    PS = shared["PS"]
    psT = shared["psT"]
    V = shared["vecs"]
    vi = shared["vi"]
    gates = {}

    def gate_wait(eng, res):
        if res in gates:
            s, v = gates[res]
            P.wait(eng, s, v)

    def gate_set(res, sem, val):
        gates[res] = (sem, val)

    def vcol(key, i):
        return V[:, vi[key] + i:vi[key] + i + 1]

    l1es = ExitStack()
    xn = l1es.enter_context(sb("xn1", [128, NDC, TPC], BF16))
    rows = l1es.enter_context(sb("rows1", [1, 5 * TPC], F32))
    nm_row = rows[:, 0:TPC]
    rstd_row = rows[:, TPC:2 * TPC]
    row_sc = rows[:, 2 * TPC:3 * TPC]
    row_sc2 = rows[:, 3 * TPC:4 * TPC]
    row_s3 = rows[:, 4 * TPC:5 * TPC]
    xr = D["x"].rearrange("(c p) t -> c p t", p=128)

    with ExitStack() as ph:
        xbuf = [ph.enter_context(sb(f"xb1{i}", [128, TPC], F32)) for i in range(2)]
        sqt = [ph.enter_context(sb(f"sq1{i}", [128, 512], F32)) for i in range(2)]
        nm_b = ph.enter_context(sb("nm1_b", [128, TPC], F32))
        rstd_b = ph.enter_context(sb("rstd1_b", [128, TPC], F32))
        xsc = ph.enter_context(sb("xsc1", [128, TPC], F32))
        xs = XStream(P, "x1ld", "sync", 2)
        for c in range(NDC):
            slot = c % 2
            gate_wait("sync", f"xb{slot}")
            tok = xs.dma(slot, xbuf[slot][:], xr[c])
            xs.wait("scalar", tok)
            for tb in range(NTB):
                gate_wait("scalar", f"sq{slot}")
                a = P.op("scalar", "activation", sqt[slot][:, :],
                         xbuf[slot][:, tb * 512:(tb + 1) * 512], AF.Square, inc=ac)
                P.wait("tensor", ac, a)
                P.op("tensor", "matmul", PS[3 + tb][0:1, :], ones_col[:, :],
                     xbuf[slot][:, tb * 512:(tb + 1) * 512],
                     start=(c == 0), stop=(c == NDC - 1))
                u = P.op("tensor", "matmul", PS[5 + tb][0:1, :], ones_col[:, :], sqt[slot][:, :],
                         start=(c == 0), stop=(c == NDC - 1), inc=pe)
                gate_set(f"sq{slot}", pe, u)
            gate_set(f"xb{slot}", pe, u)
        P.wait("vector", pe, pe.n)
        for tb in range(NTB):
            cs = slice(tb * 512, (tb + 1) * 512)
            P.op("vector", "tensor_scalar", nm_row[:, cs], PS[3 + tb][0:1, :], -1.0 / DM, None, ALU.mult)
            P.op("vector", "tensor_scalar", row_sc[:, cs], PS[5 + tb][0:1, :], 1.0 / DM, EPS_LN,
                 ALU.mult, ALU.add)
            P.op("vector", "tensor_mul", row_sc2[:, cs], nm_row[:, cs], nm_row[:, cs])
            v = P.op("vector", "tensor_sub", row_sc[:, cs], row_sc[:, cs], row_sc2[:, cs], inc=dv)
        P.wait("scalar", dv, dv.n)
        P.op("scalar", "activation", row_sc2[:, 0:TPC], row_sc[:, 0:TPC], AF.Ln)
        a = P.op("scalar", "activation", rstd_row[:, 0:TPC], row_sc2[:, 0:TPC], AF.Exp,
                 scale=-0.5, inc=ac)
        P.wait("tensor", ac, a)
        for tb in range(NTB):
            cs = slice(tb * 512, (tb + 1) * 512)
            gate_wait("tensor", "ps0")
            gate_wait("tensor", "ps1")
            P.op("tensor", "matmul", PS[0][:, :], ones_row[0:1, :], nm_row[:, cs],
                 start=True, stop=True)
            v = P.op("tensor", "matmul", PS[1][:, :], ones_row[0:1, :], rstd_row[:, cs],
                     start=True, stop=True, inc=pe)
            P.wait("vector", pe, v)
            u0 = P.op("vector", "tensor_copy", out=nm_b[:, cs], in_=PS[0][:, :], inc=dv)
            u = P.op("vector", "tensor_copy", out=rstd_b[:, cs], in_=PS[1][:, :], inc=dv)
            gate_set("ps0", dv, u0)
            gate_set("ps1", dv, u)
        for c in range(NDC):
            slot = c % 2
            gate_wait("sync", f"xb{slot}")
            tok = xs.dma(slot, xbuf[slot][:], xr[c])
            xs.wait("vector", tok)
            P.op("vector", "tensor_add", xsc[:, :], xbuf[slot][:, :], nm_b[:, :])
            u = P.op("vector", "tensor_mul", xn[:, c, :], xsc[:, :], rstd_b[:, :], inc=dv)
            gate_set(f"xb{slot}", dv, u)
    P.barrier()
    if stop_after == "ln1":
        P.wait("gpsimd", dv, dv.n)
        for c in range(NDC):
            P.op("gpsimd", "dma_start", inc=ost, incv=16,
                 out=out_dram.rearrange("(c p) t -> c p t", p=128)[c], in_=xn[:, c, :])
        P.wait("gpsimd", ost, ost.n)
        l1es.close()
        return

    with ExitStack() as ph:
        xa = ph.enter_context(sb("xa1", [128, NDC, TPC], BF16))
        q1 = ph.enter_context(sb("q1", [128, NDC, TPC], BF16))
        k1 = ph.enter_context(sb("k1b", [128, NDC, 512], BF16))
        v1 = ph.enter_context(sb("v1b", [128, 4, DM], BF16))
        sqt = [ph.enter_context(sb(f"sqa{i}", [128, 512], F32)) for i in range(2)]
        nm2_b = ph.enter_context(sb("nm2_b", [128, TPC], F32))
        rstd2_b = ph.enter_context(sb("rstd2_b", [128, TPC], F32))
        xsc2 = ph.enter_context(sb("xsc2", [128, TPC], F32))
        rsc = [ph.enter_context(sb(f"rsc1{i}", [128, 512], BF16)) for i in range(2)]
        cst = {}
        cld = P.sem("cld")
        for cname in ("RneoxT", "cos1", "sin1", "cos1q", "sin1q"):
            shp = [128, 128] if cname == "RneoxT" else [128, TPC]
            cdt = BF16 if cname == "RneoxT" else F32
            tt = ph.enter_context(sb(cname + "_s", shp, cdt))
            P.op("sync", "dma_start", inc=cld, incv=16, out=tt[:, :], in_=D[cname])
            cst[cname] = tt
        P.wait("tensor", cld, cld.n)
        P.wait("vector", cld, cld.n)
        for c in range(NDC):
            for tb in range(NTB):
                gate_wait("scalar", f"sqa{c % 2}")
                a = P.op("scalar", "activation", sqt[c % 2][:, :],
                         xn[:, c, tb * 512:(tb + 1) * 512], AF.Square, inc=ac)
                P.wait("tensor", ac, a)
                P.op("tensor", "matmul", PS[3 + tb][0:1, :], shared["ones_col_bf"][:, :],
                     xn[:, c, tb * 512:(tb + 1) * 512],
                     start=(c == 0), stop=(c == NDC - 1))
                u = P.op("tensor", "matmul", PS[5 + tb][0:1, :], ones_col[:, :],
                         sqt[c % 2][:, :],
                         start=(c == 0), stop=(c == NDC - 1), inc=pe)
                gate_set(f"sqa{c % 2}", pe, u)
        P.wait("vector", pe, pe.n)
        for tb in range(NTB):
            cs = slice(tb * 512, (tb + 1) * 512)
            P.op("vector", "tensor_scalar", row_s3[:, cs], PS[3 + tb][0:1, :], -1.0 / DM, None, ALU.mult)
            P.op("vector", "tensor_scalar", row_sc[:, cs], PS[5 + tb][0:1, :], 1.0 / DM, EPS_LN,
                 ALU.mult, ALU.add)
            P.op("vector", "tensor_mul", row_sc2[:, cs], row_s3[:, cs], row_s3[:, cs])
            v = P.op("vector", "tensor_sub", row_sc[:, cs], row_sc[:, cs], row_sc2[:, cs], inc=dv)
        P.wait("scalar", dv, dv.n)
        P.op("scalar", "activation", row_sc2[:, 0:TPC], row_sc[:, 0:TPC], AF.Ln)
        a = P.op("scalar", "activation", row_sc[:, 0:TPC], row_sc2[:, 0:TPC], AF.Exp,
                 scale=-0.5, inc=ac)
        P.wait("tensor", ac, a)
        for tb in range(NTB):
            cs = slice(tb * 512, (tb + 1) * 512)
            gate_wait("tensor", "ps0")
            gate_wait("tensor", "ps1")
            P.op("tensor", "matmul", PS[0][:, :], ones_row[0:1, :], row_s3[:, cs],
                 start=True, stop=True)
            v = P.op("tensor", "matmul", PS[1][:, :], ones_row[0:1, :], row_sc[:, cs],
                     start=True, stop=True, inc=pe)
            P.wait("vector", pe, v)
            u0 = P.op("vector", "tensor_copy", out=nm2_b[:, cs], in_=PS[0][:, :], inc=dv)
            u = P.op("vector", "tensor_copy", out=rstd2_b[:, cs], in_=PS[1][:, :], inc=dv)
            gate_set("ps0", dv, u0)
            gate_set("ps1", dv, u)
        for c in range(NDC):
            P.op("vector", "tensor_add", xsc2[:, :], xn[:, c, :], nm2_b[:, :])
            P.op("vector", "tensor_mul", xsc2[:, :], xsc2[:, :], rstd2_b[:, :])
            u = P.op("vector", "tensor_scalar", xa[:, c, :], xsc2[:, :],
                     vcol("lng", c), vcol("lnb", c), ALU.mult, ALU.add, inc=dv)
        xa_done = dv.n
        ws = WStream(P, "w1", 2, (128, NDC, 128), es=ph)
        agk1 = D["ag1k_in"].rearrange("(c p) t -> c p t", p=128)
        agv1 = D["ag1v_in"].rearrange("(b p) f -> b p f", p=128)
        wq = D["wq_t"].rearrange("(c p) m -> p c m", p=128)
        wk = D["wk_t"].rearrange("(c p) m -> p c m", p=128)
        wv = D["wv_t"].rearrange("(c p) m -> p c m", p=128)
        P.wait("tensor", dv, xa_done)
        for tb in range(NTB):
            cs = slice(tb * 512, (tb + 1) * 512)
            for hc in range(NDC):
                t, wt = ws.load(wq[:, :, hc * 128:(hc + 1) * 128])
                ps = PS[hc % 2]
                gate_wait("tensor", f"ps{hc % 2}")
                ws.begin("tensor", t)
                for kk in range(NDC):
                    v = P.op("tensor", "matmul", ps[:, :], wt[:, kk, :], xa[:, kk, cs],
                             start=(kk == 0), stop=(kk == NDC - 1),
                             inc=(pe if kk == NDC - 1 else None))
                ws.done(t, pe, v)
                P.wait("vector", pe, v)
                slot = hc % 2
                gate_wait("vector", f"rsc1{slot}")
                a = P.op("vector", "tensor_scalar", rsc[slot][:, :], ps[:, :],
                         vcol("bq", hc), None, ALU.add, inc=ac)
                P.wait("tensor", ac, a)
                gate_wait("tensor", "ps4")
                v = P.op("tensor", "matmul", PS[4][:, :], cst["RneoxT"][:, :], rsc[slot][:, :],
                         start=True, stop=True, inc=pe)
                P.wait("vector", pe, v)
                P.wait("vector", ac, a)
                P.op("vector", "tensor_tensor", xsc2[:, 0:512], PS[4][:, :],
                     cst["sin1q"][:, cs], ALU.mult)
                gate_set("ps4", P.last["vector"][0], P.last["vector"][1])
                P.op("vector", "tensor_tensor", xsc2[:, 512:1024], rsc[slot][:, :],
                     cst["cos1q"][:, cs], ALU.mult)
                u = P.op("vector", "tensor_add", q1[:, hc, cs], xsc2[:, 0:512],
                         xsc2[:, 512:1024], inc=dv)
                gate_set(f"ps{hc % 2}", ac, a)
                gate_set(f"rsc1{slot}", dv, u)
                t, wt = ws.load(wk[:, :, hc * 128:(hc + 1) * 128])
                ps = PS[2 + hc % 2]
                gate_wait("tensor", f"ps{2 + hc % 2}")
                ws.begin("tensor", t)
                for kk in range(NDC):
                    v = P.op("tensor", "matmul", ps[:, :], wt[:, kk, :], xa[:, kk, cs],
                             start=(kk == 0), stop=(kk == NDC - 1),
                             inc=(pe if kk == NDC - 1 else None))
                ws.done(t, pe, v)
                P.wait("vector", pe, v)
                slot = hc % 2
                gate_wait("vector", f"rsc1{slot}")
                a = P.op("vector", "tensor_scalar", rsc[slot][:, :], ps[:, :],
                         vcol("bk", hc), None, ALU.add, inc=ac)
                P.wait("tensor", ac, a)
                gate_wait("tensor", "ps4")
                v = P.op("tensor", "matmul", PS[4][:, :], cst["RneoxT"][:, :], rsc[slot][:, :],
                         start=True, stop=True, inc=pe)
                P.wait("vector", pe, v)
                P.wait("vector", ac, a)
                P.op("vector", "tensor_tensor", xsc2[:, 0:512], PS[4][:, :],
                     cst["sin1"][:, cs], ALU.mult)
                gate_set("ps4", P.last["vector"][0], P.last["vector"][1])
                P.op("vector", "tensor_tensor", xsc2[:, 512:1024], rsc[slot][:, :],
                     cst["cos1"][:, cs], ALU.mult)
                gate_wait("vector", "k1buf")
                u = P.op("vector", "tensor_add", k1[:, hc, :], xsc2[:, 0:512],
                         xsc2[:, 512:1024], inc=dv)
                gate_set(f"ps{2 + hc % 2}", ac, a)
                gate_set(f"rsc1{slot}", dv, u)
                t, wt = ws.load(wv[:, :, hc * 128:(hc + 1) * 128])
                ps = PS[hc % 2]
                gate_wait("tensor", f"ps{hc % 2}")
                ws.begin("tensor", t)
                for kk in range(NDC):
                    v = P.op("tensor", "matmul", ps[:, :], wt[:, kk, :], xa[:, kk, cs],
                             start=(kk == 0), stop=(kk == NDC - 1),
                             inc=(pe if kk == NDC - 1 else None))
                ws.done(t, pe, v)
                P.wait("vector", pe, v)
                slot = hc % 2
                gate_wait("vector", f"rsc1{slot}")
                u = P.op("vector", "tensor_scalar", rsc[slot][:, :], ps[:, :],
                         vcol("bv", hc), None, ALU.add, inc=dv)
                gate_set(f"ps{hc % 2}", dv, u)
                P.wait("tensor", dv, u)
                gate_wait("tensor", "psT")
                for j in range(4):
                    v = P.op("tensor", "transpose", psT[:, j * 128:(j + 1) * 128],
                             rsc[slot][:, j * 128:(j + 1) * 128], id_bf[:, :],
                             inc=(pe if j == 3 else None))
                P.wait("vector", pe, v)
                gate_wait("vector", "k1buf")
                for j in range(4):
                    u = P.op("vector", "tensor_copy",
                             out=v1[:, j, hc * 128:(hc + 1) * 128],
                             in_=psT[:, j * 128:(j + 1) * 128], inc=dv)
                gate_set("psT", dv, u)
                gate_set(f"rsc1{slot}", dv, u)
            P.wait("scalar", dv, dv.n)
            for c in range(NDC):
                P.op("scalar", "dma_start", inc=agst, incv=16,
                     out=agk1[c][:, tb * 512:(tb + 1) * 512], in_=k1[:, c, :])
            for j in range(4):
                P.op("scalar", "dma_start", inc=agst, incv=16,
                     out=agv1[tb * 4 + j], in_=v1[:, j, :])
            gate_set("k1buf", agst, agst.n)
        P.wait("scalar", dv, dv.n)
        P.op("scalar", "dma_start", inc=agst, incv=16,
             out=D["q1_dram"].rearrange("(c p) t -> p c t", p=128), in_=q1[:, :, :])
        P.wait("gpsimd", agst, agst.n)
        P.op("gpsimd", "collective_compute", "AllGather", ALU.bypass,
             replica_groups=[list(range(NCORE))],
             ins=[D["ag1k_in"].opt()], outs=[D["ag1k_out"].opt()], inc=cc)
        P.op("gpsimd", "collective_compute", "AllGather", ALU.bypass,
             replica_groups=[list(range(NCORE))],
             ins=[D["ag1v_in"].opt()], outs=[D["ag1v_out"].opt()], inc=cc)
        ag_done = cc.n
    P.barrier()

    GSZM = 4
    g_sb = l1es.enter_context(sb("g1", [128, NDC, TPC], BF16))
    with ExitStack() as ph:
        yacc = ph.enter_context(sb("yacc1", [128, NDC, TPC], F32))
        gated = ph.enter_context(sb("gated1", [128, GSZM, TPC], BF16))
        silu_sc = [ph.enter_context(sb(f"sil1{i}", [128, 512], F32)) for i in range(2)]
        cT = ph.enter_context(sb("cT", [8, TPC], F32))
        msel = ph.enter_context(sb("msel", [8, 8 * 128], F32))
        r_es = ExitStack()
        lgT = r_es.enter_context(sb("lgT", [8, TPC], F32))
        lgt = r_es.enter_context(sb("lgt", [128, 8 * 8], F32))
        comb = r_es.enter_context(sb("comb", [128, 8 * 8], F32))
        stat = r_es.enter_context(sb("stat1", [128, 8], F32))
        wsr = WStream(P, "wr", 1, (128, NDC, 8), es=r_es)
        cld = P.sem("cld")
        P.op("sync", "dma_start", inc=cld, incv=16, out=msel[:, :], in_=D["moe_sel"])
        rt = D["router_t"].rearrange("(c p) m -> p c m", p=128)
        t, wt = wsr.load(rt[:, :, :], out_ap_fn=lambda o: o[:, :, 0:8])
        for tb in range(NTB):
            ps = PS[2]
            gate_wait("tensor", "ps2")
            wsr.begin("tensor", t)
            for kk in range(NDC):
                v = P.op("tensor", "matmul", ps[0:8, :], wt[:, kk, 0:8],
                         xn[:, kk, tb * 512:(tb + 1) * 512],
                         start=(kk == 0), stop=(kk == NDC - 1),
                         inc=(pe if kk == NDC - 1 else None))
            P.wait("vector", pe, v)
            u = P.op("vector", "tensor_copy", out=lgT[:, tb * 512:(tb + 1) * 512],
                     in_=ps[0:8, :], inc=dv)
            gate_set("ps2", dv, u)
        P.wait("tensor", dv, dv.n)
        P.wait("tensor", cld, cld.n)
        for j in range(8):
            gate_wait("tensor", "ps4")
            v = P.op("tensor", "transpose", PS[4][0:128, 0:8],
                     lgT[:, j * 128:(j + 1) * 128], id_f32[0:8, 0:8], inc=pe)
            P.wait("vector", pe, v)
            u = P.op("vector", "tensor_copy", out=lgt[:, j * 8:(j + 1) * 8],
                     in_=PS[4][0:128, 0:8], inc=dv)
            gate_set("ps4", dv, u)
        for j in range(8):
            tl = lgt[:, j * 8:(j + 1) * 8]
            P.op("vector", "reduce_max", out=stat[:, 0:1], in_=tl, axis=AX.X)
            P.op("vector", "tensor_scalar", comb[:, j * 8:(j + 1) * 8], tl, stat[:, 0:1],
                 None, ALU.is_equal)
            P.op("vector", "scalar_tensor_tensor", comb[:, j * 8:(j + 1) * 8],
                 comb[:, j * 8:(j + 1) * 8], NEG, tl, ALU.mult, ALU.add)
            P.op("vector", "reduce_max", out=stat[:, 1:2], in_=comb[:, j * 8:(j + 1) * 8],
                 axis=AX.X)
            P.op("vector", "tensor_add", stat[:, 2:3], stat[:, 0:1], stat[:, 1:2])
            P.op("vector", "tensor_scalar", stat[:, 2:3], stat[:, 2:3], 0.5, None, ALU.mult)
            v = P.op("vector", "tensor_scalar", comb[:, j * 8:(j + 1) * 8], tl, stat[:, 2:3],
                     None, ALU.subtract, inc=dv)
            P.wait("scalar", dv, v)
            a = P.op("scalar", "activation", comb[:, j * 8:(j + 1) * 8],
                     comb[:, j * 8:(j + 1) * 8], AF.Sigmoid, scale=2.0, inc=ac)
            P.wait("vector", ac, a)
            P.op("vector", "tensor_scalar", lgt[:, j * 8:(j + 1) * 8], tl, stat[:, 1:2],
                 None, ALU.is_ge)
            u = P.op("vector", "tensor_mul", comb[:, j * 8:(j + 1) * 8],
                     comb[:, j * 8:(j + 1) * 8], lgt[:, j * 8:(j + 1) * 8], inc=dv)
        P.wait("tensor", dv, dv.n)
        for j in range(8):
            gate_wait("tensor", "ps4")
            v = P.op("tensor", "transpose", PS[4][0:8, (j % 4) * 128:(j % 4 + 1) * 128],
                     comb[:, j * 8:(j + 1) * 8], id_f32[:, :], inc=pe)
            P.wait("vector", pe, v)
            u = P.op("vector", "tensor_copy", out=cT[:, j * 128:(j + 1) * 128],
                     in_=PS[4][0:8, (j % 4) * 128:(j % 4 + 1) * 128], inc=dv)
            gate_set("ps4", dv, u)
        P.barrier()
        r_es.close()
        wsg = WStream(P, "mg", 2, (128, NDC, 128), es=ph)
        wsu = WStream(P, "mu", 2, (128, NDC, 128), es=ph)
        wsd = WStream(P, "md", GSZM + 1, (128, NDC, 128), es=ph)

        def ffn_branch(eidx, gw_ap, uw_ap, dw_ap, nhc, gbias_key=None, ubias_key=None,
                       first_overall=False, ce=None):
            ngrp = (nhc + GSZM - 1) // GSZM
            for grp in range(ngrp):
                hcs = list(range(grp * GSZM, min((grp + 1) * GSZM, nhc)))
                dts = []
                for gi, hc in enumerate(hcs):
                    tg, wtg = wsg.load(gw_ap[:, :, hc * 128:(hc + 1) * 128])
                    tu, wtu = wsu.load(uw_ap[:, :, hc * 128:(hc + 1) * 128])
                    td, wtd = wsd.load(dw_ap[hc])
                    dts.append((td, wtd))
                    for tb in range(NTB):
                        cs = slice(tb * 512, (tb + 1) * 512)
                        psg, psu = PS[tb], PS[2 + tb]
                        gate_wait("tensor", f"ps{tb}")
                        gate_wait("tensor", f"ps{2 + tb}")
                        wsg.begin("tensor", tg)
                        wsu.begin("tensor", tu)
                        for kk in range(NDC):
                            P.op("tensor", "matmul", psg[:, :], wtg[:, kk, :], xn[:, kk, cs],
                                 start=(kk == 0), stop=(kk == NDC - 1),
                                 inc=(pe if kk == NDC - 1 else None))
                        for kk in range(NDC):
                            P.op("tensor", "matmul", psu[:, :], wtu[:, kk, :], xn[:, kk, cs],
                                 start=(kk == 0), stop=(kk == NDC - 1),
                                 inc=(pe if kk == NDC - 1 else None))
                        v = pe.n
                        if tb == NTB - 1:
                            wsg.done(tg, pe, v - 1)
                            wsu.done(tu, pe, v)
                        P.wait("scalar", pe, v - 1)
                        slot = tb
                        gate_wait("scalar", f"sil1{slot}")
                        if gbias_key is not None:
                            a = P.op("scalar", "activation", silu_sc[slot][:, :], psg[:, :],
                                     AF.Silu, bias=vcol(gbias_key, hc), inc=ac)
                        else:
                            a = P.op("scalar", "activation", silu_sc[slot][:, :], psg[:, :],
                                     AF.Silu, inc=ac)
                        gate_set(f"ps{tb}", ac, a)
                        P.wait("vector", ac, a)
                        P.wait("vector", pe, v)
                        if ubias_key is not None:
                            P.op("vector", "scalar_tensor_tensor", silu_sc[slot][:, :],
                                 psu[:, :], vcol(ubias_key, hc), silu_sc[slot][:, :],
                                 ALU.add, ALU.mult)
                            gate_set(f"ps{2 + tb}", P.last["vector"][0], P.last["vector"][1])
                        else:
                            P.op("vector", "tensor_tensor", silu_sc[slot][:, :], psu[:, :],
                                 silu_sc[slot][:, :], ALU.mult)
                            gate_set(f"ps{2 + tb}", P.last["vector"][0], P.last["vector"][1])
                        if ce is not None:
                            u = P.op("vector", "tensor_tensor", gated[:, gi, cs],
                                     silu_sc[slot][:, :], ce[tb][:, :], ALU.mult, inc=dv)
                        else:
                            u = P.op("vector", "tensor_copy", out=gated[:, gi, cs],
                                     in_=silu_sc[slot][:, :], inc=dv)
                        gate_set(f"sil1{slot}", dv, u)
                gated_done = dv.n
                P.wait("tensor", dv, gated_done)
                for dc in range(NDC):
                    for tb in range(NTB):
                        cs = slice(tb * 512, (tb + 1) * 512)
                        gate_wait("tensor", "ps6")
                        for gi in range(len(hcs)):
                            td, wtd = dts[gi]
                            wsd.begin("tensor", td)
                            v = P.op("tensor", "matmul", PS[6][:, :], wtd[:, dc, :],
                                     gated[:, gi, cs],
                                     start=(gi == 0), stop=(gi == len(hcs) - 1),
                                     inc=(pe if gi == len(hcs) - 1 else None))
                        P.wait("vector", pe, v)
                        if first_overall and grp == 0:
                            u = P.op("vector", "tensor_copy", out=yacc[:, dc, cs],
                                     in_=PS[6][:, :], inc=dv)
                        else:
                            u = P.op("vector", "tensor_add", yacc[:, dc, cs], yacc[:, dc, cs],
                                     PS[6][:, :], inc=dv)
                        gate_set("ps6", dv, u)
                for gi in range(len(hcs)):
                    wsd.done(dts[gi][0], dv, u)

        gw = D["gate_w"]
        uw = D["up_w"]
        dwr = D["down_w"]
        for e in range(NEXP):
            P.wait("tensor", dv, dv.n)
            gate_wait("tensor", "ps4")
            gate_wait("tensor", "ps5")
            v = P.op("tensor", "matmul", PS[4][:, :], msel[:, e * 128:(e + 1) * 128],
                     cT[:, 0:512], start=True, stop=True, inc=pe)
            v = P.op("tensor", "matmul", PS[5][:, :], msel[:, e * 128:(e + 1) * 128],
                     cT[:, 512:1024], start=True, stop=True, inc=pe)
            P.wait("vector", pe, v)
            gwe = gw[e].rearrange("(c p) m -> p c m", p=128)
            uwe = uw[e].rearrange("(c p) m -> p c m", p=128)
            dwe = dwr[e].rearrange("(hc p) (c m) -> hc p c m", p=128, m=128)
            ffn_branch(e, gwe, uwe, dwe, MOE_HC, first_overall=(e == 0),
                       ce=[PS[4], PS[5]])
            gate_set("ps4", dv, dv.n)
            gate_set("ps5", dv, dv.n)
        sgw = D["s_gate_w"].rearrange("(c p) m -> p c m", p=128)
        suw = D["s_up_w"].rearrange("(c p) m -> p c m", p=128)
        sdw = D["s_down_w"].rearrange("(hc p) (c m) -> hc p c m", p=128, m=128)
        ffn_branch(-1, sgw, suw, sdw, SI_C, gbias_key="sgb", ubias_key="sub")
        P.wait("scalar", dv, dv.n)
        for dc in range(NDC):
            a = P.op("scalar", "activation", g_sb[:, dc, :], yacc[:, dc, :], AF.Silu,
                     bias=vcol("sdb", dc), inc=ac)
    P.barrier()
    if stop_after == "moe":
        P.wait("gpsimd", ac, ac.n)
        for c in range(NDC):
            P.op("gpsimd", "dma_start", inc=ost, incv=16,
                 out=out_dram.rearrange("(c p) t -> c p t", p=128)[c], in_=g_sb[:, c, :])
        P.wait("gpsimd", ost, ost.n)
        l1es.close()
        return

    with ExitStack() as ph:
        qh = ph.enter_context(sb("qh1", [128, TPC], BF16))
        kslot = ph.enter_context(sb("kslot1", [128, NSL_E0 + NSL_L0, 512], BF16))
        kown = ph.enter_context(sb("kown1", [128, NTB, 512], BF16))
        vslot = ph.enter_context(sb("vslot1", [128, (NSL_E0 + NSL_L0) * 4, DK], BF16))
        vown = ph.enter_context(sb("vown1", [128, NTB * 4, DK], BF16))
        cm = ph.enter_context(sb("cm1", [128, 8, NSL_E0 + NSL_L0 + 2], F32))
        mall = ph.enter_context(sb("mall1", [128, 8], F32))
        negm = ph.enter_context(sb("negm1", [1, 512], F32))
        mb = ph.enter_context(sb("mb1", [128, 512], F32))
        ssc = [ph.enter_context(sb(f"ssc{i}", [128, 512], F32)) for i in range(2)]
        pT = [ph.enter_context(sb(f"pT1{i}", [128, 512], BF16)) for i in range(3)]
        rbv = ph.enter_context(sb("rbv1", [128, 512], F32))
        attn1 = ph.enter_context(sb("attn1", [128, NDC, TPC], BF16))
        maskA_sb = ph.enter_context(sb("maskA1_sb", [128, 4, 512], F32))
        maskB_sb = ph.enter_context(sb("maskB1_sb", [128, 4, 512], F32))
        padA_b = ph.enter_context(sb("padA_b", [128, 32], F32))
        padB_c = ph.enter_context(sb("padB_c", [128, 32], F32))
        cmadj = ph.enter_context(sb("cmadj", [128, 32], F32))
        NSL = NSL_E0 + NSL_L0
        cld = P.sem("cld")
        P.op("sync", "dma_start", inc=cld, incv=16, out=maskA_sb[:, :, :], in_=D["maskA"])
        P.op("sync", "dma_start", inc=cld, incv=16, out=maskB_sb[:, :, :], in_=D["maskB"])
        P.op("sync", "dma_start", inc=cld, incv=16, out=padB_c[:, 0:27], in_=D["padB1"])
        P.op("sync", "dma_start", inc=cld, incv=16, out=padA_b[:, 0:26], in_=D["padA1"])
        P.wait("vector", cld, cld.n)
        P.wait("tensor", cld, cld.n)
        P.wait("sync", cc, ag_done)
        P.wait("tensor", cc, ag_done)
        agk = D["ag1k_out"].rearrange("(r c p) t -> r c p t", p=128, c=NDC)
        agv = D["ag1v_out"].rearrange("(b p) f -> p b f", p=128)
        agk_own = D["ag1k_in"].rearrange("(c p) t -> c p t", p=128)
        agv_own = D["ag1v_in"].rearrange("(b p) f -> p b f", p=128)
        q1d = D["q1_dram"].rearrange("(c p) t -> c p t", p=128)
        kv1 = XStream(P, "kv1", "sync", 1)

        kvE = P.sem("kv1E")
        kvL = P.sem("kv1L")
        for h in range(H):
            gate_wait("sync", "heads1")
            P.op("sync", "dma_start", inc=kvE, incv=16, out=qh[:, :], in_=q1d[h])
            for s in range(NTB):
                P.op("sync", "dma_start", inc=kvE, incv=16,
                     out=kown[:, s, :], in_=agk_own[h][:, s * 512:(s + 1) * 512])
                P.op("sync", "dma_start", inc=kvE, incv=16,
                     out=vown[:, 4 * s:4 * s + 4, :],
                     in_=agv_own[:, 4 * s:4 * s + 4, h * DK:(h + 1) * DK])
            for j in range(NSL_E0):
                g = j
                r, s = BLOCK_OWNER[g]
                P.op("sync", "dma_start", inc=kvE, incv=16,
                     out=kslot[:, j, :], in_=agk[r, h, :, s * 512:(s + 1) * 512])
                P.op("sync", "dma_start", inc=kvE, incv=16,
                     out=vslot[:, 4 * j:4 * j + 4, :],
                     in_=agv[:, 8 * r + 4 * s:8 * r + 4 * s + 4, h * DK:(h + 1) * DK])
            totE = kvE.n
            for j in range(NSL_E0, NSL):
                g = j - NSL_E0
                r, s = BLOCK_OWNER[g]
                P.op("sync", "dma_start", inc=kvL, incv=16,
                     out=kslot[:, j, :], in_=agk[r, h, :, s * 512:(s + 1) * 512])
                P.op("sync", "dma_start", inc=kvL, incv=16,
                     out=vslot[:, 4 * j:4 * j + 4, :],
                     in_=agv[:, 8 * r + 4 * s:8 * r + 4 * s + 4, h * DK:(h + 1) * DK])
            totL = kvL.n

            P.wait("tensor", kvE, totE)
            for role in range(2):
                if role == 1:
                    P.wait("tensor", kvL, totL)
                for qtl in range(4):
                    qs = role * 512 + qtl * 128
                    nsl = NSL_E0 if role == 0 else NSL_L0
                    base = 0 if role == 0 else NSL_E0
                    for jj in range(nsl + 1):
                        ps = PS[jj % 2]
                        gate_wait("tensor", f"ps{jj % 2}")
                        rhs = kown[:, role, :] if jj == nsl else kslot[:, base + jj, :]
                        v = P.op("tensor", "matmul", ps[:, :], qh[:, qs:qs + 128], rhs,
                                 start=True, stop=True, inc=pe)
                        P.wait("vector", pe, v)
                        if jj == nsl:
                            P.op("vector", "tensor_add", ps[:, :], ps[:, :],
                                 maskA_sb[:, qtl, :])
                        col = base + jj if jj < nsl else NSL + role
                        u = P.op("vector", "reduce_max",
                                 out=cm[:, role * 4 + qtl, col:col + 1],
                                 in_=ps[:, :], axis=AX.X, inc=dv)
                        gate_set(f"ps{jj % 2}", dv, u)
            for qtl in range(4):
                for role in range(2):
                    qi = role * 4 + qtl
                    if role == 0:
                        P.op("vector", "tensor_add", cmadj[:, 0:NSL_E0],
                             cm[:, qi, 0:NSL_E0], padA_b[:, 0:NSL_E0])
                        P.op("vector", "reduce_max", out=mall[:, qi:qi + 1],
                             in_=cmadj[:, 0:NSL_E0], axis=AX.X)
                        u = P.op("vector", "tensor_max", mall[:, qi:qi + 1],
                                 mall[:, qi:qi + 1], cm[:, qi, NSL:NSL + 1], inc=dv)
                    else:
                        P.op("vector", "tensor_add", cmadj[:, NSL_E0:NSL_E0 + NSL_L0],
                             cm[:, qi, NSL_E0:NSL_E0 + NSL_L0],
                             padA_b[:, NSL_E0:NSL_E0 + NSL_L0])
                        P.op("vector", "reduce_max", out=mall[:, qi:qi + 1],
                             in_=cmadj[:, NSL_E0:NSL_E0 + NSL_L0], axis=AX.X)
                        u = P.op("vector", "tensor_max", mall[:, qi:qi + 1],
                                 mall[:, qi:qi + 1], cm[:, qi, NSL + 1:NSL + 2], inc=dv)
            for role in range(2):
                P.wait("tensor", dv, dv.n)
                gate_wait("tensor", "ps4")
                for qtl in range(4):
                    v = P.op("tensor", "transpose", PS[4][0:1, qtl * 128:(qtl + 1) * 128],
                             mall[:, role * 4 + qtl:role * 4 + qtl + 1], id_f32[:, :],
                             inc=(pe if qtl == 3 else None))
                P.wait("vector", pe, v)
                gate_wait("vector", "negm1")
                u = P.op("vector", "tensor_copy", out=negm[0:1, :], in_=PS[4][0:1, 0:512],
                         inc=dv)
                gate_set("ps4", dv, u)
                P.wait("tensor", dv, u)
                gate_wait("tensor", "ps4")
                v = P.op("tensor", "matmul", PS[4][:, :], ones_row[0:1, :], negm[0:1, :],
                         start=True, stop=True, inc=pe)
                gate_set("negm1", pe, v)
                P.wait("vector", pe, v)
                gate_wait("vector", "mb1")
                u = P.op("vector", "tensor_copy", out=mb[:, :], in_=PS[4][:, :], inc=dv)
                gate_set("ps4", dv, u)
                P.wait("tensor", dv, u)
                mb_ready = u
                nsl = NSL_E0 if role == 0 else NSL_L0
                base = 0 if role == 0 else NSL_E0
                qs = slice(role * 512, (role + 1) * 512)
                first = True
                gate_wait("tensor", "ps6")
                gate_wait("tensor", "ps5")
                nkc = (nsl + 1) * 4
                for t_i in range(nkc):
                    jj, kc = divmod(t_i, 4)
                    own = jj == nsl
                    psb = PS[2 + (t_i % 2)]
                    gate_wait("tensor", f"ps{2 + (t_i % 2)}")
                    if own:
                        lhsT = kown[:, role, kc * 128:(kc + 1) * 128]
                    else:
                        lhsT = kslot[:, base + jj, kc * 128:(kc + 1) * 128]
                    v = P.op("tensor", "matmul", psb[:, :], lhsT, qh[:, qs],
                             start=True, stop=True, inc=pe)
                    P.wait("vector", pe, v)
                    if own:
                        P.op("vector", "tensor_add", psb[:, :], psb[:, :],
                             maskB_sb[:, kc, :])
                    slot_s = t_i % 2
                    gate_wait("vector", f"ssc{slot_s}")
                    padcol = padB_c[:, 26:27] if own else padB_c[:, base + jj:base + jj + 1]
                    u = P.op("vector", "scalar_tensor_tensor", ssc[slot_s][:, :], psb[:, :],
                             padcol, mb[:, :], ALU.add, ALU.subtract, inc=dv)
                    gate_set(f"ps{2 + (t_i % 2)}", dv, u)
                    P.wait("scalar", dv, u)
                    pt = pT[t_i % 3]
                    gate_wait("scalar", f"pT1{t_i % 3}")
                    a = P.op("scalar", "activation", pt[:, :], ssc[slot_s][:, :], AF.Exp,
                             inc=ac)
                    gate_set(f"ssc{slot_s}", ac, a)
                    P.wait("tensor", ac, a)
                    vv = vown[:, 4 * role + kc, :] if own else vslot[:, 4 * (base + jj) + kc, :]
                    P.op("tensor", "matmul", PS[5][0:1, :], shared["ones_col_bf"][:, :],
                         pt[:, :], start=first, stop=(t_i == nkc - 1))
                    u = P.op("tensor", "matmul", PS[6][:, :], vv, pt[:, :],
                             start=first, stop=(t_i == nkc - 1), inc=pe)
                    gate_set(f"pT1{t_i % 3}", pe, u)
                    first = False
                P.wait("vector", pe, pe.n)
                gate_wait("vector", "negm1")
                u = P.op("vector", "reciprocal", inc=dv, out=negm[0:1, :], in_=PS[5][0:1, :])
                gate_set("ps5", dv, u)
                P.wait("tensor", dv, u)
                gate_wait("tensor", "ps4")
                v = P.op("tensor", "matmul", PS[4][:, :], ones_row[0:1, :], negm[0:1, :],
                         start=True, stop=True, inc=pe)
                gate_set("negm1", pe, v)
                P.wait("vector", pe, v)
                P.op("vector", "tensor_copy", out=rbv[:, :], in_=PS[4][:, :])
                gate_set("ps4", P.last["vector"][0], P.last["vector"][1])
                u = P.op("vector", "tensor_tensor", attn1[:, h, qs], PS[6][:, :],
                         rbv[:, :], ALU.mult, inc=dv)
                gate_set("ps6", dv, u)
                gate_set("mb1", dv, u)
            gate_set("heads1", dv, dv.n)
        if stop_after == "attn1":
            P.wait("gpsimd", dv, dv.n)
            for c in range(NDC):
                P.op("gpsimd", "dma_start", inc=ost, incv=16,
                     out=out_dram.rearrange("(c p) t -> c p t", p=128)[c], in_=attn1[:, c, :])
            P.wait("gpsimd", ost, ost.n)
    P.barrier()
    if stop_after == "attn1":
        l1es.close()
        return

    with ExitStack() as ph:
        xbuf = [ph.enter_context(sb(f"xo{i}", [128, TPC], F32)) for i in range(2)]
        x1b = [ph.enter_context(sb(f"xob{i}", [128, TPC], F32)) for i in range(2)]
        tsc = ph.enter_context(sb("tsc1", [128, TPC], F32))
        nm_b = ph.enter_context(sb("nm1b2", [128, TPC], F32))
        rstd_b = ph.enter_context(sb("rstd1b2", [128, TPC], F32))
        ws = WStream(P, "wo", 2, (128, NDC, 128), es=ph)
        wo = D["wo_t"].rearrange("(c p) m -> p c m", p=128)
        xs2 = XStream(P, "xo", "sync", 2)
        st2 = XStream(P, "xos", "sync", 2)
        st_toks = []
        for tb in range(NTB):
            cs = slice(tb * 512, (tb + 1) * 512)
            gate_wait("tensor", "ps0")
            gate_wait("tensor", "ps1")
            P.op("tensor", "matmul", PS[0][:, :], ones_row[0:1, :], nm_row[:, cs],
                 start=True, stop=True)
            v = P.op("tensor", "matmul", PS[1][:, :], ones_row[0:1, :], rstd_row[:, cs],
                     start=True, stop=True, inc=pe)
            P.wait("vector", pe, v)
            u0 = P.op("vector", "tensor_copy", out=nm_b[:, cs], in_=PS[0][:, :], inc=dv)
            u = P.op("vector", "tensor_copy", out=rstd_b[:, cs], in_=PS[1][:, :], inc=dv)
            gate_set("ps0", dv, u0)
            gate_set("ps1", dv, u)
        for dc in range(NDC):
            t, wt = ws.load(wo[:, :, dc * 128:(dc + 1) * 128])
            slot = dc % 2
            gate_wait("sync", f"xo{slot}")
            xtok = xs2.dma(slot, xbuf[slot][:], xr[dc])
            if dc >= 2:
                st2.wait("vector", st_toks[dc - 2])
            for tb in range(NTB):
                cs = slice(tb * 512, (tb + 1) * 512)
                ps = PS[tb]
                gate_wait("tensor", f"ps{tb}")
                ws.begin("tensor", t)
                for k in range(NDC):
                    v = P.op("tensor", "matmul", ps[:, :], wt[:, k, :], attn1[:, k, cs],
                             start=(k == 0), stop=(k == NDC - 1),
                             inc=(pe if k == NDC - 1 else None))
                if tb == NTB - 1:
                    ws.done(t, pe, v)
                P.wait("vector", pe, v)
                xs2.wait("vector", xtok)
                P.op("vector", "tensor_add", tsc[:, cs], xbuf[slot][:, cs], nm_b[:, cs])
                P.op("vector", "tensor_mul", tsc[:, cs], tsc[:, cs], rstd_b[:, cs])
                P.op("vector", "scalar_tensor_tensor", tsc[:, cs], ps[:, :],
                     vcol("bo", dc), tsc[:, cs], ALU.add, ALU.add)
                gate_set(f"ps{tb}", P.last["vector"][0], P.last["vector"][1])
                P.op("vector", "tensor_mul", tsc[:, cs], tsc[:, cs], g_sb[:, dc, cs])
                u = P.op("vector", "tensor_add", x1b[slot][:, cs], tsc[:, cs],
                         xbuf[slot][:, cs], inc=dv)
            gate_set(f"xo{slot}", dv, u)
            P.wait("sync", dv, u)
            st_toks.append(st2.dma(slot, out_dram.rearrange("(c p) t -> c p t", p=128)[dc],
                                   x1b[slot][:]))
        for tok in st_toks[-2:]:
            st2.wait("sync", tok)
    l1es.close()
    P.barrier()


def declare_dram_l1(nc, with_out=True, x_ap=None, skip=()):
    D = {}

    def inp(name, shape, dt=F32):
        if name in skip:
            return
        D[name] = nc.dram_tensor(name, list(shape), dt, kind="ExternalInput").ap()

    if x_ap is None:
        inp("x", (DM, TPC))
    else:
        D["x"] = x_ap
    inp("wq_t", (DM, DM))
    inp("wk_t", (DM, DM))
    inp("wv_t", (DM, DM))
    inp("wo_t", (DM, DM))
    inp("bq1", (DM,))
    inp("bk1", (DM,))
    inp("bv1", (DM,))
    inp("bo1", (DM,))
    inp("lng1", (DM,))
    inp("lnb1", (DM,))
    inp("router_t", (DM, 8))
    inp("gate_w", (NEXP, DM, MOE_I))
    inp("up_w", (NEXP, DM, MOE_I))
    inp("down_w", (NEXP, MOE_I, DM))
    inp("s_gate_w", (DM, SI))
    inp("sgb1", (SI,))
    inp("s_up_w", (DM, SI))
    inp("sub1", (SI,))
    inp("s_down_w", (SI, DM))
    inp("sdb1", (DM,))
    inp("id_f32", (128, 128))
    inp("RneoxT", (128, 128), BF16)
    inp("cos1", (128, TPC))
    inp("sin1", (128, TPC))
    inp("cos1q", (128, TPC))
    inp("sin1q", (128, TPC))
    inp("maskA_h", (4, 128, 512))
    inp("maskB_h", (4, 128, 512))
    inp("moe_sel", (8, 8 * 128))
    inp("padA1", (128, 26))
    inp("padB1", (128, 27))
    if "maskA_h" in D:
        D["maskA"] = D["maskA_h"].rearrange("d p f -> p d f")
        D["maskB"] = D["maskB_h"].rearrange("d p f -> p d f")
    D["ag1k_in"] = nc.dram_tensor("ag1k_in", [DM, TPC], BF16).ap()
    D["ag1k_out"] = nc.dram_tensor("ag1k_out", [NCORE * DM, TPC], BF16).ap()
    D["ag1v_in"] = nc.dram_tensor("ag1v_in", [TPC, DM], BF16).ap()
    D["ag1v_out"] = nc.dram_tensor("ag1v_out", [SEQ, DM], BF16).ap()
    D["q1_dram"] = nc.dram_tensor("q1_dram", [DM, TPC], BF16).ap()
    if with_out:
        D["xout"] = nc.dram_tensor("xout", [DM, TPC], F32, kind="ExternalOutput").ap()
    return D


def host_stage_l1(core, inputs, cst, x1):
    blocks = CORE_BLOCKS[core]
    T = lambda a: np.ascontiguousarray(np.asarray(a).T)
    m = {}
    if x1 is not None:
        xs = np.concatenate([x1[g * 512:(g + 1) * 512] for g in blocks], 0)
        m["x"] = np.ascontiguousarray(xs.T.astype(np.float32))
    m.update({
        "wq_t": T(inputs["L1_attn_wq"]),
        "wk_t": T(inputs["L1_attn_wk"]),
        "wv_t": T(inputs["L1_attn_wv"]),
        "wo_t": T(inputs["L1_attn_wo"]),
        "bq1": inputs["L1_attn_bq"],
        "bk1": inputs["L1_attn_bk"],
        "bv1": inputs["L1_attn_bv"],
        "bo1": inputs["L1_attn_bo"],
        "lng1": inputs["L1_attn_ln_g"],
        "lnb1": inputs["L1_attn_ln_b"],
        "router_t": T(inputs["L1_ffn_router_w"]),
        "gate_w": inputs["L1_ffn_gate_w"],
        "up_w": inputs["L1_ffn_up_w"],
        "down_w": inputs["L1_ffn_down_w"],
        "s_gate_w": inputs["L1_ffn_s_gate_w"],
        "sgb1": inputs["L1_ffn_s_gate_b"],
        "s_up_w": inputs["L1_ffn_s_up_w"],
        "sub1": inputs["L1_ffn_s_up_b"],
        "s_down_w": inputs["L1_ffn_s_down_w"],
        "sdb1": inputs["L1_ffn_s_down_b"],
        "id_f32": cst["id_f32"],
        "RneoxT": cst["RneoxT"],
        "cos1": cst["cos1"],
        "sin1": cst["sin1"],
        "cos1q": cst["cos1q"],
        "sin1q": cst["sin1q"],
        "maskA_h": cst["maskA"],
        "maskB_h": cst["maskB"],
        "moe_sel": cst["moe_sel"],
    })
    padA = np.zeros((1, 26), np.float32)
    padB = np.zeros((1, 26), np.float32)
    for j in range(NSL_E0):
        if j >= blocks[0]:
            padA[0, j] = NEG
            padB[0, j] = NEG
    for j in range(NSL_L0):
        if j >= blocks[1]:
            padA[0, NSL_E0 + j] = NEG
            padB[0, NSL_E0 + j] = NEG
    m["padA1"] = np.broadcast_to(padA, (128, 26)).copy()
    padB27 = np.concatenate([padB, np.zeros((1, 1), np.float32)], 1)
    m["padB1"] = np.broadcast_to(padB27, (128, 27)).copy()
    out = {}
    for k, v in m.items():
        a = np.asarray(v)
        if k == "RneoxT":
            out[k] = np.ascontiguousarray(a.astype(bfloat16))
        else:
            out[k] = np.ascontiguousarray(a.astype(np.float32))
    return out


def build_fused_program():
    nc = bass.Bass("TRN2", target_bir_lowering=False, debug=False)
    es = ExitStack()
    D0 = declare_dram_l0(nc, with_out=False)
    x1d = nc.dram_tensor("x1_mid", [DM, TPC], F32)
    D1 = declare_dram_l1(nc, with_out=True, x_ap=x1d.ap(), skip=("id_f32", "maskA_h", "maskB_h"))
    D1["id_f32"] = D0["id_f32"]
    D1["maskA"] = D0["maskA"]
    D1["maskB"] = D0["maskB"]
    DD = dict(D0)
    DD.update({k: v for k, v in D1.items() if k not in DD})
    P = Prog(nc, es)
    shared = emit_shared(P, nc, DD, es, layers=[0, 1])
    emit_l0(P, nc, D0, shared, x1d.ap())
    emit_l1(P, nc, D1, shared, D1["xout"])
    P.finish()
    es.close()
    return nc


def build_l1_program(stop_after=None):
    nc = bass.Bass("TRN2", target_bir_lowering=False, debug=False)
    es = ExitStack()
    D = declare_dram_l1(nc)
    P = Prog(nc, es)
    shared = emit_shared(P, nc, D, es, layers=[1])
    emit_l1(P, nc, D, shared, D["xout"], stop_after=stop_after)
    P.finish()
    es.close()
    return nc



_CST = [host_constants(c) for c in range(NCORE)]


def kernel(**inputs):
    nc = build_fused_program()
    in_maps = []
    for c in range(NCORE):
        m = host_stage_l0(c, inputs, _CST[c])
        m1 = host_stage_l1(c, inputs, _CST[c], None)
        for k, v in m1.items():
            if k not in m and k != "x":
                m[k] = v
        in_maps.append(m)
    res = run_bass_kernel_spmd(nc, in_maps, core_ids=list(range(NCORE)))
    out = np.zeros((1, SEQ, DM), np.float32)
    for c in range(NCORE):
        sh = np.asarray(res.results[c]["xout"])
        b = CORE_BLOCKS[c]
        out[0, b[0] * 512:(b[0] + 1) * 512] = sh.T[:512]
        out[0, b[1] * 512:(b[1] + 1) * 512] = sh.T[512:]
    return out


# version 04
# speedup vs baseline: 1.0849x; 1.0401x over previous
import numpy as np
import ml_dtypes
from contextlib import ExitStack
import concourse.bass as bass
import concourse.mybir as mybir
from concourse.bass_utils import run_bass_kernel_spmd

F32 = mybir.dt.float32
BF16 = mybir.dt.bfloat16
AF = mybir.ActivationFunctionType
ALU = mybir.AluOpType
AX = mybir.AxisListType
bfloat16 = ml_dtypes.bfloat16

NCORE = 8
SEQ = 8192
DM = 2048
NDC = 16
TPC = 1024
NTB = 2
NGB = SEQ // 512

CORE_BLOCKS = [[c, NGB - 1 - c] for c in range(NCORE)]
BLOCK_OWNER = {}
for _c in range(NCORE):
    for _s, _g in enumerate(CORE_BLOCKS[_c]):
        BLOCK_OWNER[_g] = (_c, _s)

H = 16
NOPE_D, ROPE_D, V_D = 64, 32, 64
Q_RANK, KV_RANK = 768, 256
QR_C, KV_C = Q_RANK // 128, KV_RANK // 128
CAT_D = 97
VA = V_D + 1
SCALE0 = 1.0 / float(np.sqrt(NOPE_D + ROPE_D))
DK = 128
SCALE1 = 1.0 / float(np.sqrt(DK))
NEXP, MOE_I = 8, 1408
MOE_HC = MOE_I // 128
SI = 2816
SI_C = SI // 128
FH = 8192
FH_C = FH // 128

NEG = -1.0e30
DEBUG_PLAN = False
NSL_E0 = 8
NSL_L0 = 16
EPS_LN = 1e-5
EPS_RMS = 1e-6



def host_constants(core):
    cst = {}
    pos = np.concatenate([np.arange(g * 512, g * 512 + 512) for g in CORE_BLOCKS[core]])
    f0 = 1.0 / (10000.0 ** (np.arange(0, ROPE_D, 2, dtype=np.float64) / ROPE_D))
    ang0 = np.outer(pos.astype(np.float64), f0)
    c0 = np.cos(ang0)
    t0 = np.zeros((128, TPC), np.float64)
    for r in range(128):
        t0[r] = c0[:, (r % 32) // 2]
    cst["cos0q"] = (t0 * SCALE0).astype(np.float32)
    cst["cos0k"] = (t0[:32] / H).astype(np.float32)
    f1 = 1.0 / (10000.0 ** (np.arange(0, DK, 2, dtype=np.float64) / DK))
    ang1 = np.outer(pos.astype(np.float64), f1)
    c1, s1 = np.cos(ang1), np.sin(ang1)
    t1c = np.zeros((128, TPC), np.float64)
    t1s = np.zeros((128, TPC), np.float64)
    for r in range(128):
        t1c[r] = c1[:, r % 64]
        t1s[r] = s1[:, r % 64]
    cst["cos1"] = t1c.astype(np.float32)
    cst["sin1"] = t1s.astype(np.float32)
    cst["cos1q"] = (t1c * SCALE1).astype(np.float32)
    cst["sin1q"] = (t1s * SCALE1).astype(np.float32)
    i_ = np.arange(128)[:, None]
    j_ = np.arange(512)[None, :]
    mA = np.stack([np.where(j_ - i_ > d, NEG, 0.0) for d in (0, 128, 256, 384)])
    mB = np.stack([np.where(i_ - j_ > d, NEG, 0.0) for d in (0, -128, -256, -384)])
    cst["maskA"] = mA.astype(np.float32)
    cst["maskB"] = mB.astype(np.float32)
    R32 = np.zeros((32, 32), np.float32)
    for p_ in range(16):
        R32[2 * p_, 2 * p_] = 1.0
        R32[2 * p_, 2 * p_ + 1] = -1.0
        R32[2 * p_ + 1, 2 * p_] = 1.0
        R32[2 * p_ + 1, 2 * p_ + 1] = 1.0
    R4 = np.zeros((128, 128), np.float32)
    for hh in range(4):
        R4[hh * 32:(hh + 1) * 32, hh * 32:(hh + 1) * 32] = R32
    cst["R4T"] = np.ascontiguousarray(R4.T).astype(bfloat16)
    cst["R32T"] = np.ascontiguousarray(R32.T).astype(bfloat16)
    Rn = np.zeros((128, 128), np.float32)
    for i in range(64):
        Rn[i, 64 + i] = -1.0
        Rn[64 + i, i] = 1.0
    cst["RneoxT"] = np.ascontiguousarray(Rn.T).astype(bfloat16)
    cst["id_f32"] = np.eye(128, dtype=np.float32)
    sel = np.zeros((8, 8 * 128), np.float32)
    for e in range(8):
        sel[e, e * 128:(e + 1) * 128] = 1.0
    cst["moe_sel"] = sel
    return cst


MASKA_D = {0: 0, 128: 1, 256: 2, 384: 3}
MASKB_D = {0: 0, -128: 1, -256: 2, -384: 3}



class Sem:
    def __init__(self, h):
        self.h = h
        self.n = 0

    def bump(self, k=1):
        self.n += k
        return self.n


class Prog:
    ENGINES = ("tensor", "vector", "scalar", "gpsimd", "sync")

    def __init__(self, nc, es):
        self.nc = nc
        self.es = es
        self.q = {e: [] for e in self.ENGINES}
        self._sems = {}
        self.taps = {}

    def sem(self, name):
        if name not in self._sems:
            h = self.es.enter_context(self.nc.semaphore(name))
            self._sems[name] = Sem(h)
        return self._sems[name]

    _NOCHAIN = ("nop", "dma_start", "collective_compute", "wait_ge", "sem_inc")

    def op(self, eng, meth, *args, inc=None, incv=1, **kw):
        chained = eng in ("vector", "scalar", "gpsimd") and meth not in self._NOCHAIN
        if chained:
            if not hasattr(self, "last"):
                self.last = {}
            prev = self.last.get(eng)
            if prev is not None:
                self.q[eng].append(lambda e, s=prev[0].h, v=prev[1]: e.wait_ge(s, v))
            if inc is None:
                inc = self.sem("ch_" + eng)
                incv = 1
        post = inc.bump(incv) if inc is not None else None
        if chained:
            self.last[eng] = (inc, post)

        import traceback
        stk = traceback.extract_stack()[-3:-1] if DEBUG_PLAN else None

        def f(e, meth=meth, args=args, kw=kw, inc=inc, incv=incv, stk=stk):
            try:
                ins = getattr(e, meth)(*args, **kw)
            except Exception:
                if stk:
                    print("PLAN SITE:", "".join(traceback.format_list(stk)))
                raise
            if inc is not None:
                ins.then_inc(inc.h, incv)
        self.q[eng].append(f)
        return post

    def wait(self, eng, sem, val):
        self.q[eng].append(lambda e, s=sem.h, v=val: e.wait_ge(s, v))

    def barrier(self, engines=None):
        engines = engines or self.ENGINES
        b = self.sem("bar")
        for eng in ("tensor", "sync"):
            if eng in engines:
                self.op(eng, "nop", inc=b)
        nv = b.n
        lasts = []
        if hasattr(self, "last"):
            for e in ("vector", "scalar", "gpsimd"):
                lp = self.last.get(e)
                if lp is not None:
                    lasts.append(lp)
        for eng in engines:
            self.wait(eng, b, nv)
            for (s, v) in lasts:
                self.wait(eng, s, v)

    def finish(self):
        nc = self.nc
        with nc.Block() as block:
            @block.tensor
            def _(t):
                for f in self.q["tensor"]:
                    f(t)

            @block.vector
            def _(v):
                for f in self.q["vector"]:
                    f(v)

            @block.scalar
            def _(s):
                for f in self.q["scalar"]:
                    f(s)

            @block.gpsimd
            def _(g):
                for f in self.q["gpsimd"]:
                    f(g)

            @block.sync
            def _(s):
                for f in self.q["sync"]:
                    f(s)


class WStream:

    def __init__(self, P, name, nbuf, shape, es=None):
        self.P = P
        self.nbuf = nbuf
        es = es if es is not None else P.es
        self.tiles = [es.enter_context(P.nc.sbuf_tensor(f"{name}{i}", list(shape), BF16))
                      for i in range(nbuf)]
        self.ld = [P.sem(f"{name}_ld{i}") for i in range(nbuf)]
        self.done_pts = {}
        self.nload = 0

    def load(self, in_ap, out_ap_fn=None):
        t = self.nload
        slot = t % self.nbuf
        if t >= self.nbuf:
            s, v = self.done_pts[t - self.nbuf]
            self.P.wait("gpsimd", s, v)
        out = self.tiles[slot]
        out_ap = out_ap_fn(out) if out_ap_fn else out[:]
        self.P.op("gpsimd", "dma_start", inc=self.ld[slot], incv=16, out=out_ap, in_=in_ap)
        self.nload += 1
        return t, out

    def begin(self, eng, t):
        self.P.wait(eng, self.ld[t % self.nbuf], 16 * (t // self.nbuf + 1))

    def done(self, t, sem, val):
        self.done_pts[t] = (sem, val)


class XStream:

    def __init__(self, P, name, queue, nslots):
        self.P = P
        self.queue = queue
        self.nslots = nslots
        self.sems = [P.sem(f"{name}_s{i}") for i in range(nslots)]
        self.cnt = [0] * nslots

    def dma(self, slot, out, in_, **kw):
        self.cnt[slot] += 1
        self.P.op(self.queue, "dma_start", inc=self.sems[slot], incv=16,
                  out=out, in_=in_, **kw)
        return slot, 16 * self.cnt[slot]

    def wait(self, eng, token):
        slot, v = token
        self.P.wait(eng, self.sems[slot], v)


def bcast_row(P, row_ap, width, ps, dst_ap, ones_row_f32, pe, dv, src_ready=None):
    if src_ready is not None:
        P.wait("tensor", src_ready[0], src_ready[1])
    v = P.op("tensor", "matmul", ps[:, 0:width], ones_row_f32[0:1, 0:128], row_ap,
             start=True, stop=True, inc=pe)
    P.wait("vector", pe, v)
    return P.op("vector", "tensor_copy", inc=dv, out=dst_ap, in_=ps[:, 0:width])



def emit_l0(P, nc, D, shared, out_dram, stop_after=None):
    es = P.es
    pe = P.sem("pe")
    dv = P.sem("dv")
    ac = P.sem("ac")
    xld = P.sem("xld")
    kvld = P.sem("kvld")
    agst = P.sem("agst")
    cc = P.sem("cc")
    ost = P.sem("ost")
    sb = P.nc.sbuf_tensor
    ones_col = shared["ones_col"]
    ones_row = shared["ones_row"]
    id_f32 = shared["id_f32"]
    PS = shared["PS"]
    psT = shared["psT"]
    gates = {}

    def gate_wait(eng, res):
        if res in gates:
            s, v = gates[res]
            P.wait(eng, s, v)

    def gate_set(res, sem, val):
        gates[res] = (sem, val)

    l0es = ExitStack()
    xn = l0es.enter_context(sb("xn0", [128, NDC, TPC], BF16))
    g_sb = l0es.enter_context(sb("g0", [128, NDC, TPC], BF16))
    rows = l0es.enter_context(sb("rows0", [1, 4 * TPC], F32))
    nm_row = rows[:, 0:TPC]
    rstd_row = rows[:, TPC:2 * TPC]
    row_sc = rows[:, 2 * TPC:3 * TPC]
    row_sc2 = rows[:, 3 * TPC:4 * TPC]

    with ExitStack() as ph:
        xbuf = [ph.enter_context(sb(f"xb{i}", [128, TPC], F32)) for i in range(2)]
        sqt = [ph.enter_context(sb(f"sq{i}", [128, 512], F32)) for i in range(2)]
        nm_b = ph.enter_context(sb("nm_b", [128, TPC], F32))
        rstd_b = ph.enter_context(sb("rstd_b", [128, TPC], F32))
        xsc = ph.enter_context(sb("xsc", [128, TPC], F32))
        xr = D["x"].rearrange("(c p) t -> c p t", p=128)
        xs = XStream(P, "x0", "sync", 2)
        for c in range(NDC):
            slot = c % 2
            gate_wait("sync", f"xb{slot}")
            tok = xs.dma(slot, xbuf[slot][:], xr[c])
            xs.wait("scalar", tok)
            for tb in range(NTB):
                gate_wait("scalar", f"sq{slot}")
                a = P.op("scalar", "activation", sqt[slot][:, :],
                         xbuf[slot][:, tb * 512:(tb + 1) * 512], AF.Square, inc=ac)
                P.wait("tensor", ac, a)
                P.op("tensor", "matmul", PS[3 + tb][0:1, :], ones_col[:, :],
                     xbuf[slot][:, tb * 512:(tb + 1) * 512],
                     start=(c == 0), stop=(c == NDC - 1))
                u = P.op("tensor", "matmul", PS[5 + tb][0:1, :], ones_col[:, :], sqt[slot][:, :],
                         start=(c == 0), stop=(c == NDC - 1), inc=pe)
                gate_set(f"sq{slot}", pe, u)
            gate_set(f"xb{slot}", pe, u)
        P.wait("vector", pe, pe.n)
        for tb in range(NTB):
            cs = slice(tb * 512, (tb + 1) * 512)
            P.op("vector", "tensor_scalar", nm_row[:, cs], PS[3 + tb][0:1, :], -1.0 / DM, None, ALU.mult)
            P.op("vector", "tensor_scalar", row_sc[:, cs], PS[5 + tb][0:1, :], 1.0 / DM, EPS_LN,
                 ALU.mult, ALU.add)
            P.op("vector", "tensor_mul", row_sc2[:, cs], nm_row[:, cs], nm_row[:, cs])
            v = P.op("vector", "tensor_sub", row_sc[:, cs], row_sc[:, cs], row_sc2[:, cs], inc=dv)
        P.wait("scalar", dv, dv.n)
        P.op("scalar", "activation", row_sc2[:, 0:TPC], row_sc[:, 0:TPC], AF.Ln)
        a = P.op("scalar", "activation", rstd_row[:, 0:TPC], row_sc2[:, 0:TPC], AF.Exp,
                 scale=-0.5, inc=ac)
        P.wait("tensor", ac, a)
        for tb in range(NTB):
            cs = slice(tb * 512, (tb + 1) * 512)
            gate_wait("tensor", "ps0")
            gate_wait("tensor", "ps1")
            v = P.op("tensor", "matmul", PS[0][:, :], ones_row[0:1, :], nm_row[:, cs],
                     start=True, stop=True)
            v = P.op("tensor", "matmul", PS[1][:, :], ones_row[0:1, :], rstd_row[:, cs],
                     start=True, stop=True, inc=pe)
            P.wait("vector", pe, v)
            u0 = P.op("vector", "tensor_copy", out=nm_b[:, cs], in_=PS[0][:, :], inc=dv)
            u = P.op("vector", "tensor_copy", out=rstd_b[:, cs], in_=PS[1][:, :], inc=dv)
            gate_set("ps0", dv, u0)
            gate_set("ps1", dv, u)
        for c in range(NDC):
            slot = c % 2
            gate_wait("sync", f"xb{slot}")
            tok = xs.dma(slot, xbuf[slot][:], xr[c])
            xs.wait("vector", tok)
            P.op("vector", "tensor_add", xsc[:, :], xbuf[slot][:, :], nm_b[:, :])
            u = P.op("vector", "tensor_mul", xn[:, c, :], xsc[:, :], rstd_b[:, :], inc=dv)
            gate_set(f"xb{slot}", dv, u)
    P.barrier()
    if stop_after == "ln":
        P.wait("gpsimd", dv, dv.n)
        for c in range(NDC):
            P.op("gpsimd", "dma_start", inc=ost, incv=16,
                 out=out_dram.rearrange("(c p) t -> c p t", p=128)[c], in_=xn[:, c, :])
        P.wait("gpsimd", ost, ost.n)
        l0es.close()
        return

    with ExitStack() as ph:
        cq = ph.enter_context(sb("cq", [128, QR_C, TPC], BF16))
        ckv = ph.enter_context(sb("ckv", [128, KV_C, TPC], BF16))
        qcat = ph.enter_context(sb("qcat", [CAT_D, H, TPC], BF16))
        kcat = ph.enter_context(sb("kcat", [CAT_D, H, TPC], BF16))
        vaug = ph.enter_context(sb("vaug", [128, 8, H * VA], BF16))
        sqt = [ph.enter_context(sb(f"qsq{i}", [128, 512], F32)) for i in range(2)]
        rb = ph.enter_context(sb("rbq", [128, TPC], F32))
        rsc = [ph.enter_context(sb(f"rsc{i}", [128, 512], BF16)) for i in range(2)]
        ws = WStream(P, "w0", 2, (128, NDC, 128), es=ph)
        cst_l0 = {}
        for cname, shp, cdt in [("R4T", [128, 128], BF16), ("R32T", [32, 32], BF16),
                                ("cos0q", [128, TPC], F32), ("cos0k", [32, TPC], F32)]:
            tt = ph.enter_context(sb(cname + "_s", shp, cdt))
            P.op("sync", "dma_start", inc=P.sem("cld"), incv=16, out=tt[:, :], in_=D[cname])
            cst_l0[cname] = tt
        P.wait("tensor", P.sem("cld"), P.sem("cld").n)
        P.wait("vector", P.sem("cld"), P.sem("cld").n)
        P.op("vector", "memset", qcat[:, :, :], 0.0)
        P.op("vector", "memset", kcat[:, :, :], 1.0)
        P.op("vector", "memset", vaug[:, :, :], 1.0, inc=P.sem("dv"))

        def proj_block(wt_ap, nk, rhs_fn, t, psum, first_last=(True, True)):
            ws.begin("tensor", t)
            for k in range(nk):
                v = P.op("tensor", "matmul", psum, ws.tiles[t % ws.nbuf][:, k, :], rhs_fn(k),
                         start=(k == 0 and first_last[0]), stop=(k == nk - 1 and first_last[1]),
                         inc=(pe if k == nk - 1 else None))
            return v

        wcq = D["w_cq"].rearrange("(c p) m -> p c m", p=128)
        for rc in range(QR_C):
            t, _ = ws.load(wcq[:, :, rc * 128:(rc + 1) * 128])
            for tb in range(NTB):
                ps = PS[rc % 2]
                gate_wait("tensor", f"ps{rc % 2}")
                v = proj_block(None, NDC, lambda k, tb=tb: xn[:, k, tb * 512:(tb + 1) * 512],
                               t, ps[:, :])
                if tb == NTB - 1:
                    ws.done(t, pe, v)
                P.wait("scalar", pe, v)
                P.op("scalar", "activation", cq[:, rc, tb * 512:(tb + 1) * 512], ps[:, :], AF.Copy)
                gate_wait("scalar", f"qsq{rc % 2}")
                a = P.op("scalar", "activation", sqt[rc % 2][:, :], ps[:, :], AF.Square, inc=ac)
                gate_set(f"ps{rc % 2}", ac, a)
                P.wait("tensor", ac, a)
                u = P.op("tensor", "matmul", PS[5 + tb][0:1, :], ones_col[:, :], sqt[rc % 2][:, :],
                         start=(rc == 0), stop=(rc == QR_C - 1), inc=pe)
                gate_set(f"qsq{rc % 2}", pe, u)
        P.wait("vector", pe, pe.n)
        for tb in range(NTB):
            cs = slice(tb * 512, (tb + 1) * 512)
            v = P.op("vector", "tensor_scalar", row_sc[:, cs], PS[5 + tb][0:1, :],
                     1.0 / Q_RANK, EPS_RMS, ALU.mult, ALU.add, inc=dv)
        P.wait("scalar", dv, dv.n)
        P.op("scalar", "activation", row_sc2[:, 0:TPC], row_sc[:, 0:TPC], AF.Ln)
        a = P.op("scalar", "activation", row_sc[:, 0:TPC], row_sc2[:, 0:TPC], AF.Exp,
                 scale=-0.5, inc=ac)
        P.wait("tensor", ac, a)
        for tb in range(NTB):
            cs = slice(tb * 512, (tb + 1) * 512)
            gate_wait("tensor", "ps0")
            v = P.op("tensor", "matmul", PS[0][:, :], ones_row[0:1, :], row_sc[:, cs],
                     start=True, stop=True, inc=pe)
            P.wait("vector", pe, v)
            u = P.op("vector", "tensor_copy", out=rb[:, cs], in_=PS[0][:, :], inc=dv)
            gate_set("ps0", dv, u)
        for rc in range(QR_C):
            u = P.op("vector", "scalar_tensor_tensor", cq[:, rc, :], cq[:, rc, :],
                     shared["vecs"][:, shared["vi"]["qg"] + rc:shared["vi"]["qg"] + rc + 1],
                     rb[:, :], ALU.mult, ALU.mult, inc=dv)
        normq_done = dv.n

        wqn = D["w_q_nope"].rearrange("(c p) m -> p c m", p=128)
        P.wait("tensor", dv, normq_done)
        for oc in range(H * NOPE_D // 128):
            t, wt = ws.load(wqn[:, :, oc * 128:(oc + 1) * 128],
                            out_ap_fn=lambda o: o[:, 0:QR_C, :])
            for tb in range(NTB):
                ps = PS[(oc + tb) % 2]
                gate_wait("tensor", f"ps{(oc + tb) % 2}")
                ws.begin("tensor", t)
                for k in range(QR_C):
                    v = P.op("tensor", "matmul", ps[:, :], wt[:, k, :],
                             cq[:, k, tb * 512:(tb + 1) * 512],
                             start=(k == 0), stop=(k == QR_C - 1),
                             inc=(pe if k == QR_C - 1 else None))
                if tb == NTB - 1:
                    ws.done(t, pe, v)
                P.wait("vector", pe, v)
                cs = slice(tb * 512, (tb + 1) * 512)
                P.op("vector", "tensor_scalar", qcat[0:64, 2 * oc, cs], ps[0:64, :],
                     SCALE0, None, ALU.mult)
                u = P.op("vector", "tensor_scalar", qcat[0:64, 2 * oc + 1, cs], ps[64:128, :],
                         SCALE0, None, ALU.mult, inc=dv)
                gate_set(f"ps{(oc + tb) % 2}", dv, u)

        wqr = D["w_q_rope"].rearrange("(c p) m -> p c m", p=128)
        for oc4 in range(4):
            t, wt = ws.load(wqr[:, :, oc4 * 128:(oc4 + 1) * 128],
                            out_ap_fn=lambda o: o[:, 0:QR_C, :])
            for tb in range(NTB):
                ps = PS[(oc4 + tb) % 2]
                gate_wait("tensor", f"ps{(oc4 + tb) % 2}")
                ws.begin("tensor", t)
                for k in range(QR_C):
                    v = P.op("tensor", "matmul", ps[:, :], wt[:, k, :],
                             cq[:, k, tb * 512:(tb + 1) * 512],
                             start=(k == 0), stop=(k == QR_C - 1),
                             inc=(pe if k == QR_C - 1 else None))
                if tb == NTB - 1:
                    ws.done(t, pe, v)
                P.wait("scalar", pe, v)
                slot = tb % 2
                gate_wait("scalar", f"rsc{slot}")
                a = P.op("scalar", "activation", rsc[slot][:, :], ps[:, :], AF.Copy, inc=ac)
                gate_set(f"ps{(oc4 + tb) % 2}", ac, a)
                P.wait("tensor", ac, a)
                gate_wait("tensor", "ps4")
                v = P.op("tensor", "matmul", PS[4][:, :], cst_l0["R4T"][:, :], rsc[slot][:, :],
                         start=True, stop=True, inc=pe)
                P.wait("vector", pe, v)
                cs = slice(tb * 512, (tb + 1) * 512)
                u = P.op("vector", "tensor_tensor", rsc[slot][:, :], PS[4][:, :],
                         cst_l0["cos0q"][:, cs], ALU.mult, inc=dv)
                gate_set("ps4", dv, u)
                for hh in range(4):
                    u = P.op("vector", "tensor_copy",
                             out=qcat[64:96, oc4 * 4 + hh, cs],
                             in_=rsc[slot][hh * 32:(hh + 1) * 32, :], inc=dv)
                gate_set(f"rsc{slot}", dv, u)

        wckv = D["w_ckv"].rearrange("(c p) m -> p c m", p=128)
        for rc in range(KV_C):
            t, wt = ws.load(wckv[:, :, rc * 128:(rc + 1) * 128])
            for tb in range(NTB):
                ps = PS[(rc + tb) % 2]
                gate_wait("tensor", f"ps{(rc + tb) % 2}")
                ws.begin("tensor", t)
                for k in range(NDC):
                    v = P.op("tensor", "matmul", ps[:, :], wt[:, k, :],
                             xn[:, k, tb * 512:(tb + 1) * 512],
                             start=(k == 0), stop=(k == NDC - 1),
                             inc=(pe if k == NDC - 1 else None))
                if tb == NTB - 1:
                    ws.done(t, pe, v)
                P.wait("scalar", pe, v)
                P.op("scalar", "activation", ckv[:, rc, tb * 512:(tb + 1) * 512], ps[:, :], AF.Copy)
                gate_wait("scalar", f"qsq{rc % 2}")
                a = P.op("scalar", "activation", sqt[rc % 2][:, :], ps[:, :], AF.Square, inc=ac)
                gate_set(f"ps{(rc + tb) % 2}", ac, a)
                P.wait("tensor", ac, a)
                u = P.op("tensor", "matmul", PS[5 + tb][0:1, :], ones_col[:, :], sqt[rc % 2][:, :],
                         start=(rc == 0), stop=(rc == KV_C - 1), inc=pe)
                gate_set(f"qsq{rc % 2}", pe, u)
        P.wait("vector", pe, pe.n)
        for tb in range(NTB):
            cs = slice(tb * 512, (tb + 1) * 512)
            v = P.op("vector", "tensor_scalar", row_sc[:, cs], PS[5 + tb][0:1, :],
                     1.0 / KV_RANK, EPS_RMS, ALU.mult, ALU.add, inc=dv)
        P.wait("scalar", dv, dv.n)
        P.op("scalar", "activation", row_sc2[:, 0:TPC], row_sc[:, 0:TPC], AF.Ln)
        a = P.op("scalar", "activation", row_sc[:, 0:TPC], row_sc2[:, 0:TPC], AF.Exp,
                 scale=-0.5, inc=ac)
        P.wait("tensor", ac, a)
        for tb in range(NTB):
            cs = slice(tb * 512, (tb + 1) * 512)
            gate_wait("tensor", "ps0")
            v = P.op("tensor", "matmul", PS[0][:, :], ones_row[0:1, :], row_sc[:, cs],
                     start=True, stop=True, inc=pe)
            P.wait("vector", pe, v)
            u = P.op("vector", "tensor_copy", out=rb[:, cs], in_=PS[0][:, :], inc=dv)
            gate_set("ps0", dv, u)
        for rc in range(KV_C):
            u = P.op("vector", "scalar_tensor_tensor", ckv[:, rc, :], ckv[:, rc, :],
                     shared["vecs"][:, shared["vi"]["kvg"] + rc:shared["vi"]["kvg"] + rc + 1],
                     rb[:, :], ALU.mult, ALU.mult, inc=dv)
        normkv_done = dv.n

        wkn = D["w_k_nope"].rearrange("(c p) m -> p c m", p=128)
        P.wait("tensor", dv, normkv_done)
        for oc in range(H * NOPE_D // 128):
            t, wt = ws.load(wkn[:, :, oc * 128:(oc + 1) * 128],
                            out_ap_fn=lambda o: o[:, 0:KV_C, :])
            for tb in range(NTB):
                ps = PS[(oc + tb) % 2]
                gate_wait("tensor", f"ps{(oc + tb) % 2}")
                ws.begin("tensor", t)
                for k in range(KV_C):
                    v = P.op("tensor", "matmul", ps[:, :], wt[:, k, :],
                             ckv[:, k, tb * 512:(tb + 1) * 512],
                             start=(k == 0), stop=(k == KV_C - 1),
                             inc=(pe if k == KV_C - 1 else None))
                if tb == NTB - 1:
                    ws.done(t, pe, v)
                P.wait("vector", pe, v)
                cs = slice(tb * 512, (tb + 1) * 512)
                P.op("vector", "tensor_copy", out=kcat[0:64, 2 * oc, cs], in_=ps[0:64, :])
                u = P.op("vector", "tensor_copy", out=kcat[0:64, 2 * oc + 1, cs],
                         in_=ps[64:128, :], inc=dv)
                gate_set(f"ps{(oc + tb) % 2}", dv, u)

        wkr = D["w_k_rope"].rearrange("(c p) m -> p c m", p=128)
        t, wt = ws.load(wkr[:, :, :], out_ap_fn=lambda o: o[:, :, 0:32])
        for tb in range(NTB):
            ps = PS[2]
            gate_wait("tensor", "ps2")
            ws.begin("tensor", t)
            for k in range(NDC):
                v = P.op("tensor", "matmul", ps[0:32, :], wt[:, k, 0:32],
                         xn[:, k, tb * 512:(tb + 1) * 512],
                         start=(k == 0), stop=(k == NDC - 1),
                         inc=(pe if k == NDC - 1 else None))
            if tb == NTB - 1:
                ws.done(t, pe, v)
            P.wait("scalar", pe, v)
            slot = tb % 2
            gate_wait("scalar", f"rsc{slot}")
            a = P.op("scalar", "activation", rsc[slot][0:32, :], ps[0:32, :], AF.Copy, inc=ac)
            gate_set("ps2", ac, a)
            P.wait("tensor", ac, a)
            gate_wait("tensor", "ps4")
            v = P.op("tensor", "matmul", PS[4][0:32, :], cst_l0["R32T"][:, :], rsc[slot][0:32, :],
                     start=True, stop=True, inc=pe)
            P.wait("vector", pe, v)
            cs = slice(tb * 512, (tb + 1) * 512)
            u = P.op("vector", "tensor_tensor", rsc[slot][0:32, :], PS[4][0:32, :],
                     cst_l0["cos0k"][:, cs], ALU.mult, inc=dv)
            gate_set("ps4", dv, u)
            for h in range(H):
                u = P.op("vector", "tensor_copy", out=kcat[64:96, h, cs],
                         in_=rsc[slot][0:32, :], inc=dv)
            gate_set(f"rsc{slot}", dv, u)

        wv = D["w_v"].rearrange("(c p) m -> p c m", p=128)
        for oc in range(H * V_D // 128):
            t, wt = ws.load(wv[:, :, oc * 128:(oc + 1) * 128],
                            out_ap_fn=lambda o: o[:, 0:KV_C, :])
            for tb in range(NTB):
                ps = PS[(oc + tb) % 2]
                gate_wait("tensor", f"ps{(oc + tb) % 2}")
                ws.begin("tensor", t)
                for k in range(KV_C):
                    v = P.op("tensor", "matmul", ps[:, :], wt[:, k, :],
                             ckv[:, k, tb * 512:(tb + 1) * 512],
                             start=(k == 0), stop=(k == KV_C - 1),
                             inc=(pe if k == KV_C - 1 else None))
                if tb == NTB - 1:
                    ws.done(t, pe, v)
                P.wait("vector", pe, v)
                slot = tb % 2
                gate_wait("vector", f"rsc{slot}")
                u = P.op("vector", "tensor_copy", out=rsc[slot][:, :], in_=ps[:, :], inc=dv)
                gate_set(f"ps{(oc + tb) % 2}", dv, u)
                P.wait("tensor", dv, u)
                gate_wait("tensor", "psT")
                for j in range(4):
                    v = P.op("tensor", "transpose", psT[:, j * 128:(j + 1) * 128],
                             rsc[slot][:, j * 128:(j + 1) * 128], shared["id_bf"][:, :],
                             inc=(pe if j == 3 else None))
                P.wait("vector", pe, v)
                for j in range(4):
                    for half in range(2):
                        h = 2 * oc + half
                        u = P.op("vector", "tensor_copy",
                                 out=vaug[:, tb * 4 + j, h * VA:h * VA + V_D],
                                 in_=psT[:, j * 128 + half * 64: j * 128 + half * 64 + 64],
                                 inc=dv)
                gate_set("psT", dv, u)
                gate_set(f"rsc{slot}", dv, u)

        evacs_done = dv.n
        P.wait("scalar", dv, evacs_done)
        P.op("scalar", "dma_start", inc=agst, incv=16, out=D["ag0k_in"], in_=kcat[:, :, :])
        for j in range(8):
            P.op("scalar", "dma_start", inc=agst, incv=16,
                 out=D["ag0v_in_r"][j], in_=vaug[:, j, :])
        P.op("scalar", "dma_start", inc=agst, incv=16, out=D["qcat_dram"], in_=qcat[:, :, :])
        P.wait("gpsimd", agst, agst.n)
        P.op("gpsimd", "collective_compute", "AllGather", ALU.bypass,
             replica_groups=[list(range(NCORE))],
             ins=[D["ag0k_in"].opt()], outs=[D["ag0k_out"].opt()], inc=cc)
        P.op("gpsimd", "collective_compute", "AllGather", ALU.bypass,
             replica_groups=[list(range(NCORE))],
             ins=[D["ag0v_in"].opt()], outs=[D["ag0v_out"].opt()], inc=cc)
        ag_done = cc.n
        if stop_after == "qkv":
            P.wait("sync", cc, ag_done)
            for c in range(NDC):
                P.op("gpsimd", "dma_start", inc=ost, incv=16,
                     out=out_dram.rearrange("(c p) t -> c p t", p=128)[c], in_=xn[:, c, :])
            P.wait("gpsimd", ost, ost.n)
    P.barrier()
    if stop_after == "qkv":
        l0es.close()
        return

    GSZ = 4
    with ExitStack() as ph:
        yacc = ph.enter_context(sb("yacc", [128, NDC, TPC], F32))
        gated = ph.enter_context(sb("gated", [128, GSZ, TPC], BF16))
        silu_sc = [ph.enter_context(sb(f"sil{i}", [128, 512], F32)) for i in range(2)]
        wsg = WStream(P, "wg", 2, (128, NDC, 128), es=ph)
        wsu = WStream(P, "wu", 2, (128, NDC, 128), es=ph)
        wsd = WStream(P, "wd", GSZ + 1, (128, NDC, 128), es=ph)
        wg_r = D["wg_t"].rearrange("(c p) m -> p c m", p=128)
        wu_r = D["wu_t"].rearrange("(c p) m -> p c m", p=128)
        wd_r = D["wd_t"].rearrange("(hc p) (c m) -> hc p c m", p=128, m=128)
        for grp in range(FH_C // GSZ):
            dts = []
            for hcl in range(GSZ):
                hc = grp * GSZ + hcl
                tg, wtg = wsg.load(wg_r[:, :, hc * 128:(hc + 1) * 128])
                tu, wtu = wsu.load(wu_r[:, :, hc * 128:(hc + 1) * 128])
                td, wtd = wsd.load(wd_r[hc])
                dts.append((td, wtd))
                for tb in range(NTB):
                    cs = slice(tb * 512, (tb + 1) * 512)
                    psg, psu = PS[tb], PS[4 + tb]
                    gate_wait("tensor", f"ps{tb}")
                    gate_wait("tensor", f"ps{4 + tb}")
                    wsg.begin("tensor", tg)
                    wsu.begin("tensor", tu)
                    for k in range(NDC):
                        P.op("tensor", "matmul", psg[:, :], wtg[:, k, :], xn[:, k, cs],
                             start=(k == 0), stop=(k == NDC - 1),
                             inc=(pe if k == NDC - 1 else None))
                    for k in range(NDC):
                        P.op("tensor", "matmul", psu[:, :], wtu[:, k, :], xn[:, k, cs],
                             start=(k == 0), stop=(k == NDC - 1),
                             inc=(pe if k == NDC - 1 else None))
                    v = pe.n
                    if tb == NTB - 1:
                        wsg.done(tg, pe, v - 1)
                        wsu.done(tu, pe, v)
                    P.wait("scalar", pe, v - 1)
                    slot = tb
                    gate_wait("scalar", f"sil{slot}")
                    a = P.op("scalar", "activation", silu_sc[slot][:, :], psg[:, :], AF.Silu,
                             inc=ac)
                    gate_set(f"ps{tb}", ac, a)
                    P.wait("vector", ac, a)
                    P.wait("vector", pe, v)
                    u = P.op("vector", "tensor_tensor", gated[:, hcl, cs], psu[:, :],
                             silu_sc[slot][:, :], ALU.mult, inc=dv)
                    gate_set(f"ps{4 + tb}", dv, u)
                    gate_set(f"sil{slot}", dv, u)
            gated_done = dv.n
            P.wait("tensor", dv, gated_done)
            for dc in range(NDC):
                for tb in range(NTB):
                    cs = slice(tb * 512, (tb + 1) * 512)
                    pb = 6 if (dc * NTB + tb) % 2 == 0 else 2
                    gate_wait("tensor", f"ps{pb}")
                    for hcl in range(GSZ):
                        td, wtd = dts[hcl]
                        wsd.begin("tensor", td)
                        v = P.op("tensor", "matmul", PS[pb][:, :], wtd[:, dc, :],
                                 gated[:, hcl, cs],
                                 start=(hcl == 0), stop=(hcl == GSZ - 1),
                                 inc=(pe if hcl == GSZ - 1 else None))
                    P.wait("vector", pe, v)
                    if grp == 0:
                        u = P.op("vector", "tensor_copy", out=yacc[:, dc, cs],
                                 in_=PS[pb][:, :], inc=dv)
                    else:
                        u = P.op("vector", "tensor_add", yacc[:, dc, cs], yacc[:, dc, cs],
                                 PS[pb][:, :], inc=dv)
                    gate_set(f"ps{pb}", dv, u)
            for hcl in range(GSZ):
                wsd.done(dts[hcl][0], dv, u)
        P.wait("scalar", dv, dv.n)
        for dc in range(NDC):
            a = P.op("scalar", "activation", g_sb[:, dc, :], yacc[:, dc, :], AF.Silu, inc=ac)
    P.barrier()
    if stop_after == "ffn":
        P.wait("sync", cc, ag_done)
        P.wait("gpsimd", ac, ac.n)
        for c in range(NDC):
            P.op("gpsimd", "dma_start", inc=ost, incv=16,
                 out=out_dram.rearrange("(c p) t -> c p t", p=128)[c], in_=g_sb[:, c, :])
        P.wait("gpsimd", ost, ost.n)
        l0es.close()
        return

    CAT2 = CAT_D + 1
    NSL = 24
    with ExitStack() as ph:
        qh = ph.enter_context(sb("qh", [CAT2, TPC], BF16))
        kslot = ph.enter_context(sb("kslot", [CAT2, NSL, 512], BF16))
        kown = ph.enter_context(sb("kown", [CAT2, NTB, 512], BF16))
        vslot = ph.enter_context(sb("vslot", [128, NSL * 4, VA], BF16))
        vown = ph.enter_context(sb("vown", [128, NTB * 4, VA], BF16))
        cm = ph.enter_context(sb("cm", [128, 8, NSL + 2], F32))
        mall = ph.enter_context(sb("mall", [128, 8], F32))
        negm = ph.enter_context(sb("negm", [1, 512], F32))
        negm_bf = ph.enter_context(sb("negm_bf", [1, 512], BF16))
        pT = [ph.enter_context(sb(f"pT{i}", [128, 512], BF16)) for i in range(4)]
        rbv = ph.enter_context(sb("rbv", [64, 512], F32))
        attn0 = ph.enter_context(sb("attn0", [128, 8, TPC], BF16))
        maskA_sb = ph.enter_context(sb("maskA_sb", [128, 4, 512], F32))
        maskB_sb = ph.enter_context(sb("maskB_sb", [128, 4, 512], F32))
        P.op("sync", "dma_start", inc=kvld, incv=16, out=maskA_sb[:, :, :], in_=D["maskA"])
        P.op("sync", "dma_start", inc=kvld, incv=16, out=maskB_sb[:, :, :], in_=D["maskB"])
        P.op("sync", "dma_start", inc=kvld, incv=16, out=kslot[97:98, :, :], in_=D["padk0"])
        P.op("sync", "dma_start", inc=kvld, incv=16, out=qh[97:98, :], in_=D["ones_row_tpc"])
        P.op("sync", "dma_start", inc=kvld, incv=16, out=kown[97:98, :, :], in_=D["zero_row_2x512"])
        P.wait("sync", cc, ag_done)
        P.wait("tensor", cc, ag_done)
        agk = D["ag0k_out"].rearrange("(r p) (h t) -> r p h t", p=CAT_D, t=TPC)
        agv = D["ag0v_out"].rearrange("(b p) f -> p b f", p=128)
        agk_own = D["ag0k_in"].rearrange("p (h t) -> p h t", t=TPC)
        agv_own = D["ag0v_in"].rearrange("(b p) f -> p b f", p=128)

        kvE = P.sem("kv0E")
        kvL = P.sem("kv0L")
        for h in range(H):
            gate_wait("sync", "heads")
            P.op("sync", "dma_start", inc=kvE, incv=16,
                 out=qh[0:CAT_D, :], in_=D["qcat_dram_r"][:, h, :])
            for s in range(NTB):
                P.op("sync", "dma_start", inc=kvE, incv=16,
                     out=kown[0:CAT_D, s, :], in_=agk_own[:, h, s * 512:(s + 1) * 512])
                P.op("sync", "dma_start", inc=kvE, incv=16,
                     out=vown[:, 4 * s:4 * s + 4, :],
                     in_=agv_own[:, 4 * s:4 * s + 4, h * VA:(h + 1) * VA])
            for j in range(NSL_E0):
                g = j
                r, s = BLOCK_OWNER[g]
                P.op("sync", "dma_start", inc=kvE, incv=16,
                     out=kslot[0:CAT_D, j, :], in_=agk[r, :, h, s * 512:(s + 1) * 512])
                P.op("sync", "dma_start", inc=kvE, incv=16,
                     out=vslot[:, 4 * j:4 * j + 4, :],
                     in_=agv[:, 8 * r + 4 * s:8 * r + 4 * s + 4, h * VA:(h + 1) * VA])
            totE = kvE.n
            for j in range(NSL_E0, NSL):
                g = j - NSL_E0
                r, s = BLOCK_OWNER[g]
                P.op("sync", "dma_start", inc=kvL, incv=16,
                     out=kslot[0:CAT_D, j, :], in_=agk[r, :, h, s * 512:(s + 1) * 512])
                P.op("sync", "dma_start", inc=kvL, incv=16,
                     out=vslot[:, 4 * j:4 * j + 4, :],
                     in_=agv[:, 8 * r + 4 * s:8 * r + 4 * s + 4, h * VA:(h + 1) * VA])
            totL = kvL.n

            P.wait("tensor", kvE, totE)
            for role in range(2):
                if role == 1:
                    P.wait("tensor", kvL, totL)
                for qtl in range(4):
                    qs = role * 512 + qtl * 128
                    nsl = NSL_E0 if role == 0 else NSL_L0
                    base = 0 if role == 0 else NSL_E0
                    for jj in range(nsl + 1):
                        pi = jj % 4
                        ps = PS[pi]
                        gate_wait("tensor", f"ps{pi}")
                        if jj < nsl:
                            rhs = kslot[:, base + jj, :]
                        else:
                            rhs = kown[:, role, :]
                        v = P.op("tensor", "matmul", ps[:, :], qh[:, qs:qs + 128], rhs,
                                 start=True, stop=True, inc=pe)
                        P.wait("vector", pe, v)
                        if jj == nsl:
                            P.op("vector", "tensor_add", ps[:, :], ps[:, :],
                                 maskA_sb[:, qtl, :])
                        u = P.op("vector", "reduce_max",
                                 out=cm[:, role * 4 + qtl, (base + jj if jj < nsl else NSL + role):(base + jj if jj < nsl else NSL + role) + 1],
                                 in_=ps[:, :], axis=AX.X, inc=dv)
                        gate_set(f"ps{pi}", dv, u)
            P.wait("vector", dv, dv.n)
            for qtl in range(4):
                for role in range(2):
                    qi = role * 4 + qtl
                    if role == 0:
                        P.op("vector", "reduce_max", out=mall[:, qi:qi + 1],
                             in_=cm[:, qi, 0:NSL_E0], axis=AX.X)
                        u = P.op("vector", "tensor_max", mall[:, qi:qi + 1], mall[:, qi:qi + 1],
                                 cm[:, qi, NSL:NSL + 1], inc=dv)
                    else:
                        P.op("vector", "reduce_max", out=mall[:, qi:qi + 1],
                             in_=cm[:, qi, NSL_E0:NSL_E0 + NSL_L0], axis=AX.X)
                        u = P.op("vector", "tensor_max", mall[:, qi:qi + 1], mall[:, qi:qi + 1],
                                 cm[:, qi, NSL + 1:NSL + 2], inc=dv)
            for role in range(2):
                P.wait("tensor", dv, dv.n)
                gate_wait("tensor", "ps4")
                for qtl in range(4):
                    v = P.op("tensor", "transpose", PS[4][0:1, qtl * 128:(qtl + 1) * 128],
                             mall[:, role * 4 + qtl:role * 4 + qtl + 1], id_f32[:, :],
                             inc=(pe if qtl == 3 else None))
                P.wait("vector", pe, v)
                gate_wait("vector", "negm")
                P.op("vector", "tensor_scalar", negm[0:1, :], PS[4][0:1, 0:512], -1.0, None, ALU.mult)
                u = P.op("vector", "tensor_copy", out=negm_bf[0:1, :], in_=negm[0:1, :], inc=dv)
                gate_set("ps4", dv, u)
                P.wait("sync", dv, u)
                v = P.op("sync", "dma_start", inc=kvld, incv=16,
                         out=qh[96:97, role * 512:(role + 1) * 512], in_=negm_bf[0:1, :])
                gate_set("negm", kvld, v)
                P.wait("tensor", kvld, v)
                nsl = NSL_E0 if role == 0 else NSL_L0
                base = 0 if role == 0 else NSL_E0
                qs = slice(role * 512, (role + 1) * 512)
                first = True
                gate_wait("tensor", "ps6")
                nkc = (nsl + 1) * 4
                for t_i in range(nkc):
                    jj, kc = divmod(t_i, 4)
                    own = jj == nsl
                    pi = t_i % 4
                    psb = PS[pi]
                    gate_wait("tensor", f"ps{pi}")
                    if own:
                        lhsT = kown[:, role, kc * 128:(kc + 1) * 128]
                    else:
                        lhsT = kslot[:, base + jj, kc * 128:(kc + 1) * 128]
                    v = P.op("tensor", "matmul", psb[:, :], lhsT, qh[:, qs],
                             start=True, stop=True, inc=pe)
                    if own:
                        P.wait("vector", pe, v)
                        v2 = P.op("vector", "tensor_add", psb[:, :], psb[:, :],
                                  maskB_sb[:, kc, :], inc=dv)
                        P.wait("scalar", dv, v2)
                    else:
                        P.wait("scalar", pe, v)
                    pt = pT[t_i % 4]
                    gate_wait("scalar", f"pT{t_i % 4}")
                    a = P.op("scalar", "activation", pt[:, :], psb[:, :], AF.Exp, inc=ac)
                    gate_set(f"ps{pi}", ac, a)
                    P.wait("tensor", ac, a)
                    vv = vown[:, 4 * role + kc, :] if own else vslot[:, 4 * (base + jj) + kc, :]
                    u = P.op("tensor", "matmul", PS[6][0:VA, :], vv, pt[:, :],
                             start=first, stop=(t_i == nkc - 1), inc=pe)
                    gate_set(f"pT{t_i % 4}", pe, u)
                    first = False
                P.wait("vector", pe, pe.n)
                gate_wait("vector", "negm")
                u = P.op("vector", "reciprocal", inc=dv, out=negm[0:1, :], in_=PS[6][64:65, :])
                P.wait("tensor", dv, u)
                gate_wait("tensor", "ps4")
                v = P.op("tensor", "matmul", PS[4][0:64, :], ones_row[0:1, 0:64], negm[0:1, :],
                         start=True, stop=True, inc=pe)
                P.wait("vector", pe, v)
                P.op("vector", "tensor_copy", out=rbv[:, :], in_=PS[4][0:64, :])
                u = P.op("vector", "tensor_tensor",
                         attn0[(h % 2) * 64:(h % 2) * 64 + 64, h // 2, role * 512:(role + 1) * 512],
                         PS[6][0:64, :], rbv[:, :], ALU.mult, inc=dv)
                gate_set("ps6", dv, u)
                gate_set("ps4", dv, u)
            gate_set("heads", dv, dv.n)
        if stop_after == "attn":
            P.wait("gpsimd", dv, dv.n)
            for c in range(8):
                P.op("gpsimd", "dma_start", inc=ost, incv=16,
                     out=out_dram.rearrange("(c p) t -> c p t", p=128)[c], in_=attn0[:, c, :])
            P.wait("gpsimd", ost, ost.n)
    P.barrier()
    if stop_after == "attn":
        l0es.close()
        return

    with ExitStack() as ph:
        xbuf = [ph.enter_context(sb(f"xc{i}", [128, TPC], F32)) for i in range(2)]
        x1b = [ph.enter_context(sb(f"x1b{i}", [128, TPC], F32)) for i in range(2)]
        tsc = ph.enter_context(sb("tsc", [128, 512], F32))
        st_toks = []
        ws = WStream(P, "wp", 2, (128, 8, 128), es=ph)
        wp = D["w_proj"].rearrange("(c p) m -> p c m", p=128)
        xr = D["x"].rearrange("(c p) t -> c p t", p=128)
        xs2 = XStream(P, "xc", "sync", 2)
        st2 = XStream(P, "x1s", "sync", 2)
        for dc in range(NDC):
            t, wt = ws.load(wp[:, :, dc * 128:(dc + 1) * 128])
            slot = dc % 2
            gate_wait("sync", f"xc{slot}")
            xtok = xs2.dma(slot, xbuf[slot][:], xr[dc])
            if dc >= 2:
                st2.wait("vector", st_toks[dc - 2])
            for tb in range(NTB):
                cs = slice(tb * 512, (tb + 1) * 512)
                ps = PS[tb]
                gate_wait("tensor", f"ps{tb}")
                ws.begin("tensor", t)
                for k in range(8):
                    v = P.op("tensor", "matmul", ps[:, :], wt[:, k, :], attn0[:, k, cs],
                             start=(k == 0), stop=(k == 7), inc=(pe if k == 7 else None))
                if tb == NTB - 1:
                    ws.done(t, pe, v)
                P.wait("vector", pe, v)
                xs2.wait("vector", xtok)
                P.op("vector", "tensor_tensor", tsc[:, :], ps[:, :], g_sb[:, dc, cs], ALU.mult)
                u = P.op("vector", "tensor_add", x1b[slot][:, cs], tsc[:, :], xbuf[slot][:, cs],
                         inc=dv)
                gate_set(f"ps{tb}", dv, u)
            gate_set(f"xc{slot}", dv, u)
            P.wait("sync", dv, u)
            st_toks.append(st2.dma(slot, out_dram.rearrange("(c p) t -> c p t", p=128)[dc],
                                   x1b[slot][:]))
        for tok in st_toks[-2:]:
            st2.wait("sync", tok)
    l0es.close()
    P.barrier()



VI_L0 = {"qg": 0, "kvg": 6}
NV_L0 = 8
VI_L1 = {"bq": 0, "bk": 16, "bv": 32, "bo": 48, "lng": 64, "lnb": 80,
         "sgb": 96, "sub": 118, "sdb": 140}
NV_L1 = 156


def emit_shared(P, nc, D, es, layers):
    sb = nc.sbuf_tensor
    shared = {}
    shared["PS"] = [es.enter_context(nc.psum_tensor(f"ps{i}", [128, 512], F32))
                    for i in range(7)]
    shared["psT"] = es.enter_context(nc.psum_tensor("psT", [128, 1024], BF16))
    ld = P.sem("cld")
    dv = P.sem("dv")
    pe = P.sem("pe")
    ones_col = es.enter_context(sb("ones_col", [128, 1], F32))
    ones_row = es.enter_context(sb("ones_row", [1, 128], F32))
    id_f32 = es.enter_context(sb("id_f32s", [128, 128], F32))
    id_bf = es.enter_context(sb("id_bfs", [128, 128], BF16))
    ones_col_bf = es.enter_context(sb("ones_col_bf", [128, 1], BF16))
    P.op("vector", "memset", ones_col[:, :], 1.0)
    P.op("vector", "memset", ones_col_bf[:, :], 1.0)
    P.op("vector", "memset", ones_row[:, :], 1.0, inc=dv)
    shared["ones_col_bf"] = ones_col_bf
    gld = P.sem("gld")
    P.op("sync", "dma_start", inc=ld, incv=16, out=id_f32[:, :], in_=D["id_f32"])
    P.op("gpsimd", "dma_start", inc=gld, incv=16, out=id_bf[:, :], in_=D["id_f32"])
    shared.update(ones_col=ones_col, ones_row=ones_row, id_f32=id_f32, id_bf=id_bf)
    nv = (NV_L0 if 0 in layers else 0) + (NV_L1 if 1 in layers else 0)
    vecs = es.enter_context(sb("vecs", [128, nv], F32))
    row_es = ExitStack()
    rowbuf = row_es.enter_context(sb("rowbuf", [1, 4096], F32))
    vi = {}
    off = 0
    specs = []
    if 0 in layers:
        specs += [("q_g", "qg", QR_C), ("kv_g", "kvg", KV_C)]
    if 1 in layers:
        specs += [("bq1", "bq", 16), ("bk1", "bk", 16), ("bv1", "bv", 16), ("bo1", "bo", 16),
                  ("lng1", "lng", 16), ("lnb1", "lnb", 16), ("sgb1", "sgb", SI_C),
                  ("sub1", "sub", SI_C), ("sdb1", "sdb", 16)]
    P.wait("vector", ld, ld.n)
    P.wait("vector", gld, gld.n)
    P.wait("tensor", dv, dv.n)
    rld = P.sem("rld")
    for dname, key, ncol in specs:
        vi[key] = off
        P.wait("sync", dv, dv.n)
        v = P.op("sync", "dma_start", inc=rld, incv=16,
                 out=rowbuf[0:1, 0:ncol * 128], in_=D[dname])
        P.wait("tensor", rld, v)
        for c in range(ncol):
            pv = P.op("tensor", "matmul", shared["PS"][0][:, c % 4:c % 4 + 1],
                      rowbuf[0:1, c * 128:(c + 1) * 128], ones_row[0:1, 0:1],
                      start=True, stop=True, inc=pe)
            P.wait("vector", pe, pv)
            u = P.op("vector", "tensor_copy", out=vecs[:, off + c:off + c + 1],
                     in_=shared["PS"][0][:, c % 4:c % 4 + 1], inc=dv)
            P.wait("tensor", dv, u)
        off += ncol
    shared["vecs"] = vecs
    shared["vi"] = vi
    P.barrier()
    row_es.close()
    return shared


def declare_dram_l0(nc, with_out=True):
    D = {}

    def inp(name, shape, dt=F32):
        D[name] = nc.dram_tensor(name, list(shape), dt, kind="ExternalInput").ap()

    inp("x", (DM, TPC))
    inp("w_cq", (DM, Q_RANK))
    inp("w_q_nope", (Q_RANK, 1024))
    inp("w_q_rope", (Q_RANK, 512))
    inp("q_g", (Q_RANK,))
    inp("w_ckv", (DM, KV_RANK))
    inp("w_k_nope", (KV_RANK, 1024))
    inp("w_v", (KV_RANK, 1024))
    inp("kv_g", (KV_RANK,))
    inp("w_k_rope", (DM, 32))
    inp("w_proj", (1024, DM))
    inp("wg_t", (DM, FH))
    inp("wu_t", (DM, FH))
    inp("wd_t", (FH, DM))
    inp("id_f32", (128, 128))
    inp("R4T", (128, 128), BF16)
    inp("R32T", (32, 32), BF16)
    inp("cos0q", (128, TPC))
    inp("cos0k", (32, TPC))
    inp("maskA_h", (4, 128, 512))
    inp("maskB_h", (4, 128, 512))
    inp("padk0", (1, NSL_E0 + NSL_L0, 512), BF16)
    inp("ones_row_tpc", (1, TPC), BF16)
    inp("ones_row_2x512", (1, 2, 512), BF16)
    inp("zero_row_2x512", (1, 2, 512), BF16)
    D["maskA"] = D["maskA_h"].rearrange("d p f -> p d f")
    D["maskB"] = D["maskB_h"].rearrange("d p f -> p d f")
    D["qcat_dram_t"] = nc.dram_tensor("qcat_dram", [CAT_D, H * TPC], BF16)
    D["qcat_dram"] = D["qcat_dram_t"].ap().rearrange("p (h t) -> p h t", t=TPC)
    D["qcat_dram_r"] = D["qcat_dram"]
    a = nc.dram_tensor("ag0k_in", [CAT_D, H * TPC], BF16).ap()
    D["ag0k_in"] = a
    D["ag0k_in_r"] = a.rearrange("p (h t) -> p h t", t=TPC)
    D["ag0k_out"] = nc.dram_tensor("ag0k_out", [NCORE * CAT_D, H * TPC], BF16).ap()
    av = nc.dram_tensor("ag0v_in", [TPC, H * VA], BF16).ap()
    D["ag0v_in"] = av
    D["ag0v_in_r"] = [av.rearrange("(b p) f -> b p f", p=128)[j] for j in range(8)]
    D["ag0v_out"] = nc.dram_tensor("ag0v_out", [SEQ, H * VA], BF16).ap()
    if with_out:
        D["x1"] = nc.dram_tensor("x1", [DM, TPC], F32, kind="ExternalOutput").ap()
    return D


def host_stage_l0(core, inputs, cst):
    x = np.asarray(inputs["L0_attn_x"])[0]
    blocks = CORE_BLOCKS[core]
    xs = np.concatenate([x[g * 512:(g + 1) * 512] for g in blocks], 0)
    m = {
        "x": np.ascontiguousarray(xs.T),
        "w_cq": inputs["L0_attn_w_cq"],
        "w_q_nope": inputs["L0_attn_w_q_nope"],
        "w_q_rope": inputs["L0_attn_w_q_rope"],
        "q_g": inputs["L0_attn_q_g"],
        "w_ckv": inputs["L0_attn_w_ckv"],
        "w_k_nope": inputs["L0_attn_w_k_nope"],
        "w_v": inputs["L0_attn_w_v"],
        "kv_g": inputs["L0_attn_kv_g"],
        "w_k_rope": inputs["L0_attn_w_k_rope"],
        "w_proj": inputs["L0_attn_w_proj"],
        "wg_t": np.ascontiguousarray(np.asarray(inputs["L0_ffn_w_gate"]).T),
        "wu_t": np.ascontiguousarray(np.asarray(inputs["L0_ffn_w_up"]).T),
        "wd_t": np.ascontiguousarray(np.asarray(inputs["L0_ffn_w_down"]).T),
        "id_f32": cst["id_f32"],
        "R4T": cst["R4T"],
        "R32T": cst["R32T"],
        "cos0q": cst["cos0q"],
        "cos0k": cst["cos0k"],
        "maskA_h": cst["maskA"],
        "maskB_h": cst["maskB"],
        "ones_row_tpc": np.ones((1, TPC), bfloat16),
        "ones_row_2x512": np.ones((1, 2, 512), bfloat16),
        "zero_row_2x512": np.zeros((1, 2, 512), bfloat16),
    }
    c = core
    pad = np.zeros((NSL_E0 + NSL_L0, 512), np.float32)
    for j in range(NSL_E0):
        if j >= blocks[0]:
            pad[j] = NEG
    for j in range(NSL_L0):
        if j >= blocks[1]:
            pad[NSL_E0 + j] = NEG
    m["padk0"] = pad[None].astype(bfloat16)
    m = {k: np.ascontiguousarray(np.asarray(v, dtype=(bfloat16 if k in
         ("R4T", "R32T", "padk0", "ones_row_tpc", "ones_row_2x512", "zero_row_2x512")
         else np.float32)))
         for k, v in m.items()}
    return m


def build_l0_program(stop_after=None):
    nc = bass.Bass("TRN2", target_bir_lowering=False, debug=False)
    es = ExitStack()
    D = declare_dram_l0(nc)
    P = Prog(nc, es)
    shared = emit_shared(P, nc, D, es, layers=[0])
    if stop_after == "shared":
        ost = P.sem("ost")
        for c in range(NDC):
            P.op("gpsimd", "dma_start", inc=ost, incv=16,
                 out=D["x1"].rearrange("(c p) t -> c p t", p=128)[c][:, 0:1],
                 in_=shared["vecs"][:, 0:1], allow_slow_non_contiguous=True)
        P.wait("gpsimd", ost, ost.n)
    else:
        emit_l0(P, nc, D, shared, D["x1"], stop_after=stop_after)
    P.finish()
    es.close()
    return nc



def emit_l1(P, nc, D, shared, out_dram, stop_after=None):
    es = P.es
    pe = P.sem("pe")
    dv = P.sem("dv")
    ac = P.sem("ac")
    kvld = P.sem("kvld1")
    agst = P.sem("agst1")
    cc = P.sem("cc")
    ost = P.sem("ost1")
    sb = P.nc.sbuf_tensor
    ones_col = shared["ones_col"]
    ones_row = shared["ones_row"]
    id_f32 = shared["id_f32"]
    id_bf = shared["id_bf"]
    PS = shared["PS"]
    psT = shared["psT"]
    V = shared["vecs"]
    vi = shared["vi"]
    gates = {}

    def gate_wait(eng, res):
        if res in gates:
            s, v = gates[res]
            P.wait(eng, s, v)

    def gate_set(res, sem, val):
        gates[res] = (sem, val)

    def vcol(key, i):
        return V[:, vi[key] + i:vi[key] + i + 1]

    l1es = ExitStack()
    xn = l1es.enter_context(sb("xn1", [128, NDC, TPC], BF16))
    rows = l1es.enter_context(sb("rows1", [1, 5 * TPC], F32))
    nm_row = rows[:, 0:TPC]
    rstd_row = rows[:, TPC:2 * TPC]
    row_sc = rows[:, 2 * TPC:3 * TPC]
    row_sc2 = rows[:, 3 * TPC:4 * TPC]
    row_s3 = rows[:, 4 * TPC:5 * TPC]
    xr = D["x"].rearrange("(c p) t -> c p t", p=128)

    with ExitStack() as ph:
        xbuf = [ph.enter_context(sb(f"xb1{i}", [128, TPC], F32)) for i in range(2)]
        sqt = [ph.enter_context(sb(f"sq1{i}", [128, 512], F32)) for i in range(2)]
        nm_b = ph.enter_context(sb("nm1_b", [128, TPC], F32))
        rstd_b = ph.enter_context(sb("rstd1_b", [128, TPC], F32))
        xsc = ph.enter_context(sb("xsc1", [128, TPC], F32))
        xs = XStream(P, "x1ld", "sync", 2)
        for c in range(NDC):
            slot = c % 2
            gate_wait("sync", f"xb{slot}")
            tok = xs.dma(slot, xbuf[slot][:], xr[c])
            xs.wait("scalar", tok)
            for tb in range(NTB):
                gate_wait("scalar", f"sq{slot}")
                a = P.op("scalar", "activation", sqt[slot][:, :],
                         xbuf[slot][:, tb * 512:(tb + 1) * 512], AF.Square, inc=ac)
                P.wait("tensor", ac, a)
                P.op("tensor", "matmul", PS[3 + tb][0:1, :], ones_col[:, :],
                     xbuf[slot][:, tb * 512:(tb + 1) * 512],
                     start=(c == 0), stop=(c == NDC - 1))
                u = P.op("tensor", "matmul", PS[5 + tb][0:1, :], ones_col[:, :], sqt[slot][:, :],
                         start=(c == 0), stop=(c == NDC - 1), inc=pe)
                gate_set(f"sq{slot}", pe, u)
            gate_set(f"xb{slot}", pe, u)
        P.wait("vector", pe, pe.n)
        for tb in range(NTB):
            cs = slice(tb * 512, (tb + 1) * 512)
            P.op("vector", "tensor_scalar", nm_row[:, cs], PS[3 + tb][0:1, :], -1.0 / DM, None, ALU.mult)
            P.op("vector", "tensor_scalar", row_sc[:, cs], PS[5 + tb][0:1, :], 1.0 / DM, EPS_LN,
                 ALU.mult, ALU.add)
            P.op("vector", "tensor_mul", row_sc2[:, cs], nm_row[:, cs], nm_row[:, cs])
            v = P.op("vector", "tensor_sub", row_sc[:, cs], row_sc[:, cs], row_sc2[:, cs], inc=dv)
        P.wait("scalar", dv, dv.n)
        P.op("scalar", "activation", row_sc2[:, 0:TPC], row_sc[:, 0:TPC], AF.Ln)
        a = P.op("scalar", "activation", rstd_row[:, 0:TPC], row_sc2[:, 0:TPC], AF.Exp,
                 scale=-0.5, inc=ac)
        P.wait("tensor", ac, a)
        for tb in range(NTB):
            cs = slice(tb * 512, (tb + 1) * 512)
            gate_wait("tensor", "ps0")
            gate_wait("tensor", "ps1")
            P.op("tensor", "matmul", PS[0][:, :], ones_row[0:1, :], nm_row[:, cs],
                 start=True, stop=True)
            v = P.op("tensor", "matmul", PS[1][:, :], ones_row[0:1, :], rstd_row[:, cs],
                     start=True, stop=True, inc=pe)
            P.wait("vector", pe, v)
            u0 = P.op("vector", "tensor_copy", out=nm_b[:, cs], in_=PS[0][:, :], inc=dv)
            u = P.op("vector", "tensor_copy", out=rstd_b[:, cs], in_=PS[1][:, :], inc=dv)
            gate_set("ps0", dv, u0)
            gate_set("ps1", dv, u)
        for c in range(NDC):
            slot = c % 2
            gate_wait("sync", f"xb{slot}")
            tok = xs.dma(slot, xbuf[slot][:], xr[c])
            xs.wait("vector", tok)
            P.op("vector", "tensor_add", xsc[:, :], xbuf[slot][:, :], nm_b[:, :])
            u = P.op("vector", "tensor_mul", xn[:, c, :], xsc[:, :], rstd_b[:, :], inc=dv)
            gate_set(f"xb{slot}", dv, u)
    P.barrier()
    if stop_after == "ln1":
        P.wait("gpsimd", dv, dv.n)
        for c in range(NDC):
            P.op("gpsimd", "dma_start", inc=ost, incv=16,
                 out=out_dram.rearrange("(c p) t -> c p t", p=128)[c], in_=xn[:, c, :])
        P.wait("gpsimd", ost, ost.n)
        l1es.close()
        return

    with ExitStack() as ph:
        xa = ph.enter_context(sb("xa1", [128, NDC, TPC], BF16))
        q1 = ph.enter_context(sb("q1", [128, NDC, TPC], BF16))
        k1 = ph.enter_context(sb("k1b", [128, NDC, 512], BF16))
        v1 = ph.enter_context(sb("v1b", [128, 4, DM], BF16))
        sqt = [ph.enter_context(sb(f"sqa{i}", [128, 512], F32)) for i in range(2)]
        nm2_b = ph.enter_context(sb("nm2_b", [128, TPC], F32))
        rstd2_b = ph.enter_context(sb("rstd2_b", [128, TPC], F32))
        xsc2 = ph.enter_context(sb("xsc2", [128, TPC], F32))
        rsc = [ph.enter_context(sb(f"rsc1{i}", [128, 512], BF16)) for i in range(2)]
        cst = {}
        cld = P.sem("cld")
        for cname in ("RneoxT", "cos1", "sin1", "cos1q", "sin1q"):
            shp = [128, 128] if cname == "RneoxT" else [128, TPC]
            cdt = BF16 if cname == "RneoxT" else F32
            tt = ph.enter_context(sb(cname + "_s", shp, cdt))
            P.op("sync", "dma_start", inc=cld, incv=16, out=tt[:, :], in_=D[cname])
            cst[cname] = tt
        P.wait("tensor", cld, cld.n)
        P.wait("vector", cld, cld.n)
        for c in range(NDC):
            for tb in range(NTB):
                gate_wait("scalar", f"sqa{c % 2}")
                a = P.op("scalar", "activation", sqt[c % 2][:, :],
                         xn[:, c, tb * 512:(tb + 1) * 512], AF.Square, inc=ac)
                P.wait("tensor", ac, a)
                P.op("tensor", "matmul", PS[3 + tb][0:1, :], shared["ones_col_bf"][:, :],
                     xn[:, c, tb * 512:(tb + 1) * 512],
                     start=(c == 0), stop=(c == NDC - 1))
                u = P.op("tensor", "matmul", PS[5 + tb][0:1, :], ones_col[:, :],
                         sqt[c % 2][:, :],
                         start=(c == 0), stop=(c == NDC - 1), inc=pe)
                gate_set(f"sqa{c % 2}", pe, u)
        P.wait("vector", pe, pe.n)
        for tb in range(NTB):
            cs = slice(tb * 512, (tb + 1) * 512)
            P.op("vector", "tensor_scalar", row_s3[:, cs], PS[3 + tb][0:1, :], -1.0 / DM, None, ALU.mult)
            P.op("vector", "tensor_scalar", row_sc[:, cs], PS[5 + tb][0:1, :], 1.0 / DM, EPS_LN,
                 ALU.mult, ALU.add)
            P.op("vector", "tensor_mul", row_sc2[:, cs], row_s3[:, cs], row_s3[:, cs])
            v = P.op("vector", "tensor_sub", row_sc[:, cs], row_sc[:, cs], row_sc2[:, cs], inc=dv)
        P.wait("scalar", dv, dv.n)
        P.op("scalar", "activation", row_sc2[:, 0:TPC], row_sc[:, 0:TPC], AF.Ln)
        a = P.op("scalar", "activation", row_sc[:, 0:TPC], row_sc2[:, 0:TPC], AF.Exp,
                 scale=-0.5, inc=ac)
        P.wait("tensor", ac, a)
        for tb in range(NTB):
            cs = slice(tb * 512, (tb + 1) * 512)
            gate_wait("tensor", "ps0")
            gate_wait("tensor", "ps1")
            P.op("tensor", "matmul", PS[0][:, :], ones_row[0:1, :], row_s3[:, cs],
                 start=True, stop=True)
            v = P.op("tensor", "matmul", PS[1][:, :], ones_row[0:1, :], row_sc[:, cs],
                     start=True, stop=True, inc=pe)
            P.wait("vector", pe, v)
            u0 = P.op("vector", "tensor_copy", out=nm2_b[:, cs], in_=PS[0][:, :], inc=dv)
            u = P.op("vector", "tensor_copy", out=rstd2_b[:, cs], in_=PS[1][:, :], inc=dv)
            gate_set("ps0", dv, u0)
            gate_set("ps1", dv, u)
        for c in range(NDC):
            P.op("vector", "tensor_add", xsc2[:, :], xn[:, c, :], nm2_b[:, :])
            P.op("vector", "tensor_mul", xsc2[:, :], xsc2[:, :], rstd2_b[:, :])
            u = P.op("vector", "tensor_scalar", xa[:, c, :], xsc2[:, :],
                     vcol("lng", c), vcol("lnb", c), ALU.mult, ALU.add, inc=dv)
        xa_done = dv.n
        ws = WStream(P, "w1", 2, (128, NDC, 128), es=ph)
        agk1 = D["ag1k_in"].rearrange("(c p) t -> c p t", p=128)
        agv1 = D["ag1v_in"].rearrange("(b p) f -> b p f", p=128)
        wq = D["wq_t"].rearrange("(c p) m -> p c m", p=128)
        wk = D["wk_t"].rearrange("(c p) m -> p c m", p=128)
        wv = D["wv_t"].rearrange("(c p) m -> p c m", p=128)
        P.wait("tensor", dv, xa_done)
        for tb in range(NTB):
            cs = slice(tb * 512, (tb + 1) * 512)
            for hc in range(NDC):
                t, wt = ws.load(wq[:, :, hc * 128:(hc + 1) * 128])
                ps = PS[hc % 2]
                gate_wait("tensor", f"ps{hc % 2}")
                ws.begin("tensor", t)
                for kk in range(NDC):
                    v = P.op("tensor", "matmul", ps[:, :], wt[:, kk, :], xa[:, kk, cs],
                             start=(kk == 0), stop=(kk == NDC - 1),
                             inc=(pe if kk == NDC - 1 else None))
                ws.done(t, pe, v)
                P.wait("vector", pe, v)
                slot = hc % 2
                gate_wait("vector", f"rsc1{slot}")
                a = P.op("vector", "tensor_scalar", rsc[slot][:, :], ps[:, :],
                         vcol("bq", hc), None, ALU.add, inc=ac)
                P.wait("tensor", ac, a)
                gate_wait("tensor", "ps4")
                v = P.op("tensor", "matmul", PS[4][:, :], cst["RneoxT"][:, :], rsc[slot][:, :],
                         start=True, stop=True, inc=pe)
                P.wait("vector", pe, v)
                P.wait("vector", ac, a)
                P.op("vector", "tensor_tensor", xsc2[:, 0:512], PS[4][:, :],
                     cst["sin1q"][:, cs], ALU.mult)
                gate_set("ps4", P.last["vector"][0], P.last["vector"][1])
                P.op("vector", "tensor_tensor", xsc2[:, 512:1024], rsc[slot][:, :],
                     cst["cos1q"][:, cs], ALU.mult)
                u = P.op("vector", "tensor_add", q1[:, hc, cs], xsc2[:, 0:512],
                         xsc2[:, 512:1024], inc=dv)
                gate_set(f"ps{hc % 2}", ac, a)
                gate_set(f"rsc1{slot}", dv, u)
                t, wt = ws.load(wk[:, :, hc * 128:(hc + 1) * 128])
                ps = PS[2 + hc % 2]
                gate_wait("tensor", f"ps{2 + hc % 2}")
                ws.begin("tensor", t)
                for kk in range(NDC):
                    v = P.op("tensor", "matmul", ps[:, :], wt[:, kk, :], xa[:, kk, cs],
                             start=(kk == 0), stop=(kk == NDC - 1),
                             inc=(pe if kk == NDC - 1 else None))
                ws.done(t, pe, v)
                P.wait("vector", pe, v)
                slot = hc % 2
                gate_wait("vector", f"rsc1{slot}")
                a = P.op("vector", "tensor_scalar", rsc[slot][:, :], ps[:, :],
                         vcol("bk", hc), None, ALU.add, inc=ac)
                P.wait("tensor", ac, a)
                gate_wait("tensor", "ps4")
                v = P.op("tensor", "matmul", PS[4][:, :], cst["RneoxT"][:, :], rsc[slot][:, :],
                         start=True, stop=True, inc=pe)
                P.wait("vector", pe, v)
                P.wait("vector", ac, a)
                P.op("vector", "tensor_tensor", xsc2[:, 0:512], PS[4][:, :],
                     cst["sin1"][:, cs], ALU.mult)
                gate_set("ps4", P.last["vector"][0], P.last["vector"][1])
                P.op("vector", "tensor_tensor", xsc2[:, 512:1024], rsc[slot][:, :],
                     cst["cos1"][:, cs], ALU.mult)
                gate_wait("vector", "k1buf")
                u = P.op("vector", "tensor_add", k1[:, hc, :], xsc2[:, 0:512],
                         xsc2[:, 512:1024], inc=dv)
                gate_set(f"ps{2 + hc % 2}", ac, a)
                gate_set(f"rsc1{slot}", dv, u)
                t, wt = ws.load(wv[:, :, hc * 128:(hc + 1) * 128])
                ps = PS[hc % 2]
                gate_wait("tensor", f"ps{hc % 2}")
                ws.begin("tensor", t)
                for kk in range(NDC):
                    v = P.op("tensor", "matmul", ps[:, :], wt[:, kk, :], xa[:, kk, cs],
                             start=(kk == 0), stop=(kk == NDC - 1),
                             inc=(pe if kk == NDC - 1 else None))
                ws.done(t, pe, v)
                P.wait("vector", pe, v)
                slot = hc % 2
                gate_wait("vector", f"rsc1{slot}")
                u = P.op("vector", "tensor_scalar", rsc[slot][:, :], ps[:, :],
                         vcol("bv", hc), None, ALU.add, inc=dv)
                gate_set(f"ps{hc % 2}", dv, u)
                P.wait("tensor", dv, u)
                gate_wait("tensor", "psT")
                for j in range(4):
                    v = P.op("tensor", "transpose", psT[:, j * 128:(j + 1) * 128],
                             rsc[slot][:, j * 128:(j + 1) * 128], id_bf[:, :],
                             inc=(pe if j == 3 else None))
                P.wait("vector", pe, v)
                gate_wait("vector", "k1buf")
                for j in range(4):
                    u = P.op("vector", "tensor_copy",
                             out=v1[:, j, hc * 128:(hc + 1) * 128],
                             in_=psT[:, j * 128:(j + 1) * 128], inc=dv)
                gate_set("psT", dv, u)
                gate_set(f"rsc1{slot}", dv, u)
            P.wait("scalar", dv, dv.n)
            for c in range(NDC):
                P.op("scalar", "dma_start", inc=agst, incv=16,
                     out=agk1[c][:, tb * 512:(tb + 1) * 512], in_=k1[:, c, :])
            for j in range(4):
                P.op("scalar", "dma_start", inc=agst, incv=16,
                     out=agv1[tb * 4 + j], in_=v1[:, j, :])
            gate_set("k1buf", agst, agst.n)
        P.wait("scalar", dv, dv.n)
        P.op("scalar", "dma_start", inc=agst, incv=16,
             out=D["q1_dram"].rearrange("(c p) t -> p c t", p=128), in_=q1[:, :, :])
        P.wait("gpsimd", agst, agst.n)
        P.op("gpsimd", "collective_compute", "AllGather", ALU.bypass,
             replica_groups=[list(range(NCORE))],
             ins=[D["ag1k_in"].opt()], outs=[D["ag1k_out"].opt()], inc=cc)
        P.op("gpsimd", "collective_compute", "AllGather", ALU.bypass,
             replica_groups=[list(range(NCORE))],
             ins=[D["ag1v_in"].opt()], outs=[D["ag1v_out"].opt()], inc=cc)
        ag_done = cc.n
    P.barrier()

    GSZM = 4
    g_sb = l1es.enter_context(sb("g1", [128, NDC, TPC], BF16))
    with ExitStack() as ph:
        yacc = ph.enter_context(sb("yacc1", [128, NDC, TPC], F32))
        gated = ph.enter_context(sb("gated1", [128, GSZM, TPC], BF16))
        silu_sc = [ph.enter_context(sb(f"sil1{i}", [128, 512], F32)) for i in range(2)]
        cT = ph.enter_context(sb("cT", [8, TPC], F32))
        msel = ph.enter_context(sb("msel", [8, 8 * 128], F32))
        r_es = ExitStack()
        lgT = r_es.enter_context(sb("lgT", [8, TPC], F32))
        lgt = r_es.enter_context(sb("lgt", [128, 8 * 8], F32))
        comb = r_es.enter_context(sb("comb", [128, 8 * 8], F32))
        stat = r_es.enter_context(sb("stat1", [128, 8], F32))
        wsr = WStream(P, "wr", 1, (128, NDC, 8), es=r_es)
        cld = P.sem("cld")
        P.op("sync", "dma_start", inc=cld, incv=16, out=msel[:, :], in_=D["moe_sel"])
        rt = D["router_t"].rearrange("(c p) m -> p c m", p=128)
        t, wt = wsr.load(rt[:, :, :], out_ap_fn=lambda o: o[:, :, 0:8])
        for tb in range(NTB):
            ps = PS[2]
            gate_wait("tensor", "ps2")
            wsr.begin("tensor", t)
            for kk in range(NDC):
                v = P.op("tensor", "matmul", ps[0:8, :], wt[:, kk, 0:8],
                         xn[:, kk, tb * 512:(tb + 1) * 512],
                         start=(kk == 0), stop=(kk == NDC - 1),
                         inc=(pe if kk == NDC - 1 else None))
            P.wait("vector", pe, v)
            u = P.op("vector", "tensor_copy", out=lgT[:, tb * 512:(tb + 1) * 512],
                     in_=ps[0:8, :], inc=dv)
            gate_set("ps2", dv, u)
        P.wait("tensor", dv, dv.n)
        P.wait("tensor", cld, cld.n)
        for j in range(8):
            gate_wait("tensor", "ps4")
            v = P.op("tensor", "transpose", PS[4][0:128, 0:8],
                     lgT[:, j * 128:(j + 1) * 128], id_f32[0:8, 0:8], inc=pe)
            P.wait("vector", pe, v)
            u = P.op("vector", "tensor_copy", out=lgt[:, j * 8:(j + 1) * 8],
                     in_=PS[4][0:128, 0:8], inc=dv)
            gate_set("ps4", dv, u)
        for j in range(8):
            tl = lgt[:, j * 8:(j + 1) * 8]
            P.op("vector", "reduce_max", out=stat[:, 0:1], in_=tl, axis=AX.X)
            P.op("vector", "tensor_scalar", comb[:, j * 8:(j + 1) * 8], tl, stat[:, 0:1],
                 None, ALU.is_equal)
            P.op("vector", "scalar_tensor_tensor", comb[:, j * 8:(j + 1) * 8],
                 comb[:, j * 8:(j + 1) * 8], NEG, tl, ALU.mult, ALU.add)
            P.op("vector", "reduce_max", out=stat[:, 1:2], in_=comb[:, j * 8:(j + 1) * 8],
                 axis=AX.X)
            P.op("vector", "tensor_add", stat[:, 2:3], stat[:, 0:1], stat[:, 1:2])
            P.op("vector", "tensor_scalar", stat[:, 2:3], stat[:, 2:3], 0.5, None, ALU.mult)
            v = P.op("vector", "tensor_scalar", comb[:, j * 8:(j + 1) * 8], tl, stat[:, 2:3],
                     None, ALU.subtract, inc=dv)
            P.wait("scalar", dv, v)
            a = P.op("scalar", "activation", comb[:, j * 8:(j + 1) * 8],
                     comb[:, j * 8:(j + 1) * 8], AF.Sigmoid, scale=2.0, inc=ac)
            P.wait("vector", ac, a)
            P.op("vector", "tensor_scalar", lgt[:, j * 8:(j + 1) * 8], tl, stat[:, 1:2],
                 None, ALU.is_ge)
            u = P.op("vector", "tensor_mul", comb[:, j * 8:(j + 1) * 8],
                     comb[:, j * 8:(j + 1) * 8], lgt[:, j * 8:(j + 1) * 8], inc=dv)
        P.wait("tensor", dv, dv.n)
        for j in range(8):
            gate_wait("tensor", "ps4")
            v = P.op("tensor", "transpose", PS[4][0:8, (j % 4) * 128:(j % 4 + 1) * 128],
                     comb[:, j * 8:(j + 1) * 8], id_f32[:, :], inc=pe)
            P.wait("vector", pe, v)
            u = P.op("vector", "tensor_copy", out=cT[:, j * 128:(j + 1) * 128],
                     in_=PS[4][0:8, (j % 4) * 128:(j % 4 + 1) * 128], inc=dv)
            gate_set("ps4", dv, u)
        P.barrier()
        r_es.close()
        wsg = WStream(P, "mg", 2, (128, NDC, 128), es=ph)
        wsu = WStream(P, "mu", 2, (128, NDC, 128), es=ph)
        wsd = WStream(P, "md", GSZM + 1, (128, NDC, 128), es=ph)

        def ffn_branch(eidx, gw_ap, uw_ap, dw_ap, nhc, gbias_key=None, ubias_key=None,
                       first_overall=False, ce=None):
            ngrp = (nhc + GSZM - 1) // GSZM
            for grp in range(ngrp):
                hcs = list(range(grp * GSZM, min((grp + 1) * GSZM, nhc)))
                dts = []
                for gi, hc in enumerate(hcs):
                    tg, wtg = wsg.load(gw_ap[:, :, hc * 128:(hc + 1) * 128])
                    tu, wtu = wsu.load(uw_ap[:, :, hc * 128:(hc + 1) * 128])
                    td, wtd = wsd.load(dw_ap[hc])
                    dts.append((td, wtd))
                    for tb in range(NTB):
                        cs = slice(tb * 512, (tb + 1) * 512)
                        psg, psu = PS[tb], PS[2 + tb]
                        gate_wait("tensor", f"ps{tb}")
                        gate_wait("tensor", f"ps{2 + tb}")
                        wsg.begin("tensor", tg)
                        wsu.begin("tensor", tu)
                        for kk in range(NDC):
                            P.op("tensor", "matmul", psg[:, :], wtg[:, kk, :], xn[:, kk, cs],
                                 start=(kk == 0), stop=(kk == NDC - 1),
                                 inc=(pe if kk == NDC - 1 else None))
                        for kk in range(NDC):
                            P.op("tensor", "matmul", psu[:, :], wtu[:, kk, :], xn[:, kk, cs],
                                 start=(kk == 0), stop=(kk == NDC - 1),
                                 inc=(pe if kk == NDC - 1 else None))
                        v = pe.n
                        if tb == NTB - 1:
                            wsg.done(tg, pe, v - 1)
                            wsu.done(tu, pe, v)
                        P.wait("scalar", pe, v - 1)
                        slot = tb
                        gate_wait("scalar", f"sil1{slot}")
                        if gbias_key is not None:
                            a = P.op("scalar", "activation", silu_sc[slot][:, :], psg[:, :],
                                     AF.Silu, bias=vcol(gbias_key, hc), inc=ac)
                        else:
                            a = P.op("scalar", "activation", silu_sc[slot][:, :], psg[:, :],
                                     AF.Silu, inc=ac)
                        gate_set(f"ps{tb}", ac, a)
                        P.wait("vector", ac, a)
                        P.wait("vector", pe, v)
                        if ubias_key is not None:
                            P.op("vector", "scalar_tensor_tensor", silu_sc[slot][:, :],
                                 psu[:, :], vcol(ubias_key, hc), silu_sc[slot][:, :],
                                 ALU.add, ALU.mult)
                            gate_set(f"ps{2 + tb}", P.last["vector"][0], P.last["vector"][1])
                        else:
                            P.op("vector", "tensor_tensor", silu_sc[slot][:, :], psu[:, :],
                                 silu_sc[slot][:, :], ALU.mult)
                            gate_set(f"ps{2 + tb}", P.last["vector"][0], P.last["vector"][1])
                        if ce is not None:
                            u = P.op("vector", "tensor_tensor", gated[:, gi, cs],
                                     silu_sc[slot][:, :], ce[tb][:, :], ALU.mult, inc=dv)
                        else:
                            u = P.op("vector", "tensor_copy", out=gated[:, gi, cs],
                                     in_=silu_sc[slot][:, :], inc=dv)
                        gate_set(f"sil1{slot}", dv, u)
                gated_done = dv.n
                P.wait("tensor", dv, gated_done)
                for dc in range(NDC):
                    for tb in range(NTB):
                        cs = slice(tb * 512, (tb + 1) * 512)
                        gate_wait("tensor", "ps6")
                        for gi in range(len(hcs)):
                            td, wtd = dts[gi]
                            wsd.begin("tensor", td)
                            v = P.op("tensor", "matmul", PS[6][:, :], wtd[:, dc, :],
                                     gated[:, gi, cs],
                                     start=(gi == 0), stop=(gi == len(hcs) - 1),
                                     inc=(pe if gi == len(hcs) - 1 else None))
                        P.wait("vector", pe, v)
                        if first_overall and grp == 0:
                            u = P.op("vector", "tensor_copy", out=yacc[:, dc, cs],
                                     in_=PS[6][:, :], inc=dv)
                        else:
                            u = P.op("vector", "tensor_add", yacc[:, dc, cs], yacc[:, dc, cs],
                                     PS[6][:, :], inc=dv)
                        gate_set("ps6", dv, u)
                for gi in range(len(hcs)):
                    wsd.done(dts[gi][0], dv, u)

        gw = D["gate_w"]
        uw = D["up_w"]
        dwr = D["down_w"]
        for e in range(NEXP):
            P.wait("tensor", dv, dv.n)
            gate_wait("tensor", "ps4")
            gate_wait("tensor", "ps5")
            v = P.op("tensor", "matmul", PS[4][:, :], msel[:, e * 128:(e + 1) * 128],
                     cT[:, 0:512], start=True, stop=True, inc=pe)
            v = P.op("tensor", "matmul", PS[5][:, :], msel[:, e * 128:(e + 1) * 128],
                     cT[:, 512:1024], start=True, stop=True, inc=pe)
            P.wait("vector", pe, v)
            gwe = gw[e].rearrange("(c p) m -> p c m", p=128)
            uwe = uw[e].rearrange("(c p) m -> p c m", p=128)
            dwe = dwr[e].rearrange("(hc p) (c m) -> hc p c m", p=128, m=128)
            ffn_branch(e, gwe, uwe, dwe, MOE_HC, first_overall=(e == 0),
                       ce=[PS[4], PS[5]])
            gate_set("ps4", dv, dv.n)
            gate_set("ps5", dv, dv.n)
        sgw = D["s_gate_w"].rearrange("(c p) m -> p c m", p=128)
        suw = D["s_up_w"].rearrange("(c p) m -> p c m", p=128)
        sdw = D["s_down_w"].rearrange("(hc p) (c m) -> hc p c m", p=128, m=128)
        ffn_branch(-1, sgw, suw, sdw, SI_C, gbias_key="sgb", ubias_key="sub")
        P.wait("scalar", dv, dv.n)
        for dc in range(NDC):
            a = P.op("scalar", "activation", g_sb[:, dc, :], yacc[:, dc, :], AF.Silu,
                     bias=vcol("sdb", dc), inc=ac)
    P.barrier()
    if stop_after == "moe":
        P.wait("gpsimd", ac, ac.n)
        for c in range(NDC):
            P.op("gpsimd", "dma_start", inc=ost, incv=16,
                 out=out_dram.rearrange("(c p) t -> c p t", p=128)[c], in_=g_sb[:, c, :])
        P.wait("gpsimd", ost, ost.n)
        l1es.close()
        return

    with ExitStack() as ph:
        qh = ph.enter_context(sb("qh1", [128, TPC], BF16))
        kslot = ph.enter_context(sb("kslot1", [128, NSL_E0 + NSL_L0, 512], BF16))
        kown = ph.enter_context(sb("kown1", [128, NTB, 512], BF16))
        vslot = ph.enter_context(sb("vslot1", [128, (NSL_E0 + NSL_L0) * 4, DK], BF16))
        vown = ph.enter_context(sb("vown1", [128, NTB * 4, DK], BF16))
        cm = ph.enter_context(sb("cm1", [128, 8, NSL_E0 + NSL_L0 + 2], F32))
        mall = ph.enter_context(sb("mall1", [128, 8], F32))
        negm = ph.enter_context(sb("negm1", [1, 512], F32))
        mb = ph.enter_context(sb("mb1", [128, 512], F32))
        ssc = [ph.enter_context(sb(f"ssc{i}", [128, 512], F32)) for i in range(4)]
        pT = [ph.enter_context(sb(f"pT1{i}", [128, 512], BF16)) for i in range(4)]
        rbv = ph.enter_context(sb("rbv1", [128, 512], F32))
        attn1 = ph.enter_context(sb("attn1", [128, NDC, TPC], BF16))
        maskA_sb = ph.enter_context(sb("maskA1_sb", [128, 4, 512], F32))
        maskB_sb = ph.enter_context(sb("maskB1_sb", [128, 4, 512], F32))
        padA_b = ph.enter_context(sb("padA_b", [128, 32], F32))
        padB_c = ph.enter_context(sb("padB_c", [128, 32], F32))
        cmadj = ph.enter_context(sb("cmadj", [128, 32], F32))
        NSL = NSL_E0 + NSL_L0
        cld = P.sem("cld")
        P.op("sync", "dma_start", inc=cld, incv=16, out=maskA_sb[:, :, :], in_=D["maskA"])
        P.op("sync", "dma_start", inc=cld, incv=16, out=maskB_sb[:, :, :], in_=D["maskB"])
        P.op("sync", "dma_start", inc=cld, incv=16, out=padB_c[:, 0:27], in_=D["padB1"])
        P.op("sync", "dma_start", inc=cld, incv=16, out=padA_b[:, 0:26], in_=D["padA1"])
        P.wait("vector", cld, cld.n)
        P.wait("tensor", cld, cld.n)
        P.wait("sync", cc, ag_done)
        P.wait("tensor", cc, ag_done)
        agk = D["ag1k_out"].rearrange("(r c p) t -> r c p t", p=128, c=NDC)
        agv = D["ag1v_out"].rearrange("(b p) f -> p b f", p=128)
        agk_own = D["ag1k_in"].rearrange("(c p) t -> c p t", p=128)
        agv_own = D["ag1v_in"].rearrange("(b p) f -> p b f", p=128)
        q1d = D["q1_dram"].rearrange("(c p) t -> c p t", p=128)
        kv1 = XStream(P, "kv1", "sync", 1)

        kvE = P.sem("kv1E")
        kvL = P.sem("kv1L")
        for h in range(H):
            gate_wait("sync", "heads1")
            P.op("sync", "dma_start", inc=kvE, incv=16, out=qh[:, :], in_=q1d[h])
            for s in range(NTB):
                P.op("sync", "dma_start", inc=kvE, incv=16,
                     out=kown[:, s, :], in_=agk_own[h][:, s * 512:(s + 1) * 512])
                P.op("sync", "dma_start", inc=kvE, incv=16,
                     out=vown[:, 4 * s:4 * s + 4, :],
                     in_=agv_own[:, 4 * s:4 * s + 4, h * DK:(h + 1) * DK])
            for j in range(NSL_E0):
                g = j
                r, s = BLOCK_OWNER[g]
                P.op("sync", "dma_start", inc=kvE, incv=16,
                     out=kslot[:, j, :], in_=agk[r, h, :, s * 512:(s + 1) * 512])
                P.op("sync", "dma_start", inc=kvE, incv=16,
                     out=vslot[:, 4 * j:4 * j + 4, :],
                     in_=agv[:, 8 * r + 4 * s:8 * r + 4 * s + 4, h * DK:(h + 1) * DK])
            totE = kvE.n
            for j in range(NSL_E0, NSL):
                g = j - NSL_E0
                r, s = BLOCK_OWNER[g]
                P.op("sync", "dma_start", inc=kvL, incv=16,
                     out=kslot[:, j, :], in_=agk[r, h, :, s * 512:(s + 1) * 512])
                P.op("sync", "dma_start", inc=kvL, incv=16,
                     out=vslot[:, 4 * j:4 * j + 4, :],
                     in_=agv[:, 8 * r + 4 * s:8 * r + 4 * s + 4, h * DK:(h + 1) * DK])
            totL = kvL.n

            P.wait("tensor", kvE, totE)
            for role in range(2):
                if role == 1:
                    P.wait("tensor", kvL, totL)
                for qtl in range(4):
                    qs = role * 512 + qtl * 128
                    nsl = NSL_E0 if role == 0 else NSL_L0
                    base = 0 if role == 0 else NSL_E0
                    for jj in range(nsl + 1):
                        pi = jj % 4
                        ps = PS[pi]
                        gate_wait("tensor", f"ps{pi}")
                        rhs = kown[:, role, :] if jj == nsl else kslot[:, base + jj, :]
                        v = P.op("tensor", "matmul", ps[:, :], qh[:, qs:qs + 128], rhs,
                                 start=True, stop=True, inc=pe)
                        P.wait("vector", pe, v)
                        if jj == nsl:
                            P.op("vector", "tensor_add", ps[:, :], ps[:, :],
                                 maskA_sb[:, qtl, :])
                        col = base + jj if jj < nsl else NSL + role
                        u = P.op("vector", "reduce_max",
                                 out=cm[:, role * 4 + qtl, col:col + 1],
                                 in_=ps[:, :], axis=AX.X, inc=dv)
                        gate_set(f"ps{pi}", dv, u)
            for qtl in range(4):
                for role in range(2):
                    qi = role * 4 + qtl
                    if role == 0:
                        P.op("vector", "tensor_add", cmadj[:, 0:NSL_E0],
                             cm[:, qi, 0:NSL_E0], padA_b[:, 0:NSL_E0])
                        P.op("vector", "reduce_max", out=mall[:, qi:qi + 1],
                             in_=cmadj[:, 0:NSL_E0], axis=AX.X)
                        u = P.op("vector", "tensor_max", mall[:, qi:qi + 1],
                                 mall[:, qi:qi + 1], cm[:, qi, NSL:NSL + 1], inc=dv)
                    else:
                        P.op("vector", "tensor_add", cmadj[:, NSL_E0:NSL_E0 + NSL_L0],
                             cm[:, qi, NSL_E0:NSL_E0 + NSL_L0],
                             padA_b[:, NSL_E0:NSL_E0 + NSL_L0])
                        P.op("vector", "reduce_max", out=mall[:, qi:qi + 1],
                             in_=cmadj[:, NSL_E0:NSL_E0 + NSL_L0], axis=AX.X)
                        u = P.op("vector", "tensor_max", mall[:, qi:qi + 1],
                                 mall[:, qi:qi + 1], cm[:, qi, NSL + 1:NSL + 2], inc=dv)
            for role in range(2):
                P.wait("tensor", dv, dv.n)
                gate_wait("tensor", "ps4")
                for qtl in range(4):
                    v = P.op("tensor", "transpose", PS[4][0:1, qtl * 128:(qtl + 1) * 128],
                             mall[:, role * 4 + qtl:role * 4 + qtl + 1], id_f32[:, :],
                             inc=(pe if qtl == 3 else None))
                P.wait("vector", pe, v)
                gate_wait("vector", "negm1")
                u = P.op("vector", "tensor_copy", out=negm[0:1, :], in_=PS[4][0:1, 0:512],
                         inc=dv)
                gate_set("ps4", dv, u)
                P.wait("tensor", dv, u)
                gate_wait("tensor", "ps4")
                v = P.op("tensor", "matmul", PS[4][:, :], ones_row[0:1, :], negm[0:1, :],
                         start=True, stop=True, inc=pe)
                gate_set("negm1", pe, v)
                P.wait("vector", pe, v)
                gate_wait("vector", "mb1")
                u = P.op("vector", "tensor_copy", out=mb[:, :], in_=PS[4][:, :], inc=dv)
                gate_set("ps4", dv, u)
                P.wait("tensor", dv, u)
                mb_ready = u
                nsl = NSL_E0 if role == 0 else NSL_L0
                base = 0 if role == 0 else NSL_E0
                qs = slice(role * 512, (role + 1) * 512)
                first = True
                gate_wait("tensor", "ps6")
                gate_wait("tensor", "ps5")
                nkc = (nsl + 1) * 4
                for t_i in range(nkc):
                    jj, kc = divmod(t_i, 4)
                    own = jj == nsl
                    pi = t_i % 4
                    psb = PS[pi]
                    gate_wait("tensor", f"ps{pi}")
                    if own:
                        lhsT = kown[:, role, kc * 128:(kc + 1) * 128]
                    else:
                        lhsT = kslot[:, base + jj, kc * 128:(kc + 1) * 128]
                    v = P.op("tensor", "matmul", psb[:, :], lhsT, qh[:, qs],
                             start=True, stop=True, inc=pe)
                    P.wait("vector", pe, v)
                    if own:
                        P.op("vector", "tensor_add", psb[:, :], psb[:, :],
                             maskB_sb[:, kc, :])
                    slot_s = t_i % 4
                    gate_wait("vector", f"ssc{slot_s}")
                    padcol = padB_c[:, 26:27] if own else padB_c[:, base + jj:base + jj + 1]
                    u = P.op("vector", "scalar_tensor_tensor", ssc[slot_s][:, :], psb[:, :],
                             padcol, mb[:, :], ALU.add, ALU.subtract, inc=dv)
                    gate_set(f"ps{pi}", dv, u)
                    P.wait("scalar", dv, u)
                    pt = pT[t_i % 4]
                    gate_wait("scalar", f"pT1{t_i % 4}")
                    a = P.op("scalar", "activation", pt[:, :], ssc[slot_s][:, :], AF.Exp,
                             inc=ac)
                    gate_set(f"ssc{slot_s}", ac, a)
                    P.wait("tensor", ac, a)
                    vv = vown[:, 4 * role + kc, :] if own else vslot[:, 4 * (base + jj) + kc, :]
                    P.op("tensor", "matmul", PS[5][0:1, :], shared["ones_col_bf"][:, :],
                         pt[:, :], start=first, stop=(t_i == nkc - 1))
                    u = P.op("tensor", "matmul", PS[6][:, :], vv, pt[:, :],
                             start=first, stop=(t_i == nkc - 1), inc=pe)
                    gate_set(f"pT1{t_i % 4}", pe, u)
                    first = False
                P.wait("vector", pe, pe.n)
                gate_wait("vector", "negm1")
                u = P.op("vector", "reciprocal", inc=dv, out=negm[0:1, :], in_=PS[5][0:1, :])
                gate_set("ps5", dv, u)
                P.wait("tensor", dv, u)
                gate_wait("tensor", "ps4")
                v = P.op("tensor", "matmul", PS[4][:, :], ones_row[0:1, :], negm[0:1, :],
                         start=True, stop=True, inc=pe)
                gate_set("negm1", pe, v)
                P.wait("vector", pe, v)
                P.op("vector", "tensor_copy", out=rbv[:, :], in_=PS[4][:, :])
                gate_set("ps4", P.last["vector"][0], P.last["vector"][1])
                u = P.op("vector", "tensor_tensor", attn1[:, h, qs], PS[6][:, :],
                         rbv[:, :], ALU.mult, inc=dv)
                gate_set("ps6", dv, u)
                gate_set("mb1", dv, u)
            gate_set("heads1", dv, dv.n)
        if stop_after == "attn1":
            P.wait("gpsimd", dv, dv.n)
            for c in range(NDC):
                P.op("gpsimd", "dma_start", inc=ost, incv=16,
                     out=out_dram.rearrange("(c p) t -> c p t", p=128)[c], in_=attn1[:, c, :])
            P.wait("gpsimd", ost, ost.n)
    P.barrier()
    if stop_after == "attn1":
        l1es.close()
        return

    with ExitStack() as ph:
        xbuf = [ph.enter_context(sb(f"xo{i}", [128, TPC], F32)) for i in range(2)]
        x1b = [ph.enter_context(sb(f"xob{i}", [128, TPC], F32)) for i in range(2)]
        tsc = ph.enter_context(sb("tsc1", [128, TPC], F32))
        nm_b = ph.enter_context(sb("nm1b2", [128, TPC], F32))
        rstd_b = ph.enter_context(sb("rstd1b2", [128, TPC], F32))
        ws = WStream(P, "wo", 2, (128, NDC, 128), es=ph)
        wo = D["wo_t"].rearrange("(c p) m -> p c m", p=128)
        xs2 = XStream(P, "xo", "sync", 2)
        st2 = XStream(P, "xos", "sync", 2)
        st_toks = []
        for tb in range(NTB):
            cs = slice(tb * 512, (tb + 1) * 512)
            gate_wait("tensor", "ps0")
            gate_wait("tensor", "ps1")
            P.op("tensor", "matmul", PS[0][:, :], ones_row[0:1, :], nm_row[:, cs],
                 start=True, stop=True)
            v = P.op("tensor", "matmul", PS[1][:, :], ones_row[0:1, :], rstd_row[:, cs],
                     start=True, stop=True, inc=pe)
            P.wait("vector", pe, v)
            u0 = P.op("vector", "tensor_copy", out=nm_b[:, cs], in_=PS[0][:, :], inc=dv)
            u = P.op("vector", "tensor_copy", out=rstd_b[:, cs], in_=PS[1][:, :], inc=dv)
            gate_set("ps0", dv, u0)
            gate_set("ps1", dv, u)
        for dc in range(NDC):
            t, wt = ws.load(wo[:, :, dc * 128:(dc + 1) * 128])
            slot = dc % 2
            gate_wait("sync", f"xo{slot}")
            xtok = xs2.dma(slot, xbuf[slot][:], xr[dc])
            if dc >= 2:
                st2.wait("vector", st_toks[dc - 2])
            for tb in range(NTB):
                cs = slice(tb * 512, (tb + 1) * 512)
                ps = PS[tb]
                gate_wait("tensor", f"ps{tb}")
                ws.begin("tensor", t)
                for k in range(NDC):
                    v = P.op("tensor", "matmul", ps[:, :], wt[:, k, :], attn1[:, k, cs],
                             start=(k == 0), stop=(k == NDC - 1),
                             inc=(pe if k == NDC - 1 else None))
                if tb == NTB - 1:
                    ws.done(t, pe, v)
                P.wait("vector", pe, v)
                xs2.wait("vector", xtok)
                P.op("vector", "tensor_add", tsc[:, cs], xbuf[slot][:, cs], nm_b[:, cs])
                P.op("vector", "tensor_mul", tsc[:, cs], tsc[:, cs], rstd_b[:, cs])
                P.op("vector", "scalar_tensor_tensor", tsc[:, cs], ps[:, :],
                     vcol("bo", dc), tsc[:, cs], ALU.add, ALU.add)
                gate_set(f"ps{tb}", P.last["vector"][0], P.last["vector"][1])
                P.op("vector", "tensor_mul", tsc[:, cs], tsc[:, cs], g_sb[:, dc, cs])
                u = P.op("vector", "tensor_add", x1b[slot][:, cs], tsc[:, cs],
                         xbuf[slot][:, cs], inc=dv)
            gate_set(f"xo{slot}", dv, u)
            P.wait("sync", dv, u)
            st_toks.append(st2.dma(slot, out_dram.rearrange("(c p) t -> c p t", p=128)[dc],
                                   x1b[slot][:]))
        for tok in st_toks[-2:]:
            st2.wait("sync", tok)
    l1es.close()
    P.barrier()


def declare_dram_l1(nc, with_out=True, x_ap=None, skip=()):
    D = {}

    def inp(name, shape, dt=F32):
        if name in skip:
            return
        D[name] = nc.dram_tensor(name, list(shape), dt, kind="ExternalInput").ap()

    if x_ap is None:
        inp("x", (DM, TPC))
    else:
        D["x"] = x_ap
    inp("wq_t", (DM, DM))
    inp("wk_t", (DM, DM))
    inp("wv_t", (DM, DM))
    inp("wo_t", (DM, DM))
    inp("bq1", (DM,))
    inp("bk1", (DM,))
    inp("bv1", (DM,))
    inp("bo1", (DM,))
    inp("lng1", (DM,))
    inp("lnb1", (DM,))
    inp("router_t", (DM, 8))
    inp("gate_w", (NEXP, DM, MOE_I))
    inp("up_w", (NEXP, DM, MOE_I))
    inp("down_w", (NEXP, MOE_I, DM))
    inp("s_gate_w", (DM, SI))
    inp("sgb1", (SI,))
    inp("s_up_w", (DM, SI))
    inp("sub1", (SI,))
    inp("s_down_w", (SI, DM))
    inp("sdb1", (DM,))
    inp("id_f32", (128, 128))
    inp("RneoxT", (128, 128), BF16)
    inp("cos1", (128, TPC))
    inp("sin1", (128, TPC))
    inp("cos1q", (128, TPC))
    inp("sin1q", (128, TPC))
    inp("maskA_h", (4, 128, 512))
    inp("maskB_h", (4, 128, 512))
    inp("moe_sel", (8, 8 * 128))
    inp("padA1", (128, 26))
    inp("padB1", (128, 27))
    if "maskA_h" in D:
        D["maskA"] = D["maskA_h"].rearrange("d p f -> p d f")
        D["maskB"] = D["maskB_h"].rearrange("d p f -> p d f")
    D["ag1k_in"] = nc.dram_tensor("ag1k_in", [DM, TPC], BF16).ap()
    D["ag1k_out"] = nc.dram_tensor("ag1k_out", [NCORE * DM, TPC], BF16).ap()
    D["ag1v_in"] = nc.dram_tensor("ag1v_in", [TPC, DM], BF16).ap()
    D["ag1v_out"] = nc.dram_tensor("ag1v_out", [SEQ, DM], BF16).ap()
    D["q1_dram"] = nc.dram_tensor("q1_dram", [DM, TPC], BF16).ap()
    if with_out:
        D["xout"] = nc.dram_tensor("xout", [DM, TPC], F32, kind="ExternalOutput").ap()
    return D


def host_stage_l1(core, inputs, cst, x1):
    blocks = CORE_BLOCKS[core]
    T = lambda a: np.ascontiguousarray(np.asarray(a).T)
    m = {}
    if x1 is not None:
        xs = np.concatenate([x1[g * 512:(g + 1) * 512] for g in blocks], 0)
        m["x"] = np.ascontiguousarray(xs.T.astype(np.float32))
    m.update({
        "wq_t": T(inputs["L1_attn_wq"]),
        "wk_t": T(inputs["L1_attn_wk"]),
        "wv_t": T(inputs["L1_attn_wv"]),
        "wo_t": T(inputs["L1_attn_wo"]),
        "bq1": inputs["L1_attn_bq"],
        "bk1": inputs["L1_attn_bk"],
        "bv1": inputs["L1_attn_bv"],
        "bo1": inputs["L1_attn_bo"],
        "lng1": inputs["L1_attn_ln_g"],
        "lnb1": inputs["L1_attn_ln_b"],
        "router_t": T(inputs["L1_ffn_router_w"]),
        "gate_w": inputs["L1_ffn_gate_w"],
        "up_w": inputs["L1_ffn_up_w"],
        "down_w": inputs["L1_ffn_down_w"],
        "s_gate_w": inputs["L1_ffn_s_gate_w"],
        "sgb1": inputs["L1_ffn_s_gate_b"],
        "s_up_w": inputs["L1_ffn_s_up_w"],
        "sub1": inputs["L1_ffn_s_up_b"],
        "s_down_w": inputs["L1_ffn_s_down_w"],
        "sdb1": inputs["L1_ffn_s_down_b"],
        "id_f32": cst["id_f32"],
        "RneoxT": cst["RneoxT"],
        "cos1": cst["cos1"],
        "sin1": cst["sin1"],
        "cos1q": cst["cos1q"],
        "sin1q": cst["sin1q"],
        "maskA_h": cst["maskA"],
        "maskB_h": cst["maskB"],
        "moe_sel": cst["moe_sel"],
    })
    padA = np.zeros((1, 26), np.float32)
    padB = np.zeros((1, 26), np.float32)
    for j in range(NSL_E0):
        if j >= blocks[0]:
            padA[0, j] = NEG
            padB[0, j] = NEG
    for j in range(NSL_L0):
        if j >= blocks[1]:
            padA[0, NSL_E0 + j] = NEG
            padB[0, NSL_E0 + j] = NEG
    m["padA1"] = np.broadcast_to(padA, (128, 26)).copy()
    padB27 = np.concatenate([padB, np.zeros((1, 1), np.float32)], 1)
    m["padB1"] = np.broadcast_to(padB27, (128, 27)).copy()
    out = {}
    for k, v in m.items():
        a = np.asarray(v)
        if k == "RneoxT":
            out[k] = np.ascontiguousarray(a.astype(bfloat16))
        else:
            out[k] = np.ascontiguousarray(a.astype(np.float32))
    return out


def build_fused_program():
    nc = bass.Bass("TRN2", target_bir_lowering=False, debug=False)
    es = ExitStack()
    D0 = declare_dram_l0(nc, with_out=False)
    x1d = nc.dram_tensor("x1_mid", [DM, TPC], F32)
    D1 = declare_dram_l1(nc, with_out=True, x_ap=x1d.ap(), skip=("id_f32", "maskA_h", "maskB_h"))
    D1["id_f32"] = D0["id_f32"]
    D1["maskA"] = D0["maskA"]
    D1["maskB"] = D0["maskB"]
    DD = dict(D0)
    DD.update({k: v for k, v in D1.items() if k not in DD})
    P = Prog(nc, es)
    shared = emit_shared(P, nc, DD, es, layers=[0, 1])
    emit_l0(P, nc, D0, shared, x1d.ap())
    emit_l1(P, nc, D1, shared, D1["xout"])
    P.finish()
    es.close()
    return nc


def build_l1_program(stop_after=None):
    nc = bass.Bass("TRN2", target_bir_lowering=False, debug=False)
    es = ExitStack()
    D = declare_dram_l1(nc)
    P = Prog(nc, es)
    shared = emit_shared(P, nc, D, es, layers=[1])
    emit_l1(P, nc, D, shared, D["xout"], stop_after=stop_after)
    P.finish()
    es.close()
    return nc



_CST = [host_constants(c) for c in range(NCORE)]


def kernel(**inputs):
    nc = build_fused_program()
    in_maps = []
    for c in range(NCORE):
        m = host_stage_l0(c, inputs, _CST[c])
        m1 = host_stage_l1(c, inputs, _CST[c], None)
        for k, v in m1.items():
            if k not in m and k != "x":
                m[k] = v
        in_maps.append(m)
    res = run_bass_kernel_spmd(nc, in_maps, core_ids=list(range(NCORE)))
    out = np.zeros((1, SEQ, DM), np.float32)
    for c in range(NCORE):
        sh = np.asarray(res.results[c]["xout"])
        b = CORE_BLOCKS[c]
        out[0, b[0] * 512:(b[0] + 1) * 512] = sh.T[:512]
        out[0, b[1] * 512:(b[1] + 1) * 512] = sh.T[512:]
    return out
